# version 23
# baseline (speedup 1.0000x reference)
import contextlib
import numpy as np
import ml_dtypes
import concourse.bass as bass
import concourse.mybir as mybir
from concourse.bass_utils import run_bass_kernel_spmd

F32 = mybir.dt.float32
BF16 = mybir.dt.bfloat16
U8 = mybir.dt.uint8
AF = mybir.ActivationFunctionType
ALU = mybir.AluOpType
AX = mybir.AxisListType

B, S, D = 4, 4096, 1024
T = 2048
NCORES = 8
DC = 8
MAGIC = float(np.float32(3 << 22))
LN_EPS = 1e-5



class AS:
    OPS = {"tensor_tensor", "tensor_scalar", "tensor_copy", "scalar_tensor_tensor",
           "tensor_tensor_scan", "reciprocal", "memset", "tensor_reduce",
           "select", "transpose", "iota", "activation", "copy", "add", "mul",
           "tensor_single_scalar", "tensor_scalar_add", "tensor_scalar_mul"}

    def __init__(self, eng, sem):
        self._e = eng
        self._s = sem

    def __getattr__(self, name):
        attr = getattr(self._e, name)
        if name in AS.OPS:
            sem = self._s

            def wrapped(*a, **k):
                self._e.wait_ge(sem.h, sem.n)
                inst = attr(*a, **k)
                inst.then_inc(sem.h, 1)
                sem.n += 1
                return inst
            return wrapped
        return attr



class DirectBlock:
    def __init__(self, nc):
        self._nc = nc

    def tensor(self, f):
        return f(self._nc.tensor)

    def vector(self, f):
        return f(self._nc.vector)

    def scalar(self, f):
        return f(self._nc.scalar)

    def gpsimd(self, f):
        return f(self._nc.gpsimd)

    def sync(self, f):
        return f(self._nc.sync)


class Sem:
    def __init__(self, es, nc, name):
        self.h = es.enter_context(nc.semaphore(name))
        self.n = 0

    def inc(self, k=1):
        self.n += k
        return self.h


def fmT(x):
    return np.ascontiguousarray(np.asarray(x, np.float32).T)


def pack_vec(v):
    v = np.asarray(v, np.float32)
    return np.ascontiguousarray(v.reshape(-1, 128).T)



def emit_l0_attn(nc, es, block, x_in, x_out, w_aps, vec_aps, scratch, ses=None, hook=None):
    ses = es if ses is None else ses
    """L0_attn + residual + LN. x_in/x_out: DRAM APs, fm f32 [1024, T]."""
    TB = 256
    NTB = T // TB

    wt = {w: es.enter_context(nc.sbuf_tensor("l0a_" + w, [128, DC * D], BF16))
          for w in ["wp1t", "wp2t", "convt", "wat", "wxt"]}
    wt["wp3t"] = wt["wp1t"]
    vecs = {v: es.enter_context(nc.sbuf_tensor("l0a_" + v, [128, 8], F32))
            for v in ["bp1", "bp2", "convb", "ba", "bx", "lam", "bp3", "sel"]}
    lamlog = es.enter_context(nc.sbuf_tensor("lamlog", [128, 8], F32))
    vtmp = es.enter_context(nc.sbuf_tensor("vtmp", [128, 8], F32))
    onesb = es.enter_context(nc.sbuf_tensor("onesb", [128, 1], BF16))
    ones_row = es.enter_context(nc.sbuf_tensor("ones_row", [1, 128], F32))
    carry = es.enter_context(nc.sbuf_tensor("carry", [128, 8], F32))
    htail = es.enter_context(nc.sbuf_tensor("htail", [128, 8], F32))

    f32t = {n: es.enter_context(nc.sbuf_tensor("l0a_" + n, [128, DC * TB], F32))
            for n in ["x1", "x2b", "bR", "bI", "bU", "h0", "h1", "p0", "p1"]}
    bf16t = {n: es.enter_context(nc.sbuf_tensor("l0a_" + n, [128, DC * TB], BF16))
             for n in ["xb0", "xb1", "x2a", "x2bb", "prodb", "corrb"]}
    stats = es.enter_context(nc.sbuf_tensor("l0a_stats", [1, 3 * TB], F32))
    ybf = es.enter_context(nc.sbuf_tensor("l0a_ybf", [128, TB], BF16))
    ysq = es.enter_context(nc.sbuf_tensor("l0a_ysq", [128, TB], BF16))

    ps = [es.enter_context(nc.psum_tensor(f"l0a_ps{i}", [128, 512], F32)) for i in range(6)]

    sgw = Sem(ses, nc, "l0a_sgw")
    sxb = [Sem(ses, nc, f"l0a_sxb{i}") for i in range(2)]
    spst = Sem(ses, nc, "l0a_spst")
    sld = [Sem(ses, nc, f"l0a_sld{i}") for i in range(2)]
    sgc = Sem(ses, nc, "l0a_sgc")
    st = Sem(ses, nc, "l0a_st")
    sv = Sem(ses, nc, "l0a_sv")
    ss = Sem(ses, nc, "l0a_ss")
    scc = Sem(ses, nc, "l0a_scc")
    so = Sem(ses, nc, "l0a_so")

    x_in_r = x_in.rearrange("(c p) t -> p c t", p=128)
    x_out_r = x_out.rearrange("(c p) t -> p c t", p=128)
    prod_r = scratch["prod_hbm"]
    corr_r = scratch["corr_hbm"]

    def g_loads(g):
        for w in ["wp1t", "wp2t", "convt", "wat", "wxt"]:
            tile = wt[w]
            for c in range(DC):
                g.dma_start(out=tile[:, c * D:(c + 1) * D],
                            in_=w_aps[w][c * 128:(c + 1) * 128, :]).then_inc(sgw.inc(16), 16)
        for v, tile in vecs.items():
            g.dma_start(out=tile[:], in_=vec_aps[v]).then_inc(sgw.inc(16), 16)
        for i in range(2):
            for c in range(DC):
                g.dma_start(out=bf16t[f"xb{i}"][:, c * TB:(c + 1) * TB],
                            in_=x_in_r[:, c, i * TB:(i + 1) * TB]).then_inc(sxb[i].inc(16), 16)
    block.gpsimd(g_loads)
    w_loaded = sgw.n
    xb_load = {0: (sxb[0].h, sxb[0].n), 1: (sxb[1].h, sxb[1].n)}

    def v_init(v):
        v = AS(v, sv)
        v.wait_ge(sgw.h, w_loaded)
        v.memset(onesb[:], 1.0)
        v.memset(ones_row[:], 1.0)
    block.vector(v_init)
    n0 = sv.n

    def s_init(s):
        s = AS(s, ss)
        s.wait_ge(sv.h, n0)
        s.wait_ge(sgw.h, w_loaded)
        s.activation(vtmp[:], vecs["lam"][:], AF.Abs)
        s.activation(vtmp[:], vtmp[:], AF.Exp, scale=-1.0)
        s.activation(vtmp[:], vtmp[:], AF.Ln, bias=1.0)
    block.scalar(s_init)
    n1 = ss.n

    def v_init2(v):
        v = AS(v, sv)
        v.wait_ge(ss.h, n1)
        v.tensor_scalar(lamlog[:], vecs["lam"][:], -1.0, 0.0, ALU.mult, ALU.max)
        v.scalar_tensor_tensor(lamlog[:], lamlog[:], 1.0, vtmp[:], ALU.mult, ALU.add)
        v.tensor_scalar(lamlog[:], lamlog[:], -8.0, None, ALU.mult)
    block.vector(v_init2)
    lamlog_done = sv.n

    bank_free = [0] * 6
    sv_after_scan = [0] * NTB
    prod_stored = [0] * NTB

    def mm_group(wname, in_tile, out_eval, gate=None):
        for oc in range(DC):
            b = oc % 4
            bf = bank_free[b]
            g8 = gate if oc == 0 else None

            def t_sec(t, oc=oc, b=b, bf=bf, g8=g8):
                if g8 is not None:
                    for gh, gv in g8:
                        t.wait_ge(gh, gv)
                if bf:
                    t.wait_ge(ss.h, bf)
                for dcc in range(DC):
                    mm = t.matmul(ps[b][:, 0:TB],
                                  wt[wname][:, dcc * D + oc * 128: dcc * D + (oc + 1) * 128],
                                  in_tile[:, dcc * TB:(dcc + 1) * TB],
                                  start=(dcc == 0), stop=(dcc == DC - 1))
                mm.then_inc(st.inc())
            block.tensor(t_sec)
            mm_done = st.n

            dest, bias_ap, actf = out_eval(oc)

            def s_sec(s, dest=dest, bias_ap=bias_ap, actf=actf, b=b, mm_done=mm_done):
                s = AS(s, ss)
                s.wait_ge(st.h, mm_done)
                s.activation(dest, ps[b][:, 0:TB], actf, bias=bias_ap)
            block.scalar(s_sec)
            bank_free[b] = ss.n

    for tb in range(NTB):
        xb = bf16t[f"xb{tb % 2}"]
        hprev, hcur = f32t[f"h{1 - tb % 2}"], f32t[f"h{tb % 2}"]
        pprev, pcur = f32t[f"p{1 - tb % 2}"], f32t[f"p{tb % 2}"]

        mm_group("wp2t", xb,
                 lambda oc: (bf16t["x2a"][:, oc * TB:(oc + 1) * TB], vecs["bp2"][:, oc:oc + 1], AF.Identity),
                 gate=[(sgw.h, w_loaded), xb_load[tb]])
        x2a_done = ss.n
        if hook is not None:
            hook("mid")

        gate_sv = sv_after_scan[tb - 1] if tb >= 1 else lamlog_done

        def t_gate_x2b(t, x2a_done=x2a_done, gate_sv=gate_sv):
            t.wait_ge(ss.h, x2a_done)
            t.wait_ge(sv.h, gate_sv)
        block.tensor(t_gate_x2b)
        mm_group("convt", bf16t["x2a"],
                 lambda oc: (f32t["x2b"][:, oc * TB:(oc + 1) * TB], vecs["convb"][:, oc:oc + 1], AF.Relu))
        x2b_done = ss.n

        def v_cast(v, x2b_done=x2b_done):
            v = AS(v, sv)
            v.wait_ge(ss.h, x2b_done)
            v.tensor_copy(bf16t["x2bb"][:], f32t["x2b"][:])
        block.vector(v_cast)
        x2bb_done = sv.n

        if hook is not None:
            hook("mid")
        mm_group("wat", bf16t["x2bb"],
                 lambda oc: (f32t["bR"][:, oc * TB:(oc + 1) * TB], vecs["ba"][:, oc:oc + 1], AF.Sigmoid),
                 gate=[(sv.h, x2bb_done)])
        if hook is not None:
            hook("mid")
        mm_group("wxt", bf16t["x2bb"],
                 lambda oc: (f32t["bI"][:, oc * TB:(oc + 1) * TB], vecs["bx"][:, oc:oc + 1], AF.Sigmoid))
        ri_done = ss.n
        if hook is not None:
            hook("mid")
        mm_group("wp1t", xb,
                 lambda oc: (f32t["x1"][:, oc * TB:(oc + 1) * TB], vecs["bp1"][:, oc:oc + 1], AF.Gelu))
        x1_done = ss.n
        if hook is not None:
            hook("mid")
        st_tb = st.n

        if tb + 2 < NTB:
            def g_pref(g, tb=tb, st_tb=st_tb):
                g.wait_ge(st.h, st_tb)
                for c in range(DC):
                    g.dma_start(out=bf16t[f"xb{tb % 2}"][:, c * TB:(c + 1) * TB],
                                in_=x_in_r[:, c, (tb + 2) * TB:(tb + 3) * TB]).then_inc(sxb[tb % 2].inc(16), 16)
            block.gpsimd(g_pref)
            xb_load[tb + 2] = (sxb[tb % 2].h, sxb[tb % 2].n)

        def v_g1(v, ri_done=ri_done):
            v = AS(v, sv)
            v.wait_ge(ss.h, ri_done)
            for rc in range(DC):
                sl = slice(rc * TB, (rc + 1) * TB)
                v.tensor_scalar(f32t["bU"][:, sl], f32t["bR"][:, sl],
                                lamlog[:, rc:rc + 1], None, ALU.mult)
        block.vector(v_g1)
        nvu = sv.n

        def s_g1(s, nvu=nvu):
            s = AS(s, ss)
            s.wait_ge(sv.h, nvu)
            s.activation(f32t["bR"][:], f32t["bU"][:], AF.Exp)
        block.scalar(s_g1)
        na = ss.n

        def v_g2(v, na=na):
            v = AS(v, sv)
            v.wait_ge(ss.h, na)
            v.tensor_tensor(f32t["bU"][:], f32t["bI"][:], f32t["x2b"][:], ALU.mult)
            v.tensor_tensor(f32t["bI"][:], f32t["bR"][:], f32t["bR"][:], ALU.mult)
        block.vector(v_g2)
        nm = sv.n

        def s_g2(s, nm=nm):
            s = AS(s, ss)
            s.wait_ge(sv.h, nm)
            s.activation(f32t["bI"][:], f32t["bI"][:], AF.Sqrt, scale=-1.0, bias=1.0)
        block.scalar(s_g2)
        nsq = ss.n

        def v_g3(v, nsq=nsq, tb=tb, hprev=hprev, hcur=hcur, pprev=pprev, pcur=pcur, x1_done=x1_done,
                 pst_need=32 * tb):
            v = AS(v, sv)
            v.wait_ge(ss.h, nsq)
            if pst_need:
                v.wait_ge(spst.h, pst_need)
            v.tensor_tensor(f32t["bU"][:], f32t["bI"][:], f32t["bU"][:], ALU.mult)
            for dcc in range(DC):
                sl = slice(dcc * TB, (dcc + 1) * TB)
                hi = 0.0 if tb == 0 else hprev[:, dcc * TB + TB - 1: dcc * TB + TB]
                pi = 1.0 if tb == 0 else pprev[:, dcc * TB + TB - 1: dcc * TB + TB]
                v.tensor_tensor_scan(hcur[:, sl], f32t["bR"][:, sl], f32t["bU"][:, sl],
                                     hi, ALU.mult, ALU.add)
                v.tensor_tensor_scan(pcur[:, sl], f32t["bR"][:, sl], f32t["bU"][:, sl],
                                     pi, ALU.mult, ALU.bypass)
            v.wait_ge(ss.h, x1_done)
            v.tensor_tensor(bf16t["prodb"][:], f32t["x1"][:], hcur[:], ALU.mult)
            v.tensor_tensor(bf16t["corrb"][:], f32t["x1"][:], pcur[:], ALU.mult)
        block.vector(v_g3)
        sv_after_scan[tb] = sv.n
        nprod = sv.n

        def g_store(g, tb=tb, nprod=nprod):
            g.wait_ge(sv.h, nprod)
            g.dma_start(out=prod_r[:, tb * DC * TB:(tb + 1) * DC * TB], in_=bf16t["prodb"][:]).then_inc(spst.inc(16), 16)
            g.dma_start(out=corr_r[:, tb * DC * TB:(tb + 1) * DC * TB], in_=bf16t["corrb"][:]).then_inc(spst.inc(16), 16)
        block.gpsimd(g_store)
        prod_stored[tb] = spst.n
        if hook is not None:
            hook(tb)

    sgw2 = Sem(ses, nc, "l0a_sgw2")

    def g_wp3(g, gate=st.n):
        g.wait_ge(st.h, gate)
        for c in range(DC):
            g.dma_start(out=wt["wp3t"][:, c * D:(c + 1) * D],
                        in_=w_aps["wp3t"][c * 128:(c + 1) * 128, :]).then_inc(sgw2.inc(16), 16)
    block.gpsimd(g_wp3)
    wp3_loaded = sgw2.n

    hlast = f32t[f"h{(NTB - 1) % 2}"]
    nall = sv_after_scan[-1]

    def v_tail(v, nall=nall):
        v = AS(v, sv)
        v.wait_ge(sv.h, nall)
        v.tensor_copy(htail[:], hlast[:, TB - 1::TB])
    block.vector(v_tail)
    ntail = sv.n

    def g_cc(g, ntail=ntail):
        g.wait_ge(sv.h, ntail)
        g.dma_start(out=scratch["cc_in"].ap(), in_=htail[:]).then_inc(sgc.inc(16), 16)
        g.wait_ge(sgc.h, sgc.n)
        g.collective_compute(
            "AllGather", ALU.bypass,
            replica_groups=[[0, 1], [2, 3], [4, 5], [6, 7]],
            ins=[scratch["cc_in"].ap()],
            outs=[scratch["cc_out"].ap()],
        ).then_inc(scc.inc())
        g.wait_ge(scc.h, scc.n)
        g.dma_start(out=carry[:], in_=scratch["cc_out"].ap()[0:128, :]).then_inc(sgc.inc(16), 16)
    block.gpsimd(g_cc)
    ncarry = sgc.n

    def v_carry(v, ncarry=ncarry):
        v = AS(v, sv)
        v.wait_ge(sgc.h, ncarry)
        v.tensor_tensor(carry[:], carry[:], vecs["sel"][:], ALU.mult)
    block.vector(v_carry)
    carry_done = sv.n

    rhs2 = [bf16t["x2a"], bf16t["x2bb"]]
    pin2 = [bf16t["xb0"], bf16t["xb1"]]
    cin2 = [bf16t["prodb"], bf16t["corrb"]]
    xres = [f32t["x1"], f32t["x2b"]]
    yout = f32t["bR"]
    xnew = f32t["bI"]

    p2_load = [0] * NTB
    p2_fix = [0] * NTB
    p2_proj_st = [0] * NTB
    p2_resid = [0] * NTB
    p2_store = [0] * NTB
    bank2 = {0: bank_free[0], 1: bank_free[1]}
    nbc_prev = 0
    stats_read_prev = 0

    def emit_p2_load(tb):
        g1 = (sv.h, carry_done) if tb < 2 else (sv.h, max(p2_fix[tb - 2], p2_resid[tb - 2]))
        g2 = (spst.h, prod_stored[NTB - 1]) if tb < 2 else (st.h, p2_proj_st[tb - 2])
        sem = sld[tb % 2]

        def s_load(s, tb=tb, g1=g1, g2=g2, sem=sem):
            s = AS(s, ss)
            s.wait_ge(*g1)
            s.wait_ge(*g2)
            for c in range(DC):
                s.dma_start(out=xres[tb % 2][:, c * TB:(c + 1) * TB],
                            in_=x_in_r[:, c, tb * TB:(tb + 1) * TB]).then_inc(sem.inc(16), 16)
            s.dma_start(out=pin2[tb % 2][:], in_=prod_r[:, tb * DC * TB:(tb + 1) * DC * TB]).then_inc(sem.inc(16), 16)
            s.dma_start(out=cin2[tb % 2][:], in_=corr_r[:, tb * DC * TB:(tb + 1) * DC * TB]).then_inc(sem.inc(16), 16)
        block.sync(s_load)
        p2_load[tb] = (sem.h, sem.n)

    emit_p2_load(0)
    emit_p2_load(1)

    for tb in range(NTB):
        loaded = p2_load[tb]
        def v_fix(v, tb=tb, loaded=loaded):
            v = AS(v, sv)
            v.wait_ge(loaded[0], loaded[1])
            v.wait_ge(sv.h, carry_done)
            if tb >= 2:
                v.wait_ge(st.h, p2_proj_st[tb - 2])
            for hc in range(DC):
                sl = slice(hc * TB, (hc + 1) * TB)
                v.scalar_tensor_tensor(rhs2[tb % 2][:, sl], cin2[tb % 2][:, sl],
                                       carry[:, hc:hc + 1], pin2[tb % 2][:, sl], ALU.mult, ALU.add)
        block.vector(v_fix)
        p2_fix[tb] = sv.n
        nrhs = sv.n

        for dcc in range(DC):
            b = dcc % 2
            bf = bank2[b]

            def t_proj(t, dcc=dcc, b=b, nrhs=nrhs, bf=bf, tb=tb):
                if dcc == 0:
                    t.wait_ge(sv.h, nrhs)
                if dcc == 0 and tb == 0:
                    t.wait_ge(sgw2.h, wp3_loaded)
                if bf:
                    t.wait_ge(ss.h, bf)
                for hc in range(DC):
                    mm = t.matmul(ps[b][:, 0:TB],
                                  wt["wp3t"][:, hc * D + dcc * 128: hc * D + (dcc + 1) * 128],
                                  rhs2[tb % 2][:, hc * TB:(hc + 1) * TB],
                                  start=(hc == 0), stop=(hc == DC - 1))
                mm.then_inc(st.inc())
            block.tensor(t_proj)
            nmm = st.n

            def s_evac(s, dcc=dcc, b=b, nmm=nmm, tb=tb):
                s = AS(s, ss)
                s.wait_ge(st.h, nmm)
                if dcc == 0 and tb > 0:
                    s.wait_ge(sv.h, p2_resid[tb - 1])
                s.activation(yout[:, dcc * TB:(dcc + 1) * TB], ps[b][:, 0:TB],
                             AF.Identity, bias=vecs["bp3"][:, dcc:dcc + 1])
            block.scalar(s_evac)
            bank2[b] = ss.n
        proj_done = ss.n
        p2_proj_st[tb] = st.n

        def v_resid(v, tb=tb, proj_done=proj_done):
            v = AS(v, sv)
            v.wait_ge(ss.h, proj_done)
            if tb > 0:
                v.wait_ge(so.h, p2_store[tb - 1])
            v.tensor_tensor(xnew[:], yout[:], xres[tb % 2][:], ALU.add)
        block.vector(v_resid)
        p2_resid[tb] = sv.n
        nresid = sv.n

        stat_st = []
        for dcc in range(DC):
            def v_cast2(v, dcc=dcc, nresid=nresid):
                v = AS(v, sv)
                if dcc == 0:
                    v.wait_ge(sv.h, nresid)
                else:
                    v.wait_ge(st.h, stat_st[dcc - 1])
                v.tensor_copy(ybf[:], xnew[:, dcc * TB:(dcc + 1) * TB])
            block.vector(v_cast2)
            ncast = sv.n

            def s_sq(s, dcc=dcc, nresid=nresid):
                s = AS(s, ss)
                if dcc == 0:
                    s.wait_ge(sv.h, nresid)
                else:
                    s.wait_ge(st.h, stat_st[dcc - 1])
                s.activation(ysq[:], xnew[:, dcc * TB:(dcc + 1) * TB], AF.Square)
            block.scalar(s_sq)
            nsq2 = ss.n

            def t_stat(t, dcc=dcc, ncast=ncast, nsq2=nsq2, tb=tb):
                t.wait_ge(sv.h, ncast)
                t.wait_ge(ss.h, nsq2)
                if dcc == 0 and tb > 0:
                    t.wait_ge(sv.h, stats_read_prev)
                t.matmul(ps[2][0:1, 0:TB], onesb[:], ybf[:], start=(dcc == 0), stop=(dcc == DC - 1)).then_inc(st.inc())
                t.matmul(ps[3][0:1, 0:TB], onesb[:], ysq[:], start=(dcc == 0), stop=(dcc == DC - 1)).then_inc(st.inc())
            block.tensor(t_stat)
            stat_st.append(st.n)
        nstat = st.n

        def v_stats(v, nstat=nstat, nbc_prev=nbc_prev):
            v = AS(v, sv)
            v.wait_ge(st.h, nstat)
            v.tensor_scalar(stats[:, 0:TB], ps[2][0:1, 0:TB], 1.0 / 1024.0, None, ALU.mult)
            v.tensor_scalar(stats[:, TB:2 * TB], ps[3][0:1, 0:TB], 1.0 / 1024.0, None, ALU.mult)
            v.tensor_tensor(stats[:, 2 * TB:3 * TB], stats[:, 0:TB], stats[:, 0:TB], ALU.mult)
            v.tensor_tensor(stats[:, 2 * TB:3 * TB], stats[:, TB:2 * TB], stats[:, 2 * TB:3 * TB], ALU.subtract)
            v.tensor_scalar(stats[:, 2 * TB:3 * TB], stats[:, 2 * TB:3 * TB], LN_EPS, None, ALU.add)
        block.vector(v_stats)
        nvar = sv.n

        def s_std(s, nvar=nvar):
            s = AS(s, ss)
            s.wait_ge(sv.h, nvar)
            s.activation(stats[:, 2 * TB:3 * TB], stats[:, 2 * TB:3 * TB], AF.Sqrt)
        block.scalar(s_std)
        nstd = ss.n

        def v_rstd(v, nstd=nstd):
            v = AS(v, sv)
            v.wait_ge(ss.h, nstd)
            v.reciprocal(stats[:, TB:2 * TB], stats[:, 2 * TB:3 * TB])
        block.vector(v_rstd)
        nrstd = sv.n

        def t_bcast(t, nrstd=nrstd):
            t.wait_ge(sv.h, nrstd)
            t.matmul(ps[4][:, 0:TB], ones_row[:], stats[0:1, 0:TB], start=True, stop=True).then_inc(st.inc())
            t.matmul(ps[5][:, 0:TB], ones_row[:], stats[0:1, TB:2 * TB], start=True, stop=True).then_inc(st.inc())
        block.tensor(t_bcast)
        nbc = st.n

        def v_apply(v, nbc=nbc):
            v = AS(v, sv)
            v.wait_ge(st.h, nbc)
            for dcc in range(DC):
                sl = slice(dcc * TB, (dcc + 1) * TB)
                v.tensor_tensor(xnew[:, sl], xnew[:, sl], ps[4][:, 0:TB], ALU.subtract)
            for dcc in range(DC):
                sl = slice(dcc * TB, (dcc + 1) * TB)
                v.tensor_tensor(xnew[:, sl], xnew[:, sl], ps[5][:, 0:TB], ALU.mult)
        block.vector(v_apply)
        napply = sv.n
        stats_read_prev = napply
        nbc_prev = nbc

        def s_store(s, tb=tb, napply=napply):
            s = AS(s, ss)
            s.wait_ge(sv.h, napply)
            for c in range(DC):
                s.dma_start(out=x_out_r[:, c, tb * TB:(tb + 1) * TB],
                            in_=xnew[:, c * TB:(c + 1) * TB]).then_inc(so.inc(16), 16)
        block.sync(s_store)
        p2_store[tb] = so.n

        if tb + 2 < NTB:
            emit_p2_load(tb + 2)

    def g_final(g):
        g.wait_ge(so.h, so.n)
    block.gpsimd(g_final)




def emit_quant(v, dst_bf, x_f32, scr, P, F, NB, a=None):
    bq, cq = scr["b"], scr["c"]
    if a is None:
        a = x_f32
    sc, rsc = scr["sc"], scr["rsc"]
    x3 = x_f32.rearrange("p (nb k) -> p nb k", k=16)
    v.tensor_reduce(sc[:P, 0:NB], x3, AX.X, ALU.max, apply_absolute_value=True)
    v.tensor_scalar(sc[:P, 0:NB], sc[:P, 0:NB], 1.0 / 6.0, 1e-12, ALU.mult, ALU.max)
    v.reciprocal(rsc[:P, 0:NB], sc[:P, 0:NB])
    a2 = a[:P, 0:F]
    v.tensor_tensor(a2.rearrange("p (nb k) -> p nb k", k=16), x3,
                    rsc[:P, 0:NB].broadcast_to((P, NB, 16)), ALU.mult)
    xn = a2
    v.tensor_scalar(bq[:P, 0:F], xn, 1.0, MAGIC, ALU.mult, ALU.add)
    v.tensor_scalar(bq[:P, 0:F], bq[:P, 0:F], MAGIC, None, ALU.subtract)
    v.tensor_scalar(bq[:P, 0:F], bq[:P, 0:F], 4.0, -4.0, ALU.min, ALU.max)
    v.scalar_tensor_tensor(bq[:P, 0:F], xn, 5.0, bq[:P, 0:F], ALU.is_gt, ALU.add)
    v.scalar_tensor_tensor(bq[:P, 0:F], xn, 5.0, bq[:P, 0:F], ALU.is_gt, ALU.add)
    v.tensor_scalar(cq[:P, 0:F], xn, -5.0, -2.0, ALU.is_lt, ALU.mult)
    v.tensor_tensor(bq[:P, 0:F], bq[:P, 0:F], cq[:P, 0:F], ALU.add)
    v.tensor_scalar(cq[:P, 0:F], xn, 2.0, MAGIC, ALU.mult, ALU.add)
    v.tensor_scalar(cq[:P, 0:F], cq[:P, 0:F], MAGIC, 0.5, ALU.subtract, ALU.mult)
    v.tensor_scalar(cq[:P, 0:F], cq[:P, 0:F], 2.0, -2.0, ALU.min, ALU.max)
    v.tensor_tensor(xn, xn, xn, ALU.mult)
    v.tensor_scalar(xn, xn, 5.0625, None, ALU.is_lt)
    v.tensor_tensor(cq[:P, 0:F], cq[:P, 0:F], bq[:P, 0:F], ALU.subtract)
    v.tensor_tensor(cq[:P, 0:F], xn, cq[:P, 0:F], ALU.mult)
    v.tensor_tensor(xn, bq[:P, 0:F], cq[:P, 0:F], ALU.add)
    v.tensor_tensor(dst_bf.rearrange("p (nb k) -> p nb k", k=16),
                    xn.rearrange("p (nb k) -> p nb k", k=16),
                    sc[:P, 0:NB].broadcast_to((P, NB, 16)), ALU.mult)



def emit_l0f_phaseW(nc, es, block, w_aps, scratch, ses):
    wsl = w_aps["wsl"]
    qag_in, qag_out = scratch["qag_in"], scratch["qag_out"]
    wst2 = [es.enter_context(nc.sbuf_tensor(f"l0fw_wst{i}", [128, 512], F32)) for i in range(2)]
    bH = es.enter_context(nc.sbuf_tensor("l0fw_b", [128, 512], F32))
    cH = es.enter_context(nc.sbuf_tensor("l0fw_c", [128, 512], F32))
    wqb2 = [es.enter_context(nc.sbuf_tensor(f"l0fw_q{i}", [128, 512], BF16)) for i in range(2)]
    scH = es.enter_context(nc.sbuf_tensor("l0fw_sc", [128, 32], F32))
    rscH = es.enter_context(nc.sbuf_tensor("l0fw_rsc", [128, 32], F32))
    identw = es.enter_context(nc.sbuf_tensor("l0fw_id", [128, 128], BF16))
    pstw = es.enter_context(nc.psum_tensor("l0fw_pst", [128, 512], BF16))

    sw2 = [Sem(ses, nc, f"l0fw_sw{i}") for i in range(2)]
    sid = Sem(ses, nc, "l0fw_sid")
    sv = Sem(ses, nc, "l0fw_sv")
    st = Sem(ses, nc, "l0fw_st")
    sjb2 = [Sem(ses, nc, f"l0fw_sjb{i}") for i in range(2)]
    scc = Sem(ses, nc, "l0fw_scc")
    qscr = dict(b=bH, c=cH, sc=scH, rsc=rscH)

    passes = [(c0, min(512, 2816 - c0)) for c0 in range(0, 2816, 512)]
    tasks = [(s_, c0, w) for s_ in range(7) for (c0, w) in passes]
    N = len(tasks)
    ld_tot = {}
    tp_done = {}
    q_done = {}
    st_slot = [0, 0]

    def g_ld(i, gate):
        s_, c0, w = tasks[i]
        sl = i % 2

        def f(g, s_=s_, c0=c0, w=w, sl=sl, gate=gate):
            if gate:
                g.wait_ge(sv.h, gate)
            g.dma_start(out=wst2[sl][:, 0:w],
                        in_=wsl[s_ * 128:(s_ + 1) * 128, c0:c0 + w]).then_inc(sw2[sl].inc(16), 16)
        block.sync(f)
        ld_tot[i] = sw2[sl].n

    def g_id(g):
        g.dma_start(out=identw[:], in_=w_aps["ident"]).then_inc(sid.inc(16), 16)
    block.gpsimd(g_id)
    id_done = sid.n
    g_ld(0, 0)
    g_ld(1, 0)

    state = {"i": 0, "agc": None}

    def emit_pass():
        i = state["i"]
        s_, c0, w = tasks[i]
        sl = i % 2

        def v_q(v, sl=sl, w=w, ld=ld_tot[i], tpg=tp_done.get(i - 2, 0), stg=st_slot[sl]):
            v = AS(v, sv)
            v._e.wait_ge(sw2[sl].h, ld)
            if tpg:
                v._e.wait_ge(st.h, tpg)
            if stg:
                v._e.wait_ge(sjb2[sl].h, stg)
            emit_quant(v, wqb2[sl][:, 0:w], wst2[sl][:, 0:w], qscr, 128, w, w // 16)
        block.vector(v_q)
        q_done[i] = sv.n

        if i + 2 < N:
            g_ld(i + 2, q_done[i])

        def t_tp(t, i=i, sl=sl, nch=(tasks[i][2] + 127) // 128):
            if i == 0:
                t.wait_ge(sid.h, id_done)
            t.wait_ge(sv.h, q_done[i])
            for k in range(nch):
                t.transpose(pstw[:, k * 128:(k + 1) * 128],
                            wqb2[sl][:, k * 128:(k + 1) * 128], identw[:]).then_inc(st.inc())
        block.tensor(t_tp)
        tp_done[i] = st.n

        def v_cp(v, i=i, sl=sl, w=w):
            v = AS(v, sv)
            v._e.wait_ge(st.h, tp_done[i])
            v.tensor_copy(wqb2[sl][:, 0:w], pstw[:, 0:w])
        block.vector(v_cp)
        cp_done = sv.n

        def g_st(g, s_=s_, c0=c0, w=w, sl=sl, cp_done=cp_done):
            g.wait_ge(sv.h, cp_done)
            g.dma_start(out=qag_in.ap()[s_ * 128:(s_ + 1) * 128, c0:c0 + w],
                        in_=wqb2[sl][:, 0:w]).then_inc(sjb2[sl].inc(16), 16)
        block.sync(g_st)
        st_slot[sl] = sjb2[sl].n
        state["i"] = i + 1

    def pump(k):
        while state["i"] < N and k > 0:
            emit_pass()
            k -= 1

    def finish():
        pump(N)

        def g_ag(g, st0=sjb2[0].n, st1=sjb2[1].n):
            g.wait_ge(sjb2[0].h, st0)
            g.wait_ge(sjb2[1].h, st1)
            g.collective_compute("AllGather", ALU.bypass,
                                 replica_groups=[list(range(8))],
                                 ins=[qag_in.ap()], outs=[qag_out.ap()]).then_inc(scc.inc())
        block.gpsimd(g_ag)
        state["agc"] = scc.n

    return dict(pump=pump, finish=finish, scc=scc, state=state)


def emit_l0_ffn(nc, es, block, x_in, x_out, w_aps, scratch=None, ses=None, pw=None):
    ses = es if ses is None else ses
    """L0_ffn + residual + LN. x_in/x_out: DRAM APs fm f32 [1024, T].

    w_aps: wg, wu: [2816, 1024] f32; wd: [1024, 2816] f32; ident: [128,128] f32.
    """
    I = 2816
    IC = I // 128
    NTT = T // 128
    WB = 8
    NWB = I // WB

    wq = {w: es.enter_context(nc.sbuf_tensor("l0f_" + w, [128, DC * I], BF16))
          for w in ["wgq", "wuq"]}
    wdq = es.enter_context(nc.sbuf_tensor("l0f_wdq", [128, IC * D], BF16))
    identb = es.enter_context(nc.sbuf_tensor("l0f_identb", [128, 128], BF16))
    identf = es.enter_context(nc.sbuf_tensor("l0f_identf", [128, 128], F32))

    wst0 = es.enter_context(nc.sbuf_tensor("l0f_wst0", [128, I], F32))
    wst = [wst0, wst0]
    xfm = [es.enter_context(nc.sbuf_tensor(f"l0f_xfm{i}", [128, D], F32)) for i in range(2)]
    xtok2_t = [es.enter_context(nc.sbuf_tensor(f"l0f_xtok{i}", [128, D], F32)) for i in range(2)]
    xqf2_t = [es.enter_context(nc.sbuf_tensor(f"l0f_xqf{i}", [128, D], BF16)) for i in range(2)]
    xtok = xtok2_t[0]
    xqf = xqf2_t[0]
    hq = es.enter_context(nc.sbuf_tensor("l0f_hq", [128, I], BF16))
    xqt = hq
    hqf = es.enter_context(nc.sbuf_tensor("l0f_hqf", [128, I], BF16))
    lnst = es.enter_context(nc.sbuf_tensor("l0f_lnst", [128, 8], F32))
    qb_t = es.enter_context(nc.sbuf_tensor("l0f_qb", [128, I], F32))
    qc_t = es.enter_context(nc.sbuf_tensor("l0f_qc", [128, I], F32))
    silu_s = qc_t[:, 1024:1024 + NWB]
    ynew = qb_t
    qscr = dict(
        b=qb_t,
        c=qc_t,
        sc=es.enter_context(nc.sbuf_tensor("l0f_qsc", [128, I // 16], F32)),
        rsc=es.enter_context(nc.sbuf_tensor("l0f_qrsc", [128, I // 16], F32)),
    )
    xqscr = dict(b=qb_t[:, 0:1024], c=qc_t[:, 1024:2048],
                 sc=qscr["sc"], rsc=qscr["rsc"])
    hf = wst[0]
    wqb = hq
    yfm = qc_t

    ps = [es.enter_context(nc.psum_tensor(f"l0f_ps{i}", [128, 512], F32)) for i in range(4)]
    pst = es.enter_context(nc.psum_tensor("l0f_pst", [128, 1024], BF16))
    psf = [es.enter_context(nc.psum_tensor(f"l0f_psf{i}", [128, 512], F32)) for i in range(2)]

    sw = [Sem(ses, nc, f"l0f_sw{i}") for i in range(2)]
    sx = [Sem(ses, nc, f"l0f_sx{i}") for i in range(2)]
    sgw = Sem(ses, nc, "l0f_sgw")
    st = Sem(ses, nc, "l0f_st")
    sv = Sem(ses, nc, "l0f_sv")
    ss = Sem(ses, nc, "l0f_ss")
    so = Sem(ses, nc, "l0f_so")

    x_in_r = x_in.rearrange("(c p) t -> p c t", p=128)
    x_out_r = x_out.rearrange("(c p) t -> p c t", p=128)

    def g_ident(g):
        g.dma_start(out=identb[:], in_=w_aps["ident"]).then_inc(sgw.inc(16), 16)
        g.dma_start(out=identf[:], in_=w_aps["ident"]).then_inc(sgw.inc(16), 16)
    block.gpsimd(g_ident)
    ident_done = sgw.n

    scc = pw["scc"] if pw is not None else Sem(ses, nc, "l0f_scc")
    sjb = Sem(ses, nc, "l0f_sjb")
    swre = Sem(ses, nc, "l0f_swre")
    wsl = w_aps["wsl"]
    qag_in, qag_out = scratch["qag_in"], scratch["qag_out"]
    jobbuf = hqf

    def reassemble(g):
        jobs = [("wg", ch) for ch in range(IC)] + [("wu", ch) for ch in range(IC)] + \
               [("wd", ch) for ch in range(DC)]
        for j, (wname, ch) in enumerate(jobs):
            r0 = (j // 7) * 896 + (j % 7) * 128
            srcv = qag_out.ap()[r0:r0 + 128, :]
            if wname == "wd":
                dst = wdq[:].rearrange("p (ic d) -> p ic d", d=D)[:, :, ch * 128:(ch + 1) * 128]
                g.dma_start(out=dst, in_=srcv.rearrange("p (ic c) -> p ic c", c=128)).then_inc(swre.inc(16), 16)
            else:
                w_t = wq["wgq" if wname == "wg" else "wuq"]
                dst = w_t[:].rearrange("p (dc i) -> p dc i", i=I)[:, :, ch * 128:(ch + 1) * 128]
                g.dma_start(out=dst, in_=srcv.rearrange("p (ic c) -> p ic c", c=128)[:, 0:DC, :]).then_inc(swre.inc(16), 16)

    if pw is not None:
        def g_wre(g, agc=pw["agc"]):
            g.wait_ge(scc.h, agc)
            reassemble(g)
        block.gpsimd(g_wre)

    slot_free = [(sv.h, 0), (sv.h, 0)]
    F = I
    NB = F // 16
    for s_ in ([] if pw is not None else range(7)):
        sl = 0
        gate = slot_free[0]

        def g_wload(g, s_=s_, sl=sl, gate=gate):
            g.wait_ge(gate[0], gate[1])
            g.dma_start(out=wst[sl][:, 0:F], in_=wsl[s_ * 128:(s_ + 1) * 128, :]).then_inc(sw[sl].inc(16), 16)
        block.gpsimd(g_wload)
        loaded = (sw[sl].h, sw[sl].n)

        def v_wq(v, sl=sl, loaded=loaded, jb_gate=sjb.n):
            v = AS(v, sv)
            v._e.wait_ge(loaded[0], loaded[1])
            emit_quant(v, wqb[:, 0:F], wst[sl][:, 0:F], qscr, 128, F, NB)
        block.vector(v_wq)
        slot_free[sl] = (sv.h, sv.n)
        q_done = sv.n

        for bs in range(0, IC, 8):
            be = min(bs + 8, IC)

            def t_tp(t, bs=bs, be=be, q_done=q_done, vgate=sv.n):
                t.wait_ge(sgw.h, ident_done)
                t.wait_ge(sv.h, q_done if bs == 0 else vgate)
                for k in range(bs, be):
                    t.transpose(pst[:, (k - bs) * 128:(k - bs + 1) * 128],
                                wqb[:, k * 128:(k + 1) * 128], identb[:]).then_inc(st.inc())
            block.tensor(t_tp)
            tp_done = st.n

            def v_wcp(v, bs=bs, be=be, tp_done=tp_done, jb_gate=sjb.n):
                v = AS(v, sv)
                v._e.wait_ge(st.h, tp_done)
                if bs == 0 and jb_gate:
                    v._e.wait_ge(sjb.h, jb_gate)
                v.tensor_copy(jobbuf[:, bs * 128:be * 128], pst[:, 0:(be - bs) * 128])
            block.vector(v_wcp)
        cp_done = sv.n

        def g_jst(g, s_=s_, cp_done=cp_done):
            g.wait_ge(sv.h, cp_done)
            g.dma_start(out=qag_in.ap()[s_ * 128:(s_ + 1) * 128, :], in_=jobbuf[:]).then_inc(sjb.inc(16), 16)
        block.gpsimd(g_jst)

    if pw is None:
        def g_wag(g, stored=sjb.n):
            g.wait_ge(sjb.h, stored)
            g.collective_compute("AllGather", ALU.bypass,
                                 replica_groups=[list(range(8))],
                                 ins=[qag_in.ap()], outs=[qag_out.ap()]).then_inc(scc.inc())
            g.wait_ge(scc.h, scc.n)
            reassemble(g)
        block.gpsimd(g_wag)
    wq_all_re = (swre.h, swre.n)

    def g_xload(tt, sl, gate):
        def f(g_, tt=tt, sl=sl, gate=gate):
            g_.wait_ge(gate[0], gate[1])
            for c in range(DC):
                g_.dma_start(out=xfm[sl][:, c * 128:(c + 1) * 128],
                             in_=x_in_r[:, c, tt * 128:(tt + 1) * 128]).then_inc(sx[sl].inc(16), 16)
        block.gpsimd(f)
        return (sx[sl].h, sx[sl].n)

    x_loaded = [None] * NTT
    x_loaded[0] = g_xload(0, 0, (sgw.h, ident_done))
    x_loaded[1] = g_xload(1, 1, (sgw.h, ident_done))
    xfm_consumed = [0] * NTT

    def emit_xchain(tt):
        sl = tt % 2
        xtk = xtok2_t[sl]
        xqo = xqf2_t[sl]

        def t_xtp(t, tt=tt, sl=sl, ld=x_loaded[tt], vgate=sv.n):
            t.wait_ge(sgw.h, ident_done)
            t.wait_ge(ld[0], ld[1])
            if vgate:
                t.wait_ge(sv.h, vgate)
            for c in range(DC):
                t.transpose(psf[c % 2][:, (c // 2) * 128:(c // 2) * 128 + 128],
                            xfm[sl][:, c * 128:(c + 1) * 128], identf[:]).then_inc(st.inc())
        block.tensor(t_xtp)
        xfm_consumed[tt] = st.n
        xtp_done = st.n

        def v_xevac(v, xtk=xtk, xtp_done=xtp_done):
            v = AS(v, sv)
            v._e.wait_ge(st.h, xtp_done)
            for c in range(DC):
                v.tensor_copy(xtk[:, c * 128:(c + 1) * 128],
                              psf[c % 2][:, (c // 2) * 128:(c // 2) * 128 + 128])
        block.vector(v_xevac)

        if tt + 2 < NTT:
            x_loaded[tt + 2] = g_xload(tt + 2, sl, (st.h, xfm_consumed[tt]))

        def v_xq(v, xtk=xtk):
            v = AS(v, sv)
            emit_quant(v, xqt[:, 0:D], xtk[:], xqscr, 128, D, D // 16, a=qb_t[:, 1024:2048])
        block.vector(v_xq)
        xq_done = sv.n

        def t_xqtp(t, xq_done=xq_done):
            t.wait_ge(sgw.h, ident_done)
            t.wait_ge(sv.h, xq_done)
            for c in range(DC):
                t.transpose(pst[:, c * 128:(c + 1) * 128],
                            xqt[:, c * 128:(c + 1) * 128], identb[:]).then_inc(st.inc())
        block.tensor(t_xqtp)
        xqtp_done = st.n

        def v_xqevac(v, xqo=xqo, xqtp_done=xqtp_done):
            v = AS(v, sv)
            v._e.wait_ge(st.h, xqtp_done)
            v.tensor_copy(xqo[:], pst[:, 0:D])
        block.vector(v_xqevac)
        return sv.n, xqtp_done

    def make_ytail(tt, ln_done):
        stg = {}

        def e_ytp():
            def t_ytp(t, ln_done=ln_done):
                t.wait_ge(sgw.h, ident_done)
                t.wait_ge(sv.h, ln_done)
                for c in range(DC):
                    t.transpose(psf[c % 2][:, (c // 2) * 128:(c // 2) * 128 + 128],
                                ynew[:, c * 128:(c + 1) * 128], identf[:]).then_inc(st.inc())
            block.tensor(t_ytp)
            stg["ytp"] = st.n

        def e_yev():
            def v_yevac(v, ytp_done=stg["ytp"], store_gate=so.n):
                v = AS(v, sv)
                v._e.wait_ge(st.h, ytp_done)
                if store_gate:
                    v._e.wait_ge(so.h, store_gate)
                for c in range(DC):
                    v.tensor_copy(yfm[:, c * 128:(c + 1) * 128],
                                  psf[c % 2][:, (c // 2) * 128:(c // 2) * 128 + 128])
            block.vector(v_yevac)
            stg["yev"] = sv.n

        def e_store():
            def s_store(s, tt=tt, yev_done=stg["yev"]):
                s.wait_ge(sv.h, yev_done)
                for c in range(DC):
                    s.dma_start(out=x_out_r[:, c, tt * 128:(tt + 1) * 128],
                                in_=yfm[:, c * 128:(c + 1) * 128]).then_inc(so.inc(16), 16)
            block.sync(s_store)
        return [e_ytp, e_yev, e_store]

    xqf_done_l = [None] * (NTT + 1)
    xqtp_done_l = [None] * (NTT + 1)
    xqf_done_l[0], xqtp_done_l[0] = emit_xchain(0)
    silu_prev = 0
    fin_prev = 0
    ytail = None

    for tt in range(NTT):
        sl = tt % 2
        xtk = xtok2_t[sl]
        xqo = xqf2_t[sl]
        xqf_done = xqf_done_l[tt]

        for wb in range(WB):
            bg = (wb % 2) * 2
            cols = slice(wb * NWB, (wb + 1) * NWB)

            def t_gu(t, wb=wb, bg=bg, xqf_done=xqf_done, bank_gate=sv.n,
                     ss_gate=(silu_prev if wb == 0 else ss.n),
                     wre=wq_all_re, xqo=xqo):
                if wb == 0:
                    t.wait_ge(sv.h, xqf_done)
                    t.wait_ge(wre[0], wre[1])
                else:
                    t.wait_ge(sv.h, bank_gate)
                if ss_gate:
                    t.wait_ge(ss.h, ss_gate)
                for dc in range(DC):
                    t.matmul(ps[bg][:, 0:NWB], xqo[:, dc * 128:(dc + 1) * 128],
                             wq["wgq"][:, dc * I + wb * NWB: dc * I + (wb + 1) * NWB],
                             start=(dc == 0), stop=(dc == DC - 1)).then_inc(st.inc())
                for dc in range(DC):
                    t.matmul(ps[bg + 1][:, 0:NWB], xqo[:, dc * 128:(dc + 1) * 128],
                             wq["wuq"][:, dc * I + wb * NWB: dc * I + (wb + 1) * NWB],
                             start=(dc == 0), stop=(dc == DC - 1)).then_inc(st.inc())
            block.tensor(t_gu)
            gu_done = st.n

            def s_silu(s, wb=wb, bg=bg, gu_done=gu_done, fg=fin_prev):
                s = AS(s, ss)
                s._e.wait_ge(st.h, gu_done)
                if wb == 0 and fg:
                    s._e.wait_ge(sv.h, fg)
                s.activation(silu_s[:], ps[bg][:, 0:NWB], AF.Silu)
            block.scalar(s_silu)
            silu_done = ss.n

            def v_h(v, bg=bg, cols=cols, silu_done=silu_done):
                v = AS(v, sv)
                v._e.wait_ge(ss.h, silu_done)
                v.tensor_tensor(hf[:, cols], ps[bg + 1][:, 0:NWB], silu_s[:], ALU.mult)
            block.vector(v_h)
            if ytail is not None and 2 <= wb <= 4:
                ytail[wb - 2]()
        h_done = sv.n
        silu_prev = ss.n

        if tt + 1 < NTT:
            xqf_done_l[tt + 1], xqtp_done_l[tt + 1] = emit_xchain(tt + 1)

        cgs = [(0, 8), (8, 16), (16, IC)]
        hq_tp = []

        def e_hqg(gi, xg=0, store_gate=0):
            bs, be = cgs[gi]
            w = (be - bs) * 128

            def v_hqg(v, bs=bs, w=w, xg=xg, store_gate=store_gate):
                v = AS(v, sv)
                if xg:
                    v._e.wait_ge(st.h, xg)
                if store_gate:
                    v._e.wait_ge(so.h, store_gate)
                emit_quant(v, hq[:, bs * 128:bs * 128 + w], hf[:, bs * 128:bs * 128 + w],
                           qscr, 128, w, w // 16)
            block.vector(v_hqg)

        def e_hqtp(gi):
            bs, be = cgs[gi]

            def t_hqtp(t, bs=bs, be=be, g=sv.n):
                t.wait_ge(sgw.h, ident_done)
                t.wait_ge(sv.h, g)
                for k in range(bs, be):
                    t.transpose(pst[:, (k - bs) * 128:(k - bs + 1) * 128],
                                hq[:, k * 128:(k + 1) * 128], identb[:]).then_inc(st.inc())
            block.tensor(t_hqtp)
            hq_tp.append(st.n)

        def e_hqev(gi):
            bs, be = cgs[gi]

            def v_hqevac(v, bs=bs, be=be, tp_done=hq_tp[gi]):
                v = AS(v, sv)
                v._e.wait_ge(st.h, tp_done)
                v.tensor_copy(hqf[:, bs * 128:be * 128], pst[:, 0:(be - bs) * 128])
            block.vector(v_hqevac)

        e_hqg(0, xg=(xqtp_done_l[tt + 1] if tt + 1 < NTT else 0), store_gate=so.n)
        e_hqtp(0)
        e_hqg(1)
        e_hqev(0)
        e_hqtp(1)
        e_hqg(2)
        e_hqev(1)
        e_hqtp(2)
        e_hqev(2)
        hqf_done = sv.n

        def t_down(t, hqf_done=hqf_done, ss_gate=ss.n):
            t.wait_ge(sv.h, hqf_done)
            if ss_gate:
                t.wait_ge(ss.h, ss_gate)
            for half in range(2):
                for ic in range(IC):
                    t.matmul(psf[half][:, 0:512],
                             hqf[:, ic * 128:(ic + 1) * 128],
                             wdq[:, ic * D + half * 512: ic * D + (half + 1) * 512],
                             start=(ic == 0), stop=(ic == IC - 1)).then_inc(st.inc())
        block.tensor(t_down)
        down_done = st.n

        def v_fin(v, tt=tt, xtk=xtk, down_done=down_done):
            v = AS(v, sv)
            v._e.wait_ge(st.h, down_done)
            v.tensor_tensor(ynew[:, 0:512], psf[0][:, 0:512], xtk[:, 0:512], ALU.add)
            v.tensor_tensor(ynew[:, 512:1024], psf[1][:, 0:512], xtk[:, 512:1024], ALU.add)
            v.tensor_reduce(lnst[:, 0:1], ynew[:, 0:D], AX.X, ALU.add)
            v.tensor_tensor(qc_t[:, 1024:2048], ynew[:, 0:D], ynew[:, 0:D], ALU.mult)
            v.tensor_reduce(lnst[:, 1:2], qc_t[:, 1024:2048], AX.X, ALU.add)
            v.tensor_scalar(lnst[:, 0:1], lnst[:, 0:1], 1.0 / 1024.0, None, ALU.mult)
            v.tensor_scalar(lnst[:, 1:2], lnst[:, 1:2], 1.0 / 1024.0, None, ALU.mult)
            v.tensor_tensor(lnst[:, 2:3], lnst[:, 0:1], lnst[:, 0:1], ALU.mult)
            v.tensor_tensor(lnst[:, 2:3], lnst[:, 1:2], lnst[:, 2:3], ALU.subtract)
            v.tensor_scalar(lnst[:, 2:3], lnst[:, 2:3], LN_EPS, None, ALU.add)
        block.vector(v_fin)
        fin_prev = sv.n
        nvar = sv.n

        def s_sqrt(s, nvar=nvar):
            s = AS(s, ss)
            s._e.wait_ge(sv.h, nvar)
            s.activation(lnst[:, 3:4], lnst[:, 2:3], AF.Sqrt)
        block.scalar(s_sqrt)
        nstd = ss.n

        def v_ln(v, nstd=nstd):
            v = AS(v, sv)
            v._e.wait_ge(ss.h, nstd)
            v.reciprocal(lnst[:, 3:4], lnst[:, 3:4])
            v.tensor_scalar(ynew[:, 0:D], ynew[:, 0:D], lnst[:, 0:1], None, ALU.subtract)
            v.tensor_scalar(ynew[:, 0:D], ynew[:, 0:D], lnst[:, 3:4], None, ALU.mult)
        block.vector(v_ln)
        ln_done = sv.n

        ytail = make_ytail(tt, ln_done)
    for f in ytail:
        f()

    def g_fin(g):
        g.wait_ge(so.h, so.n)
    block.gpsimd(g_fin)




def host_l0f_wsl(weights, core_id):
    wg = np.asarray(weights["L0_ffn_w_gate"], np.float32)
    wu = np.asarray(weights["L0_ffn_w_up"], np.float32)
    wd = np.asarray(weights["L0_ffn_w_down"], np.float32)
    jobs = [("wg", c) for c in range(22)] + [("wu", c) for c in range(22)] + \
           [("wd", c) for c in range(8)]
    out = np.zeros((7, 128, 2816), np.float32)
    for s in range(7):
        j = core_id * 7 + s
        if j >= len(jobs):
            continue
        wname, ch = jobs[j]
        w = {"wg": wg, "wu": wu, "wd": wd}[wname]
        blkw = w.shape[1]
        out[s, :, 0:blkw] = w[ch * 128:(ch + 1) * 128, :]
    return np.ascontiguousarray(out.reshape(896, 2816))


def build_l0f_launch():
    nc = bass.Bass("TRN2", target_bir_lowering=False)
    x_in = nc.dram_tensor("x_in", [D, T], F32, kind="ExternalInput").ap()
    x_out = nc.dram_tensor("x_out", [D, T], F32, kind="ExternalOutput").ap()
    w_aps = dict(
        wsl=nc.dram_tensor("wsl", [896, 2816], F32, kind="ExternalInput").ap(),
        ident=nc.dram_tensor("ident", [128, 128], F32, kind="ExternalInput").ap(),
    )
    scratch = dict(
        qag_in=nc.dram_tensor("qag_in", [896, 2816], BF16),
        qag_out=nc.dram_tensor("qag_out", [7168, 2816], BF16, addr_space="Shared"),
    )
    es = contextlib.ExitStack()
    with es:
        block = DirectBlock(nc)
        es.enter_context(nc.Block())
        emit_l0_ffn(nc, es, block, x_in, x_out, w_aps, scratch)
    return nc


def run_l0f(x_full, weights):
    nc = build_l0f_launch()
    ident = np.eye(128, dtype=np.float32)
    in_maps = []
    for c in range(NCORES):
        b, hf = c // 2, c % 2
        xs = np.ascontiguousarray(x_full[b, hf * T:(hf + 1) * T, :])
        in_maps.append(dict(x_in=fmT(xs), wsl=host_l0f_wsl(weights, c), ident=ident))
    res = run_bass_kernel_spmd(nc, in_maps, core_ids=list(range(NCORES)))
    out = np.zeros((B, S, D), np.float32)
    for c in range(NCORES):
        b, hf = c // 2, c % 2
        out[b, hf * T:(hf + 1) * T, :] = np.asarray(res.results[c]["x_out"], np.float32).T
    return out




def emit_l1_attn(nc, es, block, x_in, x_out, w_aps, scratch, ses=None):
    ses = es if ses is None else ses
    """Differential attention block (B=1 sequence slice per core pair).

    w_aps: wq, wk: [16, 1024, 128] f32; wvT: [1024, 1024] f32 (dmodel, h*dv);
    wo: [1024, 1024] f32 (hd rows, dmodel cols);
    bqT, bkT: [128, 16]; bvrow: [1, 1024]; bo, gamma, beta: [128, 8];
    lamb: [128, 1]; onesr: [1, 128]  (all f32 DRAM APs)
    scratch: cc_kin [2048,2048]bf16, cc_kout [4096,2048]bf16,
             cc_vin [2048,1024]bf16, cc_vout [4096,1024]bf16 (dram tensors)
    """
    H, DH = 16, 64
    NTT = T // 128
    KC = S // 128
    VW = H * 65

    xbf = es.enter_context(nc.sbuf_tensor("l1a_xbf", [128, DC * T], BF16))
    ocat = xbf
    qsb = es.enter_context(nc.sbuf_tensor("l1a_qsb", [128, H * T], BF16))
    khh = [es.enter_context(nc.sbuf_tensor(f"l1a_kh{i}", [128, S], BF16)) for i in range(2)]
    wvsb = es.enter_context(nc.sbuf_tensor("l1a_wvsb", [128, DC * 1024], BF16))
    wosb = wvsb
    vecs = {v: es.enter_context(nc.sbuf_tensor("l1a_" + v, [128, 16], F32))
            for v in ["bqT", "bkT", "bo", "gamma", "beta", "lamb", "gp", "bp"]}
    onesr = es.enter_context(nc.sbuf_tensor("l1a_onesr", [1, 128], F32))
    ones65 = es.enter_context(nc.sbuf_tensor("l1a_ones65", [65, 64], F32))
    ksum = es.enter_context(nc.sbuf_tensor("l1a_ksum", [128, 2], F32))
    o1s = es.enter_context(nc.sbuf_tensor("l1a_o1s", [128, 512], F32))
    o2s = es.enter_context(nc.sbuf_tensor("l1a_o2s", [128, 512], F32))
    rs1 = es.enter_context(nc.sbuf_tensor("l1a_rs1", [1, 512], F32))
    rs2 = es.enter_context(nc.sbuf_tensor("l1a_rs2", [1, 512], F32))
    p1s = es.enter_context(nc.sbuf_tensor("l1a_p1s", [128, 1024], BF16))
    p2s = es.enter_context(nc.sbuf_tensor("l1a_p2s", [128, 1024], BF16))
    onesb = es.enter_context(nc.sbuf_tensor("l1a_onesb", [128, 1], BF16))
    yout = ytmp = xres = ystat = ybf = ysq = None
    es_attn = contextlib.ExitStack()
    vhat = None
    es_proj = contextlib.ExitStack()
    wqk = [es_proj.enter_context(nc.sbuf_tensor(f"l1a_wqk{i}", [128, 1024], BF16)) for i in range(2)]
    kstage = [es_proj.enter_context(nc.sbuf_tensor(f"l1a_kst{i}", [128, T], BF16)) for i in range(2)]
    vstage = [es_proj.enter_context(nc.sbuf_tensor(f"l1a_vst{i}", [128, 1024], BF16)) for i in range(2)]
    bvb = es_proj.enter_context(nc.sbuf_tensor("l1a_bvb", [128, 1024], F32))
    bvrow = es_proj.enter_context(nc.sbuf_tensor("l1a_bvrow", [1, 1024], F32))
    pS = [es.enter_context(nc.psum_tensor(f"l1a_pS{i}", [128, 1024], F32)) for i in range(2)]
    ps = [pS[0][:, 0:512], pS[0][:, 512:1024]]
    pO1 = es.enter_context(nc.psum_tensor("l1a_pO1", [128, 512], F32))
    pO2 = es.enter_context(nc.psum_tensor("l1a_pO2", [128, 512], F32))
    pG1 = es.enter_context(nc.psum_tensor("l1a_pG1", [64, 512], F32))
    pG2 = es.enter_context(nc.psum_tensor("l1a_pG2", [64, 512], F32))

    sgw = Sem(ses, nc, "l1a_sgw")
    swq = [Sem(ses, nc, f"l1a_swq{i}") for i in range(2)]
    svst = [Sem(ses, nc, f"l1a_svst{i}") for i in range(2)]
    skst = [Sem(ses, nc, f"l1a_skst{i}") for i in range(2)]
    sxk = [Sem(ses, nc, f"l1a_sxk{i}") for i in range(2)]
    svh = Sem(ses, nc, "l1a_svh")
    swo = Sem(ses, nc, "l1a_swo")
    sxr = [Sem(ses, nc, f"l1a_sxr{i}") for i in range(2)]
    scc = Sem(ses, nc, "l1a_scc")
    st = Sem(ses, nc, "l1a_st")
    sv = Sem(ses, nc, "l1a_sv")
    ss = Sem(ses, nc, "l1a_ss")
    so = Sem(ses, nc, "l1a_so")

    x_in_r = x_in.rearrange("(c p) t -> p c t", p=128)
    x_out_r = x_out.rearrange("(c p) t -> p c t", p=128)
    kin, kout = scratch["cc_kin"], scratch["cc_kout"]
    vin, vout = scratch["cc_vin"], scratch["cc_vout"]

    def g_loads(g):
        for c in range(DC):
            g.dma_start(out=xbf[:, c * T:(c + 1) * T], in_=x_in_r[:, c, :]).then_inc(sgw.inc(16), 16)
            g.dma_start(out=wvsb[:, c * 1024:(c + 1) * 1024],
                        in_=w_aps["wvT"][c * 128:(c + 1) * 128, :]).then_inc(sgw.inc(16), 16)
        for v, wdt in [("bqT", 16), ("bkT", 16), ("bo", 8), ("gamma", 8), ("beta", 8), ("lamb", 1)]:
            g.dma_start(out=vecs[v][:, 0:wdt], in_=w_aps[v]).then_inc(sgw.inc(16), 16)
        g.dma_start(out=bvrow[:], in_=w_aps["bvrow"]).then_inc(sgw.inc(16), 16)
        g.dma_start(out=onesr[:], in_=w_aps["onesr"]).then_inc(sgw.inc(16), 16)
        for i in range(2):
            g.dma_start(out=wqk[i][:].rearrange("p (c m) -> p c m", m=128),
                        in_=w_aps["wk"][i].rearrange(
                "(c p) m -> p c m", p=128)).then_inc(swq[i].inc(16), 16)
    block.gpsimd(g_loads)
    loads_done = sgw.n
    w_loaded_s = {0: (swq[0].h, swq[0].n), 1: (swq[1].h, swq[1].n)}

    def v_init(v):
        v = AS(v, sv)
        v._e.wait_ge(sgw.h, loads_done)
        v.memset(onesb[:], 1.0)
        v.memset(ones65[:], 1.0)
        v.tensor_scalar(vecs["lamb"][:, 0:1], vecs["lamb"][:, 0:1], -1.0, None, ALU.mult)
        v.tensor_scalar(vecs["gp"][:, 0:8], vecs["gamma"][:, 0:8], 0.2, None, ALU.mult)
        v.tensor_scalar(vecs["bp"][:, 0:8], vecs["beta"][:, 0:8], 0.2, None, ALU.mult)
    block.vector(v_init)
    init_done = sv.n

    def t_bvb(t):
        t.wait_ge(sv.h, init_done)
        t.matmul(ps[0], onesr[:], bvrow[0:1, 0:512], start=True, stop=True).then_inc(st.inc())
        t.matmul(ps[1], onesr[:], bvrow[0:1, 512:1024], start=True, stop=True).then_inc(st.inc())
    block.tensor(t_bvb)
    bvb_mm = st.n

    def v_bvb(v, bvb_mm=bvb_mm):
        v = AS(v, sv)
        v._e.wait_ge(st.h, bvb_mm)
        v.tensor_copy(bvb[:, 0:512], ps[0])
        v.tensor_copy(bvb[:, 512:1024], ps[1])
    block.vector(v_bvb)
    bvb_done = sv.n

    vst_free = [(sv.h, 0), (sv.h, 0)]
    evac_v = bvb_done
    for tt in range(NTT):
        slv = tt % 2

        def t_vproj(t, tt=tt, gate=evac_v):
            t.wait_ge(sv.h, gate)
            for half in range(2):
                for dc in range(DC):
                    t.matmul(ps[half],
                             xbf[:, dc * T + tt * 128: dc * T + (tt + 1) * 128],
                             wvsb[:, dc * 1024 + half * 512: dc * 1024 + (half + 1) * 512],
                             start=(dc == 0), stop=(dc == DC - 1)).then_inc(st.inc())
        block.tensor(t_vproj)
        vmm = st.n

        def v_vevac(v, slv=slv, vmm=vmm, gate=vst_free[slv]):
            v = AS(v, sv)
            v._e.wait_ge(st.h, vmm)
            v._e.wait_ge(gate[0], gate[1])
            v.tensor_tensor(vstage[slv][:, 0:512], ps[0], bvb[:, 0:512], ALU.add)
            v.tensor_tensor(vstage[slv][:, 512:1024], ps[1], bvb[:, 512:1024], ALU.add)
        block.vector(v_vevac)
        evac_v = sv.n

        def g_vstore(g, tt=tt, slv=slv, vev=evac_v):
            g.wait_ge(sv.h, vev)
            g.dma_start(out=vin.ap()[tt * 128:(tt + 1) * 128, :], in_=vstage[slv][:]).then_inc(svst[slv].inc(16), 16)
        block.gpsimd(g_vstore)
        vst_free[slv] = (svst[slv].h, svst[slv].n)

    def g_agv(g, v0=(svst[0].h, svst[0].n), v1=(svst[1].h, svst[1].n)):
        g.wait_ge(v0[0], v0[1])
        g.wait_ge(v1[0], v1[1])
        g.collective_compute("AllGather", ALU.bypass,
                             replica_groups=[[0, 1], [2, 3], [4, 5], [6, 7]],
                             ins=[vin.ap()], outs=[vout.ap()]).then_inc(scc.inc())
    block.gpsimd(g_agv)
    agv_done = scc.n

    kst_free = [(skst[0].h, 0), (skst[1].h, 0)]

    def g_wstream(j, gate_st):
        name, idx = ("wk", j) if j < H else ("wq", j - H)
        sl = j % 2

        def f(g, name=name, idx=idx, sl=sl, gate_st=gate_st):
            if gate_st:
                g.wait_ge(st.h, gate_st)
            g.dma_start(out=wqk[sl][:].rearrange("p (c m) -> p c m", m=128),
                        in_=w_aps[name][idx].rearrange(
                "(c p) m -> p c m", p=128)).then_inc(swq[sl].inc(16), 16)
        block.gpsimd(f)
        w_loaded_s[j] = (swq[sl].h, swq[sl].n)

    for h in range(H):
        slk = h % 2
        for tb in range(4):
            def t_kproj(t, h=h, tb=tb, wl=w_loaded_s[h],
                        bank_gate=evac_v if h == 0 and tb == 0 else None,
                        ss_gate=ss.n):
                if tb == 0:
                    t.wait_ge(wl[0], wl[1])
                if bank_gate is not None:
                    t.wait_ge(sv.h, bank_gate)
                if ss_gate:
                    t.wait_ge(ss.h, ss_gate)
                for dc in range(DC):
                    t.matmul(ps[tb % 2],
                             wqk[h % 2][:, dc * 128:(dc + 1) * 128],
                             xbf[:, dc * T + tb * 512:dc * T + (tb + 1) * 512],
                             start=(dc == 0), stop=(dc == DC - 1)).then_inc(st.inc())
            block.tensor(t_kproj)
            kmm = st.n

            def s_kevac(s, h=h, tb=tb, slk=slk, kmm=kmm, gate=kst_free[slk] if tb == 0 else None):
                s = AS(s, ss)
                s._e.wait_ge(st.h, kmm)
                if gate is not None:
                    s._e.wait_ge(gate[0], gate[1])
                s.activation(kstage[slk][:, tb * 512:(tb + 1) * 512], ps[tb % 2],
                             AF.Identity, bias=vecs["bkT"][:, h:h + 1])
            block.scalar(s_kevac)
        kev = ss.n
        if h + 2 < 2 * H:
            g_wstream(h + 2, st.n)

        def g_kstore(g, h=h, slk=slk, kev=kev):
            g.wait_ge(ss.h, kev)
            g.dma_start(out=kin.ap()[h * 128:(h + 1) * 128, :], in_=kstage[slk][:]).then_inc(skst[slk].inc(16), 16)
        block.gpsimd(g_kstore)
        kst_free[slk] = (skst[slk].h, skst[slk].n)

        if h == 7:
            def g_agk1(g, gates=(kst_free[0], kst_free[1])):
                for sh, sn in gates:
                    g.wait_ge(sh, sn)
                g.collective_compute("AllGather", ALU.bypass,
                                     replica_groups=[[0, 1], [2, 3], [4, 5], [6, 7]],
                                     ins=[kin.ap()[0:1024, :]], outs=[kout.ap()[0:2048, :]]).then_inc(scc.inc())
            block.gpsimd(g_agk1)
            agk1_done = scc.n
    k_stored = [(skst[0].h, skst[0].n), (skst[1].h, skst[1].n)]

    def g_agk2(g, gates=tuple(k_stored)):
        for sh, sn in gates:
            g.wait_ge(sh, sn)
        g.collective_compute("AllGather", ALU.bypass,
                             replica_groups=[[0, 1], [2, 3], [4, 5], [6, 7]],
                             ins=[kin.ap()[1024:2048, :]], outs=[kout.ap()[2048:4096, :]]).then_inc(scc.inc())
    block.gpsimd(g_agk2)
    agk_done = scc.n

    for h in range(H):
        for tb in range(4):
            def t_qproj(t, h=h, tb=tb, wl=w_loaded_s[H + h], ss_gate=ss.n):
                if tb == 0:
                    t.wait_ge(wl[0], wl[1])
                if ss_gate:
                    t.wait_ge(ss.h, ss_gate)
                for dc in range(DC):
                    t.matmul(ps[tb % 2],
                             wqk[h % 2][:, dc * 128:(dc + 1) * 128],
                             xbf[:, dc * T + tb * 512:dc * T + (tb + 1) * 512],
                             start=(dc == 0), stop=(dc == DC - 1)).then_inc(st.inc())
            block.tensor(t_qproj)
            qmm = st.n

            def s_qevac(s, h=h, tb=tb, qmm=qmm):
                s = AS(s, ss)
                s._e.wait_ge(st.h, qmm)
                s.activation(qsb[:, h * T + tb * 512: h * T + (tb + 1) * 512], ps[tb % 2],
                             AF.Identity, bias=vecs["bqT"][:, h:h + 1])
            block.scalar(s_qevac)
        if H + h + 2 < 2 * H:
            g_wstream(H + h + 2, st.n)
    qproj_done = ss.n


    kh_loaded = {}
    kh_free = [(ss.h, 0), (ss.h, 0)]

    def emit_khload(h):
        slk = h % 2

        r0 = h * 128 if h < 8 else 2048 + (h - 8) * 128

        def g_kh(g, h=h, slk=slk, gate=kh_free[slk], r0=r0,
                 agw=(agk1_done if h < 8 else agk_done)):
            g.wait_ge(scc.h, agw)
            g.wait_ge(gate[0], gate[1])
            g.dma_start(out=khh[slk][:, 0:T], in_=kout.ap()[r0:r0 + 128, :]).then_inc(sxk[slk].inc(16), 16)
            g.dma_start(out=khh[slk][:, T:S], in_=kout.ap()[1024 + r0: 1024 + r0 + 128, :]).then_inc(sxk[slk].inc(16), 16)
        block.gpsimd(g_kh)
        kh_loaded[h] = (sxk[slk].h, sxk[slk].n)

    emit_khload(0)
    emit_khload(1)

    es_proj.close()
    vhat = es_attn.enter_context(nc.sbuf_tensor("l1a_vhat", [128, KC * VW], BF16))

    def v_vhones(v, gates=(k_stored[0], k_stored[1], (svst[0].h, svst[0].n), (svst[1].h, svst[1].n)),
                 qd=qproj_done):
        v = AS(v, sv)
        for sh, sn in gates:
            v._e.wait_ge(sh, sn)
        v._e.wait_ge(ss.h, qd)
        v.memset(vhat[:].rearrange("p (kc h c) -> p kc h c", h=H, c=65)[:, :, :, 64:65], 1.0)
    block.vector(v_vhones)
    vhones_done = sv.n

    def g_vhload(g, ag_done=agv_done, vhones_done=vhones_done):
        g.wait_ge(scc.h, ag_done)
        g.wait_ge(sv.h, vhones_done)
        for kc in range(KC):
            dst = vhat[:, kc * VW:(kc + 1) * VW].rearrange("p (h c) -> p h c", c=65)[:, :, 0:64]
            g.dma_start(out=dst, in_=vout.ap()[kc * 128:(kc + 1) * 128, :].rearrange(
                "p (h c) -> p h c", c=64)).then_inc(svh.inc(16), 16)
    block.gpsimd(g_vhload)
    vhat_loaded = svh.n

    QB = 4
    exp_read = [0, 0]
    pv_read = [0, 0]
    comb_done = sv.n
    oevac_done = 0
    p12s = [p1s, p2s]
    pend = None

    def make_epilogue(pv_done, h, qb):
        stg = {}

        def e_oevac():
            nonlocal oevac_done

            def v_oevac(v, pv_done=pv_done):
                v = AS(v, sv)
                v._e.wait_ge(st.h, pv_done)
                v.tensor_copy(o1s[0:65, :], pO1[0:65, 0:512])
                v.tensor_copy(o2s[0:65, :], pO2[0:65, 0:512])
            block.vector(v_oevac)
            oevac_done = sv.n

        def e_recip():
            def v_recip(v):
                v = AS(v, sv)
                v.reciprocal(o1s[64:65, :], o1s[64:65, :])
                v.reciprocal(o2s[64:65, :], o2s[64:65, :])
            block.vector(v_recip)
            stg["rdone"] = sv.n

        def e_bc():
            def t_bc(t, rdone=stg["rdone"]):
                t.wait_ge(sv.h, rdone)
                t.matmul(pG1[0:64, 0:512], ones65[64:65, :], o1s[64:65, :],
                         start=True, stop=True, tile_position=(64, 0),
                         skip_group_check=True).then_inc(st.inc())
                t.matmul(pG2[0:64, 0:512], ones65[64:65, :], o2s[64:65, :],
                         start=True, stop=True, tile_position=(64, 0),
                         skip_group_check=True).then_inc(st.inc())
            block.tensor(t_bc)
            stg["bc"] = st.n

        def e_comb2():
            nonlocal comb_done

            def v_comb2(v, h=h, qb=qb, bc_done=stg["bc"]):
                v = AS(v, sv)
                v._e.wait_ge(st.h, bc_done)
                v.tensor_tensor(o1s[0:64, :], o1s[0:64, :], pG1[:, 0:512], ALU.mult)
                v.tensor_tensor(o2s[0:64, :], o2s[0:64, :], pG2[:, 0:512], ALU.mult)
                dst = ocat[(h % 2) * 64:(h % 2) * 64 + 64,
                           (h // 2) * T + qb * 512:(h // 2) * T + (qb + 1) * 512]
                v.scalar_tensor_tensor(dst, o2s[0:64, :], vecs["lamb"][0:64, 0:1], o1s[0:64, :],
                                       ALU.mult, ALU.add)
            block.vector(v_comb2)
            comb_done = sv.n

        return {"oevac": e_oevac, "recip": e_recip, "bc": e_bc, "comb2": e_comb2}

    kh_ready = None
    for h in range(H):
        slk = h % 2
        if h == 0:
            def v_center0(v, slk=slk, ld=kh_loaded[0]):
                v = AS(v, sv)
                v._e.wait_ge(ld[0], ld[1])
                v.tensor_reduce(ksum[:, 0:1], khh[slk][:], AX.X, ALU.add)
                v.tensor_scalar(ksum[:, 0:1], ksum[:, 0:1], 1.0 / 4096.0, None, ALU.mult)
                v.tensor_scalar(khh[slk][:], khh[slk][:], ksum[:, 0:1], None, ALU.subtract)
            block.vector(v_center0)
            kh_ready = sv.n

        for qb in range(QB):
            qsl = slice(h * T + qb * 512, h * T + (qb + 1) * 512)
            def emit_qk(kc, first):
                sset = kc % 2

                def t_qk(t, kc=kc, sset=sset, first=first, slk=slk,
                         kh_ready=kh_ready, er=exp_read[sset], vh=vhat_loaded,
                         qd=qproj_done):
                    if first:
                        t.wait_ge(sv.h, kh_ready)
                        t.wait_ge(svh.h, vh)
                        t.wait_ge(ss.h, qd)
                    if er:
                        t.wait_ge(ss.h, er)
                    t.matmul(pS[sset][:, 0:512], khh[slk][0:64, kc * 128:(kc + 1) * 128],
                             qsb[0:64, qsl], start=True, stop=True,
                             tile_position=(0, 0), skip_group_check=True).then_inc(st.inc())
                    t.matmul(pS[sset][:, 512:1024], khh[slk][64:128, kc * 128:(kc + 1) * 128],
                             qsb[64:128, qsl], start=True, stop=True,
                             tile_position=(64, 0), skip_group_check=True).then_inc(st.inc())
                block.tensor(t_qk)
                return st.n

            def emit_exp(kc):
                sset = kc % 2
                qk_done = qk_count[kc]

                def s_exp(s, sset=sset, qk_done=qk_done, pr=pv_read[sset]):
                    s = AS(s, ss)
                    s._e.wait_ge(st.h, qk_done)
                    if pr:
                        s._e.wait_ge(st.h, pr)
                    s.activation(p12s[sset][:], pS[sset][:], AF.Exp, scale=0.125)
                block.scalar(s_exp)
                exp_read[sset] = ss.n
                return ss.n

            def emit_pv(kc):
                sset = kc % 2
                exp_done = exp_count[kc]

                def t_pv(t, h=h, kc=kc, sset=sset, exp_done=exp_done, oe=oevac_done):
                    t.wait_ge(ss.h, exp_done)
                    if kc == 0 and oe:
                        t.wait_ge(sv.h, oe)
                    fl = (kc == 0)
                    ll = (kc == KC - 1)
                    vsl = vhat[:, kc * VW + h * 65: kc * VW + (h + 1) * 65]
                    t.matmul(pO1[0:65, 0:512], vsl, p12s[sset][:, 0:512],
                             start=fl, stop=ll, skip_group_check=True).then_inc(st.inc())
                    t.matmul(pO2[0:65, 0:512], vsl, p12s[sset][:, 512:1024],
                             start=fl, stop=ll, skip_group_check=True).then_inc(st.inc())
                block.tensor(t_pv)
                pv_read[sset] = st.n

            qk_count = {}
            exp_count = {}
            qk_count[0] = emit_qk(0, True)
            exp_count[0] = emit_exp(0)
            if pend is not None:
                pend["oevac"]()
            qk_count[1] = emit_qk(1, False)
            exp_count[1] = emit_exp(1)
            if pend is not None:
                pend["recip"]()
            emit_pv(0)
            for kc in range(2, KC):
                qk_count[kc] = emit_qk(kc, False)
                exp_count[kc] = emit_exp(kc)
                emit_pv(kc - 1)
                if pend is not None and kc == 16:
                    pend["bc"]()
                if pend is not None and kc == 18:
                    pend["comb2"]()
                if qb == 3 and kc == 20 and h + 1 < H:
                    def v_center(v, nslk=(h + 1) % 2, ld=kh_loaded[h + 1]):
                        v = AS(v, sv)
                        v._e.wait_ge(ld[0], ld[1])
                        v.tensor_reduce(ksum[:, 0:1], khh[nslk][:], AX.X, ALU.add)
                        v.tensor_scalar(ksum[:, 0:1], ksum[:, 0:1], 1.0 / 4096.0, None, ALU.mult)
                        v.tensor_scalar(khh[nslk][:], khh[nslk][:], ksum[:, 0:1], None, ALU.subtract)
                    block.vector(v_center)
                    kh_ready = sv.n
            emit_pv(KC - 1)
            pend = make_epilogue(st.n, h, qb)
        kh_free[slk] = (st.h, st.n)
        if h + 2 < H:
            emit_khload(h + 2)
    pend["oevac"]()
    pend["recip"]()
    pend["bc"]()
    pend["comb2"]()
    attn_done = sv.n
    es_attn.close()
    yout = es.enter_context(nc.sbuf_tensor("l1a_yout", [128, DC * 512], F32))
    ytmp = es.enter_context(nc.sbuf_tensor("l1a_ytmp", [128, DC * 512], F32))
    xres = [es.enter_context(nc.sbuf_tensor(f"l1a_xres{i}", [128, DC * 512], F32)) for i in range(2)]
    ystat = ytmp[0:1, 0:2048]
    ybf8 = qsb[:, 0:DC * 512]
    ysq8 = qsb[:, DC * 512:2 * DC * 512]

    def g_wo(g, attn_done=attn_done):
        g.wait_ge(sv.h, attn_done)
        for c in range(DC):
            g.dma_start(out=wosb[:, c * 1024:(c + 1) * 1024],
                        in_=w_aps["wo"][c * 128:(c + 1) * 128, :]).then_inc(swo.inc(16), 16)
        for i in range(2):
            for c in range(DC):
                g.dma_start(out=xres[i][:, c * 512:(c + 1) * 512],
                            in_=x_in_r[:, c, i * 512:(i + 1) * 512]).then_inc(sxr[i].inc(16), 16)
    block.gpsimd(g_wo)
    wo_loaded = swo.n
    xr_loaded = {0: (sxr[0].h, sxr[0].n), 1: (sxr[1].h, sxr[1].n)}

    stats_gate = 0
    for tb in range(4):
        for dc in range(DC):
            def t_oproj(t, tb=tb, dc=dc, wo_loaded=wo_loaded, attn_done=attn_done, bank_gate=ss.n, vgate=sv.n):
                if dc == 0 and tb == 0:
                    t.wait_ge(swo.h, wo_loaded)
                    t.wait_ge(sv.h, attn_done)
                if dc == 0 and tb > 0:
                    t.wait_ge(sv.h, vgate)
                if bank_gate:
                    t.wait_ge(ss.h, bank_gate)
                for hc in range(DC):
                    t.matmul(pS[1][:, (dc % 2) * 512:(dc % 2) * 512 + 512],
                             wosb[:, hc * 1024 + dc * 128: hc * 1024 + (dc + 1) * 128],
                             ocat[:, hc * T + tb * 512:hc * T + (tb + 1) * 512],
                             start=(hc == 0), stop=(hc == DC - 1)).then_inc(st.inc())
            block.tensor(t_oproj)
            omm = st.n

            def s_yevac(s, dc=dc, omm=omm, resid_gate=sv.n, store_gate=so.n):
                s = AS(s, ss)
                s._e.wait_ge(st.h, omm)
                if resid_gate:
                    s._e.wait_ge(sv.h, resid_gate)
                if store_gate:
                    s._e.wait_ge(so.h, store_gate)
                s.activation(yout[:, dc * 512:(dc + 1) * 512],
                             pS[1][:, (dc % 2) * 512:(dc % 2) * 512 + 512],
                             AF.Identity, bias=vecs["bo"][:, dc:dc + 1])
            block.scalar(s_yevac)
        yproj = ss.n

        def v_c1all(v, yproj=yproj):
            v = AS(v, sv)
            v._e.wait_ge(ss.h, yproj)
            for dc in range(DC):
                v.tensor_copy(ybf8[:, dc * 512:(dc + 1) * 512], yout[:, dc * 512:(dc + 1) * 512])
        block.vector(v_c1all)
        nca = sv.n

        def s_c1all(s, yproj=yproj):
            s = AS(s, ss)
            for dc in range(DC):
                s.activation(ysq8[:, dc * 512:(dc + 1) * 512],
                             yout[:, dc * 512:(dc + 1) * 512], AF.Square)
        block.scalar(s_c1all)
        nsq = ss.n

        def t_st1all(t, nca=nca, nsq=nsq, sg=stats_gate):
            t.wait_ge(sv.h, nca)
            t.wait_ge(ss.h, nsq)
            if sg:
                t.wait_ge(sv.h, sg)
            for dc in range(DC):
                t.matmul(pG1[0:1, 0:512], onesb[:], ybf8[:, dc * 512:(dc + 1) * 512],
                         start=(dc == 0), stop=(dc == DC - 1), skip_group_check=True).then_inc(st.inc())
            for dc in range(DC):
                t.matmul(pG2[0:1, 0:512], onesb[:], ysq8[:, dc * 512:(dc + 1) * 512],
                         start=(dc == 0), stop=(dc == DC - 1), skip_group_check=True).then_inc(st.inc())
        block.tensor(t_st1all)
        nstat = st.n

        def v_mu1(v, nstat=nstat):
            v = AS(v, sv)
            v._e.wait_ge(st.h, nstat)
            v.tensor_scalar(ystat[:, 0:512], pG1[0:1, 0:512], 1.0 / 1024.0, None, ALU.mult)
            v.tensor_scalar(ystat[:, 512:1024], pG2[0:1, 0:512], 1.0 / 1024.0, None, ALU.mult)
            v.tensor_tensor(ystat[:, 1024:1536], ystat[:, 0:512], ystat[:, 0:512], ALU.mult)
            v.tensor_tensor(ystat[:, 1024:1536], ystat[:, 512:1024], ystat[:, 1024:1536], ALU.subtract)
            v.tensor_scalar(ystat[:, 1024:1536], ystat[:, 1024:1536], LN_EPS, None, ALU.add)
        block.vector(v_mu1)
        nvar = sv.n

        def s_sd1(s, nvar=nvar):
            s = AS(s, ss)
            s._e.wait_ge(sv.h, nvar)
            s.activation(ystat[:, 1536:2048], ystat[:, 1024:1536], AF.Sqrt)
        block.scalar(s_sd1)
        nstd = ss.n

        def v_rs1(v, nstd=nstd):
            v = AS(v, sv)
            v._e.wait_ge(ss.h, nstd)
            v.reciprocal(ystat[:, 512:1024], ystat[:, 1536:2048])
        block.vector(v_rs1)
        nrst = sv.n

        def t_bc1(t, nrst=nrst, sg=stats_gate):
            t.wait_ge(sv.h, nrst)
            t.matmul(ps[0], onesr[:], ystat[0:1, 0:512], start=True, stop=True).then_inc(st.inc())
            t.matmul(ps[1], onesr[:], ystat[0:1, 512:1024], start=True, stop=True).then_inc(st.inc())
        block.tensor(t_bc1)
        nbc = st.n

        def v_ap1(v, tb=tb, nbc=nbc, xr=xr_loaded[tb]):
            v = AS(v, sv)
            v._e.wait_ge(st.h, nbc)
            v._e.wait_ge(xr[0], xr[1])
            for dc in range(DC):
                sl = slice(dc * 512, (dc + 1) * 512)
                v.tensor_tensor(ytmp[:, sl], yout[:, sl], ps[0], ALU.subtract)
            for dc in range(DC):
                sl = slice(dc * 512, (dc + 1) * 512)
                v.tensor_tensor(ytmp[:, sl], ytmp[:, sl], ps[1], ALU.mult)
            for dc in range(DC):
                sl = slice(dc * 512, (dc + 1) * 512)
                v.tensor_scalar(xres[tb % 2][:, sl], xres[tb % 2][:, sl],
                                vecs["bp"][:, dc:dc + 1], None, ALU.add)
                v.scalar_tensor_tensor(yout[:, sl], ytmp[:, sl], vecs["gp"][:, dc:dc + 1],
                                       xres[tb % 2][:, sl], ALU.mult, ALU.add)
        block.vector(v_ap1)
        nap1 = sv.n

        def v_c2all(v, nap1=nap1):
            v = AS(v, sv)
            for dc in range(DC):
                v.tensor_copy(ybf8[:, dc * 512:(dc + 1) * 512], yout[:, dc * 512:(dc + 1) * 512])
        block.vector(v_c2all)
        nca = sv.n

        def s_c2all(s, nap1=nap1):
            s = AS(s, ss)
            s._e.wait_ge(sv.h, nap1)
            for dc in range(DC):
                s.activation(ysq8[:, dc * 512:(dc + 1) * 512],
                             yout[:, dc * 512:(dc + 1) * 512], AF.Square)
        block.scalar(s_c2all)
        nsq = ss.n

        def t_st2all(t, nca=nca, nsq=nsq):
            t.wait_ge(sv.h, nca)
            t.wait_ge(ss.h, nsq)
            for dc in range(DC):
                t.matmul(pG1[0:1, 0:512], onesb[:], ybf8[:, dc * 512:(dc + 1) * 512],
                         start=(dc == 0), stop=(dc == DC - 1), skip_group_check=True).then_inc(st.inc())
            for dc in range(DC):
                t.matmul(pG2[0:1, 0:512], onesb[:], ysq8[:, dc * 512:(dc + 1) * 512],
                         start=(dc == 0), stop=(dc == DC - 1), skip_group_check=True).then_inc(st.inc())
        block.tensor(t_st2all)
        nstat2 = st.n

        def v_mu2(v, nstat2=nstat2):
            v = AS(v, sv)
            v._e.wait_ge(st.h, nstat2)
            v.tensor_scalar(ystat[:, 0:512], pG1[0:1, 0:512], 1.0 / 1024.0, None, ALU.mult)
            v.tensor_scalar(ystat[:, 512:1024], pG2[0:1, 0:512], 1.0 / 1024.0, None, ALU.mult)
            v.tensor_tensor(ystat[:, 1024:1536], ystat[:, 0:512], ystat[:, 0:512], ALU.mult)
            v.tensor_tensor(ystat[:, 1024:1536], ystat[:, 512:1024], ystat[:, 1024:1536], ALU.subtract)
            v.tensor_scalar(ystat[:, 1024:1536], ystat[:, 1024:1536], LN_EPS, None, ALU.add)
        block.vector(v_mu2)
        nvar2 = sv.n

        def s_sd2(s, nvar2=nvar2):
            s = AS(s, ss)
            s._e.wait_ge(sv.h, nvar2)
            s.activation(ystat[:, 1536:2048], ystat[:, 1024:1536], AF.Sqrt)
        block.scalar(s_sd2)
        nstd2 = ss.n

        def v_rs2b(v, nstd2=nstd2):
            v = AS(v, sv)
            v._e.wait_ge(ss.h, nstd2)
            v.reciprocal(ystat[:, 512:1024], ystat[:, 1536:2048])
        block.vector(v_rs2b)
        nrst2 = sv.n

        def t_bc2(t, nrst2=nrst2):
            t.wait_ge(sv.h, nrst2)
            t.matmul(ps[0], onesr[:], ystat[0:1, 0:512], start=True, stop=True).then_inc(st.inc())
            t.matmul(ps[1], onesr[:], ystat[0:1, 512:1024], start=True, stop=True).then_inc(st.inc())
        block.tensor(t_bc2)
        nbc2 = st.n

        def v_ap2(v, tb=tb, nbc2=nbc2, store_gate=so.n):
            v = AS(v, sv)
            v._e.wait_ge(st.h, nbc2)
            if store_gate:
                v._e.wait_ge(so.h, store_gate)
            for dc in range(DC):
                sl = slice(dc * 512, (dc + 1) * 512)
                v.tensor_tensor(ytmp[:, sl], yout[:, sl], ps[0], ALU.subtract)
            for dc in range(DC):
                sl = slice(dc * 512, (dc + 1) * 512)
                v.tensor_tensor(yout[:, sl], ytmp[:, sl], ps[1], ALU.mult)
        block.vector(v_ap2)
        nap2 = sv.n
        stats_gate = nap2

        def s_store(s, tb=tb, nap2=nap2):
            s.wait_ge(sv.h, nap2)
            for dc in range(DC):
                s.dma_start(out=x_out_r[:, dc, tb * 512:(tb + 1) * 512],
                            in_=yout[:, dc * 512:(dc + 1) * 512]).then_inc(so.inc(16), 16)
        block.sync(s_store)

        if tb + 2 < 4:
            def g_xr(g, tb=tb, gate=sv.n):
                g.wait_ge(sv.h, gate)
                for c in range(DC):
                    g.dma_start(out=xres[tb % 2][:, c * 512:(c + 1) * 512],
                                in_=x_in_r[:, c, (tb + 2) * 512:(tb + 3) * 512]).then_inc(sxr[tb % 2].inc(16), 16)
            block.gpsimd(g_xr)
            xr_loaded[tb + 2] = (sxr[tb % 2].h, sxr[tb % 2].n)

    def g_fin(g):
        g.wait_ge(so.h, so.n)
    block.gpsimd(g_fin)


def build_l1a_launch():
    nc = bass.Bass("TRN2", target_bir_lowering=False)
    x_in = nc.dram_tensor("x_in", [D, T], F32, kind="ExternalInput").ap()
    x_out = nc.dram_tensor("x_out", [D, T], F32, kind="ExternalOutput").ap()
    w_aps = dict(
        wq=nc.dram_tensor("wq", [16, 1024, 128], F32, kind="ExternalInput").ap(),
        wk=nc.dram_tensor("wk", [16, 1024, 128], F32, kind="ExternalInput").ap(),
        wvT=nc.dram_tensor("wvT", [1024, 1024], F32, kind="ExternalInput").ap(),
        wo=nc.dram_tensor("wo", [1024, 1024], F32, kind="ExternalInput").ap(),
        bqT=nc.dram_tensor("bqT", [128, 16], F32, kind="ExternalInput").ap(),
        bkT=nc.dram_tensor("bkT", [128, 16], F32, kind="ExternalInput").ap(),
        bvrow=nc.dram_tensor("bvrow", [1, 1024], F32, kind="ExternalInput").ap(),
        bo=nc.dram_tensor("bo", [128, 8], F32, kind="ExternalInput").ap(),
        gamma=nc.dram_tensor("gamma", [128, 8], F32, kind="ExternalInput").ap(),
        beta=nc.dram_tensor("beta", [128, 8], F32, kind="ExternalInput").ap(),
        lamb=nc.dram_tensor("lamb", [128, 1], F32, kind="ExternalInput").ap(),
        onesr=nc.dram_tensor("onesr", [1, 128], F32, kind="ExternalInput").ap(),
    )
    scratch = dict(
        cc_kin=nc.dram_tensor("cc_kin", [2048, 2048], BF16),
        cc_kout=nc.dram_tensor("cc_kout", [4096, 2048], BF16),
        cc_vin=nc.dram_tensor("cc_vin", [2048, 1024], BF16),
        cc_vout=nc.dram_tensor("cc_vout", [4096, 1024], BF16),
    )
    es = contextlib.ExitStack()
    with es:
        block = DirectBlock(nc)
        es.enter_context(nc.Block())
        emit_l1_attn(nc, es, block, x_in, x_out, w_aps, scratch)
    return nc


def host_l1a_inputs(weights):
    wv = np.asarray(weights["L1_attn_Wv"], np.float32)
    return dict(
        wq=np.ascontiguousarray(weights["L1_attn_Wq"], dtype=np.float32),
        wk=np.ascontiguousarray(weights["L1_attn_Wk"], dtype=np.float32),
        wvT=np.ascontiguousarray(wv.transpose(1, 0, 2).reshape(1024, 1024)),
        wo=np.ascontiguousarray(weights["L1_attn_Wo"], dtype=np.float32),
        bqT=np.ascontiguousarray(np.asarray(weights["L1_attn_bq"], np.float32).T),
        bkT=np.ascontiguousarray(np.asarray(weights["L1_attn_bk"], np.float32).T),
        bvrow=np.ascontiguousarray(np.asarray(weights["L1_attn_bv"], np.float32).reshape(1, 1024)),
        bo=pack_vec(weights["L1_attn_bo"]),
        gamma=pack_vec(weights["L1_attn_gamma"]),
        beta=pack_vec(weights["L1_attn_beta"]),
        lamb=np.ascontiguousarray(np.broadcast_to(
            np.asarray(weights["L1_attn_lam"], np.float32).reshape(1, 1), (128, 1))),
        onesr=np.ones((1, 128), np.float32),
    )


def run_l1a(x_full, weights):
    nc = build_l1a_launch()
    base = host_l1a_inputs(weights)
    in_maps = []
    for c in range(NCORES):
        b, hf = c // 2, c % 2
        xs = np.ascontiguousarray(x_full[b, hf * T:(hf + 1) * T, :])
        m = dict(base)
        m["x_in"] = fmT(xs)
        in_maps.append(m)
    res = run_bass_kernel_spmd(nc, in_maps, core_ids=list(range(NCORES)))
    out = np.zeros((B, S, D), np.float32)
    for c in range(NCORES):
        b, hf = c // 2, c % 2
        out[b, hf * T:(hf + 1) * T, :] = np.asarray(res.results[c]["x_out"], np.float32).T
    return out



def emit_l1_ffn(nc, es, block, x_in, x_out, w_aps, scratch, ses=None):
    ses = es if ses is None else ses
    """Routed FFN block.

    w_aps: lw1 [1024,512], lw2 [512,1024], hw1 [1024,4096], hw2 [4096,1024],
    rt [1024, 1] (routing token, fm), identb [128,128], ltri [128,128]
    (ltri[k,m]=1 if k<=m), halfsel [128,32] (1 on own-half columns);
    lb1 [128,4], lb2/lgam/hgam/hb2 [128,8], hb1 [128,32]  (all f32)
    scratch: s_in [128,16] f32 dram, s_out [256,16] f32 dram
    """
    NTT = T // 128
    TBL = 256
    NTB = T // TBL
    IH = 4096
    IHC = IH // 128
    LOGK = float(np.log(np.float32(1152.0)))

    whw1 = es.enter_context(nc.sbuf_tensor("l1f_whw1", [128, DC * IH], BF16))
    whw2 = es.enter_context(nc.sbuf_tensor("l1f_whw2", [128, IHC * D], BF16))
    wl1 = es.enter_context(nc.sbuf_tensor("l1f_wl1", [128, DC * 512], BF16))
    wl2 = es.enter_context(nc.sbuf_tensor("l1f_wl2", [128, 4 * D], BF16))
    identb = es.enter_context(nc.sbuf_tensor("l1f_identb", [128, 128], BF16))
    ltri = es.enter_context(nc.sbuf_tensor("l1f_ltri", [128, 128], BF16))
    halfsel = es.enter_context(nc.sbuf_tensor("l1f_halfsel", [128, 32], F32))
    rtsb = es.enter_context(nc.sbuf_tensor("l1f_rt", [128, 8], F32))
    vecs = {v: es.enter_context(nc.sbuf_tensor("l1f_" + v, [128, 32], F32))
            for v in ["lb1", "lb2", "lgam", "hgam", "hb1", "hb2"]}
    onesb = es.enter_context(nc.sbuf_tensor("l1f_onesb", [128, 1], BF16))
    onesrb = es.enter_context(nc.sbuf_tensor("l1f_onesrb", [1, 128], BF16))
    onesr = es.enter_context(nc.sbuf_tensor("l1f_onesr", [1, 128], F32))
    sfull = es.enter_context(nc.sbuf_tensor("l1f_sfull", [128, 32], F32))
    key = es.enter_context(nc.sbuf_tensor("l1f_key", [128, 32], F32))
    ebuf = es.enter_context(nc.sbuf_tensor("l1f_ebuf", [128, 32], BF16))
    rscl = es.enter_context(nc.sbuf_tensor("l1f_rscl", [1, 8], F32))
    mask = es.enter_context(nc.sbuf_tensor("l1f_mask", [128, 32], F32))
    mwork = es.enter_context(nc.sbuf_tensor("l1f_mwork", [128, 32], F32))
    mrow2 = es.enter_context(nc.sbuf_tensor("l1f_mrow2", [1, 64], F32))
    rtb = es.enter_context(nc.sbuf_tensor("l1f_rtb", [128, 8], BF16))
    mask_bf = es.enter_context(nc.sbuf_tensor("l1f_maskbf", [128, 16], BF16))
    sstage = es.enter_context(nc.sbuf_tensor("l1f_sstage", [128, 16], F32))
    xbt = es.enter_context(nc.sbuf_tensor("l1f_xbt", [128, 1024], BF16))
    xres = [es.enter_context(nc.sbuf_tensor(f"l1f_xres{i}", [128, DC * TBL], F32)) for i in range(2)]
    xg = es.enter_context(nc.sbuf_tensor("l1f_xg", [128, DC * TBL], F32))
    xlb = es.enter_context(nc.sbuf_tensor("l1f_xlb", [128, DC * TBL], BF16))
    xhb = es.enter_context(nc.sbuf_tensor("l1f_xhb", [128, DC * TBL], BF16))
    lhb = es.enter_context(nc.sbuf_tensor("l1f_lhb", [128, 4 * TBL], BF16))
    lacc = es.enter_context(nc.sbuf_tensor("l1f_lacc", [128, DC * TBL], F32))
    hv1 = es.enter_context(nc.sbuf_tensor("l1f_hv1", [128, IHC * TBL], BF16))
    yacc = lacc
    ytmp = xg
    ystat = ytmp[0:1, 0:4 * TBL]
    ybf = es.enter_context(nc.sbuf_tensor("l1f_ybf", [128, TBL], BF16))
    ysq = xlb[:, 0:TBL]
    ybf8 = xhb
    ysq8 = xlb

    pA = es.enter_context(nc.psum_tensor("l1f_pA", [128, 512], F32))
    pB = es.enter_context(nc.psum_tensor("l1f_pB", [128, 512], F32))
    pC = es.enter_context(nc.psum_tensor("l1f_pC", [128, 512], F32))
    pD = es.enter_context(nc.psum_tensor("l1f_pD", [128, 512], F32))
    pE = es.enter_context(nc.psum_tensor("l1f_pE", [128, 512], F32))
    pF = es.enter_context(nc.psum_tensor("l1f_pF", [128, 512], F32))
    pG = es.enter_context(nc.psum_tensor("l1f_pG", [1, 512], F32))
    pH = es.enter_context(nc.psum_tensor("l1f_pH", [1, 512], F32))

    sgw = Sem(ses, nc, "l1f_sgw")
    sxt = Sem(ses, nc, "l1f_sxt")
    sss = Sem(ses, nc, "l1f_sss")
    sxr = [Sem(ses, nc, f"l1f_sxr{i}") for i in range(2)]
    scc = Sem(ses, nc, "l1f_scc")
    sld = Sem(ses, nc, "l1f_sld")
    st = Sem(ses, nc, "l1f_st")
    sv = Sem(ses, nc, "l1f_sv")
    ss = Sem(ses, nc, "l1f_ss")
    so = Sem(ses, nc, "l1f_so")

    x_in_r = x_in.rearrange("(c p) t -> p c t", p=128)
    x_out_r = x_out.rearrange("(c p) t -> p c t", p=128)
    s_in, s_out = scratch["s_in"], scratch["s_out"]

    sgh = [Sem(ses, nc, f"l1f_sgh{i}") for i in range(2)]

    def g_loads(g):
        for c in range(DC):
            g.dma_start(out=wl1[:, c * 512:(c + 1) * 512],
                        in_=w_aps["lw1"][c * 128:(c + 1) * 128, :]).then_inc(sgw.inc(16), 16)
        for c in range(4):
            g.dma_start(out=wl2[:, c * D:(c + 1) * D],
                        in_=w_aps["lw2"][c * 128:(c + 1) * 128, :]).then_inc(sgw.inc(16), 16)
        g.dma_start(out=identb[:], in_=w_aps["identb"]).then_inc(sgw.inc(16), 16)
        g.dma_start(out=ltri[:], in_=w_aps["ltri"]).then_inc(sgw.inc(16), 16)
        g.dma_start(out=halfsel[:], in_=w_aps["halfsel"]).then_inc(sgw.inc(16), 16)
        g.dma_start(out=rtsb[:], in_=w_aps["rt"]).then_inc(sgw.inc(16), 16)
        for v, wd in [("lb1", 4), ("lb2", 8), ("lgam", 8), ("hgam", 8), ("hb1", 32), ("hb2", 8)]:
            g.dma_start(out=vecs[v][:, 0:wd], in_=w_aps[v]).then_inc(sgw.inc(16), 16)
    block.gpsimd(g_loads)
    loads_done = sgw.n

    def g_loadh(g):
        for c in range(DC):
            g.dma_start(out=whw1[:, c * IH:(c + 1) * IH],
                        in_=w_aps["hw1"][c * 128:(c + 1) * 128, :]).then_inc(sgh[0].inc(16), 16)
        for c in range(IHC):
            g.dma_start(out=whw2[:, c * D:(c + 1) * D],
                        in_=w_aps["hw2"][c * 128:(c + 1) * 128, :]).then_inc(sgh[1].inc(16), 16)
    block.gpsimd(g_loadh)
    h1_ld = sgh[0].n
    h2_ld = sgh[1].n

    def v_init(v):
        v = AS(v, sv)
        v._e.wait_ge(sgw.h, loads_done)
        v.memset(onesb[:], 1.0)
        v.memset(onesrb[:], 1.0)
        v.memset(onesr[:], 1.0)
        v.tensor_copy(rtb[:], rtsb[:])
    block.vector(v_init)
    init_done = sv.n

    xs32 = xres[0][:, 0:1024]
    for tt in range(NTT):
        def g_xt(g, tt=tt, gate=st.n, ld=loads_done):
            g.wait_ge(sgw.h, ld)
            if gate:
                g.wait_ge(st.h, gate)
            for c in range(DC):
                g.dma_start(out=xs32[:, c * 128:(c + 1) * 128],
                            in_=x_in_r[:, c, tt * 128:(tt + 1) * 128]).then_inc(sxt.inc(16), 16)
        block.gpsimd(g_xt)
        xt_ld = sxt.n

        def t_s(t, tt=tt, xt_ld=xt_ld, init_done=init_done, evac_gate=sv.n):
            t.wait_ge(sxt.h, xt_ld)
            t.wait_ge(sv.h, max(init_done, evac_gate))
            for dc in range(DC):
                t.matmul(pF[:, tt:tt + 1], xs32[:, dc * 128:(dc + 1) * 128],
                         rtsb[:, dc:dc + 1], start=(dc == 0), stop=(dc == DC - 1),
                         skip_group_check=True).then_inc(st.inc())
        block.tensor(t_s)
    s_mm = st.n

    def v_sevac(v, s_mm=s_mm):
        v = AS(v, sv)
        v._e.wait_ge(st.h, s_mm)
        v.tensor_copy(sstage[:], pF[:, 0:16])
    block.vector(v_sevac)
    s_done = sv.n

    def g_sag(g, s_done=s_done):
        g.wait_ge(sv.h, s_done)
        g.dma_start(out=s_in.ap(), in_=sstage[:]).then_inc(sss.inc(16), 16)
        g.wait_ge(sss.h, sss.n)
        g.collective_compute("AllGather", ALU.bypass,
                             replica_groups=[[0, 1], [2, 3], [4, 5], [6, 7]],
                             ins=[s_in.ap()], outs=[s_out.ap()]).then_inc(scc.inc())
    block.gpsimd(g_sag)
    s_ag = scc.n

    route_q = []

    def rq(fn):
        route_q.append(fn)

    def pump(n=3):
        for _ in range(min(n, len(route_q))):
            route_q.pop(0)()

    def _emit_routing():
        def g_sfl(g, s_ag=s_ag):
            g.wait_ge(scc.h, s_ag)
            g.dma_start(out=sfull[:, 0:16], in_=s_out.ap()[0:128, :]).then_inc(sld.inc(16), 16)
            g.dma_start(out=sfull[:, 16:32], in_=s_out.ap()[128:256, :]).then_inc(sld.inc(16), 16)
        rq(lambda: block.gpsimd(g_sfl))
        sfull_n = sld.n + 32

        def v_r0(v, n=sfull_n):
            v = AS(v, sv)
            v._e.wait_ge(sld.h, n)
            v.memset(rscl[:, 0:1], LOGK - float(np.log(np.float32(4096.0))))
        rq(lambda: block.vector(v_r0))

        for it in range(50):
            def step(it=it):
                def t_bc(t, vgate=sv.n):
                    t.wait_ge(sv.h, vgate)
                    t.matmul(pF[:, 20:21], onesr[:], rscl[0:1, 0:1], start=True, stop=True,
                             skip_group_check=True).then_inc(st.inc())
                block.tensor(t_bc)
                nbc = st.n

                def v_m(v, nbc=nbc):
                    v = AS(v, sv)
                    v._e.wait_ge(st.h, nbc)
                    v.tensor_scalar(key[:], sfull[:], pF[:, 20:21], 0.0, ALU.add, ALU.min)
                block.vector(v_m)
                nm = sv.n

                def s_e(s, nm=nm):
                    s = AS(s, ss)
                    s._e.wait_ge(sv.h, nm)
                    s.activation(ebuf[:], key[:], AF.Exp)
                block.scalar(s_e)
                ne = ss.n

                def t_sum(t, ne=ne):
                    t.wait_ge(ss.h, ne)
                    t.matmul(pF[0:1, 32:64], onesb[:], ebuf[:], start=True, stop=True,
                             skip_group_check=True).then_inc(st.inc())
                block.tensor(t_sum)
                nsum = st.n

                def v_red(v, nsum=nsum):
                    v = AS(v, sv)
                    v._e.wait_ge(st.h, nsum)
                    v.tensor_reduce(rscl[:, 4:5], pF[0:1, 32:64], AX.X, ALU.add)
                block.vector(v_red)
                nred = sv.n

                def s_ln(s, nred=nred):
                    s = AS(s, ss)
                    s._e.wait_ge(sv.h, nred)
                    s.activation(rscl[:, 5:6], rscl[:, 4:5], AF.Ln)
                block.scalar(s_ln)
                nln = ss.n

                def v_up(v, nln=nln):
                    v = AS(v, sv)
                    v._e.wait_ge(ss.h, nln)
                    v.tensor_scalar(rscl[:, 5:6], rscl[:, 5:6], -1.0, LOGK, ALU.mult, ALU.add)
                    v.tensor_tensor(rscl[:, 0:1], rscl[:, 0:1], rscl[:, 5:6], ALU.add)
                block.vector(v_up)
            rq(step)

        def fin_keys():
            def t_bcf(t, vgate=sv.n):
                t.wait_ge(sv.h, vgate)
                t.matmul(pF[:, 20:21], onesr[:], rscl[0:1, 0:1], start=True, stop=True,
                         skip_group_check=True).then_inc(st.inc())
            block.tensor(t_bcf)
            nbcf = st.n

            def v_kf(v, nbcf=nbcf):
                v = AS(v, sv)
                v._e.wait_ge(st.h, nbcf)
                v.tensor_scalar(key[:], sfull[:], pF[:, 20:21], 0.0, ALU.add, ALU.min)
                v.memset(rscl[:, 1:2], -120.0)
                v.memset(rscl[:, 2:3], 1.0)
            block.vector(v_kf)
        rq(fin_keys)

        for it in range(40):
            def bstep(it=it):
                def v_mid(v):
                    v = AS(v, sv)
                    v.tensor_tensor(rscl[:, 4:5], rscl[:, 1:2], rscl[:, 2:3], ALU.add)
                    v.tensor_scalar(rscl[:, 4:5], rscl[:, 4:5], 0.5, None, ALU.mult)
                block.vector(v_mid)
                nmid = sv.n

                def t_bcm(t, nmid=nmid):
                    t.wait_ge(sv.h, nmid)
                    t.matmul(pF[:, 21:22], onesr[:], rscl[0:1, 4:5], start=True, stop=True,
                             skip_group_check=True).then_inc(st.inc())
                block.tensor(t_bcm)
                nbcm = st.n

                def v_cmp(v, nbcm=nbcm):
                    v = AS(v, sv)
                    v._e.wait_ge(st.h, nbcm)
                    v.tensor_scalar(ebuf[:], key[:], pF[:, 21:22], None, ALU.is_gt)
                block.vector(v_cmp)
                ncmp = sv.n

                def t_cnt(t, ncmp=ncmp):
                    t.wait_ge(sv.h, ncmp)
                    t.matmul(pF[0:1, 32:64], onesb[:], ebuf[:], start=True, stop=True,
                             skip_group_check=True).then_inc(st.inc())
                block.tensor(t_cnt)
                ncnt = st.n

                def v_upd(v, ncnt=ncnt):
                    v = AS(v, sv)
                    v._e.wait_ge(st.h, ncnt)
                    v.tensor_reduce(rscl[:, 3:4], pF[0:1, 32:64], AX.X, ALU.add)
                    v.tensor_scalar(rscl[:, 5:6], rscl[:, 3:4], 1024.0, None, ALU.is_ge)
                    v.tensor_tensor(rscl[:, 6:7], rscl[:, 4:5], rscl[:, 1:2], ALU.subtract)
                    v.tensor_tensor(rscl[:, 6:7], rscl[:, 5:6], rscl[:, 6:7], ALU.mult)
                    v.tensor_tensor(rscl[:, 1:2], rscl[:, 1:2], rscl[:, 6:7], ALU.add)
                    v.tensor_tensor(rscl[:, 6:7], rscl[:, 4:5], rscl[:, 2:3], ALU.subtract)
                    v.tensor_tensor(rscl[:, 7:8], rscl[:, 5:6], rscl[:, 6:7], ALU.mult)
                    v.tensor_tensor(rscl[:, 6:7], rscl[:, 6:7], rscl[:, 7:8], ALU.subtract)
                    v.tensor_tensor(rscl[:, 2:3], rscl[:, 2:3], rscl[:, 6:7], ALU.add)
                block.vector(v_upd)
            rq(bstep)

        def maskp():
            def t_bch(t, vgate=sv.n):
                t.wait_ge(sv.h, vgate)
                t.matmul(pF[:, 20:21], onesr[:], rscl[0:1, 2:3], start=True, stop=True,
                         skip_group_check=True).then_inc(st.inc())
                t.matmul(pF[:, 21:22], onesr[:], rscl[0:1, 1:2], start=True, stop=True,
                         skip_group_check=True).then_inc(st.inc())
            block.tensor(t_bch)
            nbch = st.n

            def v_tc(v, nbch=nbch):
                v = AS(v, sv)
                v._e.wait_ge(st.h, nbch)
                v.tensor_scalar(mask[:], key[:], pF[:, 20:21], None, ALU.is_gt)
                v.tensor_scalar(mwork[:], key[:], pF[:, 21:22], None, ALU.is_gt)
                v.tensor_tensor(mwork[:], mwork[:], mask[:], ALU.subtract)
                v.tensor_copy(ebuf[:], mwork[:])
            block.vector(v_tc)
            ntc = sv.n

            def t_pref(t, ntc=ntc):
                t.wait_ge(sv.h, ntc)
                t.matmul(pF[:, 64:96], ltri[:], ebuf[:], start=True, stop=True,
                         skip_group_check=True).then_inc(st.inc())
                t.matmul(pF[0:1, 32:64], onesb[:], ebuf[:], start=True, stop=True,
                         skip_group_check=True).then_inc(st.inc())
            block.tensor(t_pref)
            npref = st.n

            def v_chi(v, npref=npref):
                v = AS(v, sv)
                v._e.wait_ge(st.h, npref)
                v.tensor_copy(ebuf[:], mask[:])
            block.vector(v_chi)
            nchi = sv.n

            def t_chi(t, nchi=nchi):
                t.wait_ge(sv.h, nchi)
                t.matmul(pF[0:1, 128:160], onesb[:], ebuf[:], start=True, stop=True,
                         skip_group_check=True).then_inc(st.inc())
            block.tensor(t_chi)
            nchm = st.n

            def v_rank(v, nchm=nchm):
                v = AS(v, sv)
                v._e.wait_ge(st.h, nchm)
                v.tensor_reduce(rscl[:, 3:4], pF[0:1, 128:160], AX.X, ALU.add)
                v.tensor_copy(mrow2[0:1, 32:64], pF[0:1, 32:64])
                v.tensor_tensor_scan(mrow2[0:1, 0:32], mrow2[0:1, 32:64], mrow2[0:1, 32:64], 0.0,
                                     ALU.add, ALU.bypass)
                v.tensor_tensor(mrow2[0:1, 32:64], mrow2[0:1, 0:32], mrow2[0:1, 32:64], ALU.subtract)
                v.tensor_scalar(mrow2[0:1, 32:64], mrow2[0:1, 32:64], rscl[:, 3:4], None, ALU.add)
            block.vector(v_rank)
            nrank = sv.n

            def t_bcr(t, nrank=nrank):
                t.wait_ge(sv.h, nrank)
                t.matmul(pF[:, 96:128], onesr[:], mrow2[0:1, 32:64], start=True, stop=True,
                         skip_group_check=True).then_inc(st.inc())
            block.tensor(t_bcr)
            nbcr = st.n

            def v_mask(v, nbcr=nbcr):
                v = AS(v, sv)
                v._e.wait_ge(st.h, nbcr)
                v.tensor_copy(key[:], pF[:, 64:96])
                v.tensor_tensor(key[:], key[:], pF[:, 96:128], ALU.add)
                v.tensor_scalar(key[:], key[:], 1024.5, None, ALU.is_lt)
                v.tensor_tensor(mwork[:], mwork[:], key[:], ALU.mult)
                v.tensor_tensor(mask[:], mask[:], mwork[:], ALU.add)
                v.tensor_tensor(mask[:], mask[:], halfsel[:], ALU.mult)
                v.tensor_tensor(mask[:, 0:16], mask[:, 0:16], mask[:, 16:32], ALU.add)
                v.tensor_copy(mask_bf[:], mask[:, 0:16])
            block.vector(v_mask)
        rq(maskp)

        for tt in range(NTT):
            def mrow_step(tt=tt):
                def t_mt(t, tt=tt, evg=sv.n):
                    t.wait_ge(sv.h, evg)
                    t.matmul(pF[0:1, 160:288], mask_bf[:, tt:tt + 1], identb[:], start=True, stop=True,
                             skip_group_check=True).then_inc(st.inc())
                block.tensor(t_mt)
                nmt = st.n

                def v_mt(v, tt=tt, nmt=nmt, stg=smr.n):
                    v = AS(v, sv)
                    v._e.wait_ge(st.h, nmt)
                    if stg:
                        v._e.wait_ge(smr.h, stg)
                    v.tensor_copy(mrowst[:], pF[0:1, 160:288])
                block.vector(v_mt)
                nvm = sv.n

                def g_mt(g, tt=tt, nvm=nvm):
                    g.wait_ge(sv.h, nvm)
                    g.dma_start(out=mrow_hbm.ap()[0:1, tt * 128:(tt + 1) * 128],
                                in_=mrowst[:]).then_inc(smr.inc(16), 16)
                block.gpsimd(g_mt)
            rq(mrow_step)

    mrowst = xbt[0:1, 0:128]
    mrow_tb = xbt[0:1, 128:384]
    smr = Sem(ses, nc, "l1f_smr")
    sml = Sem(ses, nc, "l1f_sml")
    mrow_hbm = scratch["mrow_hbm"]
    _emit_routing()
    mrow_done_total = 16 * 16

    def g_xres(tb, sl, gate):
        def f(g_, tb=tb, sl=sl, gate=gate):
            g_.wait_ge(gate[0], gate[1])
            for c in range(DC):
                g_.dma_start(out=xres[sl][:, c * TBL:(c + 1) * TBL],
                             in_=x_in_r[:, c, tb * TBL:(tb + 1) * TBL]).then_inc(sxr[sl].inc(16), 16)
        block.gpsimd(f)
        return (sxr[sl].h, sxr[sl].n)

    xr_ld = [None] * NTB
    xr_ld[0] = g_xres(0, 0, (st.h, s_mm))
    xr_ld[1] = g_xres(1, 1, (st.h, s_mm))

    nst2_prev = 0
    for tb in range(NTB):
        sl = tb % 2
        def s_sqall(s, ld=xr_ld[tb], pv=nst2_prev):
            s = AS(s, ss)
            s._e.wait_ge(ld[0], ld[1])
            if pv:
                s._e.wait_ge(st.h, pv)
            for dc in range(DC):
                s.activation(ysq8[:, dc * TBL:(dc + 1) * TBL],
                             xres[sl][:, dc * TBL:(dc + 1) * TBL], AF.Square)
        block.scalar(s_sqall)
        nsqall = ss.n
        if tb < 2:
            pump(4)

        def t_sqall(t, nsqall=nsqall, vgate=sv.n):
            t.wait_ge(ss.h, nsqall)
            if vgate:
                t.wait_ge(sv.h, vgate)
            for dc in range(DC):
                t.matmul(pG[0:1, 256:256 + TBL], onesb[:], ysq8[:, dc * TBL:(dc + 1) * TBL],
                         start=(dc == 0), stop=(dc == DC - 1), skip_group_check=True).then_inc(st.inc())
        block.tensor(t_sqall)
        nssq = st.n
        if tb < 2:
            pump(4)

        def s_norm(s, nssq=nssq):
            s = AS(s, ss)
            s._e.wait_ge(st.h, nssq)
            s.activation(ystat[0:1, 0:TBL], pG[0:1, 256:256 + TBL], AF.Sqrt)
        block.scalar(s_norm)
        nnorm = ss.n

        def v_rfac(v, nnorm=nnorm):
            v = AS(v, sv)
            v._e.wait_ge(ss.h, nnorm)
            v.tensor_scalar(ystat[0:1, 0:TBL], ystat[0:1, 0:TBL], 1e-12, None, ALU.max)
            v.reciprocal(ystat[0:1, TBL:2 * TBL], ystat[0:1, 0:TBL])
            v.tensor_scalar(ystat[0:1, TBL:2 * TBL], ystat[0:1, TBL:2 * TBL], 32.0, None, ALU.mult)
        block.vector(v_rfac)
        nrfac = sv.n

        def t_bcn(t, nrfac=nrfac):
            t.wait_ge(sv.h, nrfac)
            t.matmul(pE[:, 0:TBL], onesr[:], ystat[0:1, TBL:2 * TBL], start=True, stop=True,
                     skip_group_check=True).then_inc(st.inc())
        block.tensor(t_bcn)
        nbcn = st.n

        def v_xg(v, tb=tb, sl=sl, nbcn=nbcn):
            v = AS(v, sv)
            v._e.wait_ge(st.h, nbcn)
            for dc in range(DC):
                sd = slice(dc * TBL, (dc + 1) * TBL)
                v.tensor_tensor(xg[:, sd], xres[sl][:, sd], pE[:, 0:TBL], ALU.mult)
            for dc in range(DC):
                sd = slice(dc * TBL, (dc + 1) * TBL)
                v.tensor_scalar(xlb[:, sd], xg[:, sd], vecs["lgam"][:, dc:dc + 1], None, ALU.mult)
            for dc in range(DC):
                sd = slice(dc * TBL, (dc + 1) * TBL)
                v.tensor_scalar(xhb[:, sd], xg[:, sd], vecs["hgam"][:, dc:dc + 1], None, ALU.mult)
        block.vector(v_xg)
        nxg = sv.n

        if tb < 2:
            pump(4)
        for oc in range(4):
            pump(2) if tb < 2 else None
            def t_l1(t, oc=oc, nxg=nxg, bg=ss.n):
                if oc == 0:
                    t.wait_ge(sv.h, nxg)
                if bg:
                    t.wait_ge(ss.h, bg)
                for dc in range(DC):
                    t.matmul(pA[:, 0:TBL], wl1[:, dc * 512 + oc * 128: dc * 512 + (oc + 1) * 128],
                             xlb[:, dc * TBL:(dc + 1) * TBL],
                             start=(dc == 0), stop=(dc == DC - 1)).then_inc(st.inc())
            block.tensor(t_l1)
            nl1 = st.n

            def s_l1(s, oc=oc, nl1=nl1):
                s = AS(s, ss)
                s._e.wait_ge(st.h, nl1)
                s.activation(lhb[:, oc * TBL:(oc + 1) * TBL], pA[:, 0:TBL], AF.Gelu,
                             bias=vecs["lb1"][:, oc:oc + 1])
            block.scalar(s_l1)
        nlh = ss.n

        for dc in range(DC):
            def t_l2(t, dc=dc, nlh=nlh, bg=sv.n, ssg=ss.n):
                if dc == 0:
                    t.wait_ge(ss.h, nlh)
                    t.wait_ge(sv.h, bg)
                elif ssg:
                    t.wait_ge(ss.h, ssg)
                for ic in range(4):
                    t.matmul(pB[:, 0:TBL], wl2[:, ic * D + dc * 128: ic * D + (dc + 1) * 128],
                             lhb[:, ic * TBL:(ic + 1) * TBL],
                             start=(ic == 0), stop=(ic == 3)).then_inc(st.inc())
            block.tensor(t_l2)
            nl2 = st.n

            def s_l2(s, dc=dc, nl2=nl2, sg=so.n):
                s = AS(s, ss)
                s._e.wait_ge(st.h, nl2)
                if dc == 0 and sg:
                    s._e.wait_ge(so.h, sg)
                s.activation(lacc[:, dc * TBL:(dc + 1) * TBL], pB[:, 0:TBL], AF.Identity,
                             bias=vecs["lb2"][:, dc:dc + 1])
            block.scalar(s_l2)
        nlacc = ss.n

        for ic in range(IHC):
            bk = pC if ic % 2 == 0 else pD
            pump(3) if tb < 2 else None

            def t_h1(t, ic=ic, tb=tb, bk=bk, nxg=nxg, bg=ss.n):
                if ic == 0:
                    t.wait_ge(sv.h, nxg)
                if ic == 0 and tb == 0:
                    t.wait_ge(sgh[0].h, h1_ld)
                if bg:
                    t.wait_ge(ss.h, bg)
                for dc in range(DC):
                    t.matmul(bk[:, 0:TBL], whw1[:, dc * IH + ic * 128: dc * IH + (ic + 1) * 128],
                             xhb[:, dc * TBL:(dc + 1) * TBL],
                             start=(dc == 0), stop=(dc == DC - 1)).then_inc(st.inc())
            block.tensor(t_h1)
            nh1 = st.n

            def s_h1(s, ic=ic, bk=bk, nh1=nh1):
                s = AS(s, ss)
                s._e.wait_ge(st.h, nh1)
                s.activation(hv1[:, ic * TBL:(ic + 1) * TBL], bk[:, 0:TBL], AF.Gelu,
                             bias=vecs["hb1"][:, ic:ic + 1])
            block.scalar(s_h1)
        nhv1 = ss.n

        pump(len(route_q))
        def g_ml(g, tb=tb, mrow_done=mrow_done_total, consumed=st.n):
            g.wait_ge(smr.h, mrow_done)
            if consumed:
                g.wait_ge(st.h, consumed)
            g.dma_start(out=mrow_tb[:], in_=mrow_hbm.ap()[0:1, tb * 256:(tb + 1) * 256]).then_inc(sml.inc(16), 16)
        block.gpsimd(g_ml)
        nml = sml.n

        def t_mbc(t, tb=tb, nml=nml, vg=sv.n):
            t.wait_ge(sml.h, nml)
            t.wait_ge(sv.h, vg)
            t.matmul(pE[:, 256:256 + 128], onesrb[:], mrow_tb[0:1, 0:128],
                     start=True, stop=True, skip_group_check=True).then_inc(st.inc())
            t.matmul(pE[:, 384:384 + 128], onesrb[:], mrow_tb[0:1, 128:256],
                     start=True, stop=True, skip_group_check=True).then_inc(st.inc())
        block.tensor(t_mbc)
        nmbc = st.n

        for dc in range(DC):
            bk = pC if dc % 2 == 0 else pD

            def t_h2(t, dc=dc, tb=tb, bk=bk, nhv1=nhv1, bg=sv.n):
                if dc == 0:
                    t.wait_ge(ss.h, nhv1)
                if dc == 0 and tb == 0:
                    t.wait_ge(sgh[1].h, h2_ld)
                t.wait_ge(sv.h, bg)
                for ic in range(IHC):
                    t.matmul(bk[:, 256:256 + TBL], whw2[:, ic * D + dc * 128: ic * D + (dc + 1) * 128],
                             hv1[:, ic * TBL:(ic + 1) * TBL],
                             start=(ic == 0), stop=(ic == IHC - 1)).then_inc(st.inc())
            block.tensor(t_h2)
            nh2 = st.n

            def v_comb(v, dc=dc, bk=bk, nh2=nh2, nmbc=nmbc, nlacc=nlacc, tb=tb, sl=sl):
                v = AS(v, sv)
                v._e.wait_ge(st.h, max(nh2, nmbc))
                v._e.wait_ge(ss.h, nlacc)
                sd = slice(dc * TBL, (dc + 1) * TBL)
                v.tensor_scalar(ytmp[:, sd], bk[:, 256:256 + TBL], vecs["hb2"][:, dc:dc + 1],
                                None, ALU.add)
                v.tensor_tensor(ytmp[:, sd], ytmp[:, sd], pE[:, 256:512], ALU.mult)
                v.tensor_tensor(yacc[:, sd], ytmp[:, sd], lacc[:, sd], ALU.add)
                v.tensor_tensor(yacc[:, sd], yacc[:, sd], xres[sl][:, sd], ALU.add)
            block.vector(v_comb)
        ncomb = sv.n

        def v_c2all(v, ncomb=ncomb, rms_st=nssq):
            v = AS(v, sv)
            v._e.wait_ge(st.h, rms_st)
            for dc in range(DC):
                v.tensor_copy(ybf8[:, dc * TBL:(dc + 1) * TBL], yacc[:, dc * TBL:(dc + 1) * TBL])
        block.vector(v_c2all)
        nca = sv.n

        def s_c2all(s, ncomb=ncomb, rms_st=nssq):
            s = AS(s, ss)
            s._e.wait_ge(sv.h, ncomb)
            s._e.wait_ge(st.h, rms_st)
            for dc in range(DC):
                s.activation(ysq8[:, dc * TBL:(dc + 1) * TBL],
                             yacc[:, dc * TBL:(dc + 1) * TBL], AF.Square)
        block.scalar(s_c2all)
        nsq = ss.n

        def t_s2all(t, nca=nca, nsq=nsq):
            t.wait_ge(sv.h, nca)
            t.wait_ge(ss.h, nsq)
            for dc in range(DC):
                t.matmul(pG[0:1, 0:TBL], onesb[:], ybf8[:, dc * TBL:(dc + 1) * TBL],
                         start=(dc == 0), stop=(dc == DC - 1), skip_group_check=True).then_inc(st.inc())
            for dc in range(DC):
                t.matmul(pH[0:1, 0:TBL], onesb[:], ysq8[:, dc * TBL:(dc + 1) * TBL],
                         start=(dc == 0), stop=(dc == DC - 1), skip_group_check=True).then_inc(st.inc())
        block.tensor(t_s2all)
        nst2 = st.n
        nst2_prev = nst2

        def v_mu(v, nst2=nst2):
            v = AS(v, sv)
            v._e.wait_ge(st.h, nst2)
            v.tensor_scalar(ystat[0:1, 0:TBL], pG[0:1, 0:TBL], 1.0 / 1024.0, None, ALU.mult)
            v.tensor_scalar(ystat[0:1, TBL:2 * TBL], pH[0:1, 0:TBL], 1.0 / 1024.0, None, ALU.mult)
            v.tensor_tensor(ystat[0:1, 2 * TBL:3 * TBL], ystat[0:1, 0:TBL], ystat[0:1, 0:TBL], ALU.mult)
            v.tensor_tensor(ystat[0:1, 2 * TBL:3 * TBL], ystat[0:1, TBL:2 * TBL],
                            ystat[0:1, 2 * TBL:3 * TBL], ALU.subtract)
            v.tensor_scalar(ystat[0:1, 2 * TBL:3 * TBL], ystat[0:1, 2 * TBL:3 * TBL],
                            LN_EPS, None, ALU.add)
        block.vector(v_mu)
        nvar = sv.n

        def s_sd(s, nvar=nvar):
            s = AS(s, ss)
            s._e.wait_ge(sv.h, nvar)
            s.activation(ystat[0:1, 3 * TBL:4 * TBL], ystat[0:1, 2 * TBL:3 * TBL], AF.Sqrt)
        block.scalar(s_sd)
        nstd = ss.n

        def v_rst(v, nstd=nstd):
            v = AS(v, sv)
            v._e.wait_ge(ss.h, nstd)
            v.reciprocal(ystat[0:1, TBL:2 * TBL], ystat[0:1, 3 * TBL:4 * TBL])
        block.vector(v_rst)
        nrst = sv.n

        def t_bcl(t, nrst=nrst):
            t.wait_ge(sv.h, nrst)
            t.matmul(pE[:, 0:TBL], onesr[:], ystat[0:1, 0:TBL], start=True, stop=True,
                     skip_group_check=True).then_inc(st.inc())
            t.matmul(pF[:, 0:TBL], onesr[:], ystat[0:1, TBL:2 * TBL], start=True, stop=True,
                     skip_group_check=True).then_inc(st.inc())
        block.tensor(t_bcl)
        nbcl = st.n

        def v_ap(v, nbcl=nbcl, sg=so.n):
            v = AS(v, sv)
            v._e.wait_ge(st.h, nbcl)
            if sg:
                v._e.wait_ge(so.h, sg)
            for dc in range(DC):
                sd = slice(dc * TBL, (dc + 1) * TBL)
                v.tensor_tensor(ytmp[:, sd], yacc[:, sd], pE[:, 0:TBL], ALU.subtract)
            for dc in range(DC):
                sd = slice(dc * TBL, (dc + 1) * TBL)
                v.tensor_tensor(yacc[:, sd], ytmp[:, sd], pF[:, 0:TBL], ALU.mult)
        block.vector(v_ap)
        nap = sv.n

        def s_st(s, tb=tb, nap=nap):
            s.wait_ge(sv.h, nap)
            for c in range(DC):
                s.dma_start(out=x_out_r[:, c, tb * TBL:(tb + 1) * TBL],
                            in_=yacc[:, c * TBL:(c + 1) * TBL]).then_inc(so.inc(16), 16)
        block.sync(s_st)

        if tb + 2 < NTB:
            xr_ld[tb + 2] = g_xres(tb + 2, sl, (sv.h, sv.n))

    def g_fin(g):
        g.wait_ge(so.h, so.n)
    block.gpsimd(g_fin)


def build_l1f_launch():
    nc = bass.Bass("TRN2", target_bir_lowering=False)
    x_in = nc.dram_tensor("x_in", [D, T], F32, kind="ExternalInput").ap()
    x_out = nc.dram_tensor("x_out", [D, T], F32, kind="ExternalOutput").ap()
    w_aps = dict(
        lw1=nc.dram_tensor("lw1", [1024, 512], F32, kind="ExternalInput").ap(),
        lw2=nc.dram_tensor("lw2", [512, 1024], F32, kind="ExternalInput").ap(),
        hw1=nc.dram_tensor("hw1", [1024, 4096], F32, kind="ExternalInput").ap(),
        hw2=nc.dram_tensor("hw2", [4096, 1024], F32, kind="ExternalInput").ap(),
        rt=nc.dram_tensor("rt", [128, 8], F32, kind="ExternalInput").ap(),
        identb=nc.dram_tensor("identb", [128, 128], F32, kind="ExternalInput").ap(),
        ltri=nc.dram_tensor("ltri", [128, 128], F32, kind="ExternalInput").ap(),
        halfsel=nc.dram_tensor("halfsel", [128, 32], F32, kind="ExternalInput").ap(),
        lb1=nc.dram_tensor("lb1", [128, 4], F32, kind="ExternalInput").ap(),
        lb2=nc.dram_tensor("lb2", [128, 8], F32, kind="ExternalInput").ap(),
        lgam=nc.dram_tensor("lgam", [128, 8], F32, kind="ExternalInput").ap(),
        hgam=nc.dram_tensor("hgam", [128, 8], F32, kind="ExternalInput").ap(),
        hb1=nc.dram_tensor("hb1", [128, 32], F32, kind="ExternalInput").ap(),
        hb2=nc.dram_tensor("hb2", [128, 8], F32, kind="ExternalInput").ap(),
    )
    scratch = dict(
        s_in=nc.dram_tensor("s_in", [128, 16], F32),
        s_out=nc.dram_tensor("s_out", [256, 16], F32),
        mrow_hbm=nc.dram_tensor("mrow_hbm", [1, T], BF16),
    )
    es = contextlib.ExitStack()
    with es:
        block = DirectBlock(nc)
        es.enter_context(nc.Block())
        emit_l1_ffn(nc, es, block, x_in, x_out, w_aps, scratch)
    return nc


def host_l1f_inputs(weights, core_id):
    hf = core_id % 2
    halfsel = np.zeros((128, 32), np.float32)
    halfsel[:, hf * 16:(hf + 1) * 16] = 1.0
    k = np.arange(128)
    ltri = (k[:, None] <= k[None, :]).astype(np.float32)
    return dict(
        lw1=np.ascontiguousarray(weights["L1_ffn_light_w1"], dtype=np.float32),
        lw2=np.ascontiguousarray(weights["L1_ffn_light_w2"], dtype=np.float32),
        hw1=np.ascontiguousarray(weights["L1_ffn_heavy_w1"], dtype=np.float32),
        hw2=np.ascontiguousarray(weights["L1_ffn_heavy_w2"], dtype=np.float32),
        rt=pack_vec(weights["L1_ffn_routing_token"]),
        identb=np.eye(128, dtype=np.float32),
        ltri=ltri,
        halfsel=halfsel,
        lb1=pack_vec(weights["L1_ffn_light_b1"]),
        lb2=pack_vec(weights["L1_ffn_light_b2"]),
        lgam=pack_vec(weights["L1_ffn_light_gamma"]),
        hgam=pack_vec(weights["L1_ffn_heavy_gamma"]),
        hb1=pack_vec(weights["L1_ffn_heavy_b1"]),
        hb2=pack_vec(weights["L1_ffn_heavy_b2"]),
    )


def run_l1f(x_full, weights):
    nc = build_l1f_launch()
    in_maps = []
    for c in range(NCORES):
        b, hf = c // 2, c % 2
        xs = np.ascontiguousarray(x_full[b, hf * T:(hf + 1) * T, :])
        m = host_l1f_inputs(weights, c)
        m["x_in"] = fmT(xs)
        in_maps.append(m)
    res = run_bass_kernel_spmd(nc, in_maps, core_ids=list(range(NCORES)))
    out = np.zeros((B, S, D), np.float32)
    for c in range(NCORES):
        b, hf = c // 2, c % 2
        out[b, hf * T:(hf + 1) * T, :] = np.asarray(res.results[c]["x_out"], np.float32).T
    return out


def host_l0a_inputs(weights, core_id):
    sel = np.zeros((128, 8), np.float32) if core_id % 2 == 0 else np.ones((128, 8), np.float32)
    return dict(
        wp1t=fmT(weights["L0_attn_Wp1"]),
        wp2t=fmT(weights["L0_attn_Wp2"]),
        convt=fmT(weights["L0_attn_conv_w"]),
        wat=fmT(weights["L0_attn_Wa"]),
        wxt=fmT(weights["L0_attn_Wx"]),
        wp3t=fmT(weights["L0_attn_Wp3"]),
        bp1=pack_vec(weights["L0_attn_bp1"]),
        bp2=pack_vec(weights["L0_attn_bp2"]),
        convb=pack_vec(weights["L0_attn_conv_b"]),
        ba=pack_vec(weights["L0_attn_ba"]),
        bx=pack_vec(weights["L0_attn_bx"]),
        lam=pack_vec(weights["L0_attn_Lam"]),
        bp3=pack_vec(weights["L0_attn_bp3"]),
        sel=sel,
    )


def build_l0a_launch():
    nc = bass.Bass("TRN2", target_bir_lowering=False)
    x_in = nc.dram_tensor("x_in", [D, T], F32, kind="ExternalInput").ap()
    x_out = nc.dram_tensor("x_out", [D, T], F32, kind="ExternalOutput").ap()
    w_aps = {w: nc.dram_tensor(w, [D, D], F32, kind="ExternalInput").ap()
             for w in ["wp1t", "wp2t", "convt", "wat", "wxt", "wp3t"]}
    vec_aps = {v: nc.dram_tensor(v, [128, 8], F32, kind="ExternalInput").ap()
               for v in ["bp1", "bp2", "convb", "ba", "bx", "lam", "bp3", "sel"]}
    scratch = dict(
        prod_hbm=nc.dram_tensor("prod_hbm", [128, DC * T], BF16).ap(),
        corr_hbm=nc.dram_tensor("corr_hbm", [128, DC * T], BF16).ap(),
        cc_in=nc.dram_tensor("cc_in", [128, 8], F32),
        cc_out=nc.dram_tensor("cc_out", [256, 8], F32),
    )
    es = contextlib.ExitStack()
    with es:
        block = DirectBlock(nc)
        es.enter_context(nc.Block())
        emit_l0_attn(nc, es, block, x_in, x_out, w_aps, vec_aps, scratch)
    return nc


def run_l0a(x_full, weights):
    nc = build_l0a_launch()
    in_maps = []
    for c in range(NCORES):
        b, hf = c // 2, c % 2
        xs = np.ascontiguousarray(x_full[b, hf * T:(hf + 1) * T, :])
        m = host_l0a_inputs(weights, c)
        m["x_in"] = fmT(xs)
        in_maps.append(m)
    res = run_bass_kernel_spmd(nc, in_maps, core_ids=list(range(NCORES)))
    out = np.zeros((B, S, D), np.float32)
    for c in range(NCORES):
        b, hf = c // 2, c % 2
        out[b, hf * T:(hf + 1) * T, :] = np.asarray(res.results[c]["x_out"], np.float32).T
    return out






L0A_W = ["wp1t", "wp2t", "convt", "wat", "wxt", "wp3t"]
L0A_V = ["bp1", "bp2", "convb", "ba", "bx", "lam", "bp3", "sel"]
L0F_W = {"wsl": [896, 2816], "ident": [128, 128]}
L1A_W = {"wq": [16, 1024, 128], "wk": [16, 1024, 128], "wvT": [1024, 1024], "wo": [1024, 1024],
         "bqT": [128, 16], "bkT": [128, 16], "bvrow": [1, 1024], "bo": [128, 8],
         "gamma": [128, 8], "beta": [128, 8], "lamb": [128, 1], "onesr": [1, 128]}
L1F_W = {"lw1": [1024, 512], "lw2": [512, 1024], "hw1": [1024, 4096], "hw2": [4096, 1024],
         "rt": [128, 8], "identb": [128, 128], "ltri": [128, 128], "halfsel": [128, 32],
         "lb1": [128, 4], "lb2": [128, 8], "lgam": [128, 8], "hgam": [128, 8],
         "hb1": [128, 32], "hb2": [128, 8]}


def build_fused():
    nc = bass.Bass("TRN2", target_bir_lowering=False)
    x0 = nc.dram_tensor("x0", [D, T], F32, kind="ExternalInput").ap()
    xm = [nc.dram_tensor(f"xmid{i}", [D, T], F32) for i in range(3)]
    x4 = nc.dram_tensor("x_out", [D, T], F32, kind="ExternalOutput").ap()

    w0a = {w: nc.dram_tensor("i0a_" + w, [D, D], F32, kind="ExternalInput").ap() for w in L0A_W}
    v0a = {v: nc.dram_tensor("i0a_" + v, [128, 8], F32, kind="ExternalInput").ap() for v in L0A_V}
    scr0a = dict(
        prod_hbm=nc.dram_tensor("s0a_prod", [128, DC * T], BF16).ap(),
        corr_hbm=nc.dram_tensor("s0a_corr", [128, DC * T], BF16).ap(),
        cc_in=nc.dram_tensor("s0a_ccin", [128, 8], F32),
        cc_out=nc.dram_tensor("s0a_ccout", [256, 8], F32),
    )
    w0f = {w: nc.dram_tensor("i0f_" + w, sh, F32, kind="ExternalInput").ap() for w, sh in L0F_W.items()}
    scr0f = dict(
        qag_in=nc.dram_tensor("s0f_qin", [896, 2816], BF16),
        qag_out=nc.dram_tensor("s0f_qout", [7168, 2816], BF16, addr_space="Shared"),
    )
    w1a = {w: nc.dram_tensor("i1a_" + w, sh, F32, kind="ExternalInput").ap() for w, sh in L1A_W.items()}
    scr1a = dict(
        cc_kin=nc.dram_tensor("s1a_kin", [2048, 2048], BF16),
        cc_kout=nc.dram_tensor("s1a_kout", [4096, 2048], BF16),
        cc_vin=nc.dram_tensor("s1a_vin", [2048, 1024], BF16),
        cc_vout=nc.dram_tensor("s1a_vout", [4096, 1024], BF16),
    )
    w1f = {w: nc.dram_tensor("i1f_" + w, sh, F32, kind="ExternalInput").ap() for w, sh in L1F_W.items()}
    scr1f = dict(
        s_in=nc.dram_tensor("s1f_sin", [128, 16], F32),
        s_out=nc.dram_tensor("s1f_sout", [256, 16], F32),
        mrow_hbm=nc.dram_tensor("s1f_mrow", [1, T], BF16),
    )

    es_outer = contextlib.ExitStack()
    with es_outer:
        block = DirectBlock(nc)
        es_outer.enter_context(nc.Block())
        es_w = contextlib.ExitStack()
        pwst = emit_l0f_phaseW(nc, es_w, block, w0f, scr0f, ses=es_outer)

        def l0a_hook(tag, pwst=pwst):
            if tag == "mid":
                pwst["pump"](1)
            elif tag == 7:
                pwst["finish"]()
        with contextlib.ExitStack() as es_b:
            emit_l0_attn(nc, es_b, block, x0, xm[0].ap(), w0a, v0a, scr0a, ses=es_outer,
                         hook=l0a_hook)
        es_w.close()
        pw = dict(scc=pwst["scc"], agc=pwst["state"]["agc"])
        with contextlib.ExitStack() as es_b:
            emit_l0_ffn(nc, es_b, block, xm[0].ap(), xm[1].ap(), w0f, scr0f, ses=es_outer, pw=pw)
        with contextlib.ExitStack() as es_b:
            emit_l1_attn(nc, es_b, block, xm[1].ap(), xm[2].ap(), w1a, scr1a, ses=es_outer)
        with contextlib.ExitStack() as es_b:
            emit_l1_ffn(nc, es_b, block, xm[2].ap(), x4, w1f, scr1f, ses=es_outer)
    return nc


def host_fused_inputs(weights, core_id):
    m = {}
    for k, v in host_l0a_inputs(weights, core_id).items():
        m["i0a_" + k] = v
    m["i0f_wsl"] = host_l0f_wsl(weights, core_id)
    m["i0f_ident"] = np.eye(128, dtype=np.float32)
    for k, v in host_l1a_inputs(weights).items():
        m["i1a_" + k] = v
    for k, v in host_l1f_inputs(weights, core_id).items():
        m["i1f_" + k] = v
    return m


def kernel(**inputs):
    x = np.asarray(inputs["L0_attn_x"], np.float32)
    nc = build_fused()
    in_maps = []
    for c in range(NCORES):
        b, hf = c // 2, c % 2
        m = host_fused_inputs(inputs, c)
        m["x0"] = fmT(np.ascontiguousarray(x[b, hf * T:(hf + 1) * T, :]))
        in_maps.append(m)
    res = run_bass_kernel_spmd(nc, in_maps, core_ids=list(range(NCORES)))
    out = np.zeros((B, S, D), np.float32)
    for c in range(NCORES):
        b, hf = c // 2, c % 2
        out[b, hf * T:(hf + 1) * T, :] = np.asarray(res.results[c]["x_out"], np.float32).T
    return np.ascontiguousarray(out)


def kernel_unfused(**inputs):
    x = np.asarray(inputs["L0_attn_x"], np.float32)
    x = run_l0a(x, inputs)
    x = run_l0f(x, inputs)
    x = run_l1a(x, inputs)
    x = run_l1f(x, inputs)
    return np.ascontiguousarray(x, dtype=np.float32)


# version 24
# speedup vs baseline: 1.0258x; 1.0258x over previous
import contextlib
import numpy as np
import ml_dtypes
import concourse.bass as bass
import concourse.mybir as mybir
from concourse.bass_utils import run_bass_kernel_spmd

F32 = mybir.dt.float32
BF16 = mybir.dt.bfloat16
U8 = mybir.dt.uint8
AF = mybir.ActivationFunctionType
ALU = mybir.AluOpType
AX = mybir.AxisListType

B, S, D = 4, 4096, 1024
T = 2048
NCORES = 8
DC = 8
MAGIC = float(np.float32(3 << 22))
LN_EPS = 1e-5



class AS:
    OPS = {"tensor_tensor", "tensor_scalar", "tensor_copy", "scalar_tensor_tensor",
           "tensor_tensor_scan", "reciprocal", "memset", "tensor_reduce",
           "select", "transpose", "iota", "activation", "copy", "add", "mul",
           "tensor_single_scalar", "tensor_scalar_add", "tensor_scalar_mul"}

    def __init__(self, eng, sem):
        self._e = eng
        self._s = sem

    def __getattr__(self, name):
        attr = getattr(self._e, name)
        if name in AS.OPS:
            sem = self._s

            def wrapped(*a, **k):
                self._e.wait_ge(sem.h, sem.n)
                inst = attr(*a, **k)
                inst.then_inc(sem.h, 1)
                sem.n += 1
                return inst
            return wrapped
        return attr



class DirectBlock:
    def __init__(self, nc):
        self._nc = nc

    def tensor(self, f):
        return f(self._nc.tensor)

    def vector(self, f):
        return f(self._nc.vector)

    def scalar(self, f):
        return f(self._nc.scalar)

    def gpsimd(self, f):
        return f(self._nc.gpsimd)

    def sync(self, f):
        return f(self._nc.sync)


class Sem:
    def __init__(self, es, nc, name):
        self.h = es.enter_context(nc.semaphore(name))
        self.n = 0

    def inc(self, k=1):
        self.n += k
        return self.h


def fmT(x):
    return np.ascontiguousarray(np.asarray(x, np.float32).T)


def pack_vec(v):
    v = np.asarray(v, np.float32)
    return np.ascontiguousarray(v.reshape(-1, 128).T)



def emit_l0_attn(nc, es, block, x_in, x_out, w_aps, vec_aps, scratch, ses=None, hook=None):
    ses = es if ses is None else ses
    """L0_attn + residual + LN. x_in/x_out: DRAM APs, fm f32 [1024, T]."""
    TB = 256
    NTB = T // TB

    wt = {w: es.enter_context(nc.sbuf_tensor("l0a_" + w, [128, DC * D], BF16))
          for w in ["wp1t", "wp2t", "convt", "wat", "wxt"]}
    wt["wp3t"] = wt["wp1t"]
    vecs = {v: es.enter_context(nc.sbuf_tensor("l0a_" + v, [128, 8], F32))
            for v in ["bp1", "bp2", "convb", "ba", "bx", "lam", "bp3", "sel"]}
    lamlog = es.enter_context(nc.sbuf_tensor("lamlog", [128, 8], F32))
    vtmp = es.enter_context(nc.sbuf_tensor("vtmp", [128, 8], F32))
    onesb = es.enter_context(nc.sbuf_tensor("onesb", [128, 1], BF16))
    ones_row = es.enter_context(nc.sbuf_tensor("ones_row", [1, 128], F32))
    carry = es.enter_context(nc.sbuf_tensor("carry", [128, 8], F32))
    htail = es.enter_context(nc.sbuf_tensor("htail", [128, 8], F32))

    f32t = {n: es.enter_context(nc.sbuf_tensor("l0a_" + n, [128, DC * TB], F32))
            for n in ["x1", "x2b", "bR", "bI", "bU", "h0", "h1", "p0", "p1"]}
    bf16t = {n: es.enter_context(nc.sbuf_tensor("l0a_" + n, [128, DC * TB], BF16))
             for n in ["xb0", "xb1", "x2a", "x2bb", "prodb", "corrb"]}
    stats = es.enter_context(nc.sbuf_tensor("l0a_stats", [1, 3 * TB], F32))
    ybf = es.enter_context(nc.sbuf_tensor("l0a_ybf", [128, TB], BF16))
    ysq = es.enter_context(nc.sbuf_tensor("l0a_ysq", [128, TB], BF16))

    ps = [es.enter_context(nc.psum_tensor(f"l0a_ps{i}", [128, 512], F32)) for i in range(6)]

    sgw = Sem(ses, nc, "l0a_sgw")
    sxb = [Sem(ses, nc, f"l0a_sxb{i}") for i in range(2)]
    spst = Sem(ses, nc, "l0a_spst")
    sld = [Sem(ses, nc, f"l0a_sld{i}") for i in range(2)]
    sgc = Sem(ses, nc, "l0a_sgc")
    st = Sem(ses, nc, "l0a_st")
    sv = Sem(ses, nc, "l0a_sv")
    ss = Sem(ses, nc, "l0a_ss")
    scc = Sem(ses, nc, "l0a_scc")
    so = Sem(ses, nc, "l0a_so")

    x_in_r = x_in.rearrange("(c p) t -> p c t", p=128)
    x_out_r = x_out.rearrange("(c p) t -> p c t", p=128)
    prod_r = scratch["prod_hbm"]
    corr_r = scratch["corr_hbm"]

    def g_loads(g):
        for w in ["wp1t", "wp2t", "convt", "wat", "wxt"]:
            tile = wt[w]
            for c in range(DC):
                g.dma_start(out=tile[:, c * D:(c + 1) * D],
                            in_=w_aps[w][c * 128:(c + 1) * 128, :]).then_inc(sgw.inc(16), 16)
        for v, tile in vecs.items():
            g.dma_start(out=tile[:], in_=vec_aps[v]).then_inc(sgw.inc(16), 16)
        for i in range(2):
            for c in range(DC):
                g.dma_start(out=bf16t[f"xb{i}"][:, c * TB:(c + 1) * TB],
                            in_=x_in_r[:, c, i * TB:(i + 1) * TB]).then_inc(sxb[i].inc(16), 16)
    block.gpsimd(g_loads)
    w_loaded = sgw.n
    xb_load = {0: (sxb[0].h, sxb[0].n), 1: (sxb[1].h, sxb[1].n)}

    def v_init(v):
        v = AS(v, sv)
        v.wait_ge(sgw.h, w_loaded)
        v.memset(onesb[:], 1.0)
        v.memset(ones_row[:], 1.0)
    block.vector(v_init)
    n0 = sv.n

    def s_init(s):
        s = AS(s, ss)
        s.wait_ge(sv.h, n0)
        s.wait_ge(sgw.h, w_loaded)
        s.activation(vtmp[:], vecs["lam"][:], AF.Abs)
        s.activation(vtmp[:], vtmp[:], AF.Exp, scale=-1.0)
        s.activation(vtmp[:], vtmp[:], AF.Ln, bias=1.0)
    block.scalar(s_init)
    n1 = ss.n

    def v_init2(v):
        v = AS(v, sv)
        v.wait_ge(ss.h, n1)
        v.tensor_scalar(lamlog[:], vecs["lam"][:], -1.0, 0.0, ALU.mult, ALU.max)
        v.scalar_tensor_tensor(lamlog[:], lamlog[:], 1.0, vtmp[:], ALU.mult, ALU.add)
        v.tensor_scalar(lamlog[:], lamlog[:], -8.0, None, ALU.mult)
    block.vector(v_init2)
    lamlog_done = sv.n

    bank_free = [0] * 6
    sv_after_scan = [0] * NTB
    prod_stored = [0] * NTB

    def mm_group(wname, in_tile, out_eval, gate=None):
        for oc in range(DC):
            b = oc % 4
            bf = bank_free[b]
            g8 = gate if oc == 0 else None

            def t_sec(t, oc=oc, b=b, bf=bf, g8=g8):
                if g8 is not None:
                    for gh, gv in g8:
                        t.wait_ge(gh, gv)
                if bf:
                    t.wait_ge(ss.h, bf)
                for dcc in range(DC):
                    mm = t.matmul(ps[b][:, 0:TB],
                                  wt[wname][:, dcc * D + oc * 128: dcc * D + (oc + 1) * 128],
                                  in_tile[:, dcc * TB:(dcc + 1) * TB],
                                  start=(dcc == 0), stop=(dcc == DC - 1))
                mm.then_inc(st.inc())
            block.tensor(t_sec)
            mm_done = st.n

            dest, bias_ap, actf = out_eval(oc)

            def s_sec(s, dest=dest, bias_ap=bias_ap, actf=actf, b=b, mm_done=mm_done):
                s = AS(s, ss)
                s.wait_ge(st.h, mm_done)
                s.activation(dest, ps[b][:, 0:TB], actf, bias=bias_ap)
            block.scalar(s_sec)
            bank_free[b] = ss.n

    for tb in range(NTB):
        xb = bf16t[f"xb{tb % 2}"]
        hprev, hcur = f32t[f"h{1 - tb % 2}"], f32t[f"h{tb % 2}"]
        pprev, pcur = f32t[f"p{1 - tb % 2}"], f32t[f"p{tb % 2}"]

        mm_group("wp2t", xb,
                 lambda oc: (bf16t["x2a"][:, oc * TB:(oc + 1) * TB], vecs["bp2"][:, oc:oc + 1], AF.Identity),
                 gate=[(sgw.h, w_loaded), xb_load[tb]])
        x2a_done = ss.n
        if hook is not None:
            hook("mid")

        gate_sv = sv_after_scan[tb - 1] if tb >= 1 else lamlog_done

        def t_gate_x2b(t, x2a_done=x2a_done, gate_sv=gate_sv):
            t.wait_ge(ss.h, x2a_done)
            t.wait_ge(sv.h, gate_sv)
        block.tensor(t_gate_x2b)
        mm_group("convt", bf16t["x2a"],
                 lambda oc: (f32t["x2b"][:, oc * TB:(oc + 1) * TB], vecs["convb"][:, oc:oc + 1], AF.Relu))
        x2b_done = ss.n

        def v_cast(v, x2b_done=x2b_done):
            v = AS(v, sv)
            v.wait_ge(ss.h, x2b_done)
            v.tensor_copy(bf16t["x2bb"][:], f32t["x2b"][:])
        block.vector(v_cast)
        x2bb_done = sv.n

        if hook is not None:
            hook("mid")
        mm_group("wat", bf16t["x2bb"],
                 lambda oc: (f32t["bR"][:, oc * TB:(oc + 1) * TB], vecs["ba"][:, oc:oc + 1], AF.Sigmoid),
                 gate=[(sv.h, x2bb_done)])
        if hook is not None:
            hook("mid")
        mm_group("wxt", bf16t["x2bb"],
                 lambda oc: (f32t["bI"][:, oc * TB:(oc + 1) * TB], vecs["bx"][:, oc:oc + 1], AF.Sigmoid))
        ri_done = ss.n
        if hook is not None:
            hook("mid")
        mm_group("wp1t", xb,
                 lambda oc: (f32t["x1"][:, oc * TB:(oc + 1) * TB], vecs["bp1"][:, oc:oc + 1], AF.Gelu))
        x1_done = ss.n
        if hook is not None:
            hook("mid")
        st_tb = st.n

        if tb + 2 < NTB:
            def g_pref(g, tb=tb, st_tb=st_tb):
                g.wait_ge(st.h, st_tb)
                for c in range(DC):
                    g.dma_start(out=bf16t[f"xb{tb % 2}"][:, c * TB:(c + 1) * TB],
                                in_=x_in_r[:, c, (tb + 2) * TB:(tb + 3) * TB]).then_inc(sxb[tb % 2].inc(16), 16)
            block.gpsimd(g_pref)
            xb_load[tb + 2] = (sxb[tb % 2].h, sxb[tb % 2].n)

        def v_g1(v, ri_done=ri_done):
            v = AS(v, sv)
            v.wait_ge(ss.h, ri_done)
            for rc in range(DC):
                sl = slice(rc * TB, (rc + 1) * TB)
                v.tensor_scalar(f32t["bU"][:, sl], f32t["bR"][:, sl],
                                lamlog[:, rc:rc + 1], None, ALU.mult)
        block.vector(v_g1)
        nvu = sv.n

        def s_g1(s, nvu=nvu):
            s = AS(s, ss)
            s.wait_ge(sv.h, nvu)
            s.activation(f32t["bR"][:], f32t["bU"][:], AF.Exp)
        block.scalar(s_g1)
        na = ss.n

        def v_g2(v, na=na):
            v = AS(v, sv)
            v.wait_ge(ss.h, na)
            v.tensor_tensor(f32t["bU"][:], f32t["bI"][:], f32t["x2b"][:], ALU.mult)
            v.tensor_tensor(f32t["bI"][:], f32t["bR"][:], f32t["bR"][:], ALU.mult)
        block.vector(v_g2)
        nm = sv.n

        def s_g2(s, nm=nm):
            s = AS(s, ss)
            s.wait_ge(sv.h, nm)
            s.activation(f32t["bI"][:], f32t["bI"][:], AF.Sqrt, scale=-1.0, bias=1.0)
        block.scalar(s_g2)
        nsq = ss.n

        def v_g3(v, nsq=nsq, tb=tb, hprev=hprev, hcur=hcur, pprev=pprev, pcur=pcur, x1_done=x1_done,
                 pst_need=32 * tb):
            v = AS(v, sv)
            v.wait_ge(ss.h, nsq)
            if pst_need:
                v.wait_ge(spst.h, pst_need)
            v.tensor_tensor(f32t["bU"][:], f32t["bI"][:], f32t["bU"][:], ALU.mult)
            for dcc in range(DC):
                sl = slice(dcc * TB, (dcc + 1) * TB)
                hi = 0.0 if tb == 0 else hprev[:, dcc * TB + TB - 1: dcc * TB + TB]
                pi = 1.0 if tb == 0 else pprev[:, dcc * TB + TB - 1: dcc * TB + TB]
                v.tensor_tensor_scan(hcur[:, sl], f32t["bR"][:, sl], f32t["bU"][:, sl],
                                     hi, ALU.mult, ALU.add)
                v.tensor_tensor_scan(pcur[:, sl], f32t["bR"][:, sl], f32t["bU"][:, sl],
                                     pi, ALU.mult, ALU.bypass)
            v.wait_ge(ss.h, x1_done)
            v.tensor_tensor(bf16t["prodb"][:], f32t["x1"][:], hcur[:], ALU.mult)
            v.tensor_tensor(bf16t["corrb"][:], f32t["x1"][:], pcur[:], ALU.mult)
        block.vector(v_g3)
        sv_after_scan[tb] = sv.n
        nprod = sv.n

        def g_store(g, tb=tb, nprod=nprod):
            g.wait_ge(sv.h, nprod)
            g.dma_start(out=prod_r[:, tb * DC * TB:(tb + 1) * DC * TB], in_=bf16t["prodb"][:]).then_inc(spst.inc(16), 16)
            g.dma_start(out=corr_r[:, tb * DC * TB:(tb + 1) * DC * TB], in_=bf16t["corrb"][:]).then_inc(spst.inc(16), 16)
        block.gpsimd(g_store)
        prod_stored[tb] = spst.n
        if hook is not None:
            hook(tb)

    sgw2 = Sem(ses, nc, "l0a_sgw2")

    def g_wp3(g, gate=st.n):
        g.wait_ge(st.h, gate)
        for c in range(DC):
            g.dma_start(out=wt["wp3t"][:, c * D:(c + 1) * D],
                        in_=w_aps["wp3t"][c * 128:(c + 1) * 128, :]).then_inc(sgw2.inc(16), 16)
    block.gpsimd(g_wp3)
    wp3_loaded = sgw2.n

    hlast = f32t[f"h{(NTB - 1) % 2}"]
    nall = sv_after_scan[-1]

    def v_tail(v, nall=nall):
        v = AS(v, sv)
        v.wait_ge(sv.h, nall)
        v.tensor_copy(htail[:], hlast[:, TB - 1::TB])
    block.vector(v_tail)
    ntail = sv.n

    def g_cc(g, ntail=ntail):
        g.wait_ge(sv.h, ntail)
        g.dma_start(out=scratch["cc_in"].ap(), in_=htail[:]).then_inc(sgc.inc(16), 16)
        g.wait_ge(sgc.h, sgc.n)
        g.collective_compute(
            "AllGather", ALU.bypass,
            replica_groups=[[0, 1], [2, 3], [4, 5], [6, 7]],
            ins=[scratch["cc_in"].ap()],
            outs=[scratch["cc_out"].ap()],
        ).then_inc(scc.inc())
        g.wait_ge(scc.h, scc.n)
        g.dma_start(out=carry[:], in_=scratch["cc_out"].ap()[0:128, :]).then_inc(sgc.inc(16), 16)
    block.gpsimd(g_cc)
    ncarry = sgc.n

    def v_carry(v, ncarry=ncarry):
        v = AS(v, sv)
        v.wait_ge(sgc.h, ncarry)
        v.tensor_tensor(carry[:], carry[:], vecs["sel"][:], ALU.mult)
    block.vector(v_carry)
    carry_done = sv.n

    rhs2 = [bf16t["x2a"], bf16t["x2bb"]]
    pin2 = [bf16t["xb0"], bf16t["xb1"]]
    cin2 = [bf16t["prodb"], bf16t["corrb"]]
    xres = [f32t["x1"], f32t["x2b"]]
    yout = f32t["bR"]
    xnew = f32t["bI"]

    p2_load = [0] * NTB
    p2_fix = [0] * NTB
    p2_proj_st = [0] * NTB
    p2_resid = [0] * NTB
    p2_store = [0] * NTB
    bank2 = {0: bank_free[0], 1: bank_free[1]}
    nbc_prev = 0
    stats_read_prev = 0

    def emit_p2_load(tb):
        g1 = (sv.h, carry_done) if tb < 2 else (sv.h, max(p2_fix[tb - 2], p2_resid[tb - 2]))
        g2 = (spst.h, prod_stored[NTB - 1]) if tb < 2 else (st.h, p2_proj_st[tb - 2])
        sem = sld[tb % 2]

        def s_load(s, tb=tb, g1=g1, g2=g2, sem=sem):
            s = AS(s, ss)
            s.wait_ge(*g1)
            s.wait_ge(*g2)
            for c in range(DC):
                s.dma_start(out=xres[tb % 2][:, c * TB:(c + 1) * TB],
                            in_=x_in_r[:, c, tb * TB:(tb + 1) * TB]).then_inc(sem.inc(16), 16)
            s.dma_start(out=pin2[tb % 2][:], in_=prod_r[:, tb * DC * TB:(tb + 1) * DC * TB]).then_inc(sem.inc(16), 16)
            s.dma_start(out=cin2[tb % 2][:], in_=corr_r[:, tb * DC * TB:(tb + 1) * DC * TB]).then_inc(sem.inc(16), 16)
        block.sync(s_load)
        p2_load[tb] = (sem.h, sem.n)

    emit_p2_load(0)
    emit_p2_load(1)

    for tb in range(NTB):
        loaded = p2_load[tb]
        def v_fix(v, tb=tb, loaded=loaded):
            v = AS(v, sv)
            v.wait_ge(loaded[0], loaded[1])
            v.wait_ge(sv.h, carry_done)
            if tb >= 2:
                v.wait_ge(st.h, p2_proj_st[tb - 2])
            for hc in range(DC):
                sl = slice(hc * TB, (hc + 1) * TB)
                v.scalar_tensor_tensor(rhs2[tb % 2][:, sl], cin2[tb % 2][:, sl],
                                       carry[:, hc:hc + 1], pin2[tb % 2][:, sl], ALU.mult, ALU.add)
        block.vector(v_fix)
        p2_fix[tb] = sv.n
        nrhs = sv.n

        for dcc in range(DC):
            b = dcc % 2
            bf = bank2[b]

            def t_proj(t, dcc=dcc, b=b, nrhs=nrhs, bf=bf, tb=tb):
                if dcc == 0:
                    t.wait_ge(sv.h, nrhs)
                if dcc == 0 and tb == 0:
                    t.wait_ge(sgw2.h, wp3_loaded)
                if bf:
                    t.wait_ge(ss.h, bf)
                for hc in range(DC):
                    mm = t.matmul(ps[b][:, 0:TB],
                                  wt["wp3t"][:, hc * D + dcc * 128: hc * D + (dcc + 1) * 128],
                                  rhs2[tb % 2][:, hc * TB:(hc + 1) * TB],
                                  start=(hc == 0), stop=(hc == DC - 1))
                mm.then_inc(st.inc())
            block.tensor(t_proj)
            nmm = st.n

            def s_evac(s, dcc=dcc, b=b, nmm=nmm, tb=tb):
                s = AS(s, ss)
                s.wait_ge(st.h, nmm)
                if dcc == 0 and tb > 0:
                    s.wait_ge(sv.h, p2_resid[tb - 1])
                s.activation(yout[:, dcc * TB:(dcc + 1) * TB], ps[b][:, 0:TB],
                             AF.Identity, bias=vecs["bp3"][:, dcc:dcc + 1])
            block.scalar(s_evac)
            bank2[b] = ss.n
        proj_done = ss.n
        p2_proj_st[tb] = st.n

        def v_resid(v, tb=tb, proj_done=proj_done):
            v = AS(v, sv)
            v.wait_ge(ss.h, proj_done)
            if tb > 0:
                v.wait_ge(so.h, p2_store[tb - 1])
            v.tensor_tensor(xnew[:], yout[:], xres[tb % 2][:], ALU.add)
        block.vector(v_resid)
        p2_resid[tb] = sv.n
        nresid = sv.n

        stat_st = []
        for dcc in range(DC):
            def v_cast2(v, dcc=dcc, nresid=nresid):
                v = AS(v, sv)
                if dcc == 0:
                    v.wait_ge(sv.h, nresid)
                else:
                    v.wait_ge(st.h, stat_st[dcc - 1])
                v.tensor_copy(ybf[:], xnew[:, dcc * TB:(dcc + 1) * TB])
            block.vector(v_cast2)
            ncast = sv.n

            def s_sq(s, dcc=dcc, nresid=nresid):
                s = AS(s, ss)
                if dcc == 0:
                    s.wait_ge(sv.h, nresid)
                else:
                    s.wait_ge(st.h, stat_st[dcc - 1])
                s.activation(ysq[:], xnew[:, dcc * TB:(dcc + 1) * TB], AF.Square)
            block.scalar(s_sq)
            nsq2 = ss.n

            def t_stat(t, dcc=dcc, ncast=ncast, nsq2=nsq2, tb=tb):
                t.wait_ge(sv.h, ncast)
                t.wait_ge(ss.h, nsq2)
                if dcc == 0 and tb > 0:
                    t.wait_ge(sv.h, stats_read_prev)
                t.matmul(ps[2][0:1, 0:TB], onesb[:], ybf[:], start=(dcc == 0), stop=(dcc == DC - 1)).then_inc(st.inc())
                t.matmul(ps[3][0:1, 0:TB], onesb[:], ysq[:], start=(dcc == 0), stop=(dcc == DC - 1)).then_inc(st.inc())
            block.tensor(t_stat)
            stat_st.append(st.n)
        nstat = st.n

        def v_stats(v, nstat=nstat, nbc_prev=nbc_prev):
            v = AS(v, sv)
            v.wait_ge(st.h, nstat)
            v.tensor_scalar(stats[:, 0:TB], ps[2][0:1, 0:TB], 1.0 / 1024.0, None, ALU.mult)
            v.tensor_scalar(stats[:, TB:2 * TB], ps[3][0:1, 0:TB], 1.0 / 1024.0, None, ALU.mult)
            v.tensor_tensor(stats[:, 2 * TB:3 * TB], stats[:, 0:TB], stats[:, 0:TB], ALU.mult)
            v.tensor_tensor(stats[:, 2 * TB:3 * TB], stats[:, TB:2 * TB], stats[:, 2 * TB:3 * TB], ALU.subtract)
            v.tensor_scalar(stats[:, 2 * TB:3 * TB], stats[:, 2 * TB:3 * TB], LN_EPS, None, ALU.add)
        block.vector(v_stats)
        nvar = sv.n

        def s_std(s, nvar=nvar):
            s = AS(s, ss)
            s.wait_ge(sv.h, nvar)
            s.activation(stats[:, 2 * TB:3 * TB], stats[:, 2 * TB:3 * TB], AF.Sqrt)
        block.scalar(s_std)
        nstd = ss.n

        def v_rstd(v, nstd=nstd):
            v = AS(v, sv)
            v.wait_ge(ss.h, nstd)
            v.reciprocal(stats[:, TB:2 * TB], stats[:, 2 * TB:3 * TB])
        block.vector(v_rstd)
        nrstd = sv.n

        def t_bcast(t, nrstd=nrstd):
            t.wait_ge(sv.h, nrstd)
            t.matmul(ps[4][:, 0:TB], ones_row[:], stats[0:1, 0:TB], start=True, stop=True).then_inc(st.inc())
            t.matmul(ps[5][:, 0:TB], ones_row[:], stats[0:1, TB:2 * TB], start=True, stop=True).then_inc(st.inc())
        block.tensor(t_bcast)
        nbc = st.n

        def v_apply(v, nbc=nbc):
            v = AS(v, sv)
            v.wait_ge(st.h, nbc)
            for dcc in range(DC):
                sl = slice(dcc * TB, (dcc + 1) * TB)
                v.tensor_tensor(xnew[:, sl], xnew[:, sl], ps[4][:, 0:TB], ALU.subtract)
            for dcc in range(DC):
                sl = slice(dcc * TB, (dcc + 1) * TB)
                v.tensor_tensor(xnew[:, sl], xnew[:, sl], ps[5][:, 0:TB], ALU.mult)
        block.vector(v_apply)
        napply = sv.n
        stats_read_prev = napply
        nbc_prev = nbc

        def s_store(s, tb=tb, napply=napply):
            s = AS(s, ss)
            s.wait_ge(sv.h, napply)
            for c in range(DC):
                s.dma_start(out=x_out_r[:, c, tb * TB:(tb + 1) * TB],
                            in_=xnew[:, c * TB:(c + 1) * TB]).then_inc(so.inc(16), 16)
        block.sync(s_store)
        p2_store[tb] = so.n

        if tb + 2 < NTB:
            emit_p2_load(tb + 2)

    def g_final(g):
        g.wait_ge(so.h, so.n)
    block.gpsimd(g_final)




def emit_quant(v, dst_bf, x_f32, scr, P, F, NB, a=None):
    bq, cq = scr["b"], scr["c"]
    if a is None:
        a = x_f32
    sc, rsc = scr["sc"], scr["rsc"]
    x3 = x_f32.rearrange("p (nb k) -> p nb k", k=16)
    v.tensor_reduce(sc[:P, 0:NB], x3, AX.X, ALU.max, apply_absolute_value=True)
    v.tensor_scalar(sc[:P, 0:NB], sc[:P, 0:NB], 1.0 / 6.0, 1e-12, ALU.mult, ALU.max)
    v.reciprocal(rsc[:P, 0:NB], sc[:P, 0:NB])
    a2 = a[:P, 0:F]
    v.tensor_tensor(a2.rearrange("p (nb k) -> p nb k", k=16), x3,
                    rsc[:P, 0:NB].broadcast_to((P, NB, 16)), ALU.mult)
    xn = a2
    v.tensor_scalar(bq[:P, 0:F], xn, 1.0, MAGIC, ALU.mult, ALU.add)
    v.tensor_scalar(bq[:P, 0:F], bq[:P, 0:F], MAGIC, None, ALU.subtract)
    v.tensor_scalar(bq[:P, 0:F], bq[:P, 0:F], 4.0, -4.0, ALU.min, ALU.max)
    v.scalar_tensor_tensor(bq[:P, 0:F], xn, 5.0, bq[:P, 0:F], ALU.is_gt, ALU.add)
    v.scalar_tensor_tensor(bq[:P, 0:F], xn, 5.0, bq[:P, 0:F], ALU.is_gt, ALU.add)
    v.tensor_scalar(cq[:P, 0:F], xn, -5.0, -2.0, ALU.is_lt, ALU.mult)
    v.tensor_tensor(bq[:P, 0:F], bq[:P, 0:F], cq[:P, 0:F], ALU.add)
    v.tensor_scalar(cq[:P, 0:F], xn, 2.0, MAGIC, ALU.mult, ALU.add)
    v.tensor_scalar(cq[:P, 0:F], cq[:P, 0:F], MAGIC, 0.5, ALU.subtract, ALU.mult)
    v.tensor_scalar(cq[:P, 0:F], cq[:P, 0:F], 2.0, -2.0, ALU.min, ALU.max)
    v.tensor_tensor(xn, xn, xn, ALU.mult)
    v.tensor_scalar(xn, xn, 5.0625, None, ALU.is_lt)
    v.tensor_tensor(cq[:P, 0:F], cq[:P, 0:F], bq[:P, 0:F], ALU.subtract)
    v.tensor_tensor(cq[:P, 0:F], xn, cq[:P, 0:F], ALU.mult)
    v.tensor_tensor(xn, bq[:P, 0:F], cq[:P, 0:F], ALU.add)
    v.tensor_tensor(dst_bf.rearrange("p (nb k) -> p nb k", k=16),
                    xn.rearrange("p (nb k) -> p nb k", k=16),
                    sc[:P, 0:NB].broadcast_to((P, NB, 16)), ALU.mult)



def emit_l0f_phaseW(nc, es, block, w_aps, scratch, ses):
    wsl = w_aps["wsl"]
    qag_in, qag_out = scratch["qag_in"], scratch["qag_out"]
    wst2 = [es.enter_context(nc.sbuf_tensor(f"l0fw_wst{i}", [128, 512], F32)) for i in range(2)]
    bH = es.enter_context(nc.sbuf_tensor("l0fw_b", [128, 512], F32))
    cH = es.enter_context(nc.sbuf_tensor("l0fw_c", [128, 512], F32))
    wqb2 = [es.enter_context(nc.sbuf_tensor(f"l0fw_q{i}", [128, 512], BF16)) for i in range(2)]
    scH = es.enter_context(nc.sbuf_tensor("l0fw_sc", [128, 32], F32))
    rscH = es.enter_context(nc.sbuf_tensor("l0fw_rsc", [128, 32], F32))
    identw = es.enter_context(nc.sbuf_tensor("l0fw_id", [128, 128], BF16))
    pstw = es.enter_context(nc.psum_tensor("l0fw_pst", [128, 512], BF16))

    sw2 = [Sem(ses, nc, f"l0fw_sw{i}") for i in range(2)]
    sid = Sem(ses, nc, "l0fw_sid")
    sv = Sem(ses, nc, "l0fw_sv")
    st = Sem(ses, nc, "l0fw_st")
    sjb2 = [Sem(ses, nc, f"l0fw_sjb{i}") for i in range(2)]
    scc = Sem(ses, nc, "l0fw_scc")
    qscr = dict(b=bH, c=cH, sc=scH, rsc=rscH)

    passes = [(c0, min(512, 2816 - c0)) for c0 in range(0, 2816, 512)]
    tasks = [(s_, c0, w) for s_ in range(7) for (c0, w) in passes]
    N = len(tasks)
    ld_tot = {}
    tp_done = {}
    q_done = {}
    st_slot = [0, 0]

    def g_ld(i, gate):
        s_, c0, w = tasks[i]
        sl = i % 2

        def f(g, s_=s_, c0=c0, w=w, sl=sl, gate=gate):
            if gate:
                g.wait_ge(sv.h, gate)
            g.dma_start(out=wst2[sl][:, 0:w],
                        in_=wsl[s_ * 128:(s_ + 1) * 128, c0:c0 + w]).then_inc(sw2[sl].inc(16), 16)
        block.sync(f)
        ld_tot[i] = sw2[sl].n

    def g_id(g):
        g.dma_start(out=identw[:], in_=w_aps["ident"]).then_inc(sid.inc(16), 16)
    block.gpsimd(g_id)
    id_done = sid.n
    g_ld(0, 0)
    g_ld(1, 0)

    state = {"i": 0, "agc": None}

    def emit_pass():
        i = state["i"]
        s_, c0, w = tasks[i]
        sl = i % 2

        def v_q(v, sl=sl, w=w, ld=ld_tot[i], tpg=tp_done.get(i - 2, 0), stg=st_slot[sl]):
            v = AS(v, sv)
            v._e.wait_ge(sw2[sl].h, ld)
            if tpg:
                v._e.wait_ge(st.h, tpg)
            if stg:
                v._e.wait_ge(sjb2[sl].h, stg)
            emit_quant(v, wqb2[sl][:, 0:w], wst2[sl][:, 0:w], qscr, 128, w, w // 16)
        block.vector(v_q)
        q_done[i] = sv.n

        if i + 2 < N:
            g_ld(i + 2, q_done[i])

        def t_tp(t, i=i, sl=sl, nch=(tasks[i][2] + 127) // 128):
            if i == 0:
                t.wait_ge(sid.h, id_done)
            t.wait_ge(sv.h, q_done[i])
            for k in range(nch):
                t.transpose(pstw[:, k * 128:(k + 1) * 128],
                            wqb2[sl][:, k * 128:(k + 1) * 128], identw[:]).then_inc(st.inc())
        block.tensor(t_tp)
        tp_done[i] = st.n

        def v_cp(v, i=i, sl=sl, w=w):
            v = AS(v, sv)
            v._e.wait_ge(st.h, tp_done[i])
            v.tensor_copy(wqb2[sl][:, 0:w], pstw[:, 0:w])
        block.vector(v_cp)
        cp_done = sv.n

        def g_st(g, s_=s_, c0=c0, w=w, sl=sl, cp_done=cp_done):
            g.wait_ge(sv.h, cp_done)
            g.dma_start(out=qag_in.ap()[s_ * 128:(s_ + 1) * 128, c0:c0 + w],
                        in_=wqb2[sl][:, 0:w]).then_inc(sjb2[sl].inc(16), 16)
        block.sync(g_st)
        st_slot[sl] = sjb2[sl].n
        state["i"] = i + 1

    def pump(k):
        while state["i"] < N and k > 0:
            emit_pass()
            k -= 1

    def finish():
        pump(N)

        def g_ag(g, st0=sjb2[0].n, st1=sjb2[1].n):
            g.wait_ge(sjb2[0].h, st0)
            g.wait_ge(sjb2[1].h, st1)
            g.collective_compute("AllGather", ALU.bypass,
                                 replica_groups=[list(range(8))],
                                 ins=[qag_in.ap()], outs=[qag_out.ap()]).then_inc(scc.inc())
        block.gpsimd(g_ag)
        state["agc"] = scc.n

    return dict(pump=pump, finish=finish, scc=scc, state=state)


def emit_l0_ffn(nc, es, block, x_in, x_out, w_aps, scratch=None, ses=None, pw=None):
    ses = es if ses is None else ses
    """L0_ffn + residual + LN. x_in/x_out: DRAM APs fm f32 [1024, T].

    w_aps: wg, wu: [2816, 1024] f32; wd: [1024, 2816] f32; ident: [128,128] f32.
    """
    I = 2816
    IC = I // 128
    NTT = T // 128
    WB = 8
    NWB = I // WB

    wq = {w: es.enter_context(nc.sbuf_tensor("l0f_" + w, [128, DC * I], BF16))
          for w in ["wgq", "wuq"]}
    wdq = es.enter_context(nc.sbuf_tensor("l0f_wdq", [128, IC * D], BF16))
    identb = es.enter_context(nc.sbuf_tensor("l0f_identb", [128, 128], BF16))
    identf = es.enter_context(nc.sbuf_tensor("l0f_identf", [128, 128], F32))

    wst0 = es.enter_context(nc.sbuf_tensor("l0f_wst0", [128, I], F32))
    wst = [wst0, wst0]
    xfm = [es.enter_context(nc.sbuf_tensor(f"l0f_xfm{i}", [128, D], F32)) for i in range(2)]
    xtok2_t = [es.enter_context(nc.sbuf_tensor(f"l0f_xtok{i}", [128, D], F32)) for i in range(2)]
    xqf2_t = [es.enter_context(nc.sbuf_tensor(f"l0f_xqf{i}", [128, D], BF16)) for i in range(2)]
    xtok = xtok2_t[0]
    xqf = xqf2_t[0]
    hq = es.enter_context(nc.sbuf_tensor("l0f_hq", [128, I], BF16))
    xqt = hq
    hqf = es.enter_context(nc.sbuf_tensor("l0f_hqf", [128, I], BF16))
    lnst = es.enter_context(nc.sbuf_tensor("l0f_lnst", [128, 8], F32))
    qb_t = es.enter_context(nc.sbuf_tensor("l0f_qb", [128, I], F32))
    qc_t = es.enter_context(nc.sbuf_tensor("l0f_qc", [128, I], F32))
    silu_s = qc_t[:, 1024:1024 + NWB]
    ynew = qb_t
    qscr = dict(
        b=qb_t,
        c=qc_t,
        sc=es.enter_context(nc.sbuf_tensor("l0f_qsc", [128, I // 16], F32)),
        rsc=es.enter_context(nc.sbuf_tensor("l0f_qrsc", [128, I // 16], F32)),
    )
    xqscr = dict(b=qb_t[:, 0:1024], c=qc_t[:, 1024:2048],
                 sc=qscr["sc"], rsc=qscr["rsc"])
    hf = wst[0]
    wqb = hq
    yfm = qc_t

    ps = [es.enter_context(nc.psum_tensor(f"l0f_ps{i}", [128, 512], F32)) for i in range(4)]
    pst = es.enter_context(nc.psum_tensor("l0f_pst", [128, 1024], BF16))
    psf = [es.enter_context(nc.psum_tensor(f"l0f_psf{i}", [128, 512], F32)) for i in range(2)]

    sw = [Sem(ses, nc, f"l0f_sw{i}") for i in range(2)]
    sx = [Sem(ses, nc, f"l0f_sx{i}") for i in range(2)]
    sgw = Sem(ses, nc, "l0f_sgw")
    st = Sem(ses, nc, "l0f_st")
    sv = Sem(ses, nc, "l0f_sv")
    ss = Sem(ses, nc, "l0f_ss")
    so = Sem(ses, nc, "l0f_so")

    x_in_r = x_in.rearrange("(c p) t -> p c t", p=128)
    x_out_r = x_out.rearrange("(c p) t -> p c t", p=128)

    def g_ident(g):
        g.dma_start(out=identb[:], in_=w_aps["ident"]).then_inc(sgw.inc(16), 16)
        g.dma_start(out=identf[:], in_=w_aps["ident"]).then_inc(sgw.inc(16), 16)
    block.gpsimd(g_ident)
    ident_done = sgw.n

    scc = pw["scc"] if pw is not None else Sem(ses, nc, "l0f_scc")
    sjb = Sem(ses, nc, "l0f_sjb")
    swre = Sem(ses, nc, "l0f_swre")
    wsl = w_aps["wsl"]
    qag_in, qag_out = scratch["qag_in"], scratch["qag_out"]
    jobbuf = hqf

    def reassemble(g):
        jobs = [("wg", ch) for ch in range(IC)] + [("wu", ch) for ch in range(IC)] + \
               [("wd", ch) for ch in range(DC)]
        for j, (wname, ch) in enumerate(jobs):
            r0 = (j // 7) * 896 + (j % 7) * 128
            srcv = qag_out.ap()[r0:r0 + 128, :]
            if wname == "wd":
                dst = wdq[:].rearrange("p (ic d) -> p ic d", d=D)[:, :, ch * 128:(ch + 1) * 128]
                g.dma_start(out=dst, in_=srcv.rearrange("p (ic c) -> p ic c", c=128)).then_inc(swre.inc(16), 16)
            else:
                w_t = wq["wgq" if wname == "wg" else "wuq"]
                dst = w_t[:].rearrange("p (dc i) -> p dc i", i=I)[:, :, ch * 128:(ch + 1) * 128]
                g.dma_start(out=dst, in_=srcv.rearrange("p (ic c) -> p ic c", c=128)[:, 0:DC, :]).then_inc(swre.inc(16), 16)

    if pw is not None:
        def g_wre(g, agc=pw["agc"]):
            g.wait_ge(scc.h, agc)
            reassemble(g)
        block.gpsimd(g_wre)

    slot_free = [(sv.h, 0), (sv.h, 0)]
    F = I
    NB = F // 16
    for s_ in ([] if pw is not None else range(7)):
        sl = 0
        gate = slot_free[0]

        def g_wload(g, s_=s_, sl=sl, gate=gate):
            g.wait_ge(gate[0], gate[1])
            g.dma_start(out=wst[sl][:, 0:F], in_=wsl[s_ * 128:(s_ + 1) * 128, :]).then_inc(sw[sl].inc(16), 16)
        block.gpsimd(g_wload)
        loaded = (sw[sl].h, sw[sl].n)

        def v_wq(v, sl=sl, loaded=loaded, jb_gate=sjb.n):
            v = AS(v, sv)
            v._e.wait_ge(loaded[0], loaded[1])
            emit_quant(v, wqb[:, 0:F], wst[sl][:, 0:F], qscr, 128, F, NB)
        block.vector(v_wq)
        slot_free[sl] = (sv.h, sv.n)
        q_done = sv.n

        for bs in range(0, IC, 8):
            be = min(bs + 8, IC)

            def t_tp(t, bs=bs, be=be, q_done=q_done, vgate=sv.n):
                t.wait_ge(sgw.h, ident_done)
                t.wait_ge(sv.h, q_done if bs == 0 else vgate)
                for k in range(bs, be):
                    t.transpose(pst[:, (k - bs) * 128:(k - bs + 1) * 128],
                                wqb[:, k * 128:(k + 1) * 128], identb[:]).then_inc(st.inc())
            block.tensor(t_tp)
            tp_done = st.n

            def v_wcp(v, bs=bs, be=be, tp_done=tp_done, jb_gate=sjb.n):
                v = AS(v, sv)
                v._e.wait_ge(st.h, tp_done)
                if bs == 0 and jb_gate:
                    v._e.wait_ge(sjb.h, jb_gate)
                v.tensor_copy(jobbuf[:, bs * 128:be * 128], pst[:, 0:(be - bs) * 128])
            block.vector(v_wcp)
        cp_done = sv.n

        def g_jst(g, s_=s_, cp_done=cp_done):
            g.wait_ge(sv.h, cp_done)
            g.dma_start(out=qag_in.ap()[s_ * 128:(s_ + 1) * 128, :], in_=jobbuf[:]).then_inc(sjb.inc(16), 16)
        block.gpsimd(g_jst)

    if pw is None:
        def g_wag(g, stored=sjb.n):
            g.wait_ge(sjb.h, stored)
            g.collective_compute("AllGather", ALU.bypass,
                                 replica_groups=[list(range(8))],
                                 ins=[qag_in.ap()], outs=[qag_out.ap()]).then_inc(scc.inc())
            g.wait_ge(scc.h, scc.n)
            reassemble(g)
        block.gpsimd(g_wag)
    wq_all_re = (swre.h, swre.n)

    def g_xload(tt, sl, gate):
        def f(g_, tt=tt, sl=sl, gate=gate):
            g_.wait_ge(gate[0], gate[1])
            for c in range(DC):
                g_.dma_start(out=xfm[sl][:, c * 128:(c + 1) * 128],
                             in_=x_in_r[:, c, tt * 128:(tt + 1) * 128]).then_inc(sx[sl].inc(16), 16)
        block.gpsimd(f)
        return (sx[sl].h, sx[sl].n)

    x_loaded = [None] * NTT
    x_loaded[0] = g_xload(0, 0, (sgw.h, ident_done))
    x_loaded[1] = g_xload(1, 1, (sgw.h, ident_done))
    xfm_consumed = [0] * NTT

    def emit_xchain(tt):
        sl = tt % 2
        xtk = xtok2_t[sl]
        xqo = xqf2_t[sl]

        def t_xtp(t, tt=tt, sl=sl, ld=x_loaded[tt], vgate=sv.n):
            t.wait_ge(sgw.h, ident_done)
            t.wait_ge(ld[0], ld[1])
            if vgate:
                t.wait_ge(sv.h, vgate)
            for c in range(DC):
                t.transpose(psf[c % 2][:, (c // 2) * 128:(c // 2) * 128 + 128],
                            xfm[sl][:, c * 128:(c + 1) * 128], identf[:]).then_inc(st.inc())
        block.tensor(t_xtp)
        xfm_consumed[tt] = st.n
        xtp_done = st.n

        def v_xevac(v, xtk=xtk, xtp_done=xtp_done):
            v = AS(v, sv)
            v._e.wait_ge(st.h, xtp_done)
            for c in range(DC):
                v.tensor_copy(xtk[:, c * 128:(c + 1) * 128],
                              psf[c % 2][:, (c // 2) * 128:(c // 2) * 128 + 128])
        block.vector(v_xevac)

        if tt + 2 < NTT:
            x_loaded[tt + 2] = g_xload(tt + 2, sl, (st.h, xfm_consumed[tt]))

        def v_xq(v, xtk=xtk):
            v = AS(v, sv)
            emit_quant(v, xqt[:, 0:D], xtk[:], xqscr, 128, D, D // 16, a=qb_t[:, 1024:2048])
        block.vector(v_xq)
        xq_done = sv.n

        def t_xqtp(t, xq_done=xq_done):
            t.wait_ge(sgw.h, ident_done)
            t.wait_ge(sv.h, xq_done)
            for c in range(DC):
                t.transpose(pst[:, c * 128:(c + 1) * 128],
                            xqt[:, c * 128:(c + 1) * 128], identb[:]).then_inc(st.inc())
        block.tensor(t_xqtp)
        xqtp_done = st.n

        def v_xqevac(v, xqo=xqo, xqtp_done=xqtp_done):
            v = AS(v, sv)
            v._e.wait_ge(st.h, xqtp_done)
            v.tensor_copy(xqo[:], pst[:, 0:D])
        block.vector(v_xqevac)
        return sv.n, xqtp_done

    def make_ytail(tt, ln_done):
        stg = {}

        def e_ytp():
            def t_ytp(t, ln_done=ln_done):
                t.wait_ge(sgw.h, ident_done)
                t.wait_ge(sv.h, ln_done)
                for c in range(DC):
                    t.transpose(psf[c % 2][:, (c // 2) * 128:(c // 2) * 128 + 128],
                                ynew[:, c * 128:(c + 1) * 128], identf[:]).then_inc(st.inc())
            block.tensor(t_ytp)
            stg["ytp"] = st.n

        def e_yev():
            def v_yevac(v, ytp_done=stg["ytp"], store_gate=so.n):
                v = AS(v, sv)
                v._e.wait_ge(st.h, ytp_done)
                if store_gate:
                    v._e.wait_ge(so.h, store_gate)
                for c in range(DC):
                    v.tensor_copy(yfm[:, c * 128:(c + 1) * 128],
                                  psf[c % 2][:, (c // 2) * 128:(c // 2) * 128 + 128])
            block.vector(v_yevac)
            stg["yev"] = sv.n

        def e_store():
            def s_store(s, tt=tt, yev_done=stg["yev"]):
                s.wait_ge(sv.h, yev_done)
                for c in range(DC):
                    s.dma_start(out=x_out_r[:, c, tt * 128:(tt + 1) * 128],
                                in_=yfm[:, c * 128:(c + 1) * 128]).then_inc(so.inc(16), 16)
            block.sync(s_store)
        return [e_ytp, e_yev, e_store]

    xqf_done_l = [None] * (NTT + 1)
    xqtp_done_l = [None] * (NTT + 1)
    xqf_done_l[0], xqtp_done_l[0] = emit_xchain(0)
    silu_prev = 0
    fin_prev = 0
    ytail = None

    for tt in range(NTT):
        sl = tt % 2
        xtk = xtok2_t[sl]
        xqo = xqf2_t[sl]
        xqf_done = xqf_done_l[tt]

        for wb in range(WB):
            bg = (wb % 2) * 2
            cols = slice(wb * NWB, (wb + 1) * NWB)

            def t_gu(t, wb=wb, bg=bg, xqf_done=xqf_done, bank_gate=sv.n,
                     ss_gate=(silu_prev if wb == 0 else ss.n),
                     wre=wq_all_re, xqo=xqo):
                if wb == 0:
                    t.wait_ge(sv.h, xqf_done)
                    t.wait_ge(wre[0], wre[1])
                else:
                    t.wait_ge(sv.h, bank_gate)
                if ss_gate:
                    t.wait_ge(ss.h, ss_gate)
                for dc in range(DC):
                    t.matmul(ps[bg][:, 0:NWB], xqo[:, dc * 128:(dc + 1) * 128],
                             wq["wgq"][:, dc * I + wb * NWB: dc * I + (wb + 1) * NWB],
                             start=(dc == 0), stop=(dc == DC - 1)).then_inc(st.inc())
                for dc in range(DC):
                    t.matmul(ps[bg + 1][:, 0:NWB], xqo[:, dc * 128:(dc + 1) * 128],
                             wq["wuq"][:, dc * I + wb * NWB: dc * I + (wb + 1) * NWB],
                             start=(dc == 0), stop=(dc == DC - 1)).then_inc(st.inc())
            block.tensor(t_gu)
            gu_done = st.n

            def s_silu(s, wb=wb, bg=bg, gu_done=gu_done, fg=fin_prev):
                s = AS(s, ss)
                s._e.wait_ge(st.h, gu_done)
                if wb == 0 and fg:
                    s._e.wait_ge(sv.h, fg)
                s.activation(silu_s[:], ps[bg][:, 0:NWB], AF.Silu)
            block.scalar(s_silu)
            silu_done = ss.n

            def v_h(v, bg=bg, cols=cols, silu_done=silu_done):
                v = AS(v, sv)
                v._e.wait_ge(ss.h, silu_done)
                v.tensor_tensor(hf[:, cols], ps[bg + 1][:, 0:NWB], silu_s[:], ALU.mult)
            block.vector(v_h)
            if ytail is not None and 2 <= wb <= 4:
                ytail[wb - 2]()
        h_done = sv.n
        silu_prev = ss.n

        if tt + 1 < NTT:
            xqf_done_l[tt + 1], xqtp_done_l[tt + 1] = emit_xchain(tt + 1)

        def v_hq(v, h_done=h_done, xg=(xqtp_done_l[tt + 1] if tt + 1 < NTT else 0),
                 store_gate=so.n):
            v = AS(v, sv)
            if xg:
                v._e.wait_ge(st.h, xg)
            if store_gate:
                v._e.wait_ge(so.h, store_gate)
            emit_quant(v, hq[:], hf[:], qscr, 128, I, I // 16)
        block.vector(v_hq)
        hq_done = sv.n

        for bs in range(0, IC, 8):
            be = min(bs + 8, IC)

            def t_hqtp(t, bs=bs, be=be, hq_done=hq_done, evac_gate=sv.n):
                t.wait_ge(sgw.h, ident_done)
                if bs == 0:
                    t.wait_ge(sv.h, hq_done)
                else:
                    t.wait_ge(sv.h, evac_gate)
                for k in range(bs, be):
                    t.transpose(pst[:, (k - bs) * 128:(k - bs + 1) * 128],
                                hq[:, k * 128:(k + 1) * 128], identb[:]).then_inc(st.inc())
            block.tensor(t_hqtp)
            tp_done = st.n

            def v_hqevac(v, bs=bs, be=be, tp_done=tp_done):
                v = AS(v, sv)
                v._e.wait_ge(st.h, tp_done)
                v.tensor_copy(hqf[:, bs * 128:be * 128], pst[:, 0:(be - bs) * 128])
            block.vector(v_hqevac)
        hqf_done = sv.n

        def t_down(t, hqf_done=hqf_done, ss_gate=ss.n):
            t.wait_ge(sv.h, hqf_done)
            if ss_gate:
                t.wait_ge(ss.h, ss_gate)
            for half in range(2):
                for ic in range(IC):
                    t.matmul(psf[half][:, 0:512],
                             hqf[:, ic * 128:(ic + 1) * 128],
                             wdq[:, ic * D + half * 512: ic * D + (half + 1) * 512],
                             start=(ic == 0), stop=(ic == IC - 1)).then_inc(st.inc())
        block.tensor(t_down)
        down_done = st.n

        def v_fin(v, tt=tt, xtk=xtk, down_done=down_done):
            v = AS(v, sv)
            v._e.wait_ge(st.h, down_done)
            v.tensor_tensor(ynew[:, 0:512], psf[0][:, 0:512], xtk[:, 0:512], ALU.add)
            v.tensor_tensor(ynew[:, 512:1024], psf[1][:, 0:512], xtk[:, 512:1024], ALU.add)
            v.tensor_reduce(lnst[:, 0:1], ynew[:, 0:D], AX.X, ALU.add)
            v.tensor_tensor(qc_t[:, 1024:2048], ynew[:, 0:D], ynew[:, 0:D], ALU.mult)
            v.tensor_reduce(lnst[:, 1:2], qc_t[:, 1024:2048], AX.X, ALU.add)
            v.tensor_scalar(lnst[:, 0:1], lnst[:, 0:1], 1.0 / 1024.0, None, ALU.mult)
            v.tensor_scalar(lnst[:, 1:2], lnst[:, 1:2], 1.0 / 1024.0, None, ALU.mult)
            v.tensor_tensor(lnst[:, 2:3], lnst[:, 0:1], lnst[:, 0:1], ALU.mult)
            v.tensor_tensor(lnst[:, 2:3], lnst[:, 1:2], lnst[:, 2:3], ALU.subtract)
            v.tensor_scalar(lnst[:, 2:3], lnst[:, 2:3], LN_EPS, None, ALU.add)
        block.vector(v_fin)
        fin_prev = sv.n
        nvar = sv.n

        def s_sqrt(s, nvar=nvar):
            s = AS(s, ss)
            s._e.wait_ge(sv.h, nvar)
            s.activation(lnst[:, 3:4], lnst[:, 2:3], AF.Sqrt)
        block.scalar(s_sqrt)
        nstd = ss.n

        def v_ln(v, nstd=nstd):
            v = AS(v, sv)
            v._e.wait_ge(ss.h, nstd)
            v.reciprocal(lnst[:, 3:4], lnst[:, 3:4])
            v.tensor_scalar(ynew[:, 0:D], ynew[:, 0:D], lnst[:, 0:1], None, ALU.subtract)
            v.tensor_scalar(ynew[:, 0:D], ynew[:, 0:D], lnst[:, 3:4], None, ALU.mult)
        block.vector(v_ln)
        ln_done = sv.n

        ytail = make_ytail(tt, ln_done)
    for f in ytail:
        f()

    def g_fin(g):
        g.wait_ge(so.h, so.n)
    block.gpsimd(g_fin)




def host_l0f_wsl(weights, core_id):
    wg = np.asarray(weights["L0_ffn_w_gate"], np.float32)
    wu = np.asarray(weights["L0_ffn_w_up"], np.float32)
    wd = np.asarray(weights["L0_ffn_w_down"], np.float32)
    jobs = [("wg", c) for c in range(22)] + [("wu", c) for c in range(22)] + \
           [("wd", c) for c in range(8)]
    out = np.zeros((7, 128, 2816), np.float32)
    for s in range(7):
        j = core_id * 7 + s
        if j >= len(jobs):
            continue
        wname, ch = jobs[j]
        w = {"wg": wg, "wu": wu, "wd": wd}[wname]
        blkw = w.shape[1]
        out[s, :, 0:blkw] = w[ch * 128:(ch + 1) * 128, :]
    return np.ascontiguousarray(out.reshape(896, 2816))


def build_l0f_launch():
    nc = bass.Bass("TRN2", target_bir_lowering=False)
    x_in = nc.dram_tensor("x_in", [D, T], F32, kind="ExternalInput").ap()
    x_out = nc.dram_tensor("x_out", [D, T], F32, kind="ExternalOutput").ap()
    w_aps = dict(
        wsl=nc.dram_tensor("wsl", [896, 2816], F32, kind="ExternalInput").ap(),
        ident=nc.dram_tensor("ident", [128, 128], F32, kind="ExternalInput").ap(),
    )
    scratch = dict(
        qag_in=nc.dram_tensor("qag_in", [896, 2816], BF16),
        qag_out=nc.dram_tensor("qag_out", [7168, 2816], BF16, addr_space="Shared"),
    )
    es = contextlib.ExitStack()
    with es:
        block = DirectBlock(nc)
        es.enter_context(nc.Block())
        emit_l0_ffn(nc, es, block, x_in, x_out, w_aps, scratch)
    return nc


def run_l0f(x_full, weights):
    nc = build_l0f_launch()
    ident = np.eye(128, dtype=np.float32)
    in_maps = []
    for c in range(NCORES):
        b, hf = c // 2, c % 2
        xs = np.ascontiguousarray(x_full[b, hf * T:(hf + 1) * T, :])
        in_maps.append(dict(x_in=fmT(xs), wsl=host_l0f_wsl(weights, c), ident=ident))
    res = run_bass_kernel_spmd(nc, in_maps, core_ids=list(range(NCORES)))
    out = np.zeros((B, S, D), np.float32)
    for c in range(NCORES):
        b, hf = c // 2, c % 2
        out[b, hf * T:(hf + 1) * T, :] = np.asarray(res.results[c]["x_out"], np.float32).T
    return out




def emit_l1_attn(nc, es, block, x_in, x_out, w_aps, scratch, ses=None):
    ses = es if ses is None else ses
    """Differential attention block (B=1 sequence slice per core pair).

    w_aps: wq, wk: [16, 1024, 128] f32; wvT: [1024, 1024] f32 (dmodel, h*dv);
    wo: [1024, 1024] f32 (hd rows, dmodel cols);
    bqT, bkT: [128, 16]; bvrow: [1, 1024]; bo, gamma, beta: [128, 8];
    lamb: [128, 1]; onesr: [1, 128]  (all f32 DRAM APs)
    scratch: cc_kin [2048,2048]bf16, cc_kout [4096,2048]bf16,
             cc_vin [2048,1024]bf16, cc_vout [4096,1024]bf16 (dram tensors)
    """
    H, DH = 16, 64
    NTT = T // 128
    KC = S // 128
    VW = H * 65

    xbf = es.enter_context(nc.sbuf_tensor("l1a_xbf", [128, DC * T], BF16))
    ocat = xbf
    qsb = es.enter_context(nc.sbuf_tensor("l1a_qsb", [128, H * T], BF16))
    khh = [es.enter_context(nc.sbuf_tensor(f"l1a_kh{i}", [128, S], BF16)) for i in range(2)]
    wvsb = es.enter_context(nc.sbuf_tensor("l1a_wvsb", [128, DC * 1024], BF16))
    wosb = wvsb
    vecs = {v: es.enter_context(nc.sbuf_tensor("l1a_" + v, [128, 16], F32))
            for v in ["bqT", "bkT", "bo", "gamma", "beta", "lamb", "gp", "bp"]}
    onesr = es.enter_context(nc.sbuf_tensor("l1a_onesr", [1, 128], F32))
    ones65 = es.enter_context(nc.sbuf_tensor("l1a_ones65", [65, 64], F32))
    ksum = es.enter_context(nc.sbuf_tensor("l1a_ksum", [128, 2], F32))
    o1s = es.enter_context(nc.sbuf_tensor("l1a_o1s", [128, 512], F32))
    o2s = es.enter_context(nc.sbuf_tensor("l1a_o2s", [128, 512], F32))
    rs1 = es.enter_context(nc.sbuf_tensor("l1a_rs1", [1, 512], F32))
    rs2 = es.enter_context(nc.sbuf_tensor("l1a_rs2", [1, 512], F32))
    p1s = es.enter_context(nc.sbuf_tensor("l1a_p1s", [128, 1024], BF16))
    p2s = es.enter_context(nc.sbuf_tensor("l1a_p2s", [128, 1024], BF16))
    onesb = es.enter_context(nc.sbuf_tensor("l1a_onesb", [128, 1], BF16))
    yout = ytmp = xres = ystat = ybf = ysq = None
    es_attn = contextlib.ExitStack()
    vhat = None
    es_proj = contextlib.ExitStack()
    wqk = [es_proj.enter_context(nc.sbuf_tensor(f"l1a_wqk{i}", [128, 1024], BF16)) for i in range(2)]
    kstage = [es_proj.enter_context(nc.sbuf_tensor(f"l1a_kst{i}", [128, T], BF16)) for i in range(2)]
    vstage = [es_proj.enter_context(nc.sbuf_tensor(f"l1a_vst{i}", [128, 1024], BF16)) for i in range(2)]
    bvb = es_proj.enter_context(nc.sbuf_tensor("l1a_bvb", [128, 1024], F32))
    bvrow = es_proj.enter_context(nc.sbuf_tensor("l1a_bvrow", [1, 1024], F32))
    pS = [es.enter_context(nc.psum_tensor(f"l1a_pS{i}", [128, 1024], F32)) for i in range(2)]
    ps = [pS[0][:, 0:512], pS[0][:, 512:1024]]
    pO1 = es.enter_context(nc.psum_tensor("l1a_pO1", [128, 512], F32))
    pO2 = es.enter_context(nc.psum_tensor("l1a_pO2", [128, 512], F32))
    pG1 = es.enter_context(nc.psum_tensor("l1a_pG1", [64, 512], F32))
    pG2 = es.enter_context(nc.psum_tensor("l1a_pG2", [64, 512], F32))

    sgw = Sem(ses, nc, "l1a_sgw")
    swq = [Sem(ses, nc, f"l1a_swq{i}") for i in range(2)]
    svst = [Sem(ses, nc, f"l1a_svst{i}") for i in range(2)]
    skst = [Sem(ses, nc, f"l1a_skst{i}") for i in range(2)]
    sxk = [Sem(ses, nc, f"l1a_sxk{i}") for i in range(2)]
    svh = Sem(ses, nc, "l1a_svh")
    swo = Sem(ses, nc, "l1a_swo")
    sxr = [Sem(ses, nc, f"l1a_sxr{i}") for i in range(2)]
    scc = Sem(ses, nc, "l1a_scc")
    st = Sem(ses, nc, "l1a_st")
    sv = Sem(ses, nc, "l1a_sv")
    ss = Sem(ses, nc, "l1a_ss")
    so = Sem(ses, nc, "l1a_so")

    x_in_r = x_in.rearrange("(c p) t -> p c t", p=128)
    x_out_r = x_out.rearrange("(c p) t -> p c t", p=128)
    kin, kout = scratch["cc_kin"], scratch["cc_kout"]
    vin, vout = scratch["cc_vin"], scratch["cc_vout"]

    def g_loads(g):
        for c in range(DC):
            g.dma_start(out=xbf[:, c * T:(c + 1) * T], in_=x_in_r[:, c, :]).then_inc(sgw.inc(16), 16)
            g.dma_start(out=wvsb[:, c * 1024:(c + 1) * 1024],
                        in_=w_aps["wvT"][c * 128:(c + 1) * 128, :]).then_inc(sgw.inc(16), 16)
        for v, wdt in [("bqT", 16), ("bkT", 16), ("bo", 8), ("gamma", 8), ("beta", 8), ("lamb", 1)]:
            g.dma_start(out=vecs[v][:, 0:wdt], in_=w_aps[v]).then_inc(sgw.inc(16), 16)
        g.dma_start(out=bvrow[:], in_=w_aps["bvrow"]).then_inc(sgw.inc(16), 16)
        g.dma_start(out=onesr[:], in_=w_aps["onesr"]).then_inc(sgw.inc(16), 16)
        for i in range(2):
            g.dma_start(out=wqk[i][:].rearrange("p (c m) -> p c m", m=128),
                        in_=w_aps["wk"][i].rearrange(
                "(c p) m -> p c m", p=128)).then_inc(swq[i].inc(16), 16)
    block.gpsimd(g_loads)
    loads_done = sgw.n
    w_loaded_s = {0: (swq[0].h, swq[0].n), 1: (swq[1].h, swq[1].n)}

    def v_init(v):
        v = AS(v, sv)
        v._e.wait_ge(sgw.h, loads_done)
        v.memset(onesb[:], 1.0)
        v.memset(ones65[:], 1.0)
        v.tensor_scalar(vecs["lamb"][:, 0:1], vecs["lamb"][:, 0:1], -1.0, None, ALU.mult)
        v.tensor_scalar(vecs["gp"][:, 0:8], vecs["gamma"][:, 0:8], 0.2, None, ALU.mult)
        v.tensor_scalar(vecs["bp"][:, 0:8], vecs["beta"][:, 0:8], 0.2, None, ALU.mult)
    block.vector(v_init)
    init_done = sv.n

    def t_bvb(t):
        t.wait_ge(sv.h, init_done)
        t.matmul(ps[0], onesr[:], bvrow[0:1, 0:512], start=True, stop=True).then_inc(st.inc())
        t.matmul(ps[1], onesr[:], bvrow[0:1, 512:1024], start=True, stop=True).then_inc(st.inc())
    block.tensor(t_bvb)
    bvb_mm = st.n

    def v_bvb(v, bvb_mm=bvb_mm):
        v = AS(v, sv)
        v._e.wait_ge(st.h, bvb_mm)
        v.tensor_copy(bvb[:, 0:512], ps[0])
        v.tensor_copy(bvb[:, 512:1024], ps[1])
    block.vector(v_bvb)
    bvb_done = sv.n

    qb4 = [pS[0][:, 0:512], pS[0][:, 512:1024], pS[1][:, 0:512], pS[1][:, 512:1024]]
    vst_free = [(sv.h, 0), (sv.h, 0)]
    vev_hist = {}
    for tt in range(NTT):
        slv = tt % 2
        bset = (tt % 2) * 2

        def t_vproj(t, tt=tt, bset=bset, gate=vev_hist.get(tt - 2, bvb_done)):
            t.wait_ge(sv.h, gate)
            for half in range(2):
                for dc in range(DC):
                    t.matmul(qb4[bset + half],
                             xbf[:, dc * T + tt * 128: dc * T + (tt + 1) * 128],
                             wvsb[:, dc * 1024 + half * 512: dc * 1024 + (half + 1) * 512],
                             start=(dc == 0), stop=(dc == DC - 1)).then_inc(st.inc())
        block.tensor(t_vproj)
        vmm = st.n

        def v_vevac(v, slv=slv, bset=bset, vmm=vmm, gate=vst_free[slv]):
            v = AS(v, sv)
            v._e.wait_ge(st.h, vmm)
            v._e.wait_ge(gate[0], gate[1])
            v.tensor_tensor(vstage[slv][:, 0:512], qb4[bset], bvb[:, 0:512], ALU.add)
            v.tensor_tensor(vstage[slv][:, 512:1024], qb4[bset + 1], bvb[:, 512:1024], ALU.add)
        block.vector(v_vevac)
        vev_hist[tt] = sv.n
        evac_v = sv.n

        def g_vstore(g, tt=tt, slv=slv, vev=evac_v):
            g.wait_ge(sv.h, vev)
            g.dma_start(out=vin.ap()[tt * 128:(tt + 1) * 128, :], in_=vstage[slv][:]).then_inc(svst[slv].inc(16), 16)
        block.gpsimd(g_vstore)
        vst_free[slv] = (svst[slv].h, svst[slv].n)

    def g_agv(g, v0=(svst[0].h, svst[0].n), v1=(svst[1].h, svst[1].n)):
        g.wait_ge(v0[0], v0[1])
        g.wait_ge(v1[0], v1[1])
        g.collective_compute("AllGather", ALU.bypass,
                             replica_groups=[[0, 1], [2, 3], [4, 5], [6, 7]],
                             ins=[vin.ap()], outs=[vout.ap()]).then_inc(scc.inc())
    block.gpsimd(g_agv)
    agv_done = scc.n

    kst_free = [(skst[0].h, 0), (skst[1].h, 0)]

    def g_wstream(j, gate_st):
        name, idx = ("wk", j) if j < H else ("wq", j - H)
        sl = j % 2

        def f(g, name=name, idx=idx, sl=sl, gate_st=gate_st):
            if gate_st:
                g.wait_ge(st.h, gate_st)
            g.dma_start(out=wqk[sl][:].rearrange("p (c m) -> p c m", m=128),
                        in_=w_aps[name][idx].rearrange(
                "(c p) m -> p c m", p=128)).then_inc(swq[sl].inc(16), 16)
        block.gpsimd(f)
        w_loaded_s[j] = (swq[sl].h, swq[sl].n)

    bank_ev = [0, 0, 0, 0]
    for h in range(H):
        slk = h % 2
        for tb in range(4):
            i4 = (h * 4 + tb) % 4

            def t_kproj(t, h=h, tb=tb, i4=i4, wl=w_loaded_s[h],
                        bank_gate=evac_v if h == 0 else None,
                        ss_gate=bank_ev[i4]):
                if tb == 0:
                    t.wait_ge(wl[0], wl[1])
                if h == 0 and bank_gate is not None:
                    t.wait_ge(sv.h, bank_gate)
                if ss_gate:
                    t.wait_ge(ss.h, ss_gate)
                for dc in range(DC):
                    t.matmul(qb4[i4],
                             wqk[h % 2][:, dc * 128:(dc + 1) * 128],
                             xbf[:, dc * T + tb * 512:dc * T + (tb + 1) * 512],
                             start=(dc == 0), stop=(dc == DC - 1)).then_inc(st.inc())
            block.tensor(t_kproj)
            kmm = st.n

            def s_kevac(s, h=h, tb=tb, slk=slk, i4=i4, kmm=kmm, gate=kst_free[slk] if tb == 0 else None):
                s = AS(s, ss)
                s._e.wait_ge(st.h, kmm)
                if gate is not None:
                    s._e.wait_ge(gate[0], gate[1])
                s.activation(kstage[slk][:, tb * 512:(tb + 1) * 512], qb4[i4],
                             AF.Identity, bias=vecs["bkT"][:, h:h + 1])
            block.scalar(s_kevac)
            bank_ev[i4] = ss.n
        kev = ss.n
        if h + 2 < 2 * H:
            g_wstream(h + 2, st.n)

        def g_kstore(g, h=h, slk=slk, kev=kev):
            g.wait_ge(ss.h, kev)
            g.dma_start(out=kin.ap()[h * 128:(h + 1) * 128, :], in_=kstage[slk][:]).then_inc(skst[slk].inc(16), 16)
        block.gpsimd(g_kstore)
        kst_free[slk] = (skst[slk].h, skst[slk].n)

        if h == 7:
            def g_agk1(g, gates=(kst_free[0], kst_free[1])):
                for sh, sn in gates:
                    g.wait_ge(sh, sn)
                g.collective_compute("AllGather", ALU.bypass,
                                     replica_groups=[[0, 1], [2, 3], [4, 5], [6, 7]],
                                     ins=[kin.ap()[0:1024, :]], outs=[kout.ap()[0:2048, :]]).then_inc(scc.inc())
            block.gpsimd(g_agk1)
            agk1_done = scc.n
    k_stored = [(skst[0].h, skst[0].n), (skst[1].h, skst[1].n)]

    def g_agk2(g, gates=tuple(k_stored)):
        for sh, sn in gates:
            g.wait_ge(sh, sn)
        g.collective_compute("AllGather", ALU.bypass,
                             replica_groups=[[0, 1], [2, 3], [4, 5], [6, 7]],
                             ins=[kin.ap()[1024:2048, :]], outs=[kout.ap()[2048:4096, :]]).then_inc(scc.inc())
    block.gpsimd(g_agk2)
    agk_done = scc.n

    for h in range(H):
        for tb in range(4):
            i4 = (h * 4 + tb) % 4

            def t_qproj(t, h=h, tb=tb, i4=i4, wl=w_loaded_s[H + h], ss_gate=bank_ev[i4]):
                if tb == 0:
                    t.wait_ge(wl[0], wl[1])
                if ss_gate:
                    t.wait_ge(ss.h, ss_gate)
                for dc in range(DC):
                    t.matmul(qb4[i4],
                             wqk[h % 2][:, dc * 128:(dc + 1) * 128],
                             xbf[:, dc * T + tb * 512:dc * T + (tb + 1) * 512],
                             start=(dc == 0), stop=(dc == DC - 1)).then_inc(st.inc())
            block.tensor(t_qproj)
            qmm = st.n

            def s_qevac(s, h=h, tb=tb, i4=i4, qmm=qmm):
                s = AS(s, ss)
                s._e.wait_ge(st.h, qmm)
                s.activation(qsb[:, h * T + tb * 512: h * T + (tb + 1) * 512], qb4[i4],
                             AF.Identity, bias=vecs["bqT"][:, h:h + 1])
            block.scalar(s_qevac)
            bank_ev[i4] = ss.n
        if H + h + 2 < 2 * H:
            g_wstream(H + h + 2, st.n)
    qproj_done = ss.n


    kh_loaded = {}
    kh_free = [(ss.h, 0), (ss.h, 0)]

    def emit_khload(h):
        slk = h % 2

        r0 = h * 128 if h < 8 else 2048 + (h - 8) * 128

        def g_kh(g, h=h, slk=slk, gate=kh_free[slk], r0=r0,
                 agw=(agk1_done if h < 8 else agk_done)):
            g.wait_ge(scc.h, agw)
            g.wait_ge(gate[0], gate[1])
            g.dma_start(out=khh[slk][:, 0:T], in_=kout.ap()[r0:r0 + 128, :]).then_inc(sxk[slk].inc(16), 16)
            g.dma_start(out=khh[slk][:, T:S], in_=kout.ap()[1024 + r0: 1024 + r0 + 128, :]).then_inc(sxk[slk].inc(16), 16)
        block.gpsimd(g_kh)
        kh_loaded[h] = (sxk[slk].h, sxk[slk].n)

    emit_khload(0)
    emit_khload(1)

    es_proj.close()
    vhat = es_attn.enter_context(nc.sbuf_tensor("l1a_vhat", [128, KC * VW], BF16))

    def v_vhones(v, gates=(k_stored[0], k_stored[1], (svst[0].h, svst[0].n), (svst[1].h, svst[1].n)),
                 qd=qproj_done):
        v = AS(v, sv)
        for sh, sn in gates:
            v._e.wait_ge(sh, sn)
        v._e.wait_ge(ss.h, qd)
        v.memset(vhat[:].rearrange("p (kc h c) -> p kc h c", h=H, c=65)[:, :, :, 64:65], 1.0)
    block.vector(v_vhones)
    vhones_done = sv.n

    def g_vhload(g, ag_done=agv_done, vhones_done=vhones_done):
        g.wait_ge(scc.h, ag_done)
        g.wait_ge(sv.h, vhones_done)
        for kc in range(KC):
            dst = vhat[:, kc * VW:(kc + 1) * VW].rearrange("p (h c) -> p h c", c=65)[:, :, 0:64]
            g.dma_start(out=dst, in_=vout.ap()[kc * 128:(kc + 1) * 128, :].rearrange(
                "p (h c) -> p h c", c=64)).then_inc(svh.inc(16), 16)
    block.gpsimd(g_vhload)
    vhat_loaded = svh.n

    QB = 4
    exp_read = [0, 0]
    pv_read = [0, 0]
    comb_done = sv.n
    oevac_done = 0
    p12s = [p1s, p2s]
    pend = None

    def make_epilogue(pv_done, h, qb):
        stg = {}

        def e_oevac():
            nonlocal oevac_done

            def v_oevac(v, pv_done=pv_done):
                v = AS(v, sv)
                v._e.wait_ge(st.h, pv_done)
                v.tensor_copy(o1s[0:65, :], pO1[0:65, 0:512])
                v.tensor_copy(o2s[0:65, :], pO2[0:65, 0:512])
            block.vector(v_oevac)
            oevac_done = sv.n

        def e_recip():
            def v_recip(v):
                v = AS(v, sv)
                v.reciprocal(o1s[64:65, :], o1s[64:65, :])
                v.reciprocal(o2s[64:65, :], o2s[64:65, :])
            block.vector(v_recip)
            stg["rdone"] = sv.n

        def e_bc():
            def t_bc(t, rdone=stg["rdone"]):
                t.wait_ge(sv.h, rdone)
                t.matmul(pG1[0:64, 0:512], ones65[64:65, :], o1s[64:65, :],
                         start=True, stop=True, tile_position=(64, 0),
                         skip_group_check=True).then_inc(st.inc())
                t.matmul(pG2[0:64, 0:512], ones65[64:65, :], o2s[64:65, :],
                         start=True, stop=True, tile_position=(64, 0),
                         skip_group_check=True).then_inc(st.inc())
            block.tensor(t_bc)
            stg["bc"] = st.n

        def e_comb2():
            nonlocal comb_done

            def v_comb2(v, h=h, qb=qb, bc_done=stg["bc"]):
                v = AS(v, sv)
                v._e.wait_ge(st.h, bc_done)
                v.tensor_tensor(o1s[0:64, :], o1s[0:64, :], pG1[:, 0:512], ALU.mult)
                v.tensor_tensor(o2s[0:64, :], o2s[0:64, :], pG2[:, 0:512], ALU.mult)
                dst = ocat[(h % 2) * 64:(h % 2) * 64 + 64,
                           (h // 2) * T + qb * 512:(h // 2) * T + (qb + 1) * 512]
                v.scalar_tensor_tensor(dst, o2s[0:64, :], vecs["lamb"][0:64, 0:1], o1s[0:64, :],
                                       ALU.mult, ALU.add)
            block.vector(v_comb2)
            comb_done = sv.n

        return {"oevac": e_oevac, "recip": e_recip, "bc": e_bc, "comb2": e_comb2}

    kh_ready = None
    for h in range(H):
        slk = h % 2
        if h == 0:
            def v_center0(v, slk=slk, ld=kh_loaded[0]):
                v = AS(v, sv)
                v._e.wait_ge(ld[0], ld[1])
                v.tensor_reduce(ksum[:, 0:1], khh[slk][:], AX.X, ALU.add)
                v.tensor_scalar(ksum[:, 0:1], ksum[:, 0:1], 1.0 / 4096.0, None, ALU.mult)
                v.tensor_scalar(khh[slk][:], khh[slk][:], ksum[:, 0:1], None, ALU.subtract)
            block.vector(v_center0)
            kh_ready = sv.n

        for qb in range(QB):
            qsl = slice(h * T + qb * 512, h * T + (qb + 1) * 512)
            def emit_qk(kc, first):
                sset = kc % 2

                def t_qk(t, kc=kc, sset=sset, first=first, slk=slk,
                         kh_ready=kh_ready, er=exp_read[sset], vh=vhat_loaded,
                         qd=qproj_done):
                    if first:
                        t.wait_ge(sv.h, kh_ready)
                        t.wait_ge(svh.h, vh)
                        t.wait_ge(ss.h, qd)
                    if er:
                        t.wait_ge(ss.h, er)
                    t.matmul(pS[sset][:, 0:512], khh[slk][0:64, kc * 128:(kc + 1) * 128],
                             qsb[0:64, qsl], start=True, stop=True,
                             tile_position=(0, 0), skip_group_check=True).then_inc(st.inc())
                    t.matmul(pS[sset][:, 512:1024], khh[slk][64:128, kc * 128:(kc + 1) * 128],
                             qsb[64:128, qsl], start=True, stop=True,
                             tile_position=(64, 0), skip_group_check=True).then_inc(st.inc())
                block.tensor(t_qk)
                return st.n

            def emit_exp(kc):
                sset = kc % 2
                qk_done = qk_count[kc]

                def s_exp(s, sset=sset, qk_done=qk_done, pr=pv_read[sset]):
                    s = AS(s, ss)
                    s._e.wait_ge(st.h, qk_done)
                    if pr:
                        s._e.wait_ge(st.h, pr)
                    s.activation(p12s[sset][:], pS[sset][:], AF.Exp, scale=0.125)
                block.scalar(s_exp)
                exp_read[sset] = ss.n
                return ss.n

            def emit_pv(kc):
                sset = kc % 2
                exp_done = exp_count[kc]

                def t_pv(t, h=h, kc=kc, sset=sset, exp_done=exp_done, oe=oevac_done):
                    t.wait_ge(ss.h, exp_done)
                    if kc == 0 and oe:
                        t.wait_ge(sv.h, oe)
                    fl = (kc == 0)
                    ll = (kc == KC - 1)
                    vsl = vhat[:, kc * VW + h * 65: kc * VW + (h + 1) * 65]
                    t.matmul(pO1[0:65, 0:512], vsl, p12s[sset][:, 0:512],
                             start=fl, stop=ll, skip_group_check=True).then_inc(st.inc())
                    t.matmul(pO2[0:65, 0:512], vsl, p12s[sset][:, 512:1024],
                             start=fl, stop=ll, skip_group_check=True).then_inc(st.inc())
                block.tensor(t_pv)
                pv_read[sset] = st.n

            qk_count = {}
            exp_count = {}
            qk_count[0] = emit_qk(0, True)
            exp_count[0] = emit_exp(0)
            if pend is not None:
                pend["oevac"]()
            qk_count[1] = emit_qk(1, False)
            exp_count[1] = emit_exp(1)
            if pend is not None:
                pend["recip"]()
            emit_pv(0)
            for kc in range(2, KC):
                qk_count[kc] = emit_qk(kc, False)
                exp_count[kc] = emit_exp(kc)
                emit_pv(kc - 1)
                if pend is not None and kc == 16:
                    pend["bc"]()
                if pend is not None and kc == 18:
                    pend["comb2"]()
                if qb == 3 and kc == 20 and h + 1 < H:
                    def v_center(v, nslk=(h + 1) % 2, ld=kh_loaded[h + 1]):
                        v = AS(v, sv)
                        v._e.wait_ge(ld[0], ld[1])
                        v.tensor_reduce(ksum[:, 0:1], khh[nslk][:], AX.X, ALU.add)
                        v.tensor_scalar(ksum[:, 0:1], ksum[:, 0:1], 1.0 / 4096.0, None, ALU.mult)
                        v.tensor_scalar(khh[nslk][:], khh[nslk][:], ksum[:, 0:1], None, ALU.subtract)
                    block.vector(v_center)
                    kh_ready = sv.n
            emit_pv(KC - 1)
            pend = make_epilogue(st.n, h, qb)
        kh_free[slk] = (st.h, st.n)
        if h + 2 < H:
            emit_khload(h + 2)
    pend["oevac"]()
    pend["recip"]()
    pend["bc"]()
    pend["comb2"]()
    attn_done = sv.n
    es_attn.close()
    yout = es.enter_context(nc.sbuf_tensor("l1a_yout", [128, DC * 512], F32))
    ytmp = es.enter_context(nc.sbuf_tensor("l1a_ytmp", [128, DC * 512], F32))
    xres = [es.enter_context(nc.sbuf_tensor(f"l1a_xres{i}", [128, DC * 512], F32)) for i in range(2)]
    ystat = ytmp[0:1, 0:2048]
    ybf8 = qsb[:, 0:DC * 512]
    ysq8 = qsb[:, DC * 512:2 * DC * 512]

    def g_wo(g, attn_done=attn_done):
        g.wait_ge(sv.h, attn_done)
        for c in range(DC):
            g.dma_start(out=wosb[:, c * 1024:(c + 1) * 1024],
                        in_=w_aps["wo"][c * 128:(c + 1) * 128, :]).then_inc(swo.inc(16), 16)
        for i in range(2):
            for c in range(DC):
                g.dma_start(out=xres[i][:, c * 512:(c + 1) * 512],
                            in_=x_in_r[:, c, i * 512:(i + 1) * 512]).then_inc(sxr[i].inc(16), 16)
    block.gpsimd(g_wo)
    wo_loaded = swo.n
    xr_loaded = {0: (sxr[0].h, sxr[0].n), 1: (sxr[1].h, sxr[1].n)}

    stats_gate = 0
    for tb in range(4):
        for dc in range(DC):
            def t_oproj(t, tb=tb, dc=dc, wo_loaded=wo_loaded, attn_done=attn_done, bank_gate=ss.n, vgate=sv.n):
                if dc == 0 and tb == 0:
                    t.wait_ge(swo.h, wo_loaded)
                    t.wait_ge(sv.h, attn_done)
                if dc == 0 and tb > 0:
                    t.wait_ge(sv.h, vgate)
                if bank_gate:
                    t.wait_ge(ss.h, bank_gate)
                for hc in range(DC):
                    t.matmul(pS[1][:, (dc % 2) * 512:(dc % 2) * 512 + 512],
                             wosb[:, hc * 1024 + dc * 128: hc * 1024 + (dc + 1) * 128],
                             ocat[:, hc * T + tb * 512:hc * T + (tb + 1) * 512],
                             start=(hc == 0), stop=(hc == DC - 1)).then_inc(st.inc())
            block.tensor(t_oproj)
            omm = st.n

            def s_yevac(s, dc=dc, omm=omm, resid_gate=sv.n, store_gate=so.n):
                s = AS(s, ss)
                s._e.wait_ge(st.h, omm)
                if resid_gate:
                    s._e.wait_ge(sv.h, resid_gate)
                if store_gate:
                    s._e.wait_ge(so.h, store_gate)
                s.activation(yout[:, dc * 512:(dc + 1) * 512],
                             pS[1][:, (dc % 2) * 512:(dc % 2) * 512 + 512],
                             AF.Identity, bias=vecs["bo"][:, dc:dc + 1])
            block.scalar(s_yevac)
        yproj = ss.n

        def v_c1all(v, yproj=yproj):
            v = AS(v, sv)
            v._e.wait_ge(ss.h, yproj)
            for dc in range(DC):
                v.tensor_copy(ybf8[:, dc * 512:(dc + 1) * 512], yout[:, dc * 512:(dc + 1) * 512])
        block.vector(v_c1all)
        nca = sv.n

        def s_c1all(s, yproj=yproj):
            s = AS(s, ss)
            for dc in range(DC):
                s.activation(ysq8[:, dc * 512:(dc + 1) * 512],
                             yout[:, dc * 512:(dc + 1) * 512], AF.Square)
        block.scalar(s_c1all)
        nsq = ss.n

        def t_st1all(t, nca=nca, nsq=nsq, sg=stats_gate):
            t.wait_ge(sv.h, nca)
            t.wait_ge(ss.h, nsq)
            if sg:
                t.wait_ge(sv.h, sg)
            for dc in range(DC):
                t.matmul(pG1[0:1, 0:512], onesb[:], ybf8[:, dc * 512:(dc + 1) * 512],
                         start=(dc == 0), stop=(dc == DC - 1), skip_group_check=True).then_inc(st.inc())
            for dc in range(DC):
                t.matmul(pG2[0:1, 0:512], onesb[:], ysq8[:, dc * 512:(dc + 1) * 512],
                         start=(dc == 0), stop=(dc == DC - 1), skip_group_check=True).then_inc(st.inc())
        block.tensor(t_st1all)
        nstat = st.n

        def v_mu1(v, nstat=nstat):
            v = AS(v, sv)
            v._e.wait_ge(st.h, nstat)
            v.tensor_scalar(ystat[:, 0:512], pG1[0:1, 0:512], 1.0 / 1024.0, None, ALU.mult)
            v.tensor_scalar(ystat[:, 512:1024], pG2[0:1, 0:512], 1.0 / 1024.0, None, ALU.mult)
            v.tensor_tensor(ystat[:, 1024:1536], ystat[:, 0:512], ystat[:, 0:512], ALU.mult)
            v.tensor_tensor(ystat[:, 1024:1536], ystat[:, 512:1024], ystat[:, 1024:1536], ALU.subtract)
            v.tensor_scalar(ystat[:, 1024:1536], ystat[:, 1024:1536], LN_EPS, None, ALU.add)
        block.vector(v_mu1)
        nvar = sv.n

        def s_sd1(s, nvar=nvar):
            s = AS(s, ss)
            s._e.wait_ge(sv.h, nvar)
            s.activation(ystat[:, 1536:2048], ystat[:, 1024:1536], AF.Sqrt)
        block.scalar(s_sd1)
        nstd = ss.n

        def v_rs1(v, nstd=nstd):
            v = AS(v, sv)
            v._e.wait_ge(ss.h, nstd)
            v.reciprocal(ystat[:, 512:1024], ystat[:, 1536:2048])
        block.vector(v_rs1)
        nrst = sv.n

        def t_bc1(t, nrst=nrst, sg=stats_gate):
            t.wait_ge(sv.h, nrst)
            t.matmul(ps[0], onesr[:], ystat[0:1, 0:512], start=True, stop=True).then_inc(st.inc())
            t.matmul(ps[1], onesr[:], ystat[0:1, 512:1024], start=True, stop=True).then_inc(st.inc())
        block.tensor(t_bc1)
        nbc = st.n

        def v_ap1(v, tb=tb, nbc=nbc, xr=xr_loaded[tb]):
            v = AS(v, sv)
            v._e.wait_ge(st.h, nbc)
            v._e.wait_ge(xr[0], xr[1])
            for dc in range(DC):
                sl = slice(dc * 512, (dc + 1) * 512)
                v.tensor_tensor(ytmp[:, sl], yout[:, sl], ps[0], ALU.subtract)
            for dc in range(DC):
                sl = slice(dc * 512, (dc + 1) * 512)
                v.tensor_tensor(ytmp[:, sl], ytmp[:, sl], ps[1], ALU.mult)
            for dc in range(DC):
                sl = slice(dc * 512, (dc + 1) * 512)
                v.tensor_scalar(xres[tb % 2][:, sl], xres[tb % 2][:, sl],
                                vecs["bp"][:, dc:dc + 1], None, ALU.add)
                v.scalar_tensor_tensor(yout[:, sl], ytmp[:, sl], vecs["gp"][:, dc:dc + 1],
                                       xres[tb % 2][:, sl], ALU.mult, ALU.add)
        block.vector(v_ap1)
        nap1 = sv.n

        def v_c2all(v, nap1=nap1):
            v = AS(v, sv)
            for dc in range(DC):
                v.tensor_copy(ybf8[:, dc * 512:(dc + 1) * 512], yout[:, dc * 512:(dc + 1) * 512])
        block.vector(v_c2all)
        nca = sv.n

        def s_c2all(s, nap1=nap1):
            s = AS(s, ss)
            s._e.wait_ge(sv.h, nap1)
            for dc in range(DC):
                s.activation(ysq8[:, dc * 512:(dc + 1) * 512],
                             yout[:, dc * 512:(dc + 1) * 512], AF.Square)
        block.scalar(s_c2all)
        nsq = ss.n

        def t_st2all(t, nca=nca, nsq=nsq):
            t.wait_ge(sv.h, nca)
            t.wait_ge(ss.h, nsq)
            for dc in range(DC):
                t.matmul(pG1[0:1, 0:512], onesb[:], ybf8[:, dc * 512:(dc + 1) * 512],
                         start=(dc == 0), stop=(dc == DC - 1), skip_group_check=True).then_inc(st.inc())
            for dc in range(DC):
                t.matmul(pG2[0:1, 0:512], onesb[:], ysq8[:, dc * 512:(dc + 1) * 512],
                         start=(dc == 0), stop=(dc == DC - 1), skip_group_check=True).then_inc(st.inc())
        block.tensor(t_st2all)
        nstat2 = st.n

        def v_mu2(v, nstat2=nstat2):
            v = AS(v, sv)
            v._e.wait_ge(st.h, nstat2)
            v.tensor_scalar(ystat[:, 0:512], pG1[0:1, 0:512], 1.0 / 1024.0, None, ALU.mult)
            v.tensor_scalar(ystat[:, 512:1024], pG2[0:1, 0:512], 1.0 / 1024.0, None, ALU.mult)
            v.tensor_tensor(ystat[:, 1024:1536], ystat[:, 0:512], ystat[:, 0:512], ALU.mult)
            v.tensor_tensor(ystat[:, 1024:1536], ystat[:, 512:1024], ystat[:, 1024:1536], ALU.subtract)
            v.tensor_scalar(ystat[:, 1024:1536], ystat[:, 1024:1536], LN_EPS, None, ALU.add)
        block.vector(v_mu2)
        nvar2 = sv.n

        def s_sd2(s, nvar2=nvar2):
            s = AS(s, ss)
            s._e.wait_ge(sv.h, nvar2)
            s.activation(ystat[:, 1536:2048], ystat[:, 1024:1536], AF.Sqrt)
        block.scalar(s_sd2)
        nstd2 = ss.n

        def v_rs2b(v, nstd2=nstd2):
            v = AS(v, sv)
            v._e.wait_ge(ss.h, nstd2)
            v.reciprocal(ystat[:, 512:1024], ystat[:, 1536:2048])
        block.vector(v_rs2b)
        nrst2 = sv.n

        def t_bc2(t, nrst2=nrst2):
            t.wait_ge(sv.h, nrst2)
            t.matmul(ps[0], onesr[:], ystat[0:1, 0:512], start=True, stop=True).then_inc(st.inc())
            t.matmul(ps[1], onesr[:], ystat[0:1, 512:1024], start=True, stop=True).then_inc(st.inc())
        block.tensor(t_bc2)
        nbc2 = st.n

        def v_ap2(v, tb=tb, nbc2=nbc2, store_gate=so.n):
            v = AS(v, sv)
            v._e.wait_ge(st.h, nbc2)
            if store_gate:
                v._e.wait_ge(so.h, store_gate)
            for dc in range(DC):
                sl = slice(dc * 512, (dc + 1) * 512)
                v.tensor_tensor(ytmp[:, sl], yout[:, sl], ps[0], ALU.subtract)
            for dc in range(DC):
                sl = slice(dc * 512, (dc + 1) * 512)
                v.tensor_tensor(yout[:, sl], ytmp[:, sl], ps[1], ALU.mult)
        block.vector(v_ap2)
        nap2 = sv.n
        stats_gate = nap2

        def s_store(s, tb=tb, nap2=nap2):
            s.wait_ge(sv.h, nap2)
            for dc in range(DC):
                s.dma_start(out=x_out_r[:, dc, tb * 512:(tb + 1) * 512],
                            in_=yout[:, dc * 512:(dc + 1) * 512]).then_inc(so.inc(16), 16)
        block.sync(s_store)

        if tb + 2 < 4:
            def g_xr(g, tb=tb, gate=sv.n):
                g.wait_ge(sv.h, gate)
                for c in range(DC):
                    g.dma_start(out=xres[tb % 2][:, c * 512:(c + 1) * 512],
                                in_=x_in_r[:, c, (tb + 2) * 512:(tb + 3) * 512]).then_inc(sxr[tb % 2].inc(16), 16)
            block.gpsimd(g_xr)
            xr_loaded[tb + 2] = (sxr[tb % 2].h, sxr[tb % 2].n)

    def g_fin(g):
        g.wait_ge(so.h, so.n)
    block.gpsimd(g_fin)


def build_l1a_launch():
    nc = bass.Bass("TRN2", target_bir_lowering=False)
    x_in = nc.dram_tensor("x_in", [D, T], F32, kind="ExternalInput").ap()
    x_out = nc.dram_tensor("x_out", [D, T], F32, kind="ExternalOutput").ap()
    w_aps = dict(
        wq=nc.dram_tensor("wq", [16, 1024, 128], F32, kind="ExternalInput").ap(),
        wk=nc.dram_tensor("wk", [16, 1024, 128], F32, kind="ExternalInput").ap(),
        wvT=nc.dram_tensor("wvT", [1024, 1024], F32, kind="ExternalInput").ap(),
        wo=nc.dram_tensor("wo", [1024, 1024], F32, kind="ExternalInput").ap(),
        bqT=nc.dram_tensor("bqT", [128, 16], F32, kind="ExternalInput").ap(),
        bkT=nc.dram_tensor("bkT", [128, 16], F32, kind="ExternalInput").ap(),
        bvrow=nc.dram_tensor("bvrow", [1, 1024], F32, kind="ExternalInput").ap(),
        bo=nc.dram_tensor("bo", [128, 8], F32, kind="ExternalInput").ap(),
        gamma=nc.dram_tensor("gamma", [128, 8], F32, kind="ExternalInput").ap(),
        beta=nc.dram_tensor("beta", [128, 8], F32, kind="ExternalInput").ap(),
        lamb=nc.dram_tensor("lamb", [128, 1], F32, kind="ExternalInput").ap(),
        onesr=nc.dram_tensor("onesr", [1, 128], F32, kind="ExternalInput").ap(),
    )
    scratch = dict(
        cc_kin=nc.dram_tensor("cc_kin", [2048, 2048], BF16),
        cc_kout=nc.dram_tensor("cc_kout", [4096, 2048], BF16),
        cc_vin=nc.dram_tensor("cc_vin", [2048, 1024], BF16),
        cc_vout=nc.dram_tensor("cc_vout", [4096, 1024], BF16),
    )
    es = contextlib.ExitStack()
    with es:
        block = DirectBlock(nc)
        es.enter_context(nc.Block())
        emit_l1_attn(nc, es, block, x_in, x_out, w_aps, scratch)
    return nc


def host_l1a_inputs(weights):
    wv = np.asarray(weights["L1_attn_Wv"], np.float32)
    return dict(
        wq=np.ascontiguousarray(weights["L1_attn_Wq"], dtype=np.float32),
        wk=np.ascontiguousarray(weights["L1_attn_Wk"], dtype=np.float32),
        wvT=np.ascontiguousarray(wv.transpose(1, 0, 2).reshape(1024, 1024)),
        wo=np.ascontiguousarray(weights["L1_attn_Wo"], dtype=np.float32),
        bqT=np.ascontiguousarray(np.asarray(weights["L1_attn_bq"], np.float32).T),
        bkT=np.ascontiguousarray(np.asarray(weights["L1_attn_bk"], np.float32).T),
        bvrow=np.ascontiguousarray(np.asarray(weights["L1_attn_bv"], np.float32).reshape(1, 1024)),
        bo=pack_vec(weights["L1_attn_bo"]),
        gamma=pack_vec(weights["L1_attn_gamma"]),
        beta=pack_vec(weights["L1_attn_beta"]),
        lamb=np.ascontiguousarray(np.broadcast_to(
            np.asarray(weights["L1_attn_lam"], np.float32).reshape(1, 1), (128, 1))),
        onesr=np.ones((1, 128), np.float32),
    )


def run_l1a(x_full, weights):
    nc = build_l1a_launch()
    base = host_l1a_inputs(weights)
    in_maps = []
    for c in range(NCORES):
        b, hf = c // 2, c % 2
        xs = np.ascontiguousarray(x_full[b, hf * T:(hf + 1) * T, :])
        m = dict(base)
        m["x_in"] = fmT(xs)
        in_maps.append(m)
    res = run_bass_kernel_spmd(nc, in_maps, core_ids=list(range(NCORES)))
    out = np.zeros((B, S, D), np.float32)
    for c in range(NCORES):
        b, hf = c // 2, c % 2
        out[b, hf * T:(hf + 1) * T, :] = np.asarray(res.results[c]["x_out"], np.float32).T
    return out



def emit_l1_ffn(nc, es, block, x_in, x_out, w_aps, scratch, ses=None):
    ses = es if ses is None else ses
    """Routed FFN block.

    w_aps: lw1 [1024,512], lw2 [512,1024], hw1 [1024,4096], hw2 [4096,1024],
    rt [1024, 1] (routing token, fm), identb [128,128], ltri [128,128]
    (ltri[k,m]=1 if k<=m), halfsel [128,32] (1 on own-half columns);
    lb1 [128,4], lb2/lgam/hgam/hb2 [128,8], hb1 [128,32]  (all f32)
    scratch: s_in [128,16] f32 dram, s_out [256,16] f32 dram
    """
    NTT = T // 128
    TBL = 256
    NTB = T // TBL
    IH = 4096
    IHC = IH // 128
    LOGK = float(np.log(np.float32(1152.0)))

    whw1 = es.enter_context(nc.sbuf_tensor("l1f_whw1", [128, DC * IH], BF16))
    whw2 = es.enter_context(nc.sbuf_tensor("l1f_whw2", [128, IHC * D], BF16))
    wl1 = es.enter_context(nc.sbuf_tensor("l1f_wl1", [128, DC * 512], BF16))
    wl2 = es.enter_context(nc.sbuf_tensor("l1f_wl2", [128, 4 * D], BF16))
    identb = es.enter_context(nc.sbuf_tensor("l1f_identb", [128, 128], BF16))
    ltri = es.enter_context(nc.sbuf_tensor("l1f_ltri", [128, 128], BF16))
    halfsel = es.enter_context(nc.sbuf_tensor("l1f_halfsel", [128, 32], F32))
    rtsb = es.enter_context(nc.sbuf_tensor("l1f_rt", [128, 8], F32))
    vecs = {v: es.enter_context(nc.sbuf_tensor("l1f_" + v, [128, 32], F32))
            for v in ["lb1", "lb2", "lgam", "hgam", "hb1", "hb2"]}
    onesb = es.enter_context(nc.sbuf_tensor("l1f_onesb", [128, 1], BF16))
    onesrb = es.enter_context(nc.sbuf_tensor("l1f_onesrb", [1, 128], BF16))
    onesr = es.enter_context(nc.sbuf_tensor("l1f_onesr", [1, 128], F32))
    sfull = es.enter_context(nc.sbuf_tensor("l1f_sfull", [128, 32], F32))
    key = es.enter_context(nc.sbuf_tensor("l1f_key", [128, 32], F32))
    ebuf = es.enter_context(nc.sbuf_tensor("l1f_ebuf", [128, 32], BF16))
    rscl = es.enter_context(nc.sbuf_tensor("l1f_rscl", [1, 8], F32))
    mask = es.enter_context(nc.sbuf_tensor("l1f_mask", [128, 32], F32))
    mwork = es.enter_context(nc.sbuf_tensor("l1f_mwork", [128, 32], F32))
    mrow2 = es.enter_context(nc.sbuf_tensor("l1f_mrow2", [1, 64], F32))
    rtb = es.enter_context(nc.sbuf_tensor("l1f_rtb", [128, 8], BF16))
    mask_bf = es.enter_context(nc.sbuf_tensor("l1f_maskbf", [128, 16], BF16))
    sstage = es.enter_context(nc.sbuf_tensor("l1f_sstage", [128, 16], F32))
    xbt = es.enter_context(nc.sbuf_tensor("l1f_xbt", [128, 1024], BF16))
    xres = [es.enter_context(nc.sbuf_tensor(f"l1f_xres{i}", [128, DC * TBL], F32)) for i in range(2)]
    xg = es.enter_context(nc.sbuf_tensor("l1f_xg", [128, DC * TBL], F32))
    xlb = es.enter_context(nc.sbuf_tensor("l1f_xlb", [128, DC * TBL], BF16))
    xhb = es.enter_context(nc.sbuf_tensor("l1f_xhb", [128, DC * TBL], BF16))
    lhb = es.enter_context(nc.sbuf_tensor("l1f_lhb", [128, 4 * TBL], BF16))
    lacc = es.enter_context(nc.sbuf_tensor("l1f_lacc", [128, DC * TBL], F32))
    hv1 = es.enter_context(nc.sbuf_tensor("l1f_hv1", [128, IHC * TBL], BF16))
    yacc = lacc
    ytmp = xg
    ystat = ytmp[0:1, 0:4 * TBL]
    ybf = es.enter_context(nc.sbuf_tensor("l1f_ybf", [128, TBL], BF16))
    ysq = xlb[:, 0:TBL]
    ybf8 = xhb
    ysq8 = xlb

    pA = es.enter_context(nc.psum_tensor("l1f_pA", [128, 512], F32))
    pB = es.enter_context(nc.psum_tensor("l1f_pB", [128, 512], F32))
    pC = es.enter_context(nc.psum_tensor("l1f_pC", [128, 512], F32))
    pD = es.enter_context(nc.psum_tensor("l1f_pD", [128, 512], F32))
    pE = es.enter_context(nc.psum_tensor("l1f_pE", [128, 512], F32))
    pF = es.enter_context(nc.psum_tensor("l1f_pF", [128, 512], F32))
    pG = es.enter_context(nc.psum_tensor("l1f_pG", [1, 512], F32))
    pH = es.enter_context(nc.psum_tensor("l1f_pH", [1, 512], F32))

    sgw = Sem(ses, nc, "l1f_sgw")
    sxt = Sem(ses, nc, "l1f_sxt")
    sss = Sem(ses, nc, "l1f_sss")
    sxr = [Sem(ses, nc, f"l1f_sxr{i}") for i in range(2)]
    scc = Sem(ses, nc, "l1f_scc")
    sld = Sem(ses, nc, "l1f_sld")
    st = Sem(ses, nc, "l1f_st")
    sv = Sem(ses, nc, "l1f_sv")
    ss = Sem(ses, nc, "l1f_ss")
    so = Sem(ses, nc, "l1f_so")

    x_in_r = x_in.rearrange("(c p) t -> p c t", p=128)
    x_out_r = x_out.rearrange("(c p) t -> p c t", p=128)
    s_in, s_out = scratch["s_in"], scratch["s_out"]

    sgh = [Sem(ses, nc, f"l1f_sgh{i}") for i in range(2)]

    def g_loads(g):
        for c in range(DC):
            g.dma_start(out=wl1[:, c * 512:(c + 1) * 512],
                        in_=w_aps["lw1"][c * 128:(c + 1) * 128, :]).then_inc(sgw.inc(16), 16)
        for c in range(4):
            g.dma_start(out=wl2[:, c * D:(c + 1) * D],
                        in_=w_aps["lw2"][c * 128:(c + 1) * 128, :]).then_inc(sgw.inc(16), 16)
        g.dma_start(out=identb[:], in_=w_aps["identb"]).then_inc(sgw.inc(16), 16)
        g.dma_start(out=ltri[:], in_=w_aps["ltri"]).then_inc(sgw.inc(16), 16)
        g.dma_start(out=halfsel[:], in_=w_aps["halfsel"]).then_inc(sgw.inc(16), 16)
        g.dma_start(out=rtsb[:], in_=w_aps["rt"]).then_inc(sgw.inc(16), 16)
        for v, wd in [("lb1", 4), ("lb2", 8), ("lgam", 8), ("hgam", 8), ("hb1", 32), ("hb2", 8)]:
            g.dma_start(out=vecs[v][:, 0:wd], in_=w_aps[v]).then_inc(sgw.inc(16), 16)
    block.gpsimd(g_loads)
    loads_done = sgw.n

    def g_loadh(g):
        for c in range(DC):
            g.dma_start(out=whw1[:, c * IH:(c + 1) * IH],
                        in_=w_aps["hw1"][c * 128:(c + 1) * 128, :]).then_inc(sgh[0].inc(16), 16)
        for c in range(IHC):
            g.dma_start(out=whw2[:, c * D:(c + 1) * D],
                        in_=w_aps["hw2"][c * 128:(c + 1) * 128, :]).then_inc(sgh[1].inc(16), 16)
    block.gpsimd(g_loadh)
    h1_ld = sgh[0].n
    h2_ld = sgh[1].n

    def v_init(v):
        v = AS(v, sv)
        v._e.wait_ge(sgw.h, loads_done)
        v.memset(onesb[:], 1.0)
        v.memset(onesrb[:], 1.0)
        v.memset(onesr[:], 1.0)
        v.tensor_copy(rtb[:], rtsb[:])
    block.vector(v_init)
    init_done = sv.n

    xs32 = xres[0][:, 0:1024]
    for tt in range(NTT):
        def g_xt(g, tt=tt, gate=st.n, ld=loads_done):
            g.wait_ge(sgw.h, ld)
            if gate:
                g.wait_ge(st.h, gate)
            for c in range(DC):
                g.dma_start(out=xs32[:, c * 128:(c + 1) * 128],
                            in_=x_in_r[:, c, tt * 128:(tt + 1) * 128]).then_inc(sxt.inc(16), 16)
        block.gpsimd(g_xt)
        xt_ld = sxt.n

        def t_s(t, tt=tt, xt_ld=xt_ld, init_done=init_done, evac_gate=sv.n):
            t.wait_ge(sxt.h, xt_ld)
            t.wait_ge(sv.h, max(init_done, evac_gate))
            for dc in range(DC):
                t.matmul(pF[:, tt:tt + 1], xs32[:, dc * 128:(dc + 1) * 128],
                         rtsb[:, dc:dc + 1], start=(dc == 0), stop=(dc == DC - 1),
                         skip_group_check=True).then_inc(st.inc())
        block.tensor(t_s)
    s_mm = st.n

    def v_sevac(v, s_mm=s_mm):
        v = AS(v, sv)
        v._e.wait_ge(st.h, s_mm)
        v.tensor_copy(sstage[:], pF[:, 0:16])
    block.vector(v_sevac)
    s_done = sv.n

    def g_sag(g, s_done=s_done):
        g.wait_ge(sv.h, s_done)
        g.dma_start(out=s_in.ap(), in_=sstage[:]).then_inc(sss.inc(16), 16)
        g.wait_ge(sss.h, sss.n)
        g.collective_compute("AllGather", ALU.bypass,
                             replica_groups=[[0, 1], [2, 3], [4, 5], [6, 7]],
                             ins=[s_in.ap()], outs=[s_out.ap()]).then_inc(scc.inc())
    block.gpsimd(g_sag)
    s_ag = scc.n

    route_q = []

    def rq(fn):
        route_q.append(fn)

    def pump(n=3):
        for _ in range(min(n, len(route_q))):
            route_q.pop(0)()

    def _emit_routing():
        def g_sfl(g, s_ag=s_ag):
            g.wait_ge(scc.h, s_ag)
            g.dma_start(out=sfull[:, 0:16], in_=s_out.ap()[0:128, :]).then_inc(sld.inc(16), 16)
            g.dma_start(out=sfull[:, 16:32], in_=s_out.ap()[128:256, :]).then_inc(sld.inc(16), 16)
        rq(lambda: block.gpsimd(g_sfl))
        sfull_n = sld.n + 32

        def v_r0(v, n=sfull_n):
            v = AS(v, sv)
            v._e.wait_ge(sld.h, n)
            v.memset(rscl[:, 0:1], LOGK - float(np.log(np.float32(4096.0))))
        rq(lambda: block.vector(v_r0))

        for it in range(50):
            def step(it=it):
                def t_bc(t, vgate=sv.n):
                    t.wait_ge(sv.h, vgate)
                    t.matmul(pF[:, 20:21], onesr[:], rscl[0:1, 0:1], start=True, stop=True,
                             skip_group_check=True).then_inc(st.inc())
                block.tensor(t_bc)
                nbc = st.n

                def v_m(v, nbc=nbc):
                    v = AS(v, sv)
                    v._e.wait_ge(st.h, nbc)
                    v.tensor_scalar(key[:], sfull[:], pF[:, 20:21], 0.0, ALU.add, ALU.min)
                block.vector(v_m)
                nm = sv.n

                def s_e(s, nm=nm):
                    s = AS(s, ss)
                    s._e.wait_ge(sv.h, nm)
                    s.activation(ebuf[:], key[:], AF.Exp)
                block.scalar(s_e)
                ne = ss.n

                def t_sum(t, ne=ne):
                    t.wait_ge(ss.h, ne)
                    t.matmul(pF[0:1, 32:64], onesb[:], ebuf[:], start=True, stop=True,
                             skip_group_check=True).then_inc(st.inc())
                block.tensor(t_sum)
                nsum = st.n

                def v_red(v, nsum=nsum):
                    v = AS(v, sv)
                    v._e.wait_ge(st.h, nsum)
                    v.tensor_reduce(rscl[:, 4:5], pF[0:1, 32:64], AX.X, ALU.add)
                block.vector(v_red)
                nred = sv.n

                def s_ln(s, nred=nred):
                    s = AS(s, ss)
                    s._e.wait_ge(sv.h, nred)
                    s.activation(rscl[:, 5:6], rscl[:, 4:5], AF.Ln)
                block.scalar(s_ln)
                nln = ss.n

                def v_up(v, nln=nln):
                    v = AS(v, sv)
                    v._e.wait_ge(ss.h, nln)
                    v.tensor_scalar(rscl[:, 5:6], rscl[:, 5:6], -1.0, LOGK, ALU.mult, ALU.add)
                    v.tensor_tensor(rscl[:, 0:1], rscl[:, 0:1], rscl[:, 5:6], ALU.add)
                block.vector(v_up)
            rq(step)

        def fin_keys():
            def t_bcf(t, vgate=sv.n):
                t.wait_ge(sv.h, vgate)
                t.matmul(pF[:, 20:21], onesr[:], rscl[0:1, 0:1], start=True, stop=True,
                         skip_group_check=True).then_inc(st.inc())
            block.tensor(t_bcf)
            nbcf = st.n

            def v_kf(v, nbcf=nbcf):
                v = AS(v, sv)
                v._e.wait_ge(st.h, nbcf)
                v.tensor_scalar(key[:], sfull[:], pF[:, 20:21], 0.0, ALU.add, ALU.min)
                v.memset(rscl[:, 1:2], -120.0)
                v.memset(rscl[:, 2:3], 1.0)
            block.vector(v_kf)
        rq(fin_keys)

        for it in range(40):
            def bstep(it=it):
                def v_mid(v):
                    v = AS(v, sv)
                    v.tensor_tensor(rscl[:, 4:5], rscl[:, 1:2], rscl[:, 2:3], ALU.add)
                    v.tensor_scalar(rscl[:, 4:5], rscl[:, 4:5], 0.5, None, ALU.mult)
                block.vector(v_mid)
                nmid = sv.n

                def t_bcm(t, nmid=nmid):
                    t.wait_ge(sv.h, nmid)
                    t.matmul(pF[:, 21:22], onesr[:], rscl[0:1, 4:5], start=True, stop=True,
                             skip_group_check=True).then_inc(st.inc())
                block.tensor(t_bcm)
                nbcm = st.n

                def v_cmp(v, nbcm=nbcm):
                    v = AS(v, sv)
                    v._e.wait_ge(st.h, nbcm)
                    v.tensor_scalar(ebuf[:], key[:], pF[:, 21:22], None, ALU.is_gt)
                block.vector(v_cmp)
                ncmp = sv.n

                def t_cnt(t, ncmp=ncmp):
                    t.wait_ge(sv.h, ncmp)
                    t.matmul(pF[0:1, 32:64], onesb[:], ebuf[:], start=True, stop=True,
                             skip_group_check=True).then_inc(st.inc())
                block.tensor(t_cnt)
                ncnt = st.n

                def v_upd(v, ncnt=ncnt):
                    v = AS(v, sv)
                    v._e.wait_ge(st.h, ncnt)
                    v.tensor_reduce(rscl[:, 3:4], pF[0:1, 32:64], AX.X, ALU.add)
                    v.tensor_scalar(rscl[:, 5:6], rscl[:, 3:4], 1024.0, None, ALU.is_ge)
                    v.tensor_tensor(rscl[:, 6:7], rscl[:, 4:5], rscl[:, 1:2], ALU.subtract)
                    v.tensor_tensor(rscl[:, 6:7], rscl[:, 5:6], rscl[:, 6:7], ALU.mult)
                    v.tensor_tensor(rscl[:, 1:2], rscl[:, 1:2], rscl[:, 6:7], ALU.add)
                    v.tensor_tensor(rscl[:, 6:7], rscl[:, 4:5], rscl[:, 2:3], ALU.subtract)
                    v.tensor_tensor(rscl[:, 7:8], rscl[:, 5:6], rscl[:, 6:7], ALU.mult)
                    v.tensor_tensor(rscl[:, 6:7], rscl[:, 6:7], rscl[:, 7:8], ALU.subtract)
                    v.tensor_tensor(rscl[:, 2:3], rscl[:, 2:3], rscl[:, 6:7], ALU.add)
                block.vector(v_upd)
            rq(bstep)

        def maskp():
            def t_bch(t, vgate=sv.n):
                t.wait_ge(sv.h, vgate)
                t.matmul(pF[:, 20:21], onesr[:], rscl[0:1, 2:3], start=True, stop=True,
                         skip_group_check=True).then_inc(st.inc())
                t.matmul(pF[:, 21:22], onesr[:], rscl[0:1, 1:2], start=True, stop=True,
                         skip_group_check=True).then_inc(st.inc())
            block.tensor(t_bch)
            nbch = st.n

            def v_tc(v, nbch=nbch):
                v = AS(v, sv)
                v._e.wait_ge(st.h, nbch)
                v.tensor_scalar(mask[:], key[:], pF[:, 20:21], None, ALU.is_gt)
                v.tensor_scalar(mwork[:], key[:], pF[:, 21:22], None, ALU.is_gt)
                v.tensor_tensor(mwork[:], mwork[:], mask[:], ALU.subtract)
                v.tensor_copy(ebuf[:], mwork[:])
            block.vector(v_tc)
            ntc = sv.n

            def t_pref(t, ntc=ntc):
                t.wait_ge(sv.h, ntc)
                t.matmul(pF[:, 64:96], ltri[:], ebuf[:], start=True, stop=True,
                         skip_group_check=True).then_inc(st.inc())
                t.matmul(pF[0:1, 32:64], onesb[:], ebuf[:], start=True, stop=True,
                         skip_group_check=True).then_inc(st.inc())
            block.tensor(t_pref)
            npref = st.n

            def v_chi(v, npref=npref):
                v = AS(v, sv)
                v._e.wait_ge(st.h, npref)
                v.tensor_copy(ebuf[:], mask[:])
            block.vector(v_chi)
            nchi = sv.n

            def t_chi(t, nchi=nchi):
                t.wait_ge(sv.h, nchi)
                t.matmul(pF[0:1, 128:160], onesb[:], ebuf[:], start=True, stop=True,
                         skip_group_check=True).then_inc(st.inc())
            block.tensor(t_chi)
            nchm = st.n

            def v_rank(v, nchm=nchm):
                v = AS(v, sv)
                v._e.wait_ge(st.h, nchm)
                v.tensor_reduce(rscl[:, 3:4], pF[0:1, 128:160], AX.X, ALU.add)
                v.tensor_copy(mrow2[0:1, 32:64], pF[0:1, 32:64])
                v.tensor_tensor_scan(mrow2[0:1, 0:32], mrow2[0:1, 32:64], mrow2[0:1, 32:64], 0.0,
                                     ALU.add, ALU.bypass)
                v.tensor_tensor(mrow2[0:1, 32:64], mrow2[0:1, 0:32], mrow2[0:1, 32:64], ALU.subtract)
                v.tensor_scalar(mrow2[0:1, 32:64], mrow2[0:1, 32:64], rscl[:, 3:4], None, ALU.add)
            block.vector(v_rank)
            nrank = sv.n

            def t_bcr(t, nrank=nrank):
                t.wait_ge(sv.h, nrank)
                t.matmul(pF[:, 96:128], onesr[:], mrow2[0:1, 32:64], start=True, stop=True,
                         skip_group_check=True).then_inc(st.inc())
            block.tensor(t_bcr)
            nbcr = st.n

            def v_mask(v, nbcr=nbcr):
                v = AS(v, sv)
                v._e.wait_ge(st.h, nbcr)
                v.tensor_copy(key[:], pF[:, 64:96])
                v.tensor_tensor(key[:], key[:], pF[:, 96:128], ALU.add)
                v.tensor_scalar(key[:], key[:], 1024.5, None, ALU.is_lt)
                v.tensor_tensor(mwork[:], mwork[:], key[:], ALU.mult)
                v.tensor_tensor(mask[:], mask[:], mwork[:], ALU.add)
                v.tensor_tensor(mask[:], mask[:], halfsel[:], ALU.mult)
                v.tensor_tensor(mask[:, 0:16], mask[:, 0:16], mask[:, 16:32], ALU.add)
                v.tensor_copy(mask_bf[:], mask[:, 0:16])
            block.vector(v_mask)
        rq(maskp)

        for tt in range(NTT):
            def mrow_step(tt=tt):
                def t_mt(t, tt=tt, evg=sv.n):
                    t.wait_ge(sv.h, evg)
                    t.matmul(pF[0:1, 160:288], mask_bf[:, tt:tt + 1], identb[:], start=True, stop=True,
                             skip_group_check=True).then_inc(st.inc())
                block.tensor(t_mt)
                nmt = st.n

                def v_mt(v, tt=tt, nmt=nmt, stg=smr.n):
                    v = AS(v, sv)
                    v._e.wait_ge(st.h, nmt)
                    if stg:
                        v._e.wait_ge(smr.h, stg)
                    v.tensor_copy(mrowst[:], pF[0:1, 160:288])
                block.vector(v_mt)
                nvm = sv.n

                def g_mt(g, tt=tt, nvm=nvm):
                    g.wait_ge(sv.h, nvm)
                    g.dma_start(out=mrow_hbm.ap()[0:1, tt * 128:(tt + 1) * 128],
                                in_=mrowst[:]).then_inc(smr.inc(16), 16)
                block.gpsimd(g_mt)
            rq(mrow_step)

    mrowst = xbt[0:1, 0:128]
    mrow_tb = xbt[0:1, 128:384]
    smr = Sem(ses, nc, "l1f_smr")
    sml = Sem(ses, nc, "l1f_sml")
    mrow_hbm = scratch["mrow_hbm"]
    _emit_routing()
    mrow_done_total = 16 * 16

    def g_xres(tb, sl, gate):
        def f(g_, tb=tb, sl=sl, gate=gate):
            g_.wait_ge(gate[0], gate[1])
            for c in range(DC):
                g_.dma_start(out=xres[sl][:, c * TBL:(c + 1) * TBL],
                             in_=x_in_r[:, c, tb * TBL:(tb + 1) * TBL]).then_inc(sxr[sl].inc(16), 16)
        block.gpsimd(f)
        return (sxr[sl].h, sxr[sl].n)

    xr_ld = [None] * NTB
    xr_ld[0] = g_xres(0, 0, (st.h, s_mm))
    xr_ld[1] = g_xres(1, 1, (st.h, s_mm))

    nst2_prev = 0
    for tb in range(NTB):
        sl = tb % 2
        def s_sqall(s, ld=xr_ld[tb], pv=nst2_prev):
            s = AS(s, ss)
            s._e.wait_ge(ld[0], ld[1])
            if pv:
                s._e.wait_ge(st.h, pv)
            for dc in range(DC):
                s.activation(ysq8[:, dc * TBL:(dc + 1) * TBL],
                             xres[sl][:, dc * TBL:(dc + 1) * TBL], AF.Square)
        block.scalar(s_sqall)
        nsqall = ss.n
        if tb < 2:
            pump(4)

        def t_sqall(t, nsqall=nsqall, vgate=sv.n):
            t.wait_ge(ss.h, nsqall)
            if vgate:
                t.wait_ge(sv.h, vgate)
            for dc in range(DC):
                t.matmul(pG[0:1, 256:256 + TBL], onesb[:], ysq8[:, dc * TBL:(dc + 1) * TBL],
                         start=(dc == 0), stop=(dc == DC - 1), skip_group_check=True).then_inc(st.inc())
        block.tensor(t_sqall)
        nssq = st.n
        if tb < 2:
            pump(4)

        def s_norm(s, nssq=nssq):
            s = AS(s, ss)
            s._e.wait_ge(st.h, nssq)
            s.activation(ystat[0:1, 0:TBL], pG[0:1, 256:256 + TBL], AF.Sqrt)
        block.scalar(s_norm)
        nnorm = ss.n

        def v_rfac(v, nnorm=nnorm):
            v = AS(v, sv)
            v._e.wait_ge(ss.h, nnorm)
            v.tensor_scalar(ystat[0:1, 0:TBL], ystat[0:1, 0:TBL], 1e-12, None, ALU.max)
            v.reciprocal(ystat[0:1, TBL:2 * TBL], ystat[0:1, 0:TBL])
            v.tensor_scalar(ystat[0:1, TBL:2 * TBL], ystat[0:1, TBL:2 * TBL], 32.0, None, ALU.mult)
        block.vector(v_rfac)
        nrfac = sv.n

        def t_bcn(t, nrfac=nrfac):
            t.wait_ge(sv.h, nrfac)
            t.matmul(pE[:, 0:TBL], onesr[:], ystat[0:1, TBL:2 * TBL], start=True, stop=True,
                     skip_group_check=True).then_inc(st.inc())
        block.tensor(t_bcn)
        nbcn = st.n

        def v_xg(v, tb=tb, sl=sl, nbcn=nbcn):
            v = AS(v, sv)
            v._e.wait_ge(st.h, nbcn)
            for dc in range(DC):
                sd = slice(dc * TBL, (dc + 1) * TBL)
                v.tensor_tensor(xg[:, sd], xres[sl][:, sd], pE[:, 0:TBL], ALU.mult)
            for dc in range(DC):
                sd = slice(dc * TBL, (dc + 1) * TBL)
                v.tensor_scalar(xlb[:, sd], xg[:, sd], vecs["lgam"][:, dc:dc + 1], None, ALU.mult)
            for dc in range(DC):
                sd = slice(dc * TBL, (dc + 1) * TBL)
                v.tensor_scalar(xhb[:, sd], xg[:, sd], vecs["hgam"][:, dc:dc + 1], None, ALU.mult)
        block.vector(v_xg)
        nxg = sv.n

        if tb < 2:
            pump(4)
        for oc in range(4):
            pump(2) if tb < 2 else None
            def t_l1(t, oc=oc, nxg=nxg, bg=ss.n):
                if oc == 0:
                    t.wait_ge(sv.h, nxg)
                if bg:
                    t.wait_ge(ss.h, bg)
                for dc in range(DC):
                    t.matmul(pA[:, 0:TBL], wl1[:, dc * 512 + oc * 128: dc * 512 + (oc + 1) * 128],
                             xlb[:, dc * TBL:(dc + 1) * TBL],
                             start=(dc == 0), stop=(dc == DC - 1)).then_inc(st.inc())
            block.tensor(t_l1)
            nl1 = st.n

            def s_l1(s, oc=oc, nl1=nl1):
                s = AS(s, ss)
                s._e.wait_ge(st.h, nl1)
                s.activation(lhb[:, oc * TBL:(oc + 1) * TBL], pA[:, 0:TBL], AF.Gelu,
                             bias=vecs["lb1"][:, oc:oc + 1])
            block.scalar(s_l1)
        nlh = ss.n

        for dc in range(DC):
            def t_l2(t, dc=dc, nlh=nlh, bg=sv.n, ssg=ss.n):
                if dc == 0:
                    t.wait_ge(ss.h, nlh)
                    t.wait_ge(sv.h, bg)
                elif ssg:
                    t.wait_ge(ss.h, ssg)
                for ic in range(4):
                    t.matmul(pB[:, 0:TBL], wl2[:, ic * D + dc * 128: ic * D + (dc + 1) * 128],
                             lhb[:, ic * TBL:(ic + 1) * TBL],
                             start=(ic == 0), stop=(ic == 3)).then_inc(st.inc())
            block.tensor(t_l2)
            nl2 = st.n

            def s_l2(s, dc=dc, nl2=nl2, sg=so.n):
                s = AS(s, ss)
                s._e.wait_ge(st.h, nl2)
                if dc == 0 and sg:
                    s._e.wait_ge(so.h, sg)
                s.activation(lacc[:, dc * TBL:(dc + 1) * TBL], pB[:, 0:TBL], AF.Identity,
                             bias=vecs["lb2"][:, dc:dc + 1])
            block.scalar(s_l2)
        nlacc = ss.n

        for ic in range(IHC):
            bk = pC if ic % 2 == 0 else pD
            pump(3) if tb < 2 else None

            def t_h1(t, ic=ic, tb=tb, bk=bk, nxg=nxg, bg=ss.n):
                if ic == 0:
                    t.wait_ge(sv.h, nxg)
                if ic == 0 and tb == 0:
                    t.wait_ge(sgh[0].h, h1_ld)
                if bg:
                    t.wait_ge(ss.h, bg)
                for dc in range(DC):
                    t.matmul(bk[:, 0:TBL], whw1[:, dc * IH + ic * 128: dc * IH + (ic + 1) * 128],
                             xhb[:, dc * TBL:(dc + 1) * TBL],
                             start=(dc == 0), stop=(dc == DC - 1)).then_inc(st.inc())
            block.tensor(t_h1)
            nh1 = st.n

            def s_h1(s, ic=ic, bk=bk, nh1=nh1):
                s = AS(s, ss)
                s._e.wait_ge(st.h, nh1)
                s.activation(hv1[:, ic * TBL:(ic + 1) * TBL], bk[:, 0:TBL], AF.Gelu,
                             bias=vecs["hb1"][:, ic:ic + 1])
            block.scalar(s_h1)
        nhv1 = ss.n

        pump(len(route_q))
        def g_ml(g, tb=tb, mrow_done=mrow_done_total, consumed=st.n):
            g.wait_ge(smr.h, mrow_done)
            if consumed:
                g.wait_ge(st.h, consumed)
            g.dma_start(out=mrow_tb[:], in_=mrow_hbm.ap()[0:1, tb * 256:(tb + 1) * 256]).then_inc(sml.inc(16), 16)
        block.gpsimd(g_ml)
        nml = sml.n

        def t_mbc(t, tb=tb, nml=nml, vg=sv.n):
            t.wait_ge(sml.h, nml)
            t.wait_ge(sv.h, vg)
            t.matmul(pE[:, 256:256 + 128], onesrb[:], mrow_tb[0:1, 0:128],
                     start=True, stop=True, skip_group_check=True).then_inc(st.inc())
            t.matmul(pE[:, 384:384 + 128], onesrb[:], mrow_tb[0:1, 128:256],
                     start=True, stop=True, skip_group_check=True).then_inc(st.inc())
        block.tensor(t_mbc)
        nmbc = st.n

        for dc in range(DC):
            bk = pC if dc % 2 == 0 else pD

            def t_h2(t, dc=dc, tb=tb, bk=bk, nhv1=nhv1, bg=sv.n):
                if dc == 0:
                    t.wait_ge(ss.h, nhv1)
                if dc == 0 and tb == 0:
                    t.wait_ge(sgh[1].h, h2_ld)
                t.wait_ge(sv.h, bg)
                for ic in range(IHC):
                    t.matmul(bk[:, 256:256 + TBL], whw2[:, ic * D + dc * 128: ic * D + (dc + 1) * 128],
                             hv1[:, ic * TBL:(ic + 1) * TBL],
                             start=(ic == 0), stop=(ic == IHC - 1)).then_inc(st.inc())
            block.tensor(t_h2)
            nh2 = st.n

            def v_comb(v, dc=dc, bk=bk, nh2=nh2, nmbc=nmbc, nlacc=nlacc, tb=tb, sl=sl):
                v = AS(v, sv)
                v._e.wait_ge(st.h, max(nh2, nmbc))
                v._e.wait_ge(ss.h, nlacc)
                sd = slice(dc * TBL, (dc + 1) * TBL)
                v.tensor_scalar(ytmp[:, sd], bk[:, 256:256 + TBL], vecs["hb2"][:, dc:dc + 1],
                                None, ALU.add)
                v.tensor_tensor(ytmp[:, sd], ytmp[:, sd], pE[:, 256:512], ALU.mult)
                v.tensor_tensor(yacc[:, sd], ytmp[:, sd], lacc[:, sd], ALU.add)
                v.tensor_tensor(yacc[:, sd], yacc[:, sd], xres[sl][:, sd], ALU.add)
            block.vector(v_comb)
        ncomb = sv.n

        def v_c2all(v, ncomb=ncomb, rms_st=nssq):
            v = AS(v, sv)
            v._e.wait_ge(st.h, rms_st)
            for dc in range(DC):
                v.tensor_copy(ybf8[:, dc * TBL:(dc + 1) * TBL], yacc[:, dc * TBL:(dc + 1) * TBL])
        block.vector(v_c2all)
        nca = sv.n

        def s_c2all(s, ncomb=ncomb, rms_st=nssq):
            s = AS(s, ss)
            s._e.wait_ge(sv.h, ncomb)
            s._e.wait_ge(st.h, rms_st)
            for dc in range(DC):
                s.activation(ysq8[:, dc * TBL:(dc + 1) * TBL],
                             yacc[:, dc * TBL:(dc + 1) * TBL], AF.Square)
        block.scalar(s_c2all)
        nsq = ss.n

        def t_s2all(t, nca=nca, nsq=nsq):
            t.wait_ge(sv.h, nca)
            t.wait_ge(ss.h, nsq)
            for dc in range(DC):
                t.matmul(pG[0:1, 0:TBL], onesb[:], ybf8[:, dc * TBL:(dc + 1) * TBL],
                         start=(dc == 0), stop=(dc == DC - 1), skip_group_check=True).then_inc(st.inc())
            for dc in range(DC):
                t.matmul(pH[0:1, 0:TBL], onesb[:], ysq8[:, dc * TBL:(dc + 1) * TBL],
                         start=(dc == 0), stop=(dc == DC - 1), skip_group_check=True).then_inc(st.inc())
        block.tensor(t_s2all)
        nst2 = st.n
        nst2_prev = nst2

        def v_mu(v, nst2=nst2):
            v = AS(v, sv)
            v._e.wait_ge(st.h, nst2)
            v.tensor_scalar(ystat[0:1, 0:TBL], pG[0:1, 0:TBL], 1.0 / 1024.0, None, ALU.mult)
            v.tensor_scalar(ystat[0:1, TBL:2 * TBL], pH[0:1, 0:TBL], 1.0 / 1024.0, None, ALU.mult)
            v.tensor_tensor(ystat[0:1, 2 * TBL:3 * TBL], ystat[0:1, 0:TBL], ystat[0:1, 0:TBL], ALU.mult)
            v.tensor_tensor(ystat[0:1, 2 * TBL:3 * TBL], ystat[0:1, TBL:2 * TBL],
                            ystat[0:1, 2 * TBL:3 * TBL], ALU.subtract)
            v.tensor_scalar(ystat[0:1, 2 * TBL:3 * TBL], ystat[0:1, 2 * TBL:3 * TBL],
                            LN_EPS, None, ALU.add)
        block.vector(v_mu)
        nvar = sv.n

        def s_sd(s, nvar=nvar):
            s = AS(s, ss)
            s._e.wait_ge(sv.h, nvar)
            s.activation(ystat[0:1, 3 * TBL:4 * TBL], ystat[0:1, 2 * TBL:3 * TBL], AF.Sqrt)
        block.scalar(s_sd)
        nstd = ss.n

        def v_rst(v, nstd=nstd):
            v = AS(v, sv)
            v._e.wait_ge(ss.h, nstd)
            v.reciprocal(ystat[0:1, TBL:2 * TBL], ystat[0:1, 3 * TBL:4 * TBL])
        block.vector(v_rst)
        nrst = sv.n

        def t_bcl(t, nrst=nrst):
            t.wait_ge(sv.h, nrst)
            t.matmul(pE[:, 0:TBL], onesr[:], ystat[0:1, 0:TBL], start=True, stop=True,
                     skip_group_check=True).then_inc(st.inc())
            t.matmul(pF[:, 0:TBL], onesr[:], ystat[0:1, TBL:2 * TBL], start=True, stop=True,
                     skip_group_check=True).then_inc(st.inc())
        block.tensor(t_bcl)
        nbcl = st.n

        def v_ap(v, nbcl=nbcl, sg=so.n):
            v = AS(v, sv)
            v._e.wait_ge(st.h, nbcl)
            if sg:
                v._e.wait_ge(so.h, sg)
            for dc in range(DC):
                sd = slice(dc * TBL, (dc + 1) * TBL)
                v.tensor_tensor(ytmp[:, sd], yacc[:, sd], pE[:, 0:TBL], ALU.subtract)
            for dc in range(DC):
                sd = slice(dc * TBL, (dc + 1) * TBL)
                v.tensor_tensor(yacc[:, sd], ytmp[:, sd], pF[:, 0:TBL], ALU.mult)
        block.vector(v_ap)
        nap = sv.n

        def s_st(s, tb=tb, nap=nap):
            s.wait_ge(sv.h, nap)
            for c in range(DC):
                s.dma_start(out=x_out_r[:, c, tb * TBL:(tb + 1) * TBL],
                            in_=yacc[:, c * TBL:(c + 1) * TBL]).then_inc(so.inc(16), 16)
        block.sync(s_st)

        if tb + 2 < NTB:
            xr_ld[tb + 2] = g_xres(tb + 2, sl, (sv.h, sv.n))

    def g_fin(g):
        g.wait_ge(so.h, so.n)
    block.gpsimd(g_fin)


def build_l1f_launch():
    nc = bass.Bass("TRN2", target_bir_lowering=False)
    x_in = nc.dram_tensor("x_in", [D, T], F32, kind="ExternalInput").ap()
    x_out = nc.dram_tensor("x_out", [D, T], F32, kind="ExternalOutput").ap()
    w_aps = dict(
        lw1=nc.dram_tensor("lw1", [1024, 512], F32, kind="ExternalInput").ap(),
        lw2=nc.dram_tensor("lw2", [512, 1024], F32, kind="ExternalInput").ap(),
        hw1=nc.dram_tensor("hw1", [1024, 4096], F32, kind="ExternalInput").ap(),
        hw2=nc.dram_tensor("hw2", [4096, 1024], F32, kind="ExternalInput").ap(),
        rt=nc.dram_tensor("rt", [128, 8], F32, kind="ExternalInput").ap(),
        identb=nc.dram_tensor("identb", [128, 128], F32, kind="ExternalInput").ap(),
        ltri=nc.dram_tensor("ltri", [128, 128], F32, kind="ExternalInput").ap(),
        halfsel=nc.dram_tensor("halfsel", [128, 32], F32, kind="ExternalInput").ap(),
        lb1=nc.dram_tensor("lb1", [128, 4], F32, kind="ExternalInput").ap(),
        lb2=nc.dram_tensor("lb2", [128, 8], F32, kind="ExternalInput").ap(),
        lgam=nc.dram_tensor("lgam", [128, 8], F32, kind="ExternalInput").ap(),
        hgam=nc.dram_tensor("hgam", [128, 8], F32, kind="ExternalInput").ap(),
        hb1=nc.dram_tensor("hb1", [128, 32], F32, kind="ExternalInput").ap(),
        hb2=nc.dram_tensor("hb2", [128, 8], F32, kind="ExternalInput").ap(),
    )
    scratch = dict(
        s_in=nc.dram_tensor("s_in", [128, 16], F32),
        s_out=nc.dram_tensor("s_out", [256, 16], F32),
        mrow_hbm=nc.dram_tensor("mrow_hbm", [1, T], BF16),
    )
    es = contextlib.ExitStack()
    with es:
        block = DirectBlock(nc)
        es.enter_context(nc.Block())
        emit_l1_ffn(nc, es, block, x_in, x_out, w_aps, scratch)
    return nc


def host_l1f_inputs(weights, core_id):
    hf = core_id % 2
    halfsel = np.zeros((128, 32), np.float32)
    halfsel[:, hf * 16:(hf + 1) * 16] = 1.0
    k = np.arange(128)
    ltri = (k[:, None] <= k[None, :]).astype(np.float32)
    return dict(
        lw1=np.ascontiguousarray(weights["L1_ffn_light_w1"], dtype=np.float32),
        lw2=np.ascontiguousarray(weights["L1_ffn_light_w2"], dtype=np.float32),
        hw1=np.ascontiguousarray(weights["L1_ffn_heavy_w1"], dtype=np.float32),
        hw2=np.ascontiguousarray(weights["L1_ffn_heavy_w2"], dtype=np.float32),
        rt=pack_vec(weights["L1_ffn_routing_token"]),
        identb=np.eye(128, dtype=np.float32),
        ltri=ltri,
        halfsel=halfsel,
        lb1=pack_vec(weights["L1_ffn_light_b1"]),
        lb2=pack_vec(weights["L1_ffn_light_b2"]),
        lgam=pack_vec(weights["L1_ffn_light_gamma"]),
        hgam=pack_vec(weights["L1_ffn_heavy_gamma"]),
        hb1=pack_vec(weights["L1_ffn_heavy_b1"]),
        hb2=pack_vec(weights["L1_ffn_heavy_b2"]),
    )


def run_l1f(x_full, weights):
    nc = build_l1f_launch()
    in_maps = []
    for c in range(NCORES):
        b, hf = c // 2, c % 2
        xs = np.ascontiguousarray(x_full[b, hf * T:(hf + 1) * T, :])
        m = host_l1f_inputs(weights, c)
        m["x_in"] = fmT(xs)
        in_maps.append(m)
    res = run_bass_kernel_spmd(nc, in_maps, core_ids=list(range(NCORES)))
    out = np.zeros((B, S, D), np.float32)
    for c in range(NCORES):
        b, hf = c // 2, c % 2
        out[b, hf * T:(hf + 1) * T, :] = np.asarray(res.results[c]["x_out"], np.float32).T
    return out


def host_l0a_inputs(weights, core_id):
    sel = np.zeros((128, 8), np.float32) if core_id % 2 == 0 else np.ones((128, 8), np.float32)
    return dict(
        wp1t=fmT(weights["L0_attn_Wp1"]),
        wp2t=fmT(weights["L0_attn_Wp2"]),
        convt=fmT(weights["L0_attn_conv_w"]),
        wat=fmT(weights["L0_attn_Wa"]),
        wxt=fmT(weights["L0_attn_Wx"]),
        wp3t=fmT(weights["L0_attn_Wp3"]),
        bp1=pack_vec(weights["L0_attn_bp1"]),
        bp2=pack_vec(weights["L0_attn_bp2"]),
        convb=pack_vec(weights["L0_attn_conv_b"]),
        ba=pack_vec(weights["L0_attn_ba"]),
        bx=pack_vec(weights["L0_attn_bx"]),
        lam=pack_vec(weights["L0_attn_Lam"]),
        bp3=pack_vec(weights["L0_attn_bp3"]),
        sel=sel,
    )


def build_l0a_launch():
    nc = bass.Bass("TRN2", target_bir_lowering=False)
    x_in = nc.dram_tensor("x_in", [D, T], F32, kind="ExternalInput").ap()
    x_out = nc.dram_tensor("x_out", [D, T], F32, kind="ExternalOutput").ap()
    w_aps = {w: nc.dram_tensor(w, [D, D], F32, kind="ExternalInput").ap()
             for w in ["wp1t", "wp2t", "convt", "wat", "wxt", "wp3t"]}
    vec_aps = {v: nc.dram_tensor(v, [128, 8], F32, kind="ExternalInput").ap()
               for v in ["bp1", "bp2", "convb", "ba", "bx", "lam", "bp3", "sel"]}
    scratch = dict(
        prod_hbm=nc.dram_tensor("prod_hbm", [128, DC * T], BF16).ap(),
        corr_hbm=nc.dram_tensor("corr_hbm", [128, DC * T], BF16).ap(),
        cc_in=nc.dram_tensor("cc_in", [128, 8], F32),
        cc_out=nc.dram_tensor("cc_out", [256, 8], F32),
    )
    es = contextlib.ExitStack()
    with es:
        block = DirectBlock(nc)
        es.enter_context(nc.Block())
        emit_l0_attn(nc, es, block, x_in, x_out, w_aps, vec_aps, scratch)
    return nc


def run_l0a(x_full, weights):
    nc = build_l0a_launch()
    in_maps = []
    for c in range(NCORES):
        b, hf = c // 2, c % 2
        xs = np.ascontiguousarray(x_full[b, hf * T:(hf + 1) * T, :])
        m = host_l0a_inputs(weights, c)
        m["x_in"] = fmT(xs)
        in_maps.append(m)
    res = run_bass_kernel_spmd(nc, in_maps, core_ids=list(range(NCORES)))
    out = np.zeros((B, S, D), np.float32)
    for c in range(NCORES):
        b, hf = c // 2, c % 2
        out[b, hf * T:(hf + 1) * T, :] = np.asarray(res.results[c]["x_out"], np.float32).T
    return out






L0A_W = ["wp1t", "wp2t", "convt", "wat", "wxt", "wp3t"]
L0A_V = ["bp1", "bp2", "convb", "ba", "bx", "lam", "bp3", "sel"]
L0F_W = {"wsl": [896, 2816], "ident": [128, 128]}
L1A_W = {"wq": [16, 1024, 128], "wk": [16, 1024, 128], "wvT": [1024, 1024], "wo": [1024, 1024],
         "bqT": [128, 16], "bkT": [128, 16], "bvrow": [1, 1024], "bo": [128, 8],
         "gamma": [128, 8], "beta": [128, 8], "lamb": [128, 1], "onesr": [1, 128]}
L1F_W = {"lw1": [1024, 512], "lw2": [512, 1024], "hw1": [1024, 4096], "hw2": [4096, 1024],
         "rt": [128, 8], "identb": [128, 128], "ltri": [128, 128], "halfsel": [128, 32],
         "lb1": [128, 4], "lb2": [128, 8], "lgam": [128, 8], "hgam": [128, 8],
         "hb1": [128, 32], "hb2": [128, 8]}


def build_fused():
    nc = bass.Bass("TRN2", target_bir_lowering=False)
    x0 = nc.dram_tensor("x0", [D, T], F32, kind="ExternalInput").ap()
    xm = [nc.dram_tensor(f"xmid{i}", [D, T], F32) for i in range(3)]
    x4 = nc.dram_tensor("x_out", [D, T], F32, kind="ExternalOutput").ap()

    w0a = {w: nc.dram_tensor("i0a_" + w, [D, D], F32, kind="ExternalInput").ap() for w in L0A_W}
    v0a = {v: nc.dram_tensor("i0a_" + v, [128, 8], F32, kind="ExternalInput").ap() for v in L0A_V}
    scr0a = dict(
        prod_hbm=nc.dram_tensor("s0a_prod", [128, DC * T], BF16).ap(),
        corr_hbm=nc.dram_tensor("s0a_corr", [128, DC * T], BF16).ap(),
        cc_in=nc.dram_tensor("s0a_ccin", [128, 8], F32),
        cc_out=nc.dram_tensor("s0a_ccout", [256, 8], F32),
    )
    w0f = {w: nc.dram_tensor("i0f_" + w, sh, F32, kind="ExternalInput").ap() for w, sh in L0F_W.items()}
    scr0f = dict(
        qag_in=nc.dram_tensor("s0f_qin", [896, 2816], BF16),
        qag_out=nc.dram_tensor("s0f_qout", [7168, 2816], BF16, addr_space="Shared"),
    )
    w1a = {w: nc.dram_tensor("i1a_" + w, sh, F32, kind="ExternalInput").ap() for w, sh in L1A_W.items()}
    scr1a = dict(
        cc_kin=nc.dram_tensor("s1a_kin", [2048, 2048], BF16),
        cc_kout=nc.dram_tensor("s1a_kout", [4096, 2048], BF16),
        cc_vin=nc.dram_tensor("s1a_vin", [2048, 1024], BF16),
        cc_vout=nc.dram_tensor("s1a_vout", [4096, 1024], BF16),
    )
    w1f = {w: nc.dram_tensor("i1f_" + w, sh, F32, kind="ExternalInput").ap() for w, sh in L1F_W.items()}
    scr1f = dict(
        s_in=nc.dram_tensor("s1f_sin", [128, 16], F32),
        s_out=nc.dram_tensor("s1f_sout", [256, 16], F32),
        mrow_hbm=nc.dram_tensor("s1f_mrow", [1, T], BF16),
    )

    es_outer = contextlib.ExitStack()
    with es_outer:
        block = DirectBlock(nc)
        es_outer.enter_context(nc.Block())
        es_w = contextlib.ExitStack()
        pwst = emit_l0f_phaseW(nc, es_w, block, w0f, scr0f, ses=es_outer)

        def l0a_hook(tag, pwst=pwst):
            if tag == "mid":
                pwst["pump"](1)
            elif tag == 7:
                pwst["finish"]()
        with contextlib.ExitStack() as es_b:
            emit_l0_attn(nc, es_b, block, x0, xm[0].ap(), w0a, v0a, scr0a, ses=es_outer,
                         hook=l0a_hook)
        es_w.close()
        pw = dict(scc=pwst["scc"], agc=pwst["state"]["agc"])
        with contextlib.ExitStack() as es_b:
            emit_l0_ffn(nc, es_b, block, xm[0].ap(), xm[1].ap(), w0f, scr0f, ses=es_outer, pw=pw)
        with contextlib.ExitStack() as es_b:
            emit_l1_attn(nc, es_b, block, xm[1].ap(), xm[2].ap(), w1a, scr1a, ses=es_outer)
        with contextlib.ExitStack() as es_b:
            emit_l1_ffn(nc, es_b, block, xm[2].ap(), x4, w1f, scr1f, ses=es_outer)
    return nc


def host_fused_inputs(weights, core_id):
    m = {}
    for k, v in host_l0a_inputs(weights, core_id).items():
        m["i0a_" + k] = v
    m["i0f_wsl"] = host_l0f_wsl(weights, core_id)
    m["i0f_ident"] = np.eye(128, dtype=np.float32)
    for k, v in host_l1a_inputs(weights).items():
        m["i1a_" + k] = v
    for k, v in host_l1f_inputs(weights, core_id).items():
        m["i1f_" + k] = v
    return m


def kernel(**inputs):
    x = np.asarray(inputs["L0_attn_x"], np.float32)
    nc = build_fused()
    in_maps = []
    for c in range(NCORES):
        b, hf = c // 2, c % 2
        m = host_fused_inputs(inputs, c)
        m["x0"] = fmT(np.ascontiguousarray(x[b, hf * T:(hf + 1) * T, :]))
        in_maps.append(m)
    res = run_bass_kernel_spmd(nc, in_maps, core_ids=list(range(NCORES)))
    out = np.zeros((B, S, D), np.float32)
    for c in range(NCORES):
        b, hf = c // 2, c % 2
        out[b, hf * T:(hf + 1) * T, :] = np.asarray(res.results[c]["x_out"], np.float32).T
    return np.ascontiguousarray(out)


def kernel_unfused(**inputs):
    x = np.asarray(inputs["L0_attn_x"], np.float32)
    x = run_l0a(x, inputs)
    x = run_l0f(x, inputs)
    x = run_l1a(x, inputs)
    x = run_l1f(x, inputs)
    return np.ascontiguousarray(x, dtype=np.float32)


# version 26
# speedup vs baseline: 1.0876x; 1.0602x over previous
import contextlib
import numpy as np
import ml_dtypes
import concourse.bass as bass
import concourse.mybir as mybir
from concourse.bass_utils import run_bass_kernel_spmd

F32 = mybir.dt.float32
BF16 = mybir.dt.bfloat16
U8 = mybir.dt.uint8
AF = mybir.ActivationFunctionType
ALU = mybir.AluOpType
AX = mybir.AxisListType

B, S, D = 4, 4096, 1024
T = 2048
NCORES = 8
DC = 8
MAGIC = float(np.float32(3 << 22))
LN_EPS = 1e-5



class AS:
    OPS = {"tensor_tensor", "tensor_scalar", "tensor_copy", "scalar_tensor_tensor",
           "tensor_tensor_scan", "reciprocal", "memset", "tensor_reduce",
           "select", "transpose", "iota", "activation", "copy", "add", "mul",
           "tensor_single_scalar", "tensor_scalar_add", "tensor_scalar_mul"}

    def __init__(self, eng, sem):
        self._e = eng
        self._s = sem

    def __getattr__(self, name):
        attr = getattr(self._e, name)
        if name in AS.OPS:
            sem = self._s

            def wrapped(*a, **k):
                self._e.wait_ge(sem.h, sem.n)
                inst = attr(*a, **k)
                inst.then_inc(sem.h, 1)
                sem.n += 1
                return inst
            return wrapped
        return attr



class DirectBlock:
    def __init__(self, nc):
        self._nc = nc

    def tensor(self, f):
        return f(self._nc.tensor)

    def vector(self, f):
        return f(self._nc.vector)

    def scalar(self, f):
        return f(self._nc.scalar)

    def gpsimd(self, f):
        return f(self._nc.gpsimd)

    def sync(self, f):
        return f(self._nc.sync)


class Sem:
    def __init__(self, es, nc, name):
        self.h = es.enter_context(nc.semaphore(name))
        self.n = 0

    def inc(self, k=1):
        self.n += k
        return self.h


def fmT(x):
    return np.ascontiguousarray(np.asarray(x, np.float32).T)


def pack_vec(v):
    v = np.asarray(v, np.float32)
    return np.ascontiguousarray(v.reshape(-1, 128).T)



def emit_l0_attn(nc, es, block, x_in, x_out, w_aps, vec_aps, scratch, ses=None, hook=None):
    ses = es if ses is None else ses
    """L0_attn + residual + LN. x_in/x_out: DRAM APs, fm f32 [1024, T]."""
    TB = 256
    NTB = T // TB

    wt = {w: es.enter_context(nc.sbuf_tensor("l0a_" + w, [128, DC * D], BF16))
          for w in ["wp1t", "wp2t", "convt", "wat", "wxt"]}
    wt["wp3t"] = wt["wp1t"]
    vecs = {v: es.enter_context(nc.sbuf_tensor("l0a_" + v, [128, 8], F32))
            for v in ["bp1", "bp2", "convb", "ba", "bx", "lam", "bp3", "sel"]}
    lamlog = es.enter_context(nc.sbuf_tensor("lamlog", [128, 8], F32))
    vtmp = es.enter_context(nc.sbuf_tensor("vtmp", [128, 8], F32))
    onesb = es.enter_context(nc.sbuf_tensor("onesb", [128, 1], BF16))
    ones_row = es.enter_context(nc.sbuf_tensor("ones_row", [1, 128], F32))
    carry = es.enter_context(nc.sbuf_tensor("carry", [128, 8], F32))
    htail = es.enter_context(nc.sbuf_tensor("htail", [128, 8], F32))

    f32t = {n: es.enter_context(nc.sbuf_tensor("l0a_" + n, [128, DC * TB], F32))
            for n in ["x1", "x2b", "bR", "bI", "bU", "h0", "h1", "p0", "p1"]}
    bf16t = {n: es.enter_context(nc.sbuf_tensor("l0a_" + n, [128, DC * TB], BF16))
             for n in ["xb0", "xb1", "x2a", "x2bb", "prodb", "corrb"]}
    stats = es.enter_context(nc.sbuf_tensor("l0a_stats", [1, 3 * TB], F32))
    ybf = es.enter_context(nc.sbuf_tensor("l0a_ybf", [128, TB], BF16))
    ysq = es.enter_context(nc.sbuf_tensor("l0a_ysq", [128, TB], BF16))

    ps = [es.enter_context(nc.psum_tensor(f"l0a_ps{i}", [128, 512], F32)) for i in range(6)]

    sgw = Sem(ses, nc, "l0a_sgw")
    sxb = [Sem(ses, nc, f"l0a_sxb{i}") for i in range(2)]
    spst = Sem(ses, nc, "l0a_spst")
    sld = [Sem(ses, nc, f"l0a_sld{i}") for i in range(2)]
    sgc = Sem(ses, nc, "l0a_sgc")
    st = Sem(ses, nc, "l0a_st")
    sv = Sem(ses, nc, "l0a_sv")
    ss = Sem(ses, nc, "l0a_ss")
    scc = Sem(ses, nc, "l0a_scc")
    so = Sem(ses, nc, "l0a_so")

    x_in_r = x_in.rearrange("(c p) t -> p c t", p=128)
    x_out_r = x_out.rearrange("(c p) t -> p c t", p=128)
    prod_r = scratch["prod_hbm"]
    corr_r = scratch["corr_hbm"]

    def g_loads(g):
        for w in ["wp1t", "wp2t", "convt", "wat", "wxt"]:
            tile = wt[w]
            for c in range(DC):
                g.dma_start(out=tile[:, c * D:(c + 1) * D],
                            in_=w_aps[w][c * 128:(c + 1) * 128, :]).then_inc(sgw.inc(16), 16)
        for v, tile in vecs.items():
            g.dma_start(out=tile[:], in_=vec_aps[v]).then_inc(sgw.inc(16), 16)
        for i in range(2):
            for c in range(DC):
                g.dma_start(out=bf16t[f"xb{i}"][:, c * TB:(c + 1) * TB],
                            in_=x_in_r[:, c, i * TB:(i + 1) * TB]).then_inc(sxb[i].inc(16), 16)
    block.gpsimd(g_loads)
    w_loaded = sgw.n
    xb_load = {0: (sxb[0].h, sxb[0].n), 1: (sxb[1].h, sxb[1].n)}

    def v_init(v):
        v = AS(v, sv)
        v.wait_ge(sgw.h, w_loaded)
        v.memset(onesb[:], 1.0)
        v.memset(ones_row[:], 1.0)
    block.vector(v_init)
    n0 = sv.n

    def s_init(s):
        s = AS(s, ss)
        s.wait_ge(sv.h, n0)
        s.wait_ge(sgw.h, w_loaded)
        s.activation(vtmp[:], vecs["lam"][:], AF.Abs)
        s.activation(vtmp[:], vtmp[:], AF.Exp, scale=-1.0)
        s.activation(vtmp[:], vtmp[:], AF.Ln, bias=1.0)
    block.scalar(s_init)
    n1 = ss.n

    def v_init2(v):
        v = AS(v, sv)
        v.wait_ge(ss.h, n1)
        v.tensor_scalar(lamlog[:], vecs["lam"][:], -1.0, 0.0, ALU.mult, ALU.max)
        v.scalar_tensor_tensor(lamlog[:], lamlog[:], 1.0, vtmp[:], ALU.mult, ALU.add)
        v.tensor_scalar(lamlog[:], lamlog[:], -8.0, None, ALU.mult)
    block.vector(v_init2)
    lamlog_done = sv.n

    bank_free = [0] * 6
    sv_after_scan = [0] * NTB
    prod_stored = [0] * NTB

    def mm_group(wname, in_tile, out_eval, gate=None):
        for oc in range(DC):
            b = oc % 4
            bf = bank_free[b]
            g8 = gate if oc == 0 else None

            def t_sec(t, oc=oc, b=b, bf=bf, g8=g8):
                if g8 is not None:
                    for gh, gv in g8:
                        t.wait_ge(gh, gv)
                if bf:
                    t.wait_ge(ss.h, bf)
                for dcc in range(DC):
                    mm = t.matmul(ps[b][:, 0:TB],
                                  wt[wname][:, dcc * D + oc * 128: dcc * D + (oc + 1) * 128],
                                  in_tile[:, dcc * TB:(dcc + 1) * TB],
                                  start=(dcc == 0), stop=(dcc == DC - 1))
                mm.then_inc(st.inc())
            block.tensor(t_sec)
            mm_done = st.n

            dest, bias_ap, actf = out_eval(oc)

            def s_sec(s, dest=dest, bias_ap=bias_ap, actf=actf, b=b, mm_done=mm_done):
                s = AS(s, ss)
                s.wait_ge(st.h, mm_done)
                s.activation(dest, ps[b][:, 0:TB], actf, bias=bias_ap)
            block.scalar(s_sec)
            bank_free[b] = ss.n

    for tb in range(NTB):
        xb = bf16t[f"xb{tb % 2}"]
        hprev, hcur = f32t[f"h{1 - tb % 2}"], f32t[f"h{tb % 2}"]
        pprev, pcur = f32t[f"p{1 - tb % 2}"], f32t[f"p{tb % 2}"]

        mm_group("wp2t", xb,
                 lambda oc: (bf16t["x2a"][:, oc * TB:(oc + 1) * TB], vecs["bp2"][:, oc:oc + 1], AF.Identity),
                 gate=[(sgw.h, w_loaded), xb_load[tb]])
        x2a_done = ss.n
        if hook is not None:
            hook("mid")

        gate_sv = sv_after_scan[tb - 1] if tb >= 1 else lamlog_done

        def t_gate_x2b(t, x2a_done=x2a_done, gate_sv=gate_sv):
            t.wait_ge(ss.h, x2a_done)
            t.wait_ge(sv.h, gate_sv)
        block.tensor(t_gate_x2b)
        mm_group("convt", bf16t["x2a"],
                 lambda oc: (f32t["x2b"][:, oc * TB:(oc + 1) * TB], vecs["convb"][:, oc:oc + 1], AF.Relu))
        x2b_done = ss.n

        def v_cast(v, x2b_done=x2b_done):
            v = AS(v, sv)
            v.wait_ge(ss.h, x2b_done)
            v.tensor_copy(bf16t["x2bb"][:], f32t["x2b"][:])
        block.vector(v_cast)
        x2bb_done = sv.n

        if hook is not None:
            hook("mid")
        mm_group("wat", bf16t["x2bb"],
                 lambda oc: (f32t["bR"][:, oc * TB:(oc + 1) * TB], vecs["ba"][:, oc:oc + 1], AF.Sigmoid),
                 gate=[(sv.h, x2bb_done)])
        if hook is not None:
            hook("mid")
        mm_group("wxt", bf16t["x2bb"],
                 lambda oc: (f32t["bI"][:, oc * TB:(oc + 1) * TB], vecs["bx"][:, oc:oc + 1], AF.Sigmoid))
        ri_done = ss.n
        if hook is not None:
            hook("mid")
        mm_group("wp1t", xb,
                 lambda oc: (f32t["x1"][:, oc * TB:(oc + 1) * TB], vecs["bp1"][:, oc:oc + 1], AF.Gelu))
        x1_done = ss.n
        if hook is not None:
            hook("mid")
        st_tb = st.n

        if tb + 2 < NTB:
            def g_pref(g, tb=tb, st_tb=st_tb):
                g.wait_ge(st.h, st_tb)
                for c in range(DC):
                    g.dma_start(out=bf16t[f"xb{tb % 2}"][:, c * TB:(c + 1) * TB],
                                in_=x_in_r[:, c, (tb + 2) * TB:(tb + 3) * TB]).then_inc(sxb[tb % 2].inc(16), 16)
            block.gpsimd(g_pref)
            xb_load[tb + 2] = (sxb[tb % 2].h, sxb[tb % 2].n)

        def v_g1(v, ri_done=ri_done):
            v = AS(v, sv)
            v.wait_ge(ss.h, ri_done)
            for rc in range(DC):
                sl = slice(rc * TB, (rc + 1) * TB)
                v.tensor_scalar(f32t["bU"][:, sl], f32t["bR"][:, sl],
                                lamlog[:, rc:rc + 1], None, ALU.mult)
        block.vector(v_g1)
        nvu = sv.n

        def s_g1(s, nvu=nvu):
            s = AS(s, ss)
            s.wait_ge(sv.h, nvu)
            s.activation(f32t["bR"][:], f32t["bU"][:], AF.Exp)
        block.scalar(s_g1)
        na = ss.n

        def v_g2(v, na=na):
            v = AS(v, sv)
            v.wait_ge(ss.h, na)
            v.tensor_tensor(f32t["bU"][:], f32t["bI"][:], f32t["x2b"][:], ALU.mult)
            v.tensor_tensor(f32t["bI"][:], f32t["bR"][:], f32t["bR"][:], ALU.mult)
        block.vector(v_g2)
        nm = sv.n

        def s_g2(s, nm=nm):
            s = AS(s, ss)
            s.wait_ge(sv.h, nm)
            s.activation(f32t["bI"][:], f32t["bI"][:], AF.Sqrt, scale=-1.0, bias=1.0)
        block.scalar(s_g2)
        nsq = ss.n

        def v_g3(v, nsq=nsq, tb=tb, hprev=hprev, hcur=hcur, pprev=pprev, pcur=pcur, x1_done=x1_done,
                 pst_need=32 * tb):
            v = AS(v, sv)
            v.wait_ge(ss.h, nsq)
            if pst_need:
                v.wait_ge(spst.h, pst_need)
            v.tensor_tensor(f32t["bU"][:], f32t["bI"][:], f32t["bU"][:], ALU.mult)
            for dcc in range(DC):
                sl = slice(dcc * TB, (dcc + 1) * TB)
                hi = 0.0 if tb == 0 else hprev[:, dcc * TB + TB - 1: dcc * TB + TB]
                pi = 1.0 if tb == 0 else pprev[:, dcc * TB + TB - 1: dcc * TB + TB]
                v.tensor_tensor_scan(hcur[:, sl], f32t["bR"][:, sl], f32t["bU"][:, sl],
                                     hi, ALU.mult, ALU.add)
                v.tensor_tensor_scan(pcur[:, sl], f32t["bR"][:, sl], f32t["bU"][:, sl],
                                     pi, ALU.mult, ALU.bypass)
            v.wait_ge(ss.h, x1_done)
            v.tensor_tensor(bf16t["prodb"][:], f32t["x1"][:], hcur[:], ALU.mult)
            v.tensor_tensor(bf16t["corrb"][:], f32t["x1"][:], pcur[:], ALU.mult)
        block.vector(v_g3)
        sv_after_scan[tb] = sv.n
        nprod = sv.n

        def g_store(g, tb=tb, nprod=nprod):
            g.wait_ge(sv.h, nprod)
            g.dma_start(out=prod_r[:, tb * DC * TB:(tb + 1) * DC * TB], in_=bf16t["prodb"][:]).then_inc(spst.inc(16), 16)
            g.dma_start(out=corr_r[:, tb * DC * TB:(tb + 1) * DC * TB], in_=bf16t["corrb"][:]).then_inc(spst.inc(16), 16)
        block.gpsimd(g_store)
        prod_stored[tb] = spst.n
        if hook is not None:
            hook(tb)

    sgw2 = Sem(ses, nc, "l0a_sgw2")

    def g_wp3(g, gate=st.n):
        g.wait_ge(st.h, gate)
        for c in range(DC):
            g.dma_start(out=wt["wp3t"][:, c * D:(c + 1) * D],
                        in_=w_aps["wp3t"][c * 128:(c + 1) * 128, :]).then_inc(sgw2.inc(16), 16)
    block.gpsimd(g_wp3)
    wp3_loaded = sgw2.n

    hlast = f32t[f"h{(NTB - 1) % 2}"]
    nall = sv_after_scan[-1]

    def v_tail(v, nall=nall):
        v = AS(v, sv)
        v.wait_ge(sv.h, nall)
        v.tensor_copy(htail[:], hlast[:, TB - 1::TB])
    block.vector(v_tail)
    ntail = sv.n

    def g_cc(g, ntail=ntail):
        g.wait_ge(sv.h, ntail)
        g.dma_start(out=scratch["cc_in"].ap(), in_=htail[:]).then_inc(sgc.inc(16), 16)
        g.wait_ge(sgc.h, sgc.n)
        g.collective_compute(
            "AllGather", ALU.bypass,
            replica_groups=[[0, 1], [2, 3], [4, 5], [6, 7]],
            ins=[scratch["cc_in"].ap()],
            outs=[scratch["cc_out"].ap()],
        ).then_inc(scc.inc())
        g.wait_ge(scc.h, scc.n)
        g.dma_start(out=carry[:], in_=scratch["cc_out"].ap()[0:128, :]).then_inc(sgc.inc(16), 16)
    block.gpsimd(g_cc)
    ncarry = sgc.n

    def v_carry(v, ncarry=ncarry):
        v = AS(v, sv)
        v.wait_ge(sgc.h, ncarry)
        v.tensor_tensor(carry[:], carry[:], vecs["sel"][:], ALU.mult)
    block.vector(v_carry)
    carry_done = sv.n

    rhs2 = [bf16t["x2a"], bf16t["x2bb"]]
    pin2 = [bf16t["xb0"], bf16t["xb1"]]
    cin2 = [bf16t["prodb"], bf16t["corrb"]]
    xres = [f32t["x1"], f32t["x2b"]]
    yout = f32t["bR"]
    xnew = f32t["bI"]

    p2_load = [0] * NTB
    p2_fix = [0] * NTB
    p2_proj_st = [0] * NTB
    p2_resid = [0] * NTB
    p2_store = [0] * NTB
    bank2 = {0: bank_free[0], 1: bank_free[1]}
    nbc_prev = 0
    stats_read_prev = 0

    def emit_p2_load(tb):
        g1 = (sv.h, carry_done) if tb < 2 else (sv.h, max(p2_fix[tb - 2], p2_resid[tb - 2]))
        g2 = (spst.h, prod_stored[NTB - 1]) if tb < 2 else (st.h, p2_proj_st[tb - 2])
        sem = sld[tb % 2]

        def s_load(s, tb=tb, g1=g1, g2=g2, sem=sem):
            s = AS(s, ss)
            s.wait_ge(*g1)
            s.wait_ge(*g2)
            for c in range(DC):
                s.dma_start(out=xres[tb % 2][:, c * TB:(c + 1) * TB],
                            in_=x_in_r[:, c, tb * TB:(tb + 1) * TB]).then_inc(sem.inc(16), 16)
            s.dma_start(out=pin2[tb % 2][:], in_=prod_r[:, tb * DC * TB:(tb + 1) * DC * TB]).then_inc(sem.inc(16), 16)
            s.dma_start(out=cin2[tb % 2][:], in_=corr_r[:, tb * DC * TB:(tb + 1) * DC * TB]).then_inc(sem.inc(16), 16)
        block.sync(s_load)
        p2_load[tb] = (sem.h, sem.n)

    emit_p2_load(0)
    emit_p2_load(1)

    for tb in range(NTB):
        loaded = p2_load[tb]
        def v_fix(v, tb=tb, loaded=loaded):
            v = AS(v, sv)
            v.wait_ge(loaded[0], loaded[1])
            v.wait_ge(sv.h, carry_done)
            if tb >= 2:
                v.wait_ge(st.h, p2_proj_st[tb - 2])
            for hc in range(DC):
                sl = slice(hc * TB, (hc + 1) * TB)
                v.scalar_tensor_tensor(rhs2[tb % 2][:, sl], cin2[tb % 2][:, sl],
                                       carry[:, hc:hc + 1], pin2[tb % 2][:, sl], ALU.mult, ALU.add)
        block.vector(v_fix)
        p2_fix[tb] = sv.n
        nrhs = sv.n

        for dcc in range(DC):
            b = dcc % 2
            bf = bank2[b]

            def t_proj(t, dcc=dcc, b=b, nrhs=nrhs, bf=bf, tb=tb):
                if dcc == 0:
                    t.wait_ge(sv.h, nrhs)
                if dcc == 0 and tb == 0:
                    t.wait_ge(sgw2.h, wp3_loaded)
                if bf:
                    t.wait_ge(ss.h, bf)
                for hc in range(DC):
                    mm = t.matmul(ps[b][:, 0:TB],
                                  wt["wp3t"][:, hc * D + dcc * 128: hc * D + (dcc + 1) * 128],
                                  rhs2[tb % 2][:, hc * TB:(hc + 1) * TB],
                                  start=(hc == 0), stop=(hc == DC - 1))
                mm.then_inc(st.inc())
            block.tensor(t_proj)
            nmm = st.n

            def s_evac(s, dcc=dcc, b=b, nmm=nmm, tb=tb):
                s = AS(s, ss)
                s.wait_ge(st.h, nmm)
                if dcc == 0 and tb > 0:
                    s.wait_ge(sv.h, p2_resid[tb - 1])
                s.activation(yout[:, dcc * TB:(dcc + 1) * TB], ps[b][:, 0:TB],
                             AF.Identity, bias=vecs["bp3"][:, dcc:dcc + 1])
            block.scalar(s_evac)
            bank2[b] = ss.n
        proj_done = ss.n
        p2_proj_st[tb] = st.n

        def v_resid(v, tb=tb, proj_done=proj_done):
            v = AS(v, sv)
            v.wait_ge(ss.h, proj_done)
            if tb > 0:
                v.wait_ge(so.h, p2_store[tb - 1])
            v.tensor_tensor(xnew[:], yout[:], xres[tb % 2][:], ALU.add)
        block.vector(v_resid)
        p2_resid[tb] = sv.n
        nresid = sv.n

        stat_st = []
        for dcc in range(DC):
            def v_cast2(v, dcc=dcc, nresid=nresid):
                v = AS(v, sv)
                if dcc == 0:
                    v.wait_ge(sv.h, nresid)
                else:
                    v.wait_ge(st.h, stat_st[dcc - 1])
                v.tensor_copy(ybf[:], xnew[:, dcc * TB:(dcc + 1) * TB])
            block.vector(v_cast2)
            ncast = sv.n

            def s_sq(s, dcc=dcc, nresid=nresid):
                s = AS(s, ss)
                if dcc == 0:
                    s.wait_ge(sv.h, nresid)
                else:
                    s.wait_ge(st.h, stat_st[dcc - 1])
                s.activation(ysq[:], xnew[:, dcc * TB:(dcc + 1) * TB], AF.Square)
            block.scalar(s_sq)
            nsq2 = ss.n

            def t_stat(t, dcc=dcc, ncast=ncast, nsq2=nsq2, tb=tb):
                t.wait_ge(sv.h, ncast)
                t.wait_ge(ss.h, nsq2)
                if dcc == 0 and tb > 0:
                    t.wait_ge(sv.h, stats_read_prev)
                t.matmul(ps[2][0:1, 0:TB], onesb[:], ybf[:], start=(dcc == 0), stop=(dcc == DC - 1)).then_inc(st.inc())
                t.matmul(ps[3][0:1, 0:TB], onesb[:], ysq[:], start=(dcc == 0), stop=(dcc == DC - 1)).then_inc(st.inc())
            block.tensor(t_stat)
            stat_st.append(st.n)
        nstat = st.n

        def v_stats(v, nstat=nstat, nbc_prev=nbc_prev):
            v = AS(v, sv)
            v.wait_ge(st.h, nstat)
            v.tensor_scalar(stats[:, 0:TB], ps[2][0:1, 0:TB], 1.0 / 1024.0, None, ALU.mult)
            v.tensor_scalar(stats[:, TB:2 * TB], ps[3][0:1, 0:TB], 1.0 / 1024.0, None, ALU.mult)
            v.tensor_tensor(stats[:, 2 * TB:3 * TB], stats[:, 0:TB], stats[:, 0:TB], ALU.mult)
            v.tensor_tensor(stats[:, 2 * TB:3 * TB], stats[:, TB:2 * TB], stats[:, 2 * TB:3 * TB], ALU.subtract)
            v.tensor_scalar(stats[:, 2 * TB:3 * TB], stats[:, 2 * TB:3 * TB], LN_EPS, None, ALU.add)
        block.vector(v_stats)
        nvar = sv.n

        def s_std(s, nvar=nvar):
            s = AS(s, ss)
            s.wait_ge(sv.h, nvar)
            s.activation(stats[:, 2 * TB:3 * TB], stats[:, 2 * TB:3 * TB], AF.Sqrt)
        block.scalar(s_std)
        nstd = ss.n

        def v_rstd(v, nstd=nstd):
            v = AS(v, sv)
            v.wait_ge(ss.h, nstd)
            v.reciprocal(stats[:, TB:2 * TB], stats[:, 2 * TB:3 * TB])
        block.vector(v_rstd)
        nrstd = sv.n

        def t_bcast(t, nrstd=nrstd):
            t.wait_ge(sv.h, nrstd)
            t.matmul(ps[4][:, 0:TB], ones_row[:], stats[0:1, 0:TB], start=True, stop=True).then_inc(st.inc())
            t.matmul(ps[5][:, 0:TB], ones_row[:], stats[0:1, TB:2 * TB], start=True, stop=True).then_inc(st.inc())
        block.tensor(t_bcast)
        nbc = st.n

        def v_apply(v, nbc=nbc):
            v = AS(v, sv)
            v.wait_ge(st.h, nbc)
            for dcc in range(DC):
                sl = slice(dcc * TB, (dcc + 1) * TB)
                v.tensor_tensor(xnew[:, sl], xnew[:, sl], ps[4][:, 0:TB], ALU.subtract)
            for dcc in range(DC):
                sl = slice(dcc * TB, (dcc + 1) * TB)
                v.tensor_tensor(xnew[:, sl], xnew[:, sl], ps[5][:, 0:TB], ALU.mult)
        block.vector(v_apply)
        napply = sv.n
        stats_read_prev = napply
        nbc_prev = nbc

        def s_store(s, tb=tb, napply=napply):
            s = AS(s, ss)
            s.wait_ge(sv.h, napply)
            for c in range(DC):
                s.dma_start(out=x_out_r[:, c, tb * TB:(tb + 1) * TB],
                            in_=xnew[:, c * TB:(c + 1) * TB]).then_inc(so.inc(16), 16)
        block.sync(s_store)
        p2_store[tb] = so.n

        if tb + 2 < NTB:
            emit_p2_load(tb + 2)

    def g_final(g):
        g.wait_ge(so.h, so.n)
    block.gpsimd(g_final)




def emit_quant(v, dst_bf, x_f32, scr, P, F, NB, a=None):
    bq, cq = scr["b"], scr["c"]
    if a is None:
        a = x_f32
    sc, rsc = scr["sc"], scr["rsc"]
    x3 = x_f32.rearrange("p (nb k) -> p nb k", k=16)
    v.tensor_reduce(sc[:P, 0:NB], x3, AX.X, ALU.max, apply_absolute_value=True)
    v.tensor_scalar(sc[:P, 0:NB], sc[:P, 0:NB], 1.0 / 6.0, 1e-12, ALU.mult, ALU.max)
    v.reciprocal(rsc[:P, 0:NB], sc[:P, 0:NB])
    a2 = a[:P, 0:F]
    v.tensor_tensor(a2.rearrange("p (nb k) -> p nb k", k=16), x3,
                    rsc[:P, 0:NB].broadcast_to((P, NB, 16)), ALU.mult)
    xn = a2
    v.tensor_scalar(bq[:P, 0:F], xn, 1.0, MAGIC, ALU.mult, ALU.add)
    v.tensor_scalar(bq[:P, 0:F], bq[:P, 0:F], MAGIC, None, ALU.subtract)
    v.tensor_scalar(bq[:P, 0:F], bq[:P, 0:F], 4.0, -4.0, ALU.min, ALU.max)
    v.scalar_tensor_tensor(bq[:P, 0:F], xn, 5.0, bq[:P, 0:F], ALU.is_gt, ALU.add)
    v.scalar_tensor_tensor(bq[:P, 0:F], xn, 5.0, bq[:P, 0:F], ALU.is_gt, ALU.add)
    v.tensor_scalar(cq[:P, 0:F], xn, -5.0, -2.0, ALU.is_lt, ALU.mult)
    v.tensor_tensor(bq[:P, 0:F], bq[:P, 0:F], cq[:P, 0:F], ALU.add)
    v.tensor_scalar(cq[:P, 0:F], xn, 2.0, MAGIC, ALU.mult, ALU.add)
    v.tensor_scalar(cq[:P, 0:F], cq[:P, 0:F], MAGIC, 0.5, ALU.subtract, ALU.mult)
    v.tensor_scalar(cq[:P, 0:F], cq[:P, 0:F], 2.0, -2.0, ALU.min, ALU.max)
    v.tensor_tensor(xn, xn, xn, ALU.mult)
    v.tensor_scalar(xn, xn, 5.0625, None, ALU.is_lt)
    v.tensor_tensor(cq[:P, 0:F], cq[:P, 0:F], bq[:P, 0:F], ALU.subtract)
    v.tensor_tensor(cq[:P, 0:F], xn, cq[:P, 0:F], ALU.mult)
    v.tensor_tensor(xn, bq[:P, 0:F], cq[:P, 0:F], ALU.add)
    v.tensor_tensor(dst_bf.rearrange("p (nb k) -> p nb k", k=16),
                    xn.rearrange("p (nb k) -> p nb k", k=16),
                    sc[:P, 0:NB].broadcast_to((P, NB, 16)), ALU.mult)



def emit_l0f_phaseW(nc, es, block, w_aps, scratch, ses):
    wsl = w_aps["wsl"]
    qag_in, qag_out = scratch["qag_in"], scratch["qag_out"]
    wst2 = [es.enter_context(nc.sbuf_tensor(f"l0fw_wst{i}", [128, 512], F32)) for i in range(2)]
    bH = es.enter_context(nc.sbuf_tensor("l0fw_b", [128, 512], F32))
    cH = es.enter_context(nc.sbuf_tensor("l0fw_c", [128, 512], F32))
    wqb2 = [es.enter_context(nc.sbuf_tensor(f"l0fw_q{i}", [128, 512], BF16)) for i in range(2)]
    scH = es.enter_context(nc.sbuf_tensor("l0fw_sc", [128, 32], F32))
    rscH = es.enter_context(nc.sbuf_tensor("l0fw_rsc", [128, 32], F32))
    identw = es.enter_context(nc.sbuf_tensor("l0fw_id", [128, 128], BF16))
    pstw = es.enter_context(nc.psum_tensor("l0fw_pst", [128, 512], BF16))

    sw2 = [Sem(ses, nc, f"l0fw_sw{i}") for i in range(2)]
    sid = Sem(ses, nc, "l0fw_sid")
    sv = Sem(ses, nc, "l0fw_sv")
    st = Sem(ses, nc, "l0fw_st")
    sjb2 = [Sem(ses, nc, f"l0fw_sjb{i}") for i in range(2)]
    scc = Sem(ses, nc, "l0fw_scc")
    qscr = dict(b=bH, c=cH, sc=scH, rsc=rscH)

    passes = [(c0, min(512, 2816 - c0)) for c0 in range(0, 2816, 512)]
    tasks = [(s_, c0, w) for s_ in range(7) for (c0, w) in passes]
    N = len(tasks)
    ld_tot = {}
    tp_done = {}
    q_done = {}
    st_slot = [0, 0]

    def g_ld(i, gate):
        s_, c0, w = tasks[i]
        sl = i % 2

        def f(g, s_=s_, c0=c0, w=w, sl=sl, gate=gate):
            if gate:
                g.wait_ge(sv.h, gate)
            g.dma_start(out=wst2[sl][:, 0:w],
                        in_=wsl[s_ * 128:(s_ + 1) * 128, c0:c0 + w]).then_inc(sw2[sl].inc(16), 16)
        block.sync(f)
        ld_tot[i] = sw2[sl].n

    def g_id(g):
        g.dma_start(out=identw[:], in_=w_aps["ident"]).then_inc(sid.inc(16), 16)
    block.gpsimd(g_id)
    id_done = sid.n
    g_ld(0, 0)
    g_ld(1, 0)

    state = {"i": 0, "agc": None}

    def emit_pass():
        i = state["i"]
        s_, c0, w = tasks[i]
        sl = i % 2

        def v_q(v, sl=sl, w=w, ld=ld_tot[i], tpg=tp_done.get(i - 2, 0), stg=st_slot[sl]):
            v = AS(v, sv)
            v._e.wait_ge(sw2[sl].h, ld)
            if tpg:
                v._e.wait_ge(st.h, tpg)
            if stg:
                v._e.wait_ge(sjb2[sl].h, stg)
            emit_quant(v, wqb2[sl][:, 0:w], wst2[sl][:, 0:w], qscr, 128, w, w // 16)
        block.vector(v_q)
        q_done[i] = sv.n

        if i + 2 < N:
            g_ld(i + 2, q_done[i])

        def t_tp(t, i=i, sl=sl, nch=(tasks[i][2] + 127) // 128):
            if i == 0:
                t.wait_ge(sid.h, id_done)
            t.wait_ge(sv.h, q_done[i])
            for k in range(nch):
                t.transpose(pstw[:, k * 128:(k + 1) * 128],
                            wqb2[sl][:, k * 128:(k + 1) * 128], identw[:]).then_inc(st.inc())
        block.tensor(t_tp)
        tp_done[i] = st.n

        def v_cp(v, i=i, sl=sl, w=w):
            v = AS(v, sv)
            v._e.wait_ge(st.h, tp_done[i])
            v.tensor_copy(wqb2[sl][:, 0:w], pstw[:, 0:w])
        block.vector(v_cp)
        cp_done = sv.n

        def g_st(g, s_=s_, c0=c0, w=w, sl=sl, cp_done=cp_done):
            g.wait_ge(sv.h, cp_done)
            g.dma_start(out=qag_in.ap()[s_ * 128:(s_ + 1) * 128, c0:c0 + w],
                        in_=wqb2[sl][:, 0:w]).then_inc(sjb2[sl].inc(16), 16)
        block.sync(g_st)
        st_slot[sl] = sjb2[sl].n
        state["i"] = i + 1

    def pump(k):
        while state["i"] < N and k > 0:
            emit_pass()
            k -= 1

    def finish():
        pump(N)

        def g_ag(g, st0=sjb2[0].n, st1=sjb2[1].n):
            g.wait_ge(sjb2[0].h, st0)
            g.wait_ge(sjb2[1].h, st1)
            g.collective_compute("AllGather", ALU.bypass,
                                 replica_groups=[list(range(8))],
                                 ins=[qag_in.ap()], outs=[qag_out.ap()]).then_inc(scc.inc())
        block.gpsimd(g_ag)
        state["agc"] = scc.n

    return dict(pump=pump, finish=finish, scc=scc, state=state)


def emit_l0_ffn(nc, es, block, x_in, x_out, w_aps, scratch=None, ses=None, pw=None):
    ses = es if ses is None else ses
    """L0_ffn + residual + LN. x_in/x_out: DRAM APs fm f32 [1024, T].

    w_aps: wg, wu: [2816, 1024] f32; wd: [1024, 2816] f32; ident: [128,128] f32.
    """
    I = 2816
    IC = I // 128
    NTT = T // 128
    WB = 8
    NWB = I // WB

    wq = {w: es.enter_context(nc.sbuf_tensor("l0f_" + w, [128, DC * I], BF16))
          for w in ["wgq", "wuq"]}
    wdq = es.enter_context(nc.sbuf_tensor("l0f_wdq", [128, IC * D], BF16))
    identb = es.enter_context(nc.sbuf_tensor("l0f_identb", [128, 128], BF16))
    identf = es.enter_context(nc.sbuf_tensor("l0f_identf", [128, 128], F32))

    wst0 = es.enter_context(nc.sbuf_tensor("l0f_wst0", [128, I], F32))
    wst = [wst0, wst0]
    xfm = [es.enter_context(nc.sbuf_tensor(f"l0f_xfm{i}", [128, D], F32)) for i in range(2)]
    xtok2_t = [es.enter_context(nc.sbuf_tensor(f"l0f_xtok{i}", [128, D], F32)) for i in range(2)]
    xqf2_t = [es.enter_context(nc.sbuf_tensor(f"l0f_xqf{i}", [128, D], BF16)) for i in range(2)]
    xtok = xtok2_t[0]
    xqf = xqf2_t[0]
    hq = es.enter_context(nc.sbuf_tensor("l0f_hq", [128, I], BF16))
    xqt = hq
    hqf = es.enter_context(nc.sbuf_tensor("l0f_hqf", [128, I], BF16))
    lnst = es.enter_context(nc.sbuf_tensor("l0f_lnst", [128, 8], F32))
    qb_t = es.enter_context(nc.sbuf_tensor("l0f_qb", [128, I], F32))
    qc_t = es.enter_context(nc.sbuf_tensor("l0f_qc", [128, I], F32))
    silu_s = qc_t[:, 1024:1024 + NWB]
    ynew = qb_t
    qscr = dict(
        b=qb_t,
        c=qc_t,
        sc=es.enter_context(nc.sbuf_tensor("l0f_qsc", [128, I // 16], F32)),
        rsc=es.enter_context(nc.sbuf_tensor("l0f_qrsc", [128, I // 16], F32)),
    )
    xqscr = dict(b=qb_t[:, 0:1024], c=qc_t[:, 1024:2048],
                 sc=qscr["sc"], rsc=qscr["rsc"])
    hf = wst[0]
    wqb = hq
    yfm = qc_t

    ps = [es.enter_context(nc.psum_tensor(f"l0f_ps{i}", [128, 512], F32)) for i in range(4)]
    pst = es.enter_context(nc.psum_tensor("l0f_pst", [128, 1024], BF16))
    psf = [es.enter_context(nc.psum_tensor(f"l0f_psf{i}", [128, 512], F32)) for i in range(2)]

    sw = [Sem(ses, nc, f"l0f_sw{i}") for i in range(2)]
    sx = [Sem(ses, nc, f"l0f_sx{i}") for i in range(2)]
    sgw = Sem(ses, nc, "l0f_sgw")
    st = Sem(ses, nc, "l0f_st")
    sv = Sem(ses, nc, "l0f_sv")
    ss = Sem(ses, nc, "l0f_ss")
    so = Sem(ses, nc, "l0f_so")

    x_in_r = x_in.rearrange("(c p) t -> p c t", p=128)
    x_out_r = x_out.rearrange("(c p) t -> p c t", p=128)

    def g_ident(g):
        g.dma_start(out=identb[:], in_=w_aps["ident"]).then_inc(sgw.inc(16), 16)
        g.dma_start(out=identf[:], in_=w_aps["ident"]).then_inc(sgw.inc(16), 16)
    block.gpsimd(g_ident)
    ident_done = sgw.n

    scc = pw["scc"] if pw is not None else Sem(ses, nc, "l0f_scc")
    sjb = Sem(ses, nc, "l0f_sjb")
    swre = Sem(ses, nc, "l0f_swre")
    wsl = w_aps["wsl"]
    qag_in, qag_out = scratch["qag_in"], scratch["qag_out"]
    jobbuf = hqf

    def reassemble(g):
        jobs = [("wg", ch) for ch in range(IC)] + [("wu", ch) for ch in range(IC)] + \
               [("wd", ch) for ch in range(DC)]
        for j, (wname, ch) in enumerate(jobs):
            r0 = (j // 7) * 896 + (j % 7) * 128
            srcv = qag_out.ap()[r0:r0 + 128, :]
            if wname == "wd":
                dst = wdq[:].rearrange("p (ic d) -> p ic d", d=D)[:, :, ch * 128:(ch + 1) * 128]
                g.dma_start(out=dst, in_=srcv.rearrange("p (ic c) -> p ic c", c=128)).then_inc(swre.inc(16), 16)
            else:
                w_t = wq["wgq" if wname == "wg" else "wuq"]
                dst = w_t[:].rearrange("p (dc i) -> p dc i", i=I)[:, :, ch * 128:(ch + 1) * 128]
                g.dma_start(out=dst, in_=srcv.rearrange("p (ic c) -> p ic c", c=128)[:, 0:DC, :]).then_inc(swre.inc(16), 16)

    if pw is not None:
        def g_wre(g, agc=pw["agc"]):
            g.wait_ge(scc.h, agc)
            reassemble(g)
        block.gpsimd(g_wre)

    slot_free = [(sv.h, 0), (sv.h, 0)]
    F = I
    NB = F // 16
    for s_ in ([] if pw is not None else range(7)):
        sl = 0
        gate = slot_free[0]

        def g_wload(g, s_=s_, sl=sl, gate=gate):
            g.wait_ge(gate[0], gate[1])
            g.dma_start(out=wst[sl][:, 0:F], in_=wsl[s_ * 128:(s_ + 1) * 128, :]).then_inc(sw[sl].inc(16), 16)
        block.gpsimd(g_wload)
        loaded = (sw[sl].h, sw[sl].n)

        def v_wq(v, sl=sl, loaded=loaded, jb_gate=sjb.n):
            v = AS(v, sv)
            v._e.wait_ge(loaded[0], loaded[1])
            emit_quant(v, wqb[:, 0:F], wst[sl][:, 0:F], qscr, 128, F, NB)
        block.vector(v_wq)
        slot_free[sl] = (sv.h, sv.n)
        q_done = sv.n

        for bs in range(0, IC, 8):
            be = min(bs + 8, IC)

            def t_tp(t, bs=bs, be=be, q_done=q_done, vgate=sv.n):
                t.wait_ge(sgw.h, ident_done)
                t.wait_ge(sv.h, q_done if bs == 0 else vgate)
                for k in range(bs, be):
                    t.transpose(pst[:, (k - bs) * 128:(k - bs + 1) * 128],
                                wqb[:, k * 128:(k + 1) * 128], identb[:]).then_inc(st.inc())
            block.tensor(t_tp)
            tp_done = st.n

            def v_wcp(v, bs=bs, be=be, tp_done=tp_done, jb_gate=sjb.n):
                v = AS(v, sv)
                v._e.wait_ge(st.h, tp_done)
                if bs == 0 and jb_gate:
                    v._e.wait_ge(sjb.h, jb_gate)
                v.tensor_copy(jobbuf[:, bs * 128:be * 128], pst[:, 0:(be - bs) * 128])
            block.vector(v_wcp)
        cp_done = sv.n

        def g_jst(g, s_=s_, cp_done=cp_done):
            g.wait_ge(sv.h, cp_done)
            g.dma_start(out=qag_in.ap()[s_ * 128:(s_ + 1) * 128, :], in_=jobbuf[:]).then_inc(sjb.inc(16), 16)
        block.gpsimd(g_jst)

    if pw is None:
        def g_wag(g, stored=sjb.n):
            g.wait_ge(sjb.h, stored)
            g.collective_compute("AllGather", ALU.bypass,
                                 replica_groups=[list(range(8))],
                                 ins=[qag_in.ap()], outs=[qag_out.ap()]).then_inc(scc.inc())
            g.wait_ge(scc.h, scc.n)
            reassemble(g)
        block.gpsimd(g_wag)
    wq_all_re = (swre.h, swre.n)

    def g_xload(tt, sl, gate):
        def f(g_, tt=tt, sl=sl, gate=gate):
            g_.wait_ge(gate[0], gate[1])
            for c in range(DC):
                g_.dma_start(out=xfm[sl][:, c * 128:(c + 1) * 128],
                             in_=x_in_r[:, c, tt * 128:(tt + 1) * 128]).then_inc(sx[sl].inc(16), 16)
        block.gpsimd(f)
        return (sx[sl].h, sx[sl].n)

    x_loaded = [None] * NTT
    x_loaded[0] = g_xload(0, 0, (sgw.h, ident_done))
    x_loaded[1] = g_xload(1, 1, (sgw.h, ident_done))
    xfm_consumed = [0] * NTT

    def emit_xchain(tt):
        sl = tt % 2
        xtk = xtok2_t[sl]
        xqo = xqf2_t[sl]

        def t_xtp(t, tt=tt, sl=sl, ld=x_loaded[tt], vgate=sv.n):
            t.wait_ge(sgw.h, ident_done)
            t.wait_ge(ld[0], ld[1])
            if vgate:
                t.wait_ge(sv.h, vgate)
            for c in range(DC):
                t.transpose(psf[c % 2][:, (c // 2) * 128:(c // 2) * 128 + 128],
                            xfm[sl][:, c * 128:(c + 1) * 128], identf[:]).then_inc(st.inc())
        block.tensor(t_xtp)
        xfm_consumed[tt] = st.n
        xtp_done = st.n

        def v_xevac(v, xtk=xtk, xtp_done=xtp_done):
            v = AS(v, sv)
            v._e.wait_ge(st.h, xtp_done)
            for c in range(DC):
                v.tensor_copy(xtk[:, c * 128:(c + 1) * 128],
                              psf[c % 2][:, (c // 2) * 128:(c // 2) * 128 + 128])
        block.vector(v_xevac)

        if tt + 2 < NTT:
            x_loaded[tt + 2] = g_xload(tt + 2, sl, (st.h, xfm_consumed[tt]))

        def v_xq(v, xtk=xtk):
            v = AS(v, sv)
            emit_quant(v, xqt[:, 0:D], xtk[:], xqscr, 128, D, D // 16, a=qb_t[:, 1024:2048])
        block.vector(v_xq)
        xq_done = sv.n

        def t_xqtp(t, xq_done=xq_done):
            t.wait_ge(sgw.h, ident_done)
            t.wait_ge(sv.h, xq_done)
            for c in range(DC):
                t.transpose(pst[:, c * 128:(c + 1) * 128],
                            xqt[:, c * 128:(c + 1) * 128], identb[:]).then_inc(st.inc())
        block.tensor(t_xqtp)
        xqtp_done = st.n

        def v_xqevac(v, xqo=xqo, xqtp_done=xqtp_done):
            v = AS(v, sv)
            v._e.wait_ge(st.h, xqtp_done)
            v.tensor_copy(xqo[:], pst[:, 0:D])
        block.vector(v_xqevac)
        return sv.n, xqtp_done

    def make_ytail(tt, ln_done):
        stg = {}

        def e_ytp():
            def t_ytp(t, ln_done=ln_done):
                t.wait_ge(sgw.h, ident_done)
                t.wait_ge(sv.h, ln_done)
                for c in range(DC):
                    t.transpose(psf[c % 2][:, (c // 2) * 128:(c // 2) * 128 + 128],
                                ynew[:, c * 128:(c + 1) * 128], identf[:]).then_inc(st.inc())
            block.tensor(t_ytp)
            stg["ytp"] = st.n

        def e_yev():
            def v_yevac(v, ytp_done=stg["ytp"], store_gate=so.n):
                v = AS(v, sv)
                v._e.wait_ge(st.h, ytp_done)
                if store_gate:
                    v._e.wait_ge(so.h, store_gate)
                for c in range(DC):
                    v.tensor_copy(yfm[:, c * 128:(c + 1) * 128],
                                  psf[c % 2][:, (c // 2) * 128:(c // 2) * 128 + 128])
            block.vector(v_yevac)
            stg["yev"] = sv.n

        def e_store():
            def s_store(s, tt=tt, yev_done=stg["yev"]):
                s.wait_ge(sv.h, yev_done)
                for c in range(DC):
                    s.dma_start(out=x_out_r[:, c, tt * 128:(tt + 1) * 128],
                                in_=yfm[:, c * 128:(c + 1) * 128]).then_inc(so.inc(16), 16)
            block.sync(s_store)
        return [e_ytp, e_yev, e_store]

    xqf_done_l = [None] * (NTT + 1)
    xqtp_done_l = [None] * (NTT + 1)
    xqf_done_l[0], xqtp_done_l[0] = emit_xchain(0)
    silu_prev = 0
    fin_prev = 0
    ytail = None

    for tt in range(NTT):
        sl = tt % 2
        xtk = xtok2_t[sl]
        xqo = xqf2_t[sl]
        xqf_done = xqf_done_l[tt]

        for wb in range(WB):
            bg = (wb % 2) * 2
            cols = slice(wb * NWB, (wb + 1) * NWB)

            def t_gu(t, wb=wb, bg=bg, xqf_done=xqf_done, bank_gate=sv.n,
                     ss_gate=(silu_prev if wb == 0 else ss.n),
                     wre=wq_all_re, xqo=xqo):
                if wb == 0:
                    t.wait_ge(sv.h, xqf_done)
                    t.wait_ge(wre[0], wre[1])
                else:
                    t.wait_ge(sv.h, bank_gate)
                if ss_gate:
                    t.wait_ge(ss.h, ss_gate)
                for dc in range(DC):
                    t.matmul(ps[bg][:, 0:NWB], xqo[:, dc * 128:(dc + 1) * 128],
                             wq["wgq"][:, dc * I + wb * NWB: dc * I + (wb + 1) * NWB],
                             start=(dc == 0), stop=(dc == DC - 1)).then_inc(st.inc())
                for dc in range(DC):
                    t.matmul(ps[bg + 1][:, 0:NWB], xqo[:, dc * 128:(dc + 1) * 128],
                             wq["wuq"][:, dc * I + wb * NWB: dc * I + (wb + 1) * NWB],
                             start=(dc == 0), stop=(dc == DC - 1)).then_inc(st.inc())
            block.tensor(t_gu)
            gu_done = st.n

            def s_silu(s, wb=wb, bg=bg, gu_done=gu_done, fg=fin_prev):
                s = AS(s, ss)
                s._e.wait_ge(st.h, gu_done)
                if wb == 0 and fg:
                    s._e.wait_ge(sv.h, fg)
                s.activation(silu_s[:], ps[bg][:, 0:NWB], AF.Silu)
            block.scalar(s_silu)
            silu_done = ss.n

            def v_h(v, bg=bg, cols=cols, silu_done=silu_done):
                v = AS(v, sv)
                v._e.wait_ge(ss.h, silu_done)
                v.tensor_tensor(hf[:, cols], ps[bg + 1][:, 0:NWB], silu_s[:], ALU.mult)
            block.vector(v_h)
            if ytail is not None and 2 <= wb <= 4:
                ytail[wb - 2]()
        h_done = sv.n
        silu_prev = ss.n

        if tt + 1 < NTT:
            xqf_done_l[tt + 1], xqtp_done_l[tt + 1] = emit_xchain(tt + 1)

        def v_hq(v, h_done=h_done, xg=(xqtp_done_l[tt + 1] if tt + 1 < NTT else 0),
                 store_gate=so.n):
            v = AS(v, sv)
            if xg:
                v._e.wait_ge(st.h, xg)
            if store_gate:
                v._e.wait_ge(so.h, store_gate)
            emit_quant(v, hq[:], hf[:], qscr, 128, I, I // 16)
        block.vector(v_hq)
        hq_done = sv.n

        for bs in range(0, IC, 8):
            be = min(bs + 8, IC)

            def t_hqtp(t, bs=bs, be=be, hq_done=hq_done, evac_gate=sv.n):
                t.wait_ge(sgw.h, ident_done)
                if bs == 0:
                    t.wait_ge(sv.h, hq_done)
                else:
                    t.wait_ge(sv.h, evac_gate)
                for k in range(bs, be):
                    t.transpose(pst[:, (k - bs) * 128:(k - bs + 1) * 128],
                                hq[:, k * 128:(k + 1) * 128], identb[:]).then_inc(st.inc())
            block.tensor(t_hqtp)
            tp_done = st.n

            def v_hqevac(v, bs=bs, be=be, tp_done=tp_done):
                v = AS(v, sv)
                v._e.wait_ge(st.h, tp_done)
                v.tensor_copy(hqf[:, bs * 128:be * 128], pst[:, 0:(be - bs) * 128])
            block.vector(v_hqevac)
        hqf_done = sv.n

        def t_down(t, hqf_done=hqf_done, ss_gate=ss.n):
            t.wait_ge(sv.h, hqf_done)
            if ss_gate:
                t.wait_ge(ss.h, ss_gate)
            for half in range(2):
                for ic in range(IC):
                    t.matmul(psf[half][:, 0:512],
                             hqf[:, ic * 128:(ic + 1) * 128],
                             wdq[:, ic * D + half * 512: ic * D + (half + 1) * 512],
                             start=(ic == 0), stop=(ic == IC - 1)).then_inc(st.inc())
        block.tensor(t_down)
        down_done = st.n

        def v_fin(v, tt=tt, xtk=xtk, down_done=down_done):
            v = AS(v, sv)
            v._e.wait_ge(st.h, down_done)
            v.tensor_tensor(ynew[:, 0:512], psf[0][:, 0:512], xtk[:, 0:512], ALU.add)
            v.tensor_tensor(ynew[:, 512:1024], psf[1][:, 0:512], xtk[:, 512:1024], ALU.add)
            v.tensor_reduce(lnst[:, 0:1], ynew[:, 0:D], AX.X, ALU.add)
            v.tensor_tensor(qc_t[:, 1024:2048], ynew[:, 0:D], ynew[:, 0:D], ALU.mult)
            v.tensor_reduce(lnst[:, 1:2], qc_t[:, 1024:2048], AX.X, ALU.add)
            v.tensor_scalar(lnst[:, 0:1], lnst[:, 0:1], 1.0 / 1024.0, None, ALU.mult)
            v.tensor_scalar(lnst[:, 1:2], lnst[:, 1:2], 1.0 / 1024.0, None, ALU.mult)
            v.tensor_tensor(lnst[:, 2:3], lnst[:, 0:1], lnst[:, 0:1], ALU.mult)
            v.tensor_tensor(lnst[:, 2:3], lnst[:, 1:2], lnst[:, 2:3], ALU.subtract)
            v.tensor_scalar(lnst[:, 2:3], lnst[:, 2:3], LN_EPS, None, ALU.add)
        block.vector(v_fin)
        fin_prev = sv.n
        nvar = sv.n

        def s_sqrt(s, nvar=nvar):
            s = AS(s, ss)
            s._e.wait_ge(sv.h, nvar)
            s.activation(lnst[:, 3:4], lnst[:, 2:3], AF.Sqrt)
        block.scalar(s_sqrt)
        nstd = ss.n

        def v_ln(v, nstd=nstd):
            v = AS(v, sv)
            v._e.wait_ge(ss.h, nstd)
            v.reciprocal(lnst[:, 3:4], lnst[:, 3:4])
            v.tensor_scalar(ynew[:, 0:D], ynew[:, 0:D], lnst[:, 0:1], None, ALU.subtract)
            v.tensor_scalar(ynew[:, 0:D], ynew[:, 0:D], lnst[:, 3:4], None, ALU.mult)
        block.vector(v_ln)
        ln_done = sv.n

        ytail = make_ytail(tt, ln_done)
    for f in ytail:
        f()

    def g_fin(g):
        g.wait_ge(so.h, so.n)
    block.gpsimd(g_fin)




def host_l0f_wsl(weights, core_id):
    wg = np.asarray(weights["L0_ffn_w_gate"], np.float32)
    wu = np.asarray(weights["L0_ffn_w_up"], np.float32)
    wd = np.asarray(weights["L0_ffn_w_down"], np.float32)
    jobs = [("wg", c) for c in range(22)] + [("wu", c) for c in range(22)] + \
           [("wd", c) for c in range(8)]
    out = np.zeros((7, 128, 2816), np.float32)
    for s in range(7):
        j = core_id * 7 + s
        if j >= len(jobs):
            continue
        wname, ch = jobs[j]
        w = {"wg": wg, "wu": wu, "wd": wd}[wname]
        blkw = w.shape[1]
        out[s, :, 0:blkw] = w[ch * 128:(ch + 1) * 128, :]
    return np.ascontiguousarray(out.reshape(896, 2816))


def build_l0f_launch():
    nc = bass.Bass("TRN2", target_bir_lowering=False)
    x_in = nc.dram_tensor("x_in", [D, T], F32, kind="ExternalInput").ap()
    x_out = nc.dram_tensor("x_out", [D, T], F32, kind="ExternalOutput").ap()
    w_aps = dict(
        wsl=nc.dram_tensor("wsl", [896, 2816], F32, kind="ExternalInput").ap(),
        ident=nc.dram_tensor("ident", [128, 128], F32, kind="ExternalInput").ap(),
    )
    scratch = dict(
        qag_in=nc.dram_tensor("qag_in", [896, 2816], BF16),
        qag_out=nc.dram_tensor("qag_out", [7168, 2816], BF16, addr_space="Shared"),
    )
    es = contextlib.ExitStack()
    with es:
        block = DirectBlock(nc)
        es.enter_context(nc.Block())
        emit_l0_ffn(nc, es, block, x_in, x_out, w_aps, scratch)
    return nc


def run_l0f(x_full, weights):
    nc = build_l0f_launch()
    ident = np.eye(128, dtype=np.float32)
    in_maps = []
    for c in range(NCORES):
        b, hf = c // 2, c % 2
        xs = np.ascontiguousarray(x_full[b, hf * T:(hf + 1) * T, :])
        in_maps.append(dict(x_in=fmT(xs), wsl=host_l0f_wsl(weights, c), ident=ident))
    res = run_bass_kernel_spmd(nc, in_maps, core_ids=list(range(NCORES)))
    out = np.zeros((B, S, D), np.float32)
    for c in range(NCORES):
        b, hf = c // 2, c % 2
        out[b, hf * T:(hf + 1) * T, :] = np.asarray(res.results[c]["x_out"], np.float32).T
    return out




def emit_l1_attn(nc, es, block, x_in, x_out, w_aps, scratch, ses=None):
    ses = es if ses is None else ses
    """Differential attention block (B=1 sequence slice per core pair).

    w_aps: wq, wk: [16, 1024, 128] f32; wvT: [1024, 1024] f32 (dmodel, h*dv);
    wo: [1024, 1024] f32 (hd rows, dmodel cols);
    bqT, bkT: [128, 16]; bvrow: [1, 1024]; bo, gamma, beta: [128, 8];
    lamb: [128, 1]; onesr: [1, 128]  (all f32 DRAM APs)
    scratch: cc_kin [2048,2048]bf16, cc_kout [4096,2048]bf16,
             cc_vin [2048,1024]bf16, cc_vout [4096,1024]bf16 (dram tensors)
    """
    H, DH = 16, 64
    NTT = T // 128
    KC = S // 128
    VW = H * 65

    xbf = es.enter_context(nc.sbuf_tensor("l1a_xbf", [128, DC * T], BF16))
    ocat = xbf
    qsb = es.enter_context(nc.sbuf_tensor("l1a_qsb", [128, H * T], BF16))
    khh = [es.enter_context(nc.sbuf_tensor(f"l1a_kh{i}", [128, S], BF16)) for i in range(2)]
    wvsb = es.enter_context(nc.sbuf_tensor("l1a_wvsb", [128, DC * 1024], BF16))
    wosb = wvsb
    vecs = {v: es.enter_context(nc.sbuf_tensor("l1a_" + v, [128, 16], F32))
            for v in ["bqT", "bkT", "bo", "gamma", "beta", "lamb", "gp", "bp"]}
    onesr = es.enter_context(nc.sbuf_tensor("l1a_onesr", [1, 128], F32))
    ones65 = es.enter_context(nc.sbuf_tensor("l1a_ones65", [65, 64], F32))
    ksum = es.enter_context(nc.sbuf_tensor("l1a_ksum", [128, 2], F32))
    o1s = es.enter_context(nc.sbuf_tensor("l1a_o1s", [128, 512], F32))
    o2s = es.enter_context(nc.sbuf_tensor("l1a_o2s", [128, 512], F32))
    rs1 = es.enter_context(nc.sbuf_tensor("l1a_rs1", [1, 512], F32))
    rs2 = es.enter_context(nc.sbuf_tensor("l1a_rs2", [1, 512], F32))
    p1s = es.enter_context(nc.sbuf_tensor("l1a_p1s", [128, 1024], BF16))
    p2s = es.enter_context(nc.sbuf_tensor("l1a_p2s", [128, 1024], BF16))
    onesb = es.enter_context(nc.sbuf_tensor("l1a_onesb", [128, 1], BF16))
    yout = ytmp = xres = ystat = ybf = ysq = None
    es_attn = contextlib.ExitStack()
    vhat = None
    es_proj = contextlib.ExitStack()
    wqk = [es_proj.enter_context(nc.sbuf_tensor(f"l1a_wqk{i}", [128, 1024], BF16)) for i in range(2)]
    kstage = [es_proj.enter_context(nc.sbuf_tensor(f"l1a_kst{i}", [128, T], BF16)) for i in range(2)]
    vstage = [es_proj.enter_context(nc.sbuf_tensor(f"l1a_vst{i}", [128, 1024], BF16)) for i in range(2)]
    bvb = es_proj.enter_context(nc.sbuf_tensor("l1a_bvb", [128, 1024], F32))
    bvrow = es_proj.enter_context(nc.sbuf_tensor("l1a_bvrow", [1, 1024], F32))
    pS = [es.enter_context(nc.psum_tensor(f"l1a_pS{i}", [128, 1024], F32)) for i in range(2)]
    ps = [pS[0][:, 0:512], pS[0][:, 512:1024]]
    pO1 = es.enter_context(nc.psum_tensor("l1a_pO1", [128, 512], F32))
    pO2 = es.enter_context(nc.psum_tensor("l1a_pO2", [128, 512], F32))
    pG1 = es.enter_context(nc.psum_tensor("l1a_pG1", [64, 512], F32))
    pG2 = es.enter_context(nc.psum_tensor("l1a_pG2", [64, 512], F32))

    sgw = Sem(ses, nc, "l1a_sgw")
    swq = [Sem(ses, nc, f"l1a_swq{i}") for i in range(2)]
    svst = [Sem(ses, nc, f"l1a_svst{i}") for i in range(2)]
    skst = [Sem(ses, nc, f"l1a_skst{i}") for i in range(2)]
    sxk = [Sem(ses, nc, f"l1a_sxk{i}") for i in range(2)]
    svh = Sem(ses, nc, "l1a_svh")
    swo = Sem(ses, nc, "l1a_swo")
    sxr = [Sem(ses, nc, f"l1a_sxr{i}") for i in range(2)]
    scc = Sem(ses, nc, "l1a_scc")
    st = Sem(ses, nc, "l1a_st")
    sv = Sem(ses, nc, "l1a_sv")
    ss = Sem(ses, nc, "l1a_ss")
    so = Sem(ses, nc, "l1a_so")

    x_in_r = x_in.rearrange("(c p) t -> p c t", p=128)
    x_out_r = x_out.rearrange("(c p) t -> p c t", p=128)
    kin, kout = scratch["cc_kin"], scratch["cc_kout"]
    vin, vout = scratch["cc_vin"], scratch["cc_vout"]

    def g_loads(g):
        for c in range(DC):
            g.dma_start(out=xbf[:, c * T:(c + 1) * T], in_=x_in_r[:, c, :]).then_inc(sgw.inc(16), 16)
            g.dma_start(out=wvsb[:, c * 1024:(c + 1) * 1024],
                        in_=w_aps["wvT"][c * 128:(c + 1) * 128, :]).then_inc(sgw.inc(16), 16)
        for v, wdt in [("bqT", 16), ("bkT", 16), ("bo", 8), ("gamma", 8), ("beta", 8), ("lamb", 1)]:
            g.dma_start(out=vecs[v][:, 0:wdt], in_=w_aps[v]).then_inc(sgw.inc(16), 16)
        g.dma_start(out=bvrow[:], in_=w_aps["bvrow"]).then_inc(sgw.inc(16), 16)
        g.dma_start(out=onesr[:], in_=w_aps["onesr"]).then_inc(sgw.inc(16), 16)
        for i in range(2):
            g.dma_start(out=wqk[i][:].rearrange("p (c m) -> p c m", m=128),
                        in_=w_aps["wk"][i].rearrange(
                "(c p) m -> p c m", p=128)).then_inc(swq[i].inc(16), 16)
    block.gpsimd(g_loads)
    loads_done = sgw.n
    w_loaded_s = {0: (swq[0].h, swq[0].n), 1: (swq[1].h, swq[1].n)}

    def v_init(v):
        v = AS(v, sv)
        v._e.wait_ge(sgw.h, loads_done)
        v.memset(onesb[:], 1.0)
        v.memset(ones65[:], 1.0)
        v.tensor_scalar(vecs["lamb"][:, 0:1], vecs["lamb"][:, 0:1], -1.0, None, ALU.mult)
        v.tensor_scalar(vecs["gp"][:, 0:8], vecs["gamma"][:, 0:8], 0.2, None, ALU.mult)
        v.tensor_scalar(vecs["bp"][:, 0:8], vecs["beta"][:, 0:8], 0.2, None, ALU.mult)
    block.vector(v_init)
    init_done = sv.n

    def t_bvb(t):
        t.wait_ge(sv.h, init_done)
        t.matmul(ps[0], onesr[:], bvrow[0:1, 0:512], start=True, stop=True).then_inc(st.inc())
        t.matmul(ps[1], onesr[:], bvrow[0:1, 512:1024], start=True, stop=True).then_inc(st.inc())
    block.tensor(t_bvb)
    bvb_mm = st.n

    def v_bvb(v, bvb_mm=bvb_mm):
        v = AS(v, sv)
        v._e.wait_ge(st.h, bvb_mm)
        v.tensor_copy(bvb[:, 0:512], ps[0])
        v.tensor_copy(bvb[:, 512:1024], ps[1])
    block.vector(v_bvb)
    bvb_done = sv.n

    qb4 = [pS[0][:, 0:512], pS[0][:, 512:1024], pS[1][:, 0:512], pS[1][:, 512:1024]]
    vst_free = [(sv.h, 0), (sv.h, 0)]
    vev_hist = {}
    for tt in range(NTT):
        slv = tt % 2
        bset = (tt % 2) * 2

        def t_vproj(t, tt=tt, bset=bset, gate=vev_hist.get(tt - 2, bvb_done)):
            t.wait_ge(sv.h, gate)
            for half in range(2):
                for dc in range(DC):
                    t.matmul(qb4[bset + half],
                             xbf[:, dc * T + tt * 128: dc * T + (tt + 1) * 128],
                             wvsb[:, dc * 1024 + half * 512: dc * 1024 + (half + 1) * 512],
                             start=(dc == 0), stop=(dc == DC - 1)).then_inc(st.inc())
        block.tensor(t_vproj)
        vmm = st.n

        def v_vevac(v, slv=slv, bset=bset, vmm=vmm, gate=vst_free[slv]):
            v = AS(v, sv)
            v._e.wait_ge(st.h, vmm)
            v._e.wait_ge(gate[0], gate[1])
            v.tensor_tensor(vstage[slv][:, 0:512], qb4[bset], bvb[:, 0:512], ALU.add)
            v.tensor_tensor(vstage[slv][:, 512:1024], qb4[bset + 1], bvb[:, 512:1024], ALU.add)
        block.vector(v_vevac)
        vev_hist[tt] = sv.n
        evac_v = sv.n

        def g_vstore(g, tt=tt, slv=slv, vev=evac_v):
            g.wait_ge(sv.h, vev)
            g.dma_start(out=vin.ap()[tt * 128:(tt + 1) * 128, :], in_=vstage[slv][:]).then_inc(svst[slv].inc(16), 16)
        block.gpsimd(g_vstore)
        vst_free[slv] = (svst[slv].h, svst[slv].n)

    def g_agv(g, v0=(svst[0].h, svst[0].n), v1=(svst[1].h, svst[1].n)):
        g.wait_ge(v0[0], v0[1])
        g.wait_ge(v1[0], v1[1])
        g.collective_compute("AllGather", ALU.bypass,
                             replica_groups=[[0, 1], [2, 3], [4, 5], [6, 7]],
                             ins=[vin.ap()], outs=[vout.ap()]).then_inc(scc.inc())
    block.gpsimd(g_agv)
    agv_done = scc.n

    kst_free = [(skst[0].h, 0), (skst[1].h, 0)]

    def g_wstream(j, gate_st):
        name, idx = ("wk", j) if j < H else ("wq", j - H)
        sl = j % 2

        def f(g, name=name, idx=idx, sl=sl, gate_st=gate_st):
            if gate_st:
                g.wait_ge(st.h, gate_st)
            g.dma_start(out=wqk[sl][:].rearrange("p (c m) -> p c m", m=128),
                        in_=w_aps[name][idx].rearrange(
                "(c p) m -> p c m", p=128)).then_inc(swq[sl].inc(16), 16)
        block.gpsimd(f)
        w_loaded_s[j] = (swq[sl].h, swq[sl].n)

    bank_ev = [0, 0, 0, 0]
    for h in range(H):
        slk = h % 2
        for tb in range(4):
            i4 = (h * 4 + tb) % 4

            def t_kproj(t, h=h, tb=tb, i4=i4, wl=w_loaded_s[h],
                        bank_gate=evac_v if h == 0 else None,
                        ss_gate=bank_ev[i4]):
                if tb == 0:
                    t.wait_ge(wl[0], wl[1])
                if h == 0 and bank_gate is not None:
                    t.wait_ge(sv.h, bank_gate)
                if ss_gate:
                    t.wait_ge(ss.h, ss_gate)
                for dc in range(DC):
                    t.matmul(qb4[i4],
                             wqk[h % 2][:, dc * 128:(dc + 1) * 128],
                             xbf[:, dc * T + tb * 512:dc * T + (tb + 1) * 512],
                             start=(dc == 0), stop=(dc == DC - 1)).then_inc(st.inc())
            block.tensor(t_kproj)
            kmm = st.n

            def s_kevac(s, h=h, tb=tb, slk=slk, i4=i4, kmm=kmm, gate=kst_free[slk] if tb == 0 else None):
                s = AS(s, ss)
                s._e.wait_ge(st.h, kmm)
                if gate is not None:
                    s._e.wait_ge(gate[0], gate[1])
                s.activation(kstage[slk][:, tb * 512:(tb + 1) * 512], qb4[i4],
                             AF.Identity, bias=vecs["bkT"][:, h:h + 1])
            block.scalar(s_kevac)
            bank_ev[i4] = ss.n
        kev = ss.n
        if h + 2 < 2 * H:
            g_wstream(h + 2, st.n)

        def g_kstore(g, h=h, slk=slk, kev=kev):
            g.wait_ge(ss.h, kev)
            g.dma_start(out=kin.ap()[h * 128:(h + 1) * 128, :], in_=kstage[slk][:]).then_inc(skst[slk].inc(16), 16)
        block.gpsimd(g_kstore)
        kst_free[slk] = (skst[slk].h, skst[slk].n)

        if h == 7:
            def g_agk1(g, gates=(kst_free[0], kst_free[1])):
                for sh, sn in gates:
                    g.wait_ge(sh, sn)
                g.collective_compute("AllGather", ALU.bypass,
                                     replica_groups=[[0, 1], [2, 3], [4, 5], [6, 7]],
                                     ins=[kin.ap()[0:1024, :]], outs=[kout.ap()[0:2048, :]]).then_inc(scc.inc())
            block.gpsimd(g_agk1)
            agk1_done = scc.n
    k_stored = [(skst[0].h, skst[0].n), (skst[1].h, skst[1].n)]

    def g_agk2(g, gates=tuple(k_stored)):
        for sh, sn in gates:
            g.wait_ge(sh, sn)
        g.collective_compute("AllGather", ALU.bypass,
                             replica_groups=[[0, 1], [2, 3], [4, 5], [6, 7]],
                             ins=[kin.ap()[1024:2048, :]], outs=[kout.ap()[2048:4096, :]]).then_inc(scc.inc())
    block.gpsimd(g_agk2)
    agk_done = scc.n

    for h in range(H):
        for tb in range(4):
            i4 = (h * 4 + tb) % 4

            def t_qproj(t, h=h, tb=tb, i4=i4, wl=w_loaded_s[H + h], ss_gate=bank_ev[i4]):
                if tb == 0:
                    t.wait_ge(wl[0], wl[1])
                if ss_gate:
                    t.wait_ge(ss.h, ss_gate)
                for dc in range(DC):
                    t.matmul(qb4[i4],
                             wqk[h % 2][:, dc * 128:(dc + 1) * 128],
                             xbf[:, dc * T + tb * 512:dc * T + (tb + 1) * 512],
                             start=(dc == 0), stop=(dc == DC - 1)).then_inc(st.inc())
            block.tensor(t_qproj)
            qmm = st.n

            def s_qevac(s, h=h, tb=tb, i4=i4, qmm=qmm):
                s = AS(s, ss)
                s._e.wait_ge(st.h, qmm)
                s.activation(qsb[:, h * T + tb * 512: h * T + (tb + 1) * 512], qb4[i4],
                             AF.Identity, bias=vecs["bqT"][:, h:h + 1])
            block.scalar(s_qevac)
            bank_ev[i4] = ss.n
        if H + h + 2 < 2 * H:
            g_wstream(H + h + 2, st.n)
    qproj_done = ss.n


    kh_loaded = {}
    kh_free = [(ss.h, 0), (ss.h, 0)]

    def emit_khload(h):
        slk = h % 2

        r0 = h * 128 if h < 8 else 2048 + (h - 8) * 128

        def g_kh(g, h=h, slk=slk, gate=kh_free[slk], r0=r0,
                 agw=(agk1_done if h < 8 else agk_done)):
            g.wait_ge(scc.h, agw)
            g.wait_ge(gate[0], gate[1])
            g.dma_start(out=khh[slk][:, 0:T], in_=kout.ap()[r0:r0 + 128, :]).then_inc(sxk[slk].inc(16), 16)
            g.dma_start(out=khh[slk][:, T:S], in_=kout.ap()[1024 + r0: 1024 + r0 + 128, :]).then_inc(sxk[slk].inc(16), 16)
        block.gpsimd(g_kh)
        kh_loaded[h] = (sxk[slk].h, sxk[slk].n)

    emit_khload(0)
    emit_khload(1)

    es_proj.close()
    vhat = es_attn.enter_context(nc.sbuf_tensor("l1a_vhat", [128, KC * VW], BF16))

    def v_vhones(v, gates=(k_stored[0], k_stored[1], (svst[0].h, svst[0].n), (svst[1].h, svst[1].n)),
                 qd=qproj_done):
        v = AS(v, sv)
        for sh, sn in gates:
            v._e.wait_ge(sh, sn)
        v._e.wait_ge(ss.h, qd)
        v.memset(vhat[:].rearrange("p (kc h c) -> p kc h c", h=H, c=65)[:, :, :, 64:65], 1.0)
    block.vector(v_vhones)
    vhones_done = sv.n

    def g_vhload(g, ag_done=agv_done, vhones_done=vhones_done):
        g.wait_ge(scc.h, ag_done)
        g.wait_ge(sv.h, vhones_done)
        for kc in range(KC):
            dst = vhat[:, kc * VW:(kc + 1) * VW].rearrange("p (h c) -> p h c", c=65)[:, :, 0:64]
            g.dma_start(out=dst, in_=vout.ap()[kc * 128:(kc + 1) * 128, :].rearrange(
                "p (h c) -> p h c", c=64)).then_inc(svh.inc(16), 16)
    block.gpsimd(g_vhload)
    vhat_loaded = svh.n

    QB = 4
    exp_read = [0, 0]
    pv_read = [0, 0]
    comb_done = sv.n
    oevac_done = 0
    p12s = [p1s, p2s]
    pend = None

    def make_epilogue(pv_done, h, qb):
        stg = {}

        def e_oevac():
            nonlocal oevac_done

            def v_oevac(v, pv_done=pv_done):
                v = AS(v, sv)
                v._e.wait_ge(st.h, pv_done)
                v.tensor_copy(o1s[0:65, :], pO1[0:65, 0:512])
                v.tensor_copy(o2s[0:65, :], pO2[0:65, 0:512])
            block.vector(v_oevac)
            oevac_done = sv.n

        def e_recip():
            def v_recip(v):
                v = AS(v, sv)
                v.reciprocal(o1s[64:65, :], o1s[64:65, :])
                v.reciprocal(o2s[64:65, :], o2s[64:65, :])
            block.vector(v_recip)
            stg["rdone"] = sv.n

        def e_bc():
            def t_bc(t, rdone=stg["rdone"]):
                t.wait_ge(sv.h, rdone)
                t.matmul(pG1[0:64, 0:512], ones65[64:65, :], o1s[64:65, :],
                         start=True, stop=True, tile_position=(64, 0),
                         skip_group_check=True).then_inc(st.inc())
                t.matmul(pG2[0:64, 0:512], ones65[64:65, :], o2s[64:65, :],
                         start=True, stop=True, tile_position=(64, 0),
                         skip_group_check=True).then_inc(st.inc())
            block.tensor(t_bc)
            stg["bc"] = st.n

        def e_comb2():
            nonlocal comb_done

            def v_comb2(v, h=h, qb=qb, bc_done=stg["bc"]):
                v = AS(v, sv)
                v._e.wait_ge(st.h, bc_done)
                v.tensor_tensor(o1s[0:64, :], o1s[0:64, :], pG1[:, 0:512], ALU.mult)
                v.tensor_tensor(o2s[0:64, :], o2s[0:64, :], pG2[:, 0:512], ALU.mult)
                dst = ocat[(h % 2) * 64:(h % 2) * 64 + 64,
                           (h // 2) * T + qb * 512:(h // 2) * T + (qb + 1) * 512]
                v.scalar_tensor_tensor(dst, o2s[0:64, :], vecs["lamb"][0:64, 0:1], o1s[0:64, :],
                                       ALU.mult, ALU.add)
            block.vector(v_comb2)
            comb_done = sv.n

        return {"oevac": e_oevac, "recip": e_recip, "bc": e_bc, "comb2": e_comb2}

    kh_ready = None
    for h in range(H):
        slk = h % 2
        if h == 0:
            def v_center0(v, slk=slk, ld=kh_loaded[0]):
                v = AS(v, sv)
                v._e.wait_ge(ld[0], ld[1])
                v.tensor_reduce(ksum[:, 0:1], khh[slk][:], AX.X, ALU.add)
                v.tensor_scalar(ksum[:, 0:1], ksum[:, 0:1], 1.0 / 4096.0, None, ALU.mult)
                v.tensor_scalar(khh[slk][:], khh[slk][:], ksum[:, 0:1], None, ALU.subtract)
            block.vector(v_center0)
            kh_ready = sv.n

        for qb in range(QB):
            qsl = slice(h * T + qb * 512, h * T + (qb + 1) * 512)
            def emit_qk(kc, first):
                sset = kc % 2

                def t_qk(t, kc=kc, sset=sset, first=first, slk=slk,
                         kh_ready=kh_ready, er=exp_read[sset], vh=vhat_loaded,
                         qd=qproj_done):
                    if first:
                        t.wait_ge(sv.h, kh_ready)
                        t.wait_ge(svh.h, vh)
                        t.wait_ge(ss.h, qd)
                    if er:
                        t.wait_ge(ss.h, er)
                    t.matmul(pS[sset][:, 0:512], khh[slk][0:64, kc * 128:(kc + 1) * 128],
                             qsb[0:64, qsl], start=True, stop=True,
                             tile_position=(0, 0), skip_group_check=True).then_inc(st.inc())
                    t.matmul(pS[sset][:, 512:1024], khh[slk][64:128, kc * 128:(kc + 1) * 128],
                             qsb[64:128, qsl], start=True, stop=True,
                             tile_position=(64, 0), skip_group_check=True).then_inc(st.inc())
                block.tensor(t_qk)
                return st.n

            def emit_exp(kc):
                sset = kc % 2
                qk_done = qk_count[kc]

                def s_exp(s, sset=sset, qk_done=qk_done, pr=pv_read[sset]):
                    s = AS(s, ss)
                    s._e.wait_ge(st.h, qk_done)
                    if pr:
                        s._e.wait_ge(st.h, pr)
                    s.activation(p12s[sset][:], pS[sset][:], AF.Exp, scale=0.125)
                block.scalar(s_exp)
                exp_read[sset] = ss.n
                return ss.n

            def emit_pv(kc):
                sset = kc % 2
                exp_done = exp_count[kc]

                def t_pv(t, h=h, kc=kc, sset=sset, exp_done=exp_done, oe=oevac_done):
                    t.wait_ge(ss.h, exp_done)
                    if kc == 0 and oe:
                        t.wait_ge(sv.h, oe)
                    fl = (kc == 0)
                    ll = (kc == KC - 1)
                    vsl = vhat[:, kc * VW + h * 65: kc * VW + (h + 1) * 65]
                    t.matmul(pO1[0:65, 0:512], vsl, p12s[sset][:, 0:512],
                             start=fl, stop=ll, skip_group_check=True).then_inc(st.inc())
                    t.matmul(pO2[0:65, 0:512], vsl, p12s[sset][:, 512:1024],
                             start=fl, stop=ll, skip_group_check=True).then_inc(st.inc())
                block.tensor(t_pv)
                pv_read[sset] = st.n

            qk_count = {}
            exp_count = {}
            qk_count[0] = emit_qk(0, True)
            exp_count[0] = emit_exp(0)
            if pend is not None:
                pend["oevac"]()
            qk_count[1] = emit_qk(1, False)
            exp_count[1] = emit_exp(1)
            if pend is not None:
                pend["recip"]()
            emit_pv(0)
            for kc in range(2, KC):
                qk_count[kc] = emit_qk(kc, False)
                exp_count[kc] = emit_exp(kc)
                emit_pv(kc - 1)
                if pend is not None and kc == 16:
                    pend["bc"]()
                if pend is not None and kc == 18:
                    pend["comb2"]()
                if qb == 3 and kc == 20 and h + 1 < H:
                    def v_center(v, nslk=(h + 1) % 2, ld=kh_loaded[h + 1]):
                        v = AS(v, sv)
                        v._e.wait_ge(ld[0], ld[1])
                        v.tensor_reduce(ksum[:, 0:1], khh[nslk][:], AX.X, ALU.add)
                        v.tensor_scalar(ksum[:, 0:1], ksum[:, 0:1], 1.0 / 4096.0, None, ALU.mult)
                        v.tensor_scalar(khh[nslk][:], khh[nslk][:], ksum[:, 0:1], None, ALU.subtract)
                    block.vector(v_center)
                    kh_ready = sv.n
            emit_pv(KC - 1)
            pend = make_epilogue(st.n, h, qb)
        kh_free[slk] = (st.h, st.n)
        if h + 2 < H:
            emit_khload(h + 2)
    pend["oevac"]()
    pend["recip"]()
    pend["bc"]()
    pend["comb2"]()
    attn_done = sv.n
    es_attn.close()
    yout = es.enter_context(nc.sbuf_tensor("l1a_yout", [128, DC * 512], F32))
    ytmp = es.enter_context(nc.sbuf_tensor("l1a_ytmp", [128, DC * 512], F32))
    xres = [es.enter_context(nc.sbuf_tensor(f"l1a_xres{i}", [128, DC * 512], F32)) for i in range(2)]
    ystat = ytmp[0:1, 0:2048]
    ybf8 = qsb[:, 0:DC * 512]
    ysq8 = qsb[:, DC * 512:2 * DC * 512]

    def g_wo(g, attn_done=attn_done):
        g.wait_ge(sv.h, attn_done)
        for c in range(DC):
            g.dma_start(out=wosb[:, c * 1024:(c + 1) * 1024],
                        in_=w_aps["wo"][c * 128:(c + 1) * 128, :]).then_inc(swo.inc(16), 16)
        for i in range(2):
            for c in range(DC):
                g.dma_start(out=xres[i][:, c * 512:(c + 1) * 512],
                            in_=x_in_r[:, c, i * 512:(i + 1) * 512]).then_inc(sxr[i].inc(16), 16)
    block.gpsimd(g_wo)
    wo_loaded = swo.n
    xr_loaded = {0: (sxr[0].h, sxr[0].n), 1: (sxr[1].h, sxr[1].n)}

    stats_gate = 0
    for tb in range(4):
        for dc in range(DC):
            def t_oproj(t, tb=tb, dc=dc, wo_loaded=wo_loaded, attn_done=attn_done, bank_gate=ss.n, vgate=sv.n):
                if dc == 0 and tb == 0:
                    t.wait_ge(swo.h, wo_loaded)
                    t.wait_ge(sv.h, attn_done)
                if dc == 0 and tb > 0:
                    t.wait_ge(sv.h, vgate)
                if bank_gate:
                    t.wait_ge(ss.h, bank_gate)
                for hc in range(DC):
                    t.matmul(pS[1][:, (dc % 2) * 512:(dc % 2) * 512 + 512],
                             wosb[:, hc * 1024 + dc * 128: hc * 1024 + (dc + 1) * 128],
                             ocat[:, hc * T + tb * 512:hc * T + (tb + 1) * 512],
                             start=(hc == 0), stop=(hc == DC - 1)).then_inc(st.inc())
            block.tensor(t_oproj)
            omm = st.n

            def s_yevac(s, dc=dc, omm=omm, resid_gate=sv.n, store_gate=so.n):
                s = AS(s, ss)
                s._e.wait_ge(st.h, omm)
                if resid_gate:
                    s._e.wait_ge(sv.h, resid_gate)
                if store_gate:
                    s._e.wait_ge(so.h, store_gate)
                s.activation(yout[:, dc * 512:(dc + 1) * 512],
                             pS[1][:, (dc % 2) * 512:(dc % 2) * 512 + 512],
                             AF.Identity, bias=vecs["bo"][:, dc:dc + 1])
            block.scalar(s_yevac)
        yproj = ss.n

        def v_c1all(v, yproj=yproj):
            v = AS(v, sv)
            v._e.wait_ge(ss.h, yproj)
            for dc in range(DC):
                v.tensor_copy(ybf8[:, dc * 512:(dc + 1) * 512], yout[:, dc * 512:(dc + 1) * 512])
        block.vector(v_c1all)
        nca = sv.n

        def s_c1all(s, yproj=yproj):
            s = AS(s, ss)
            for dc in range(DC):
                s.activation(ysq8[:, dc * 512:(dc + 1) * 512],
                             yout[:, dc * 512:(dc + 1) * 512], AF.Square)
        block.scalar(s_c1all)
        nsq = ss.n

        def t_st1all(t, nca=nca, nsq=nsq, sg=stats_gate):
            t.wait_ge(sv.h, nca)
            t.wait_ge(ss.h, nsq)
            if sg:
                t.wait_ge(sv.h, sg)
            for dc in range(DC):
                t.matmul(pG1[0:1, 0:512], onesb[:], ybf8[:, dc * 512:(dc + 1) * 512],
                         start=(dc == 0), stop=(dc == DC - 1), skip_group_check=True).then_inc(st.inc())
            for dc in range(DC):
                t.matmul(pG2[0:1, 0:512], onesb[:], ysq8[:, dc * 512:(dc + 1) * 512],
                         start=(dc == 0), stop=(dc == DC - 1), skip_group_check=True).then_inc(st.inc())
        block.tensor(t_st1all)
        nstat = st.n

        def v_mu1(v, nstat=nstat):
            v = AS(v, sv)
            v._e.wait_ge(st.h, nstat)
            v.tensor_scalar(ystat[:, 0:512], pG1[0:1, 0:512], 1.0 / 1024.0, None, ALU.mult)
            v.tensor_scalar(ystat[:, 512:1024], pG2[0:1, 0:512], 1.0 / 1024.0, None, ALU.mult)
            v.tensor_tensor(ystat[:, 1024:1536], ystat[:, 0:512], ystat[:, 0:512], ALU.mult)
            v.tensor_tensor(ystat[:, 1024:1536], ystat[:, 512:1024], ystat[:, 1024:1536], ALU.subtract)
            v.tensor_scalar(ystat[:, 1024:1536], ystat[:, 1024:1536], LN_EPS, None, ALU.add)
        block.vector(v_mu1)
        nvar = sv.n

        def s_sd1(s, nvar=nvar):
            s = AS(s, ss)
            s._e.wait_ge(sv.h, nvar)
            s.activation(ystat[:, 1536:2048], ystat[:, 1024:1536], AF.Sqrt)
        block.scalar(s_sd1)
        nstd = ss.n

        def v_rs1(v, nstd=nstd):
            v = AS(v, sv)
            v._e.wait_ge(ss.h, nstd)
            v.reciprocal(ystat[:, 512:1024], ystat[:, 1536:2048])
        block.vector(v_rs1)
        nrst = sv.n

        def t_bc1(t, nrst=nrst, sg=stats_gate):
            t.wait_ge(sv.h, nrst)
            t.matmul(ps[0], onesr[:], ystat[0:1, 0:512], start=True, stop=True).then_inc(st.inc())
            t.matmul(ps[1], onesr[:], ystat[0:1, 512:1024], start=True, stop=True).then_inc(st.inc())
        block.tensor(t_bc1)
        nbc = st.n

        def v_ap1(v, tb=tb, nbc=nbc, xr=xr_loaded[tb]):
            v = AS(v, sv)
            v._e.wait_ge(st.h, nbc)
            v._e.wait_ge(xr[0], xr[1])
            for dc in range(DC):
                sl = slice(dc * 512, (dc + 1) * 512)
                v.tensor_tensor(ytmp[:, sl], yout[:, sl], ps[0], ALU.subtract)
            for dc in range(DC):
                sl = slice(dc * 512, (dc + 1) * 512)
                v.tensor_tensor(ytmp[:, sl], ytmp[:, sl], ps[1], ALU.mult)
            for dc in range(DC):
                sl = slice(dc * 512, (dc + 1) * 512)
                v.tensor_scalar(xres[tb % 2][:, sl], xres[tb % 2][:, sl],
                                vecs["bp"][:, dc:dc + 1], None, ALU.add)
                v.scalar_tensor_tensor(yout[:, sl], ytmp[:, sl], vecs["gp"][:, dc:dc + 1],
                                       xres[tb % 2][:, sl], ALU.mult, ALU.add)
        block.vector(v_ap1)
        nap1 = sv.n

        def v_c2all(v, nap1=nap1):
            v = AS(v, sv)
            for dc in range(DC):
                v.tensor_copy(ybf8[:, dc * 512:(dc + 1) * 512], yout[:, dc * 512:(dc + 1) * 512])
        block.vector(v_c2all)
        nca = sv.n

        def s_c2all(s, nap1=nap1):
            s = AS(s, ss)
            s._e.wait_ge(sv.h, nap1)
            for dc in range(DC):
                s.activation(ysq8[:, dc * 512:(dc + 1) * 512],
                             yout[:, dc * 512:(dc + 1) * 512], AF.Square)
        block.scalar(s_c2all)
        nsq = ss.n

        def t_st2all(t, nca=nca, nsq=nsq):
            t.wait_ge(sv.h, nca)
            t.wait_ge(ss.h, nsq)
            for dc in range(DC):
                t.matmul(pG1[0:1, 0:512], onesb[:], ybf8[:, dc * 512:(dc + 1) * 512],
                         start=(dc == 0), stop=(dc == DC - 1), skip_group_check=True).then_inc(st.inc())
            for dc in range(DC):
                t.matmul(pG2[0:1, 0:512], onesb[:], ysq8[:, dc * 512:(dc + 1) * 512],
                         start=(dc == 0), stop=(dc == DC - 1), skip_group_check=True).then_inc(st.inc())
        block.tensor(t_st2all)
        nstat2 = st.n

        def v_mu2(v, nstat2=nstat2):
            v = AS(v, sv)
            v._e.wait_ge(st.h, nstat2)
            v.tensor_scalar(ystat[:, 0:512], pG1[0:1, 0:512], 1.0 / 1024.0, None, ALU.mult)
            v.tensor_scalar(ystat[:, 512:1024], pG2[0:1, 0:512], 1.0 / 1024.0, None, ALU.mult)
            v.tensor_tensor(ystat[:, 1024:1536], ystat[:, 0:512], ystat[:, 0:512], ALU.mult)
            v.tensor_tensor(ystat[:, 1024:1536], ystat[:, 512:1024], ystat[:, 1024:1536], ALU.subtract)
            v.tensor_scalar(ystat[:, 1024:1536], ystat[:, 1024:1536], LN_EPS, None, ALU.add)
        block.vector(v_mu2)
        nvar2 = sv.n

        def s_sd2(s, nvar2=nvar2):
            s = AS(s, ss)
            s._e.wait_ge(sv.h, nvar2)
            s.activation(ystat[:, 1536:2048], ystat[:, 1024:1536], AF.Sqrt)
        block.scalar(s_sd2)
        nstd2 = ss.n

        def v_rs2b(v, nstd2=nstd2):
            v = AS(v, sv)
            v._e.wait_ge(ss.h, nstd2)
            v.reciprocal(ystat[:, 512:1024], ystat[:, 1536:2048])
        block.vector(v_rs2b)
        nrst2 = sv.n

        def t_bc2(t, nrst2=nrst2):
            t.wait_ge(sv.h, nrst2)
            t.matmul(ps[0], onesr[:], ystat[0:1, 0:512], start=True, stop=True).then_inc(st.inc())
            t.matmul(ps[1], onesr[:], ystat[0:1, 512:1024], start=True, stop=True).then_inc(st.inc())
        block.tensor(t_bc2)
        nbc2 = st.n

        def v_ap2(v, tb=tb, nbc2=nbc2, store_gate=so.n):
            v = AS(v, sv)
            v._e.wait_ge(st.h, nbc2)
            if store_gate:
                v._e.wait_ge(so.h, store_gate)
            for dc in range(DC):
                sl = slice(dc * 512, (dc + 1) * 512)
                v.tensor_tensor(ytmp[:, sl], yout[:, sl], ps[0], ALU.subtract)
            for dc in range(DC):
                sl = slice(dc * 512, (dc + 1) * 512)
                v.tensor_tensor(yout[:, sl], ytmp[:, sl], ps[1], ALU.mult)
        block.vector(v_ap2)
        nap2 = sv.n
        stats_gate = nap2

        def s_store(s, tb=tb, nap2=nap2):
            s.wait_ge(sv.h, nap2)
            for dc in range(DC):
                s.dma_start(out=x_out_r[:, dc, tb * 512:(tb + 1) * 512],
                            in_=yout[:, dc * 512:(dc + 1) * 512]).then_inc(so.inc(16), 16)
        block.sync(s_store)

        if tb + 2 < 4:
            def g_xr(g, tb=tb, gate=sv.n):
                g.wait_ge(sv.h, gate)
                for c in range(DC):
                    g.dma_start(out=xres[tb % 2][:, c * 512:(c + 1) * 512],
                                in_=x_in_r[:, c, (tb + 2) * 512:(tb + 3) * 512]).then_inc(sxr[tb % 2].inc(16), 16)
            block.gpsimd(g_xr)
            xr_loaded[tb + 2] = (sxr[tb % 2].h, sxr[tb % 2].n)

    def g_fin(g):
        g.wait_ge(so.h, so.n)
    block.gpsimd(g_fin)


def build_l1a_launch():
    nc = bass.Bass("TRN2", target_bir_lowering=False)
    x_in = nc.dram_tensor("x_in", [D, T], F32, kind="ExternalInput").ap()
    x_out = nc.dram_tensor("x_out", [D, T], F32, kind="ExternalOutput").ap()
    w_aps = dict(
        wq=nc.dram_tensor("wq", [16, 1024, 128], F32, kind="ExternalInput").ap(),
        wk=nc.dram_tensor("wk", [16, 1024, 128], F32, kind="ExternalInput").ap(),
        wvT=nc.dram_tensor("wvT", [1024, 1024], F32, kind="ExternalInput").ap(),
        wo=nc.dram_tensor("wo", [1024, 1024], F32, kind="ExternalInput").ap(),
        bqT=nc.dram_tensor("bqT", [128, 16], F32, kind="ExternalInput").ap(),
        bkT=nc.dram_tensor("bkT", [128, 16], F32, kind="ExternalInput").ap(),
        bvrow=nc.dram_tensor("bvrow", [1, 1024], F32, kind="ExternalInput").ap(),
        bo=nc.dram_tensor("bo", [128, 8], F32, kind="ExternalInput").ap(),
        gamma=nc.dram_tensor("gamma", [128, 8], F32, kind="ExternalInput").ap(),
        beta=nc.dram_tensor("beta", [128, 8], F32, kind="ExternalInput").ap(),
        lamb=nc.dram_tensor("lamb", [128, 1], F32, kind="ExternalInput").ap(),
        onesr=nc.dram_tensor("onesr", [1, 128], F32, kind="ExternalInput").ap(),
    )
    scratch = dict(
        cc_kin=nc.dram_tensor("cc_kin", [2048, 2048], BF16),
        cc_kout=nc.dram_tensor("cc_kout", [4096, 2048], BF16),
        cc_vin=nc.dram_tensor("cc_vin", [2048, 1024], BF16),
        cc_vout=nc.dram_tensor("cc_vout", [4096, 1024], BF16),
    )
    es = contextlib.ExitStack()
    with es:
        block = DirectBlock(nc)
        es.enter_context(nc.Block())
        emit_l1_attn(nc, es, block, x_in, x_out, w_aps, scratch)
    return nc


def host_l1a_inputs(weights):
    wv = np.asarray(weights["L1_attn_Wv"], np.float32)
    return dict(
        wq=np.ascontiguousarray(weights["L1_attn_Wq"], dtype=np.float32),
        wk=np.ascontiguousarray(weights["L1_attn_Wk"], dtype=np.float32),
        wvT=np.ascontiguousarray(wv.transpose(1, 0, 2).reshape(1024, 1024)),
        wo=np.ascontiguousarray(weights["L1_attn_Wo"], dtype=np.float32),
        bqT=np.ascontiguousarray(np.asarray(weights["L1_attn_bq"], np.float32).T),
        bkT=np.ascontiguousarray(np.asarray(weights["L1_attn_bk"], np.float32).T),
        bvrow=np.ascontiguousarray(np.asarray(weights["L1_attn_bv"], np.float32).reshape(1, 1024)),
        bo=pack_vec(weights["L1_attn_bo"]),
        gamma=pack_vec(weights["L1_attn_gamma"]),
        beta=pack_vec(weights["L1_attn_beta"]),
        lamb=np.ascontiguousarray(np.broadcast_to(
            np.asarray(weights["L1_attn_lam"], np.float32).reshape(1, 1), (128, 1))),
        onesr=np.ones((1, 128), np.float32),
    )


def run_l1a(x_full, weights):
    nc = build_l1a_launch()
    base = host_l1a_inputs(weights)
    in_maps = []
    for c in range(NCORES):
        b, hf = c // 2, c % 2
        xs = np.ascontiguousarray(x_full[b, hf * T:(hf + 1) * T, :])
        m = dict(base)
        m["x_in"] = fmT(xs)
        in_maps.append(m)
    res = run_bass_kernel_spmd(nc, in_maps, core_ids=list(range(NCORES)))
    out = np.zeros((B, S, D), np.float32)
    for c in range(NCORES):
        b, hf = c // 2, c % 2
        out[b, hf * T:(hf + 1) * T, :] = np.asarray(res.results[c]["x_out"], np.float32).T
    return out



def emit_l1_ffn(nc, es, block, x_in, x_out, w_aps, scratch, ses=None):
    ses = es if ses is None else ses
    """Routed FFN block.

    w_aps: lw1 [1024,512], lw2 [512,1024], hw1 [1024,4096], hw2 [4096,1024],
    rt [1024, 1] (routing token, fm), identb [128,128], ltri [128,128]
    (ltri[k,m]=1 if k<=m), halfsel [128,32] (1 on own-half columns);
    lb1 [128,4], lb2/lgam/hgam/hb2 [128,8], hb1 [128,32]  (all f32)
    scratch: s_in [128,16] f32 dram, s_out [256,16] f32 dram
    """
    NTT = T // 128
    TBL = 256
    NTB = T // TBL
    IH = 4096
    IHC = IH // 128
    LOGK = float(np.log(np.float32(1152.0)))

    whw1 = es.enter_context(nc.sbuf_tensor("l1f_whw1", [128, DC * IH], BF16))
    whw2 = es.enter_context(nc.sbuf_tensor("l1f_whw2", [128, IHC * D], BF16))
    wl1 = es.enter_context(nc.sbuf_tensor("l1f_wl1", [128, DC * 512], BF16))
    wl2 = es.enter_context(nc.sbuf_tensor("l1f_wl2", [128, 4 * D], BF16))
    identb = es.enter_context(nc.sbuf_tensor("l1f_identb", [128, 128], BF16))
    ltri = es.enter_context(nc.sbuf_tensor("l1f_ltri", [128, 128], BF16))
    halfsel = es.enter_context(nc.sbuf_tensor("l1f_halfsel", [128, 32], F32))
    rtsb = es.enter_context(nc.sbuf_tensor("l1f_rt", [128, 8], F32))
    vecs = {v: es.enter_context(nc.sbuf_tensor("l1f_" + v, [128, 32], F32))
            for v in ["lb1", "lb2", "lgam", "hgam", "hb1", "hb2"]}
    onesb = es.enter_context(nc.sbuf_tensor("l1f_onesb", [128, 1], BF16))
    onesrb = es.enter_context(nc.sbuf_tensor("l1f_onesrb", [1, 128], BF16))
    onesr = es.enter_context(nc.sbuf_tensor("l1f_onesr", [1, 128], F32))
    sfull = es.enter_context(nc.sbuf_tensor("l1f_sfull", [128, 32], F32))
    key = es.enter_context(nc.sbuf_tensor("l1f_key", [128, 32], F32))
    ebuf = es.enter_context(nc.sbuf_tensor("l1f_ebuf", [128, 32], BF16))
    rscl = es.enter_context(nc.sbuf_tensor("l1f_rscl", [1, 8], F32))
    mask = es.enter_context(nc.sbuf_tensor("l1f_mask", [128, 32], F32))
    mwork = es.enter_context(nc.sbuf_tensor("l1f_mwork", [128, 32], F32))
    mrow2 = es.enter_context(nc.sbuf_tensor("l1f_mrow2", [1, 64], F32))
    rtb = es.enter_context(nc.sbuf_tensor("l1f_rtb", [128, 8], BF16))
    mask_bf = es.enter_context(nc.sbuf_tensor("l1f_maskbf", [128, 16], BF16))
    sstage = es.enter_context(nc.sbuf_tensor("l1f_sstage", [128, 16], F32))
    xbt = es.enter_context(nc.sbuf_tensor("l1f_xbt", [128, 1024], BF16))
    xres = [es.enter_context(nc.sbuf_tensor(f"l1f_xres{i}", [128, DC * TBL], F32)) for i in range(2)]
    xg = es.enter_context(nc.sbuf_tensor("l1f_xg", [128, DC * TBL], F32))
    xlb = es.enter_context(nc.sbuf_tensor("l1f_xlb", [128, DC * TBL], BF16))
    xhb = es.enter_context(nc.sbuf_tensor("l1f_xhb", [128, DC * TBL], BF16))
    lhb = es.enter_context(nc.sbuf_tensor("l1f_lhb", [128, 4 * TBL], BF16))
    lacc = es.enter_context(nc.sbuf_tensor("l1f_lacc", [128, DC * TBL], F32))
    hv1 = es.enter_context(nc.sbuf_tensor("l1f_hv1", [128, IHC * TBL], BF16))
    yacc = lacc
    ytmp = xg
    ystat = ytmp[0:1, 0:4 * TBL]
    ybf = es.enter_context(nc.sbuf_tensor("l1f_ybf", [128, TBL], BF16))
    ysq = xlb[:, 0:TBL]
    ybf8 = xhb
    ysq8 = xlb

    pA = es.enter_context(nc.psum_tensor("l1f_pA", [128, 512], F32))
    pB = es.enter_context(nc.psum_tensor("l1f_pB", [128, 512], F32))
    pC = es.enter_context(nc.psum_tensor("l1f_pC", [128, 512], F32))
    pD = es.enter_context(nc.psum_tensor("l1f_pD", [128, 512], F32))
    pE = es.enter_context(nc.psum_tensor("l1f_pE", [128, 512], F32))
    pF = es.enter_context(nc.psum_tensor("l1f_pF", [128, 512], F32))
    pG = es.enter_context(nc.psum_tensor("l1f_pG", [1, 512], F32))
    pH = es.enter_context(nc.psum_tensor("l1f_pH", [1, 512], F32))

    sgw = Sem(ses, nc, "l1f_sgw")
    sxt = Sem(ses, nc, "l1f_sxt")
    sss = Sem(ses, nc, "l1f_sss")
    sxr = [Sem(ses, nc, f"l1f_sxr{i}") for i in range(2)]
    scc = Sem(ses, nc, "l1f_scc")
    sld = Sem(ses, nc, "l1f_sld")
    st = Sem(ses, nc, "l1f_st")
    sv = Sem(ses, nc, "l1f_sv")
    ss = Sem(ses, nc, "l1f_ss")
    so = Sem(ses, nc, "l1f_so")

    x_in_r = x_in.rearrange("(c p) t -> p c t", p=128)
    x_out_r = x_out.rearrange("(c p) t -> p c t", p=128)
    s_in, s_out = scratch["s_in"], scratch["s_out"]

    sgh = [Sem(ses, nc, f"l1f_sgh{i}") for i in range(2)]

    def g_loads(g):
        for c in range(DC):
            g.dma_start(out=wl1[:, c * 512:(c + 1) * 512],
                        in_=w_aps["lw1"][c * 128:(c + 1) * 128, :]).then_inc(sgw.inc(16), 16)
        for c in range(4):
            g.dma_start(out=wl2[:, c * D:(c + 1) * D],
                        in_=w_aps["lw2"][c * 128:(c + 1) * 128, :]).then_inc(sgw.inc(16), 16)
        g.dma_start(out=identb[:], in_=w_aps["identb"]).then_inc(sgw.inc(16), 16)
        g.dma_start(out=ltri[:], in_=w_aps["ltri"]).then_inc(sgw.inc(16), 16)
        g.dma_start(out=halfsel[:], in_=w_aps["halfsel"]).then_inc(sgw.inc(16), 16)
        g.dma_start(out=rtsb[:], in_=w_aps["rt"]).then_inc(sgw.inc(16), 16)
        for v, wd in [("lb1", 4), ("lb2", 8), ("lgam", 8), ("hgam", 8), ("hb1", 32), ("hb2", 8)]:
            g.dma_start(out=vecs[v][:, 0:wd], in_=w_aps[v]).then_inc(sgw.inc(16), 16)
    block.gpsimd(g_loads)
    loads_done = sgw.n

    def g_loadh(g):
        for c in range(DC):
            g.dma_start(out=whw1[:, c * IH:(c + 1) * IH],
                        in_=w_aps["hw1"][c * 128:(c + 1) * 128, :]).then_inc(sgh[0].inc(16), 16)
        for c in range(IHC):
            g.dma_start(out=whw2[:, c * D:(c + 1) * D],
                        in_=w_aps["hw2"][c * 128:(c + 1) * 128, :]).then_inc(sgh[1].inc(16), 16)
    block.gpsimd(g_loadh)
    h1_ld = sgh[0].n
    h2_ld = sgh[1].n

    def v_init(v):
        v = AS(v, sv)
        v._e.wait_ge(sgw.h, loads_done)
        v.memset(onesb[:], 1.0)
        v.memset(onesrb[:], 1.0)
        v.memset(onesr[:], 1.0)
        v.tensor_copy(rtb[:], rtsb[:])
    block.vector(v_init)
    init_done = sv.n

    xs32 = xres[0][:, 0:1024]
    for tt in range(NTT):
        def g_xt(g, tt=tt, gate=st.n, ld=loads_done):
            g.wait_ge(sgw.h, ld)
            if gate:
                g.wait_ge(st.h, gate)
            for c in range(DC):
                g.dma_start(out=xs32[:, c * 128:(c + 1) * 128],
                            in_=x_in_r[:, c, tt * 128:(tt + 1) * 128]).then_inc(sxt.inc(16), 16)
        block.gpsimd(g_xt)
        xt_ld = sxt.n

        def t_s(t, tt=tt, xt_ld=xt_ld, init_done=init_done, evac_gate=sv.n):
            t.wait_ge(sxt.h, xt_ld)
            t.wait_ge(sv.h, max(init_done, evac_gate))
            for dc in range(DC):
                t.matmul(pF[:, tt:tt + 1], xs32[:, dc * 128:(dc + 1) * 128],
                         rtsb[:, dc:dc + 1], start=(dc == 0), stop=(dc == DC - 1),
                         skip_group_check=True).then_inc(st.inc())
        block.tensor(t_s)
    s_mm = st.n

    def v_sevac(v, s_mm=s_mm):
        v = AS(v, sv)
        v._e.wait_ge(st.h, s_mm)
        v.tensor_copy(sstage[:], pF[:, 0:16])
    block.vector(v_sevac)
    s_done = sv.n

    def g_sag(g, s_done=s_done):
        g.wait_ge(sv.h, s_done)
        g.dma_start(out=s_in.ap(), in_=sstage[:]).then_inc(sss.inc(16), 16)
        g.wait_ge(sss.h, sss.n)
        g.collective_compute("AllGather", ALU.bypass,
                             replica_groups=[[0, 1], [2, 3], [4, 5], [6, 7]],
                             ins=[s_in.ap()], outs=[s_out.ap()]).then_inc(scc.inc())
    block.gpsimd(g_sag)
    s_ag = scc.n

    route_q = []

    def rq(fn):
        route_q.append(fn)

    def pump(n=3):
        for _ in range(min(n, len(route_q))):
            route_q.pop(0)()

    def _emit_routing():
        def g_sfl(g, s_ag=s_ag):
            g.wait_ge(scc.h, s_ag)
            g.dma_start(out=sfull[:, 0:16], in_=s_out.ap()[0:128, :]).then_inc(sld.inc(16), 16)
            g.dma_start(out=sfull[:, 16:32], in_=s_out.ap()[128:256, :]).then_inc(sld.inc(16), 16)
        rq(lambda: block.gpsimd(g_sfl))
        sfull_n = sld.n + 32

        def v_r0(v, n=sfull_n):
            v = AS(v, sv)
            v._e.wait_ge(sld.h, n)
            v.memset(rscl[:, 0:1], LOGK - float(np.log(np.float32(4096.0))))
        rq(lambda: block.vector(v_r0))

        for it in range(50):
            def step(it=it):
                def t_bc(t, vgate=sv.n):
                    t.wait_ge(sv.h, vgate)
                    t.matmul(pF[:, 20:21], onesr[:], rscl[0:1, 0:1], start=True, stop=True,
                             skip_group_check=True).then_inc(st.inc())
                block.tensor(t_bc)
                nbc = st.n

                def v_m(v, nbc=nbc):
                    v = AS(v, sv)
                    v._e.wait_ge(st.h, nbc)
                    v.tensor_scalar(key[:], sfull[:], pF[:, 20:21], 0.0, ALU.add, ALU.min)
                block.vector(v_m)
                nm = sv.n

                def s_e(s, nm=nm):
                    s = AS(s, ss)
                    s._e.wait_ge(sv.h, nm)
                    s.activation(ebuf[:], key[:], AF.Exp)
                block.scalar(s_e)
                ne = ss.n

                def t_sum(t, ne=ne):
                    t.wait_ge(ss.h, ne)
                    t.matmul(pF[0:1, 32:64], onesb[:], ebuf[:], start=True, stop=True,
                             skip_group_check=True).then_inc(st.inc())
                block.tensor(t_sum)
                nsum = st.n

                def v_red(v, nsum=nsum):
                    v = AS(v, sv)
                    v._e.wait_ge(st.h, nsum)
                    v.tensor_reduce(rscl[:, 4:5], pF[0:1, 32:64], AX.X, ALU.add)
                block.vector(v_red)
                nred = sv.n

                def s_ln(s, nred=nred):
                    s = AS(s, ss)
                    s._e.wait_ge(sv.h, nred)
                    s.activation(rscl[:, 5:6], rscl[:, 4:5], AF.Ln)
                block.scalar(s_ln)
                nln = ss.n

                def v_up(v, nln=nln):
                    v = AS(v, sv)
                    v._e.wait_ge(ss.h, nln)
                    v.tensor_scalar(rscl[:, 5:6], rscl[:, 5:6], -1.0, LOGK, ALU.mult, ALU.add)
                    v.tensor_tensor(rscl[:, 0:1], rscl[:, 0:1], rscl[:, 5:6], ALU.add)
                block.vector(v_up)
            rq(step)

        def fin_keys():
            def t_bcf(t, vgate=sv.n):
                t.wait_ge(sv.h, vgate)
                t.matmul(pF[:, 20:21], onesr[:], rscl[0:1, 0:1], start=True, stop=True,
                         skip_group_check=True).then_inc(st.inc())
            block.tensor(t_bcf)
            nbcf = st.n

            def v_kf(v, nbcf=nbcf):
                v = AS(v, sv)
                v._e.wait_ge(st.h, nbcf)
                v.tensor_scalar(key[:], sfull[:], pF[:, 20:21], 0.0, ALU.add, ALU.min)
                v.memset(rscl[:, 1:2], -120.0)
                v.memset(rscl[:, 2:3], 1.0)
            block.vector(v_kf)
        rq(fin_keys)

        for it in range(40):
            def bstep(it=it):
                def v_mid(v):
                    v = AS(v, sv)
                    v.tensor_tensor(rscl[:, 4:5], rscl[:, 1:2], rscl[:, 2:3], ALU.add)
                    v.tensor_scalar(rscl[:, 4:5], rscl[:, 4:5], 0.5, None, ALU.mult)
                block.vector(v_mid)
                nmid = sv.n

                def t_bcm(t, nmid=nmid):
                    t.wait_ge(sv.h, nmid)
                    t.matmul(pF[:, 21:22], onesr[:], rscl[0:1, 4:5], start=True, stop=True,
                             skip_group_check=True).then_inc(st.inc())
                block.tensor(t_bcm)
                nbcm = st.n

                def v_cmp(v, nbcm=nbcm):
                    v = AS(v, sv)
                    v._e.wait_ge(st.h, nbcm)
                    v.tensor_scalar(ebuf[:], key[:], pF[:, 21:22], None, ALU.is_gt)
                block.vector(v_cmp)
                ncmp = sv.n

                def t_cnt(t, ncmp=ncmp):
                    t.wait_ge(sv.h, ncmp)
                    t.matmul(pF[0:1, 32:64], onesb[:], ebuf[:], start=True, stop=True,
                             skip_group_check=True).then_inc(st.inc())
                block.tensor(t_cnt)
                ncnt = st.n

                def v_upd(v, ncnt=ncnt):
                    v = AS(v, sv)
                    v._e.wait_ge(st.h, ncnt)
                    v.tensor_reduce(rscl[:, 3:4], pF[0:1, 32:64], AX.X, ALU.add)
                    v.tensor_scalar(rscl[:, 5:6], rscl[:, 3:4], 1024.0, None, ALU.is_ge)
                    v.tensor_tensor(rscl[:, 6:7], rscl[:, 4:5], rscl[:, 1:2], ALU.subtract)
                    v.tensor_tensor(rscl[:, 6:7], rscl[:, 5:6], rscl[:, 6:7], ALU.mult)
                    v.tensor_tensor(rscl[:, 1:2], rscl[:, 1:2], rscl[:, 6:7], ALU.add)
                    v.tensor_tensor(rscl[:, 6:7], rscl[:, 4:5], rscl[:, 2:3], ALU.subtract)
                    v.tensor_tensor(rscl[:, 7:8], rscl[:, 5:6], rscl[:, 6:7], ALU.mult)
                    v.tensor_tensor(rscl[:, 6:7], rscl[:, 6:7], rscl[:, 7:8], ALU.subtract)
                    v.tensor_tensor(rscl[:, 2:3], rscl[:, 2:3], rscl[:, 6:7], ALU.add)
                block.vector(v_upd)
            rq(bstep)

        def maskp():
            def t_bch(t, vgate=sv.n):
                t.wait_ge(sv.h, vgate)
                t.matmul(pF[:, 20:21], onesr[:], rscl[0:1, 2:3], start=True, stop=True,
                         skip_group_check=True).then_inc(st.inc())
                t.matmul(pF[:, 21:22], onesr[:], rscl[0:1, 1:2], start=True, stop=True,
                         skip_group_check=True).then_inc(st.inc())
            block.tensor(t_bch)
            nbch = st.n

            def v_tc(v, nbch=nbch):
                v = AS(v, sv)
                v._e.wait_ge(st.h, nbch)
                v.tensor_scalar(mask[:], key[:], pF[:, 20:21], None, ALU.is_gt)
                v.tensor_scalar(mwork[:], key[:], pF[:, 21:22], None, ALU.is_gt)
                v.tensor_tensor(mwork[:], mwork[:], mask[:], ALU.subtract)
                v.tensor_copy(ebuf[:], mwork[:])
            block.vector(v_tc)
            ntc = sv.n

            def t_pref(t, ntc=ntc):
                t.wait_ge(sv.h, ntc)
                t.matmul(pF[:, 64:96], ltri[:], ebuf[:], start=True, stop=True,
                         skip_group_check=True).then_inc(st.inc())
                t.matmul(pF[0:1, 32:64], onesb[:], ebuf[:], start=True, stop=True,
                         skip_group_check=True).then_inc(st.inc())
            block.tensor(t_pref)
            npref = st.n

            def v_chi(v, npref=npref):
                v = AS(v, sv)
                v._e.wait_ge(st.h, npref)
                v.tensor_copy(ebuf[:], mask[:])
            block.vector(v_chi)
            nchi = sv.n

            def t_chi(t, nchi=nchi):
                t.wait_ge(sv.h, nchi)
                t.matmul(pF[0:1, 128:160], onesb[:], ebuf[:], start=True, stop=True,
                         skip_group_check=True).then_inc(st.inc())
            block.tensor(t_chi)
            nchm = st.n

            def v_rank(v, nchm=nchm):
                v = AS(v, sv)
                v._e.wait_ge(st.h, nchm)
                v.tensor_reduce(rscl[:, 3:4], pF[0:1, 128:160], AX.X, ALU.add)
                v.tensor_copy(mrow2[0:1, 32:64], pF[0:1, 32:64])
                v.tensor_tensor_scan(mrow2[0:1, 0:32], mrow2[0:1, 32:64], mrow2[0:1, 32:64], 0.0,
                                     ALU.add, ALU.bypass)
                v.tensor_tensor(mrow2[0:1, 32:64], mrow2[0:1, 0:32], mrow2[0:1, 32:64], ALU.subtract)
                v.tensor_scalar(mrow2[0:1, 32:64], mrow2[0:1, 32:64], rscl[:, 3:4], None, ALU.add)
            block.vector(v_rank)
            nrank = sv.n

            def t_bcr(t, nrank=nrank):
                t.wait_ge(sv.h, nrank)
                t.matmul(pF[:, 96:128], onesr[:], mrow2[0:1, 32:64], start=True, stop=True,
                         skip_group_check=True).then_inc(st.inc())
            block.tensor(t_bcr)
            nbcr = st.n

            def v_mask(v, nbcr=nbcr):
                v = AS(v, sv)
                v._e.wait_ge(st.h, nbcr)
                v.tensor_copy(key[:], pF[:, 64:96])
                v.tensor_tensor(key[:], key[:], pF[:, 96:128], ALU.add)
                v.tensor_scalar(key[:], key[:], 1024.5, None, ALU.is_lt)
                v.tensor_tensor(mwork[:], mwork[:], key[:], ALU.mult)
                v.tensor_tensor(mask[:], mask[:], mwork[:], ALU.add)
                v.tensor_tensor(mask[:], mask[:], halfsel[:], ALU.mult)
                v.tensor_tensor(mask[:, 0:16], mask[:, 0:16], mask[:, 16:32], ALU.add)
                v.tensor_copy(mask_bf[:], mask[:, 0:16])
            block.vector(v_mask)
        rq(maskp)

        for tt in range(NTT):
            def mrow_step(tt=tt):
                def t_mt(t, tt=tt, evg=sv.n):
                    t.wait_ge(sv.h, evg)
                    t.matmul(pF[0:1, 160:288], mask_bf[:, tt:tt + 1], identb[:], start=True, stop=True,
                             skip_group_check=True).then_inc(st.inc())
                block.tensor(t_mt)
                nmt = st.n

                def v_mt(v, tt=tt, nmt=nmt, stg=smr.n):
                    v = AS(v, sv)
                    v._e.wait_ge(st.h, nmt)
                    if stg:
                        v._e.wait_ge(smr.h, stg)
                    v.tensor_copy(mrowst[:], pF[0:1, 160:288])
                block.vector(v_mt)
                nvm = sv.n

                def g_mt(g, tt=tt, nvm=nvm):
                    g.wait_ge(sv.h, nvm)
                    g.dma_start(out=mrow_hbm.ap()[0:1, tt * 128:(tt + 1) * 128],
                                in_=mrowst[:]).then_inc(smr.inc(16), 16)
                block.gpsimd(g_mt)
            rq(mrow_step)

    mrowst = xbt[0:1, 0:128]
    mrow_tb = xbt[0:1, 128:384]
    smr = Sem(ses, nc, "l1f_smr")
    sml = Sem(ses, nc, "l1f_sml")
    mrow_hbm = scratch["mrow_hbm"]
    _emit_routing()
    mrow_done_total = 16 * 16

    def g_xres(tb, sl, gate):
        def f(g_, tb=tb, sl=sl, gate=gate):
            g_.wait_ge(gate[0], gate[1])
            for c in range(DC):
                g_.dma_start(out=xres[sl][:, c * TBL:(c + 1) * TBL],
                             in_=x_in_r[:, c, tb * TBL:(tb + 1) * TBL]).then_inc(sxr[sl].inc(16), 16)
        block.gpsimd(f)
        return (sxr[sl].h, sxr[sl].n)

    xr_ld = [None] * NTB
    xr_ld[0] = g_xres(0, 0, (st.h, s_mm))
    xr_ld[1] = g_xres(1, 1, (st.h, s_mm))

    nst2_prev = 0
    for tb in range(NTB):
        sl = tb % 2
        def s_sqall(s, ld=xr_ld[tb], pv=nst2_prev):
            s = AS(s, ss)
            s._e.wait_ge(ld[0], ld[1])
            if pv:
                s._e.wait_ge(st.h, pv)
            for dc in range(DC):
                s.activation(ysq8[:, dc * TBL:(dc + 1) * TBL],
                             xres[sl][:, dc * TBL:(dc + 1) * TBL], AF.Square)
        block.scalar(s_sqall)
        nsqall = ss.n
        if tb < 2:
            pump(4)

        def t_sqall(t, nsqall=nsqall, vgate=sv.n):
            t.wait_ge(ss.h, nsqall)
            if vgate:
                t.wait_ge(sv.h, vgate)
            for dc in range(DC):
                t.matmul(pG[0:1, 256:256 + TBL], onesb[:], ysq8[:, dc * TBL:(dc + 1) * TBL],
                         start=(dc == 0), stop=(dc == DC - 1), skip_group_check=True).then_inc(st.inc())
        block.tensor(t_sqall)
        nssq = st.n
        if tb < 2:
            pump(4)

        def s_norm(s, nssq=nssq):
            s = AS(s, ss)
            s._e.wait_ge(st.h, nssq)
            s.activation(ystat[0:1, 0:TBL], pG[0:1, 256:256 + TBL], AF.Sqrt)
        block.scalar(s_norm)
        nnorm = ss.n

        def v_rfac(v, nnorm=nnorm):
            v = AS(v, sv)
            v._e.wait_ge(ss.h, nnorm)
            v.tensor_scalar(ystat[0:1, 0:TBL], ystat[0:1, 0:TBL], 1e-12, None, ALU.max)
            v.reciprocal(ystat[0:1, TBL:2 * TBL], ystat[0:1, 0:TBL])
            v.tensor_scalar(ystat[0:1, TBL:2 * TBL], ystat[0:1, TBL:2 * TBL], 32.0, None, ALU.mult)
        block.vector(v_rfac)
        nrfac = sv.n

        def t_bcn(t, nrfac=nrfac):
            t.wait_ge(sv.h, nrfac)
            t.matmul(pE[:, 0:TBL], onesr[:], ystat[0:1, TBL:2 * TBL], start=True, stop=True,
                     skip_group_check=True).then_inc(st.inc())
        block.tensor(t_bcn)
        nbcn = st.n

        def v_xg(v, tb=tb, sl=sl, nbcn=nbcn):
            v = AS(v, sv)
            v._e.wait_ge(st.h, nbcn)
            for dc in range(DC):
                sd = slice(dc * TBL, (dc + 1) * TBL)
                v.tensor_tensor(xg[:, sd], xres[sl][:, sd], pE[:, 0:TBL], ALU.mult)
            for dc in range(DC):
                sd = slice(dc * TBL, (dc + 1) * TBL)
                v.tensor_scalar(xlb[:, sd], xg[:, sd], vecs["lgam"][:, dc:dc + 1], None, ALU.mult)
            for dc in range(DC):
                sd = slice(dc * TBL, (dc + 1) * TBL)
                v.tensor_scalar(xhb[:, sd], xg[:, sd], vecs["hgam"][:, dc:dc + 1], None, ALU.mult)
        block.vector(v_xg)
        nxg = sv.n

        if tb < 2:
            pump(4)
        for oc in range(4):
            pump(2) if tb < 2 else None
            def t_l1(t, oc=oc, nxg=nxg, bg=ss.n):
                if oc == 0:
                    t.wait_ge(sv.h, nxg)
                if bg:
                    t.wait_ge(ss.h, bg)
                for dc in range(DC):
                    t.matmul(pA[:, 0:TBL], wl1[:, dc * 512 + oc * 128: dc * 512 + (oc + 1) * 128],
                             xlb[:, dc * TBL:(dc + 1) * TBL],
                             start=(dc == 0), stop=(dc == DC - 1)).then_inc(st.inc())
            block.tensor(t_l1)
            nl1 = st.n

            def s_l1(s, oc=oc, nl1=nl1):
                s = AS(s, ss)
                s._e.wait_ge(st.h, nl1)
                s.activation(lhb[:, oc * TBL:(oc + 1) * TBL], pA[:, 0:TBL], AF.Gelu,
                             bias=vecs["lb1"][:, oc:oc + 1])
            block.scalar(s_l1)
        nlh = ss.n

        for dc in range(DC):
            def t_l2(t, dc=dc, nlh=nlh, bg=sv.n, ssg=ss.n):
                if dc == 0:
                    t.wait_ge(ss.h, nlh)
                    t.wait_ge(sv.h, bg)
                elif ssg:
                    t.wait_ge(ss.h, ssg)
                for ic in range(4):
                    t.matmul(pB[:, 0:TBL], wl2[:, ic * D + dc * 128: ic * D + (dc + 1) * 128],
                             lhb[:, ic * TBL:(ic + 1) * TBL],
                             start=(ic == 0), stop=(ic == 3)).then_inc(st.inc())
            block.tensor(t_l2)
            nl2 = st.n

            def s_l2(s, dc=dc, nl2=nl2, sg=so.n):
                s = AS(s, ss)
                s._e.wait_ge(st.h, nl2)
                if dc == 0 and sg:
                    s._e.wait_ge(so.h, sg)
                s.activation(lacc[:, dc * TBL:(dc + 1) * TBL], pB[:, 0:TBL], AF.Identity,
                             bias=vecs["lb2"][:, dc:dc + 1])
            block.scalar(s_l2)
        nlacc = ss.n

        h1_bank = [0, 0]
        for ic in range(IHC):
            bk = pC if ic % 2 == 0 else pD
            pump(3) if tb < 2 else None

            def t_h1(t, ic=ic, tb=tb, bk=bk, nxg=nxg, bg=h1_bank[ic % 2]):
                if ic == 0:
                    t.wait_ge(sv.h, nxg)
                if ic == 0 and tb == 0:
                    t.wait_ge(sgh[0].h, h1_ld)
                if bg:
                    t.wait_ge(ss.h, bg)
                for dc in range(DC):
                    t.matmul(bk[:, 0:TBL], whw1[:, dc * IH + ic * 128: dc * IH + (ic + 1) * 128],
                             xhb[:, dc * TBL:(dc + 1) * TBL],
                             start=(dc == 0), stop=(dc == DC - 1)).then_inc(st.inc())
            block.tensor(t_h1)
            nh1 = st.n

            def s_h1(s, ic=ic, bk=bk, nh1=nh1):
                s = AS(s, ss)
                s._e.wait_ge(st.h, nh1)
                s.activation(hv1[:, ic * TBL:(ic + 1) * TBL], bk[:, 0:TBL], AF.Gelu,
                             bias=vecs["hb1"][:, ic:ic + 1])
            block.scalar(s_h1)
            h1_bank[ic % 2] = ss.n
        nhv1 = ss.n

        pump(len(route_q))
        def g_ml(g, tb=tb, mrow_done=mrow_done_total, consumed=st.n):
            g.wait_ge(smr.h, mrow_done)
            if consumed:
                g.wait_ge(st.h, consumed)
            g.dma_start(out=mrow_tb[:], in_=mrow_hbm.ap()[0:1, tb * 256:(tb + 1) * 256]).then_inc(sml.inc(16), 16)
        block.gpsimd(g_ml)
        nml = sml.n

        def t_mbc(t, tb=tb, nml=nml, vg=sv.n):
            t.wait_ge(sml.h, nml)
            t.wait_ge(sv.h, vg)
            t.matmul(pE[:, 256:256 + 128], onesrb[:], mrow_tb[0:1, 0:128],
                     start=True, stop=True, skip_group_check=True).then_inc(st.inc())
            t.matmul(pE[:, 384:384 + 128], onesrb[:], mrow_tb[0:1, 128:256],
                     start=True, stop=True, skip_group_check=True).then_inc(st.inc())
        block.tensor(t_mbc)
        nmbc = st.n

        h2_bank = [0, 0]
        for dc in range(DC):
            bk = pC if dc % 2 == 0 else pD

            def t_h2(t, dc=dc, tb=tb, bk=bk, nhv1=nhv1, bg=h2_bank[dc % 2]):
                if dc == 0:
                    t.wait_ge(ss.h, nhv1)
                if dc == 0 and tb == 0:
                    t.wait_ge(sgh[1].h, h2_ld)
                if bg:
                    t.wait_ge(sv.h, bg)
                for ic in range(IHC):
                    t.matmul(bk[:, 256:256 + TBL], whw2[:, ic * D + dc * 128: ic * D + (dc + 1) * 128],
                             hv1[:, ic * TBL:(ic + 1) * TBL],
                             start=(ic == 0), stop=(ic == IHC - 1)).then_inc(st.inc())
            block.tensor(t_h2)
            nh2 = st.n

            def v_comb(v, dc=dc, bk=bk, nh2=nh2, nmbc=nmbc, nlacc=nlacc, tb=tb, sl=sl):
                v = AS(v, sv)
                v._e.wait_ge(st.h, max(nh2, nmbc))
                v._e.wait_ge(ss.h, nlacc)
                sd = slice(dc * TBL, (dc + 1) * TBL)
                v.tensor_scalar(ytmp[:, sd], bk[:, 256:256 + TBL], vecs["hb2"][:, dc:dc + 1],
                                None, ALU.add)
                v.tensor_tensor(ytmp[:, sd], ytmp[:, sd], pE[:, 256:512], ALU.mult)
                v.tensor_tensor(yacc[:, sd], ytmp[:, sd], lacc[:, sd], ALU.add)
                v.tensor_tensor(yacc[:, sd], yacc[:, sd], xres[sl][:, sd], ALU.add)
            block.vector(v_comb)
            h2_bank[dc % 2] = sv.n
        ncomb = sv.n

        def v_c2all(v, ncomb=ncomb, rms_st=nssq):
            v = AS(v, sv)
            v._e.wait_ge(st.h, rms_st)
            for dc in range(DC):
                v.tensor_copy(ybf8[:, dc * TBL:(dc + 1) * TBL], yacc[:, dc * TBL:(dc + 1) * TBL])
        block.vector(v_c2all)
        nca = sv.n

        def s_c2all(s, ncomb=ncomb, rms_st=nssq):
            s = AS(s, ss)
            s._e.wait_ge(sv.h, ncomb)
            s._e.wait_ge(st.h, rms_st)
            for dc in range(DC):
                s.activation(ysq8[:, dc * TBL:(dc + 1) * TBL],
                             yacc[:, dc * TBL:(dc + 1) * TBL], AF.Square)
        block.scalar(s_c2all)
        nsq = ss.n

        def t_s2all(t, nca=nca, nsq=nsq):
            t.wait_ge(sv.h, nca)
            t.wait_ge(ss.h, nsq)
            for dc in range(DC):
                t.matmul(pG[0:1, 0:TBL], onesb[:], ybf8[:, dc * TBL:(dc + 1) * TBL],
                         start=(dc == 0), stop=(dc == DC - 1), skip_group_check=True).then_inc(st.inc())
            for dc in range(DC):
                t.matmul(pH[0:1, 0:TBL], onesb[:], ysq8[:, dc * TBL:(dc + 1) * TBL],
                         start=(dc == 0), stop=(dc == DC - 1), skip_group_check=True).then_inc(st.inc())
        block.tensor(t_s2all)
        nst2 = st.n
        nst2_prev = nst2

        def v_mu(v, nst2=nst2):
            v = AS(v, sv)
            v._e.wait_ge(st.h, nst2)
            v.tensor_scalar(ystat[0:1, 0:TBL], pG[0:1, 0:TBL], 1.0 / 1024.0, None, ALU.mult)
            v.tensor_scalar(ystat[0:1, TBL:2 * TBL], pH[0:1, 0:TBL], 1.0 / 1024.0, None, ALU.mult)
            v.tensor_tensor(ystat[0:1, 2 * TBL:3 * TBL], ystat[0:1, 0:TBL], ystat[0:1, 0:TBL], ALU.mult)
            v.tensor_tensor(ystat[0:1, 2 * TBL:3 * TBL], ystat[0:1, TBL:2 * TBL],
                            ystat[0:1, 2 * TBL:3 * TBL], ALU.subtract)
            v.tensor_scalar(ystat[0:1, 2 * TBL:3 * TBL], ystat[0:1, 2 * TBL:3 * TBL],
                            LN_EPS, None, ALU.add)
        block.vector(v_mu)
        nvar = sv.n

        def s_sd(s, nvar=nvar):
            s = AS(s, ss)
            s._e.wait_ge(sv.h, nvar)
            s.activation(ystat[0:1, 3 * TBL:4 * TBL], ystat[0:1, 2 * TBL:3 * TBL], AF.Sqrt)
        block.scalar(s_sd)
        nstd = ss.n

        def v_rst(v, nstd=nstd):
            v = AS(v, sv)
            v._e.wait_ge(ss.h, nstd)
            v.reciprocal(ystat[0:1, TBL:2 * TBL], ystat[0:1, 3 * TBL:4 * TBL])
        block.vector(v_rst)
        nrst = sv.n

        def t_bcl(t, nrst=nrst):
            t.wait_ge(sv.h, nrst)
            t.matmul(pE[:, 0:TBL], onesr[:], ystat[0:1, 0:TBL], start=True, stop=True,
                     skip_group_check=True).then_inc(st.inc())
            t.matmul(pF[:, 0:TBL], onesr[:], ystat[0:1, TBL:2 * TBL], start=True, stop=True,
                     skip_group_check=True).then_inc(st.inc())
        block.tensor(t_bcl)
        nbcl = st.n

        def v_ap(v, nbcl=nbcl, sg=so.n):
            v = AS(v, sv)
            v._e.wait_ge(st.h, nbcl)
            if sg:
                v._e.wait_ge(so.h, sg)
            for dc in range(DC):
                sd = slice(dc * TBL, (dc + 1) * TBL)
                v.tensor_tensor(ytmp[:, sd], yacc[:, sd], pE[:, 0:TBL], ALU.subtract)
            for dc in range(DC):
                sd = slice(dc * TBL, (dc + 1) * TBL)
                v.tensor_tensor(yacc[:, sd], ytmp[:, sd], pF[:, 0:TBL], ALU.mult)
        block.vector(v_ap)
        nap = sv.n

        def s_st(s, tb=tb, nap=nap):
            s.wait_ge(sv.h, nap)
            for c in range(DC):
                s.dma_start(out=x_out_r[:, c, tb * TBL:(tb + 1) * TBL],
                            in_=yacc[:, c * TBL:(c + 1) * TBL]).then_inc(so.inc(16), 16)
        block.sync(s_st)

        if tb + 2 < NTB:
            xr_ld[tb + 2] = g_xres(tb + 2, sl, (sv.h, sv.n))

    def g_fin(g):
        g.wait_ge(so.h, so.n)
    block.gpsimd(g_fin)


def build_l1f_launch():
    nc = bass.Bass("TRN2", target_bir_lowering=False)
    x_in = nc.dram_tensor("x_in", [D, T], F32, kind="ExternalInput").ap()
    x_out = nc.dram_tensor("x_out", [D, T], F32, kind="ExternalOutput").ap()
    w_aps = dict(
        lw1=nc.dram_tensor("lw1", [1024, 512], F32, kind="ExternalInput").ap(),
        lw2=nc.dram_tensor("lw2", [512, 1024], F32, kind="ExternalInput").ap(),
        hw1=nc.dram_tensor("hw1", [1024, 4096], F32, kind="ExternalInput").ap(),
        hw2=nc.dram_tensor("hw2", [4096, 1024], F32, kind="ExternalInput").ap(),
        rt=nc.dram_tensor("rt", [128, 8], F32, kind="ExternalInput").ap(),
        identb=nc.dram_tensor("identb", [128, 128], F32, kind="ExternalInput").ap(),
        ltri=nc.dram_tensor("ltri", [128, 128], F32, kind="ExternalInput").ap(),
        halfsel=nc.dram_tensor("halfsel", [128, 32], F32, kind="ExternalInput").ap(),
        lb1=nc.dram_tensor("lb1", [128, 4], F32, kind="ExternalInput").ap(),
        lb2=nc.dram_tensor("lb2", [128, 8], F32, kind="ExternalInput").ap(),
        lgam=nc.dram_tensor("lgam", [128, 8], F32, kind="ExternalInput").ap(),
        hgam=nc.dram_tensor("hgam", [128, 8], F32, kind="ExternalInput").ap(),
        hb1=nc.dram_tensor("hb1", [128, 32], F32, kind="ExternalInput").ap(),
        hb2=nc.dram_tensor("hb2", [128, 8], F32, kind="ExternalInput").ap(),
    )
    scratch = dict(
        s_in=nc.dram_tensor("s_in", [128, 16], F32),
        s_out=nc.dram_tensor("s_out", [256, 16], F32),
        mrow_hbm=nc.dram_tensor("mrow_hbm", [1, T], BF16),
    )
    es = contextlib.ExitStack()
    with es:
        block = DirectBlock(nc)
        es.enter_context(nc.Block())
        emit_l1_ffn(nc, es, block, x_in, x_out, w_aps, scratch)
    return nc


def host_l1f_inputs(weights, core_id):
    hf = core_id % 2
    halfsel = np.zeros((128, 32), np.float32)
    halfsel[:, hf * 16:(hf + 1) * 16] = 1.0
    k = np.arange(128)
    ltri = (k[:, None] <= k[None, :]).astype(np.float32)
    return dict(
        lw1=np.ascontiguousarray(weights["L1_ffn_light_w1"], dtype=np.float32),
        lw2=np.ascontiguousarray(weights["L1_ffn_light_w2"], dtype=np.float32),
        hw1=np.ascontiguousarray(weights["L1_ffn_heavy_w1"], dtype=np.float32),
        hw2=np.ascontiguousarray(weights["L1_ffn_heavy_w2"], dtype=np.float32),
        rt=pack_vec(weights["L1_ffn_routing_token"]),
        identb=np.eye(128, dtype=np.float32),
        ltri=ltri,
        halfsel=halfsel,
        lb1=pack_vec(weights["L1_ffn_light_b1"]),
        lb2=pack_vec(weights["L1_ffn_light_b2"]),
        lgam=pack_vec(weights["L1_ffn_light_gamma"]),
        hgam=pack_vec(weights["L1_ffn_heavy_gamma"]),
        hb1=pack_vec(weights["L1_ffn_heavy_b1"]),
        hb2=pack_vec(weights["L1_ffn_heavy_b2"]),
    )


def run_l1f(x_full, weights):
    nc = build_l1f_launch()
    in_maps = []
    for c in range(NCORES):
        b, hf = c // 2, c % 2
        xs = np.ascontiguousarray(x_full[b, hf * T:(hf + 1) * T, :])
        m = host_l1f_inputs(weights, c)
        m["x_in"] = fmT(xs)
        in_maps.append(m)
    res = run_bass_kernel_spmd(nc, in_maps, core_ids=list(range(NCORES)))
    out = np.zeros((B, S, D), np.float32)
    for c in range(NCORES):
        b, hf = c // 2, c % 2
        out[b, hf * T:(hf + 1) * T, :] = np.asarray(res.results[c]["x_out"], np.float32).T
    return out


def host_l0a_inputs(weights, core_id):
    sel = np.zeros((128, 8), np.float32) if core_id % 2 == 0 else np.ones((128, 8), np.float32)
    return dict(
        wp1t=fmT(weights["L0_attn_Wp1"]),
        wp2t=fmT(weights["L0_attn_Wp2"]),
        convt=fmT(weights["L0_attn_conv_w"]),
        wat=fmT(weights["L0_attn_Wa"]),
        wxt=fmT(weights["L0_attn_Wx"]),
        wp3t=fmT(weights["L0_attn_Wp3"]),
        bp1=pack_vec(weights["L0_attn_bp1"]),
        bp2=pack_vec(weights["L0_attn_bp2"]),
        convb=pack_vec(weights["L0_attn_conv_b"]),
        ba=pack_vec(weights["L0_attn_ba"]),
        bx=pack_vec(weights["L0_attn_bx"]),
        lam=pack_vec(weights["L0_attn_Lam"]),
        bp3=pack_vec(weights["L0_attn_bp3"]),
        sel=sel,
    )


def build_l0a_launch():
    nc = bass.Bass("TRN2", target_bir_lowering=False)
    x_in = nc.dram_tensor("x_in", [D, T], F32, kind="ExternalInput").ap()
    x_out = nc.dram_tensor("x_out", [D, T], F32, kind="ExternalOutput").ap()
    w_aps = {w: nc.dram_tensor(w, [D, D], F32, kind="ExternalInput").ap()
             for w in ["wp1t", "wp2t", "convt", "wat", "wxt", "wp3t"]}
    vec_aps = {v: nc.dram_tensor(v, [128, 8], F32, kind="ExternalInput").ap()
               for v in ["bp1", "bp2", "convb", "ba", "bx", "lam", "bp3", "sel"]}
    scratch = dict(
        prod_hbm=nc.dram_tensor("prod_hbm", [128, DC * T], BF16).ap(),
        corr_hbm=nc.dram_tensor("corr_hbm", [128, DC * T], BF16).ap(),
        cc_in=nc.dram_tensor("cc_in", [128, 8], F32),
        cc_out=nc.dram_tensor("cc_out", [256, 8], F32),
    )
    es = contextlib.ExitStack()
    with es:
        block = DirectBlock(nc)
        es.enter_context(nc.Block())
        emit_l0_attn(nc, es, block, x_in, x_out, w_aps, vec_aps, scratch)
    return nc


def run_l0a(x_full, weights):
    nc = build_l0a_launch()
    in_maps = []
    for c in range(NCORES):
        b, hf = c // 2, c % 2
        xs = np.ascontiguousarray(x_full[b, hf * T:(hf + 1) * T, :])
        m = host_l0a_inputs(weights, c)
        m["x_in"] = fmT(xs)
        in_maps.append(m)
    res = run_bass_kernel_spmd(nc, in_maps, core_ids=list(range(NCORES)))
    out = np.zeros((B, S, D), np.float32)
    for c in range(NCORES):
        b, hf = c // 2, c % 2
        out[b, hf * T:(hf + 1) * T, :] = np.asarray(res.results[c]["x_out"], np.float32).T
    return out






L0A_W = ["wp1t", "wp2t", "convt", "wat", "wxt", "wp3t"]
L0A_V = ["bp1", "bp2", "convb", "ba", "bx", "lam", "bp3", "sel"]
L0F_W = {"wsl": [896, 2816], "ident": [128, 128]}
L1A_W = {"wq": [16, 1024, 128], "wk": [16, 1024, 128], "wvT": [1024, 1024], "wo": [1024, 1024],
         "bqT": [128, 16], "bkT": [128, 16], "bvrow": [1, 1024], "bo": [128, 8],
         "gamma": [128, 8], "beta": [128, 8], "lamb": [128, 1], "onesr": [1, 128]}
L1F_W = {"lw1": [1024, 512], "lw2": [512, 1024], "hw1": [1024, 4096], "hw2": [4096, 1024],
         "rt": [128, 8], "identb": [128, 128], "ltri": [128, 128], "halfsel": [128, 32],
         "lb1": [128, 4], "lb2": [128, 8], "lgam": [128, 8], "hgam": [128, 8],
         "hb1": [128, 32], "hb2": [128, 8]}


def build_fused():
    nc = bass.Bass("TRN2", target_bir_lowering=False)
    x0 = nc.dram_tensor("x0", [D, T], F32, kind="ExternalInput").ap()
    xm = [nc.dram_tensor(f"xmid{i}", [D, T], F32) for i in range(3)]
    x4 = nc.dram_tensor("x_out", [D, T], F32, kind="ExternalOutput").ap()

    w0a = {w: nc.dram_tensor("i0a_" + w, [D, D], F32, kind="ExternalInput").ap() for w in L0A_W}
    v0a = {v: nc.dram_tensor("i0a_" + v, [128, 8], F32, kind="ExternalInput").ap() for v in L0A_V}
    scr0a = dict(
        prod_hbm=nc.dram_tensor("s0a_prod", [128, DC * T], BF16).ap(),
        corr_hbm=nc.dram_tensor("s0a_corr", [128, DC * T], BF16).ap(),
        cc_in=nc.dram_tensor("s0a_ccin", [128, 8], F32),
        cc_out=nc.dram_tensor("s0a_ccout", [256, 8], F32),
    )
    w0f = {w: nc.dram_tensor("i0f_" + w, sh, F32, kind="ExternalInput").ap() for w, sh in L0F_W.items()}
    scr0f = dict(
        qag_in=nc.dram_tensor("s0f_qin", [896, 2816], BF16),
        qag_out=nc.dram_tensor("s0f_qout", [7168, 2816], BF16, addr_space="Shared"),
    )
    w1a = {w: nc.dram_tensor("i1a_" + w, sh, F32, kind="ExternalInput").ap() for w, sh in L1A_W.items()}
    scr1a = dict(
        cc_kin=nc.dram_tensor("s1a_kin", [2048, 2048], BF16),
        cc_kout=nc.dram_tensor("s1a_kout", [4096, 2048], BF16),
        cc_vin=nc.dram_tensor("s1a_vin", [2048, 1024], BF16),
        cc_vout=nc.dram_tensor("s1a_vout", [4096, 1024], BF16),
    )
    w1f = {w: nc.dram_tensor("i1f_" + w, sh, F32, kind="ExternalInput").ap() for w, sh in L1F_W.items()}
    scr1f = dict(
        s_in=nc.dram_tensor("s1f_sin", [128, 16], F32),
        s_out=nc.dram_tensor("s1f_sout", [256, 16], F32),
        mrow_hbm=nc.dram_tensor("s1f_mrow", [1, T], BF16),
    )

    es_outer = contextlib.ExitStack()
    with es_outer:
        block = DirectBlock(nc)
        es_outer.enter_context(nc.Block())
        es_w = contextlib.ExitStack()
        pwst = emit_l0f_phaseW(nc, es_w, block, w0f, scr0f, ses=es_outer)

        def l0a_hook(tag, pwst=pwst):
            if tag == "mid":
                pwst["pump"](1)
            elif tag == 7:
                pwst["finish"]()
        with contextlib.ExitStack() as es_b:
            emit_l0_attn(nc, es_b, block, x0, xm[0].ap(), w0a, v0a, scr0a, ses=es_outer,
                         hook=l0a_hook)
        es_w.close()
        pw = dict(scc=pwst["scc"], agc=pwst["state"]["agc"])
        with contextlib.ExitStack() as es_b:
            emit_l0_ffn(nc, es_b, block, xm[0].ap(), xm[1].ap(), w0f, scr0f, ses=es_outer, pw=pw)
        with contextlib.ExitStack() as es_b:
            emit_l1_attn(nc, es_b, block, xm[1].ap(), xm[2].ap(), w1a, scr1a, ses=es_outer)
        with contextlib.ExitStack() as es_b:
            emit_l1_ffn(nc, es_b, block, xm[2].ap(), x4, w1f, scr1f, ses=es_outer)
    return nc


def host_fused_inputs(weights, core_id):
    m = {}
    for k, v in host_l0a_inputs(weights, core_id).items():
        m["i0a_" + k] = v
    m["i0f_wsl"] = host_l0f_wsl(weights, core_id)
    m["i0f_ident"] = np.eye(128, dtype=np.float32)
    for k, v in host_l1a_inputs(weights).items():
        m["i1a_" + k] = v
    for k, v in host_l1f_inputs(weights, core_id).items():
        m["i1f_" + k] = v
    return m


def kernel(**inputs):
    x = np.asarray(inputs["L0_attn_x"], np.float32)
    nc = build_fused()
    in_maps = []
    for c in range(NCORES):
        b, hf = c // 2, c % 2
        m = host_fused_inputs(inputs, c)
        m["x0"] = fmT(np.ascontiguousarray(x[b, hf * T:(hf + 1) * T, :]))
        in_maps.append(m)
    res = run_bass_kernel_spmd(nc, in_maps, core_ids=list(range(NCORES)))
    out = np.zeros((B, S, D), np.float32)
    for c in range(NCORES):
        b, hf = c // 2, c % 2
        out[b, hf * T:(hf + 1) * T, :] = np.asarray(res.results[c]["x_out"], np.float32).T
    return np.ascontiguousarray(out)


def kernel_unfused(**inputs):
    x = np.asarray(inputs["L0_attn_x"], np.float32)
    x = run_l0a(x, inputs)
    x = run_l0f(x, inputs)
    x = run_l1a(x, inputs)
    x = run_l1f(x, inputs)
    return np.ascontiguousarray(x, dtype=np.float32)
